# version 01
# baseline (speedup 1.0000x reference)
import contextlib
import numpy as np
import ml_dtypes

import concourse.bass as bass
import concourse.mybir as mybir
from concourse.bass_utils import run_bass_kernel_spmd

F32 = mybir.dt.float32
BF16 = mybir.dt.bfloat16
I32 = mybir.dt.int32
AF = mybir.ActivationFunctionType
ALU = mybir.AluOpType
AX = mybir.AxisListType

T = 2048
D = 1024
NT = T // 128
ND = D // 128

BF = ml_dtypes.bfloat16


class Prog:

    ENGINES = ("tensor", "vector", "scalar", "gpsimd", "sync")

    def __init__(self, nc):
        self.nc = nc
        self.ops = []
        self.buf = {}
        self.out_dmas = []

    @staticmethod
    def key(t):
        if isinstance(t, str):
            return t
        if hasattr(t, "tensor"):
            t = t.tensor
        return t.name

    def op(self, engine, fn, reads=(), writes=(), is_dma=False, is_out=False,
           extra_deps=()):
        opid = len(self.ops)
        deps = set(int(d) for d in extra_deps)
        rk = [self.key(t) for t in reads]
        wk = [self.key(t) for t in writes]
        for k in rk:
            st = self.buf.get(k)
            if st is not None and st["w"] is not None:
                deps.add(st["w"])
        for k in wk:
            st = self.buf.get(k)
            if st is not None:
                if st["w"] is not None:
                    deps.add(st["w"])
                deps.update(st["r"])
        for k in rk:
            st = self.buf.setdefault(k, {"w": None, "r": []})
            st["r"].append(opid)
        for k in wk:
            self.buf[k] = {"w": opid, "r": []}
        deps.discard(opid)
        self.ops.append(dict(engine=engine, fn=fn, deps=sorted(deps),
                             is_dma=is_dma))
        if is_out:
            self.out_dmas.append(opid)
        return opid

    def dma(self, engine, fn, reads=(), writes=(), is_out=False,
            extra_deps=()):
        return self.op(engine, fn, reads, writes, is_dma=True, is_out=is_out,
                       extra_deps=extra_deps)

    def full_barrier(self):
        last = {}
        for i, o in enumerate(self.ops):
            last[o["engine"]] = i
        deps = sorted(last.values())
        for e in self.ENGINES:
            self.op(e, lambda eng: eng.nop(), extra_deps=deps)

    def alias_barrier(self, old_names, new_name, extra=()):
        carry = set(extra)
        for k in old_names:
            st = self.buf.get(self.key(k))
            if st is not None:
                if st["w"] is not None:
                    carry.add(st["w"])
                carry.update(st["r"])
        self.buf[self.key(new_name)] = {"w": None, "r": sorted(carry)}

    def finalize(self):
        nc = self.nc
        n = len(self.ops)
        needs_inc = [False] * n
        for i, o in enumerate(self.ops):
            for d in o["deps"]:
                dop = self.ops[d]
                if (not dop["is_dma"] and dop["engine"] == "tensor"
                        and o["engine"] == "tensor" and not o["is_dma"]):
                    continue
                needs_inc[d] = True
        for i in self.out_dmas:
            needs_inc[i] = True
        for i, o in enumerate(self.ops):
            if o["is_dma"]:
                needs_inc[i] = True

        NSLOT = 8
        incidx = [0] * n
        dma_slot = [0] * n
        counters = {}
        dma_counters = {}
        for i, o in enumerate(self.ops):
            if o["is_dma"]:
                k = dma_counters.get(o["engine"], 0)
                dma_counters[o["engine"]] = k + 1
                slot = k % NSLOT
                dma_slot[i] = slot
                skey = (o["engine"], True, slot)
                counters[skey] = counters.get(skey, 0) + 1
                incidx[i] = counters[skey]
            elif needs_inc[i]:
                skey = (o["engine"], False)
                counters[skey] = counters.get(skey, 0) + 1
                incidx[i] = counters[skey]

        stack = contextlib.ExitStack()
        sems = {}
        for e in self.ENGINES:
            sems[(e, False)] = stack.enter_context(nc.semaphore(f"s_{e}"))
            for s in range(NSLOT):
                sems[(e, True, s)] = stack.enter_context(
                    nc.semaphore(f"d_{e}{s}"))

        per_engine = {e: [] for e in self.ENGINES}
        for i, o in enumerate(self.ops):
            per_engine[o["engine"]].append(i)

        ops = self.ops
        incs_total = sum(1 for x in needs_inc if x)
        watermark = {e: {} for e in self.ENGINES}

        def emit_engine(eng_name, eng):
            wm = watermark[eng_name]
            for i in per_engine[eng_name]:
                o = ops[i]
                for d in o["deps"]:
                    dop = ops[d]
                    if (not dop["is_dma"] and dop["engine"] == "tensor"
                            and eng_name == "tensor" and not o["is_dma"]):
                        continue
                    if dop["is_dma"]:
                        skey = (dop["engine"], True, dma_slot[d])
                    else:
                        skey = (dop["engine"], False)
                    tgt = incidx[d] * (16 if dop["is_dma"] else 1)
                    if tgt == 0:
                        raise RuntimeError(f"op {i} depends on {d}: no inc")
                    if wm.get(skey, 0) < tgt:
                        eng.wait_ge(sems[skey], tgt)
                        wm[skey] = tgt
                if o["is_dma"]:
                    skey = (eng_name, True, dma_slot[i])
                    prev = (incidx[i] - 1) * 16
                    if prev > 0 and wm.get(skey, 0) < prev:
                        eng.wait_ge(sems[skey], prev)
                        wm[skey] = prev
                inst = o["fn"](eng)
                if o["is_dma"]:
                    inst.then_inc(sems[(eng_name, True, dma_slot[i])], 16)
                elif needs_inc[i]:
                    inst.then_inc(sems[(eng_name, False)], 1)
            if eng_name == "sync":
                done = {}
                for i in self.out_dmas:
                    skey = (ops[i]["engine"], True, dma_slot[i])
                    done[skey] = max(done.get(skey, 0), incidx[i] * 16)
                for skey, tgt in sorted(done.items()):
                    if wm.get(skey, 0) < tgt:
                        eng.wait_ge(sems[skey], tgt)

        with nc.Block() as block:
            block.vector(lambda e: emit_engine("vector", e))
            block.scalar(lambda e: emit_engine("scalar", e))
            block.tensor(lambda e: emit_engine("tensor", e))
            block.gpsimd(lambda e: emit_engine("gpsimd", e))
            block.sync(lambda e: emit_engine("sync", e))
        stack.close()


_ISIZE = {F32: 4, BF16: 2, I32: 4}


class Ctx:
    def __init__(self, nc, p, dev_taps=()):
        self.nc = nc
        self.p = p
        self.sb = {}
        self.hbm = {}
        self.consts = {}
        self.consts32 = {}
        self.dev_taps = set(dev_taps)
        self.taps = {}
        self.fconst_vals = {}
        self.extra_in = {}
        self.extra_out = {}
        self.dyn_started = False
        self.freelist = []
        self.live = {}
        self.dead = []

    def start_dynamic(self):
        self.dyn_started = True
        base = self.nc.sbuf_base
        top = self.nc.sbuf_top
        self.freelist = [[base, top - base]]

    def alloc(self, name, shape, dtype):
        if not self.dyn_started:
            h = self.nc.alloc_sbuf_tensor(name, shape, dtype)
            self.sb[name] = h
            return h
        size = int(np.prod(shape[1:])) * _ISIZE[dtype] if len(shape) > 1 \
            else _ISIZE[dtype]
        size = (size + 31) // 32 * 32
        for blk in self.freelist:
            if blk[1] >= size:
                off = blk[0]
                blk[0] += size
                blk[1] -= size
                break
        else:
            raise RuntimeError(
                f"arena: no space for {name} ({size}B); "
                f"free={[(b[0], b[1]) for b in self.freelist]}")
        olds = []
        guards = set()
        newdead = []
        for rec in self.dead:
            (o, s, n, g) = rec
            if o < off + size and off < o + s:
                olds.append(n)
                guards.update(g)
                if o < off:
                    newdead.append((o, off - o, n, g))
                if o + s > off + size:
                    newdead.append((off + size, o + s - (off + size), n, g))
            else:
                newdead.append(rec)
        self.dead = newdead
        self._pending_guards = sorted(guards)
        h = self.nc.alloc_sbuf_tensor_at(name, list(shape), dtype,
                                         offset=off)
        self.sb[name] = h
        self.live[name] = (off, size, h.name)
        if not hasattr(self, "alloc_log"):
            self.alloc_log = []
        self.alloc_log.append((name, off, size, tuple(shape), dtype))
        if olds or self._pending_guards:
            self.p.alias_barrier(olds, h.name, extra=self._pending_guards)
        return h

    def free(self, *names):
        last = {}
        for i, o in enumerate(self.p.ops):
            last[o["engine"]] = i
        guard = tuple(sorted(last.values()))
        for name in names:
            off, size, mangled = self.live.pop(name)
            self.dead.append((off, size, mangled, guard))
            self.freelist.append([off, size])
            self.freelist.sort()
            merged = []
            for blk in self.freelist:
                if merged and merged[-1][0] + merged[-1][1] == blk[0]:
                    merged[-1][1] += blk[1]
                else:
                    merged.append(blk)
            self.freelist = merged

    def state_in(self, name, shape, dtype=F32):
        ap = self.nc.dram_tensor(name, list(shape), dtype,
                                 kind="ExternalInput").ap()
        self.extra_in[name] = shape
        return ap

    def state_out(self, name, shape, dtype=F32):
        ap = self.nc.dram_tensor(name, list(shape), dtype,
                                 kind="ExternalOutput").ap()
        self.extra_out[name] = shape
        return ap

    def fconst(self, value):
        v = float(value)
        if v not in self.fconst_vals:
            assert len(self.fconst_vals) < self.sb["fconsts"].shape[1]
            self.fconst_vals[v] = len(self.fconst_vals)
        j = self.fconst_vals[v]
        return self.sb["fconsts"][:, j:j + 1]

    def psum(self, name, shape, dtype=F32):
        h = self.nc.alloc_psum_tensor(name, shape, dtype)
        self.sb[name] = h
        return h

    def tap_full(self, name, sbuf_t):
        if name not in self.dev_taps:
            return
        sh = list(sbuf_t.shape)
        out = self.nc.dram_tensor(f"tap_{name}", sh, sbuf_t.dtype,
                                  kind="ExternalOutput").ap()
        self.p.dma("sync", lambda e: e.dma_start(out=out, in_=sbuf_t[:]),
                   reads=[sbuf_t], is_out=True)
        self.taps[name] = sh

    def tile_tap(self, name, tile_ap, tb, reads):
        if name not in self.dev_taps:
            return
        if name not in self.taps:
            self.nc.dram_tensor(f"tap_{name}", [128, NT, D], F32,
                                kind="ExternalOutput")
            self.taps[name] = [128, NT, D]
        dram = self.nc.get_tensor(f"tap_{name}") if hasattr(self.nc, "get_tensor") else None
        if not hasattr(self, "_tap_handles"):
            self._tap_handles = {}
        if name not in self._tap_handles:
            pass
        raise NotImplementedError


def stage_view(ctx, rows, cols):
    ws = ctx.sb["wstage"]
    assert rows <= 128 and cols <= 2048
    return ws[0:rows, 0:cols]


def load_convert(ctx, dst_bf_ap, hbm_ap, rows, cols, tag=""):
    p = ctx.p
    ws = ctx.sb["wstage"]
    sv = ws[0:rows, 0:cols]
    p.dma("sync", lambda e: e.dma_start(out=sv, in_=hbm_ap),
          reads=[], writes=[ws])
    p.op("gpsimd", lambda e: e.tensor_copy(out=dst_bf_ap, in_=sv),
         reads=[ws], writes=[dst_bf_ap])


def load_weight_bf16(ctx, name, hbm_ap, rows, cols):
    p = ctx.p
    rc = rows // 128
    bf = ctx.alloc(f"{name}_bf", [128, rc, cols], BF16)
    ws = ctx.sb["wstage"]
    piece = min(cols, 1024 // rc)
    for c0 in range(0, cols, piece):
        cw = min(piece, cols - c0)
        sv = ws[:, 0:rc * cw].rearrange("p (c m) -> p c m", c=rc)
        hb = hbm_ap[:, c0:c0 + cw].rearrange("(c p) m -> p c m", p=128)
        p.dma("sync", lambda e, sv=sv, hb=hb: e.dma_start(out=sv, in_=hb),
              reads=[], writes=[ws])
        dst = bf[:, :, c0:c0 + cw]
        p.op("gpsimd", lambda e, dst=dst, sv=sv: e.tensor_copy(
            out=dst, in_=sv), reads=[ws], writes=[bf])
    return bf


def pe_transpose(ctx, out_ap, in_ap, n_in_part, n_in_free):
    p = ctx.p
    pst = ctx.sb["ps_tr"]
    ident = ctx.consts["ident"]
    p.op("tensor", lambda e: e.transpose(
        pst.bitcast(BF16)[0:n_in_free, 0:n_in_part], in_ap,
        ident[0:n_in_part, 0:n_in_part]),
        reads=[in_ap, ident], writes=[pst])
    p.op("vector", lambda e: e.tensor_copy(
        out=out_ap, in_=pst.bitcast(BF16)[0:n_in_free, 0:n_in_part]),
        reads=[pst], writes=[out_ap])


def emit_ln(ctx):
    p = ctx.p
    x = ctx.sb["x_tm"]
    xn_fm = ctx.sb["xn_fm"]
    mu = ctx.sb["ln_mu"]
    rstd = ctx.sb["ln_rstd"]
    cen = ctx.sb["scratch_f32_a"]
    sq = ctx.sb["scratch_f32_b"]
    stage = ctx.sb["stage_bf"]
    col = ctx.sb["scratch_col"]
    for i in range(NT):
        xt = x[:, i, :]
        mcol = mu[:, i:i + 1]
        rcol = rstd[:, i:i + 1]
        p.op("vector", lambda e, xt=xt: e.reduce_sum(
            out=col[:, 0:1], in_=xt, axis=AX.X), reads=[x], writes=[col])
        p.op("vector", lambda e, mcol=mcol: e.tensor_scalar_mul(
            out=mcol, in0=col[:, 0:1], scalar1=1.0 / D),
            reads=[col], writes=[mu])
        p.op("vector", lambda e, xt=xt, mcol=mcol: e.tensor_scalar(
            out=cen[:], in0=xt, scalar1=mcol, scalar2=None,
            op0=ALU.subtract), reads=[x, mu], writes=[cen])
        p.op("scalar", lambda e: e.activation(
            out=sq[:], in_=cen[:], func=AF.Square), reads=[cen], writes=[sq])
        p.op("vector", lambda e: e.reduce_sum(
            out=col[:, 1:2], in_=sq[:], axis=AX.X), reads=[sq], writes=[col])
        epsc = ctx.fconst(1e-5)
        p.op("scalar", lambda e, epsc=epsc: e.activation(
            out=col[:, 2:3], in_=col[:, 1:2], func=AF.Sqrt,
            bias=epsc, scale=1.0 / D),
            reads=[col, "fconsts"], writes=[col])
        p.op("vector", lambda e, rcol=rcol: e.reciprocal(
            out=rcol, in_=col[:, 2:3]), reads=[col], writes=[rstd])
        p.op("scalar", lambda e, rcol=rcol: e.activation(
            out=stage[:], in_=cen[:], func=AF.Copy, scale=rcol),
            reads=[cen, rstd], writes=[stage])
        for c in range(ND):
            pe_transpose(ctx, xn_fm[:, c, i * 128:(i + 1) * 128],
                         stage[:, c * 128:(c + 1) * 128], 128, 128)


def xn_rec(ctx, i, out_tile, half=None):
    p = ctx.p
    x = ctx.sb["x_tm"]
    mu = ctx.sb["ln_mu"]
    rstd = ctx.sb["ln_rstd"]
    src = x[:, i, :] if half is None else x[:, i, half * 512:(half + 1) * 512]
    p.op("vector", lambda e: e.tensor_scalar(
        out=out_tile, in0=src, scalar1=mu[:, i:i + 1],
        scalar2=rstd[:, i:i + 1], op0=ALU.subtract, op1=ALU.mult),
        reads=[x, mu, rstd], writes=[out_tile])



def flash256(ctx, Qp, Kp_fn, V_fn, bias_a, bias_b, o_sink, kts_for_qb,
             mask_for, scale=0.125):
    p = ctx.p
    PT = [ctx.sb["fl_PT0"], ctx.sb["fl_PT1"]]
    ps_sa = ctx.sb["ps_sa"]
    ps_sb = ctx.sb["ps_sb"]
    ps_ava = ctx.sb["ps_ava"]
    ps_avb = ctx.sb["ps_avb"]
    rc_a = ctx.sb["fl_rc_a"]
    rc_b = ctx.sb["fl_rc_b"]
    for qb in range(8):
        qsl = slice(qb * 256, (qb + 1) * 256)
        kts = kts_for_qb(qb)
        for g0 in range(0, len(kts), 4):
            grp = kts[g0:g0 + 4]
            for j, kt in enumerate(grp):
                ksl = slice(kt * 128, (kt + 1) * 128)
                psl = slice(j * 256, (j + 1) * 256)
                p.op("tensor", lambda e, ksl=ksl, qsl=qsl, psl=psl: e.matmul(
                    ps_sa[:, psl], Kp_fn(0, ksl), Qp[0:64, qsl],
                    start=True, stop=True, tile_position=(0, 0)),
                    reads=[ctx.sb["fl_K0"], Qp], writes=[ps_sa])
                p.op("tensor", lambda e, ksl=ksl, qsl=qsl, psl=psl: e.matmul(
                    ps_sb[:, psl], Kp_fn(1, ksl), Qp[64:128, qsl],
                    start=True, stop=True, tile_position=(64, 0)),
                    reads=[ctx.sb["fl_K1"], Qp], writes=[ps_sb])
            n = len(grp)
            for h2, (pssrc, biast) in enumerate(((ps_sa, bias_a),
                                                 (ps_sb, bias_b))):
                p.op("scalar", lambda e, h2=h2, pssrc=pssrc, biast=biast,
                     n=n, g0=g0: e.activation(
                         out=PT[h2][:, g0:g0 + n, :],
                         in_=pssrc[:, 0:n * 256].rearrange(
                             "p (k q) -> p k q", k=n),
                         func=AF.Exp, bias=biast[:], scale=scale),
                     reads=[pssrc, biast], writes=[PT[h2]])
        for j, kt in enumerate(kts):
            for qs in range(2):
                m = mask_for(qb, kt, qs)
                if m is None or isinstance(m, str):
                    continue
                for h2 in range(2):
                    p.op("vector", lambda e, h2=h2, j=j, qs=qs, m=m:
                         e.tensor_tensor(
                             out=PT[h2][:, j, qs * 128:(qs + 1) * 128],
                             in0=PT[h2][:, j, qs * 128:(qs + 1) * 128],
                             in1=m[:], op=ALU.mult),
                         reads=[PT[h2], "c_tril_incl", "c_trilx"],
                         writes=[PT[h2]])
        for qs in range(2):
            qtile = qb * 2 + qs
            for h2, (psav, rc) in enumerate(((ps_ava, rc_a),
                                             (ps_avb, rc_b))):
                valid = [(j, kt) for j, kt in enumerate(kts)
                         if not isinstance(mask_for(qb, kt, qs), str)]
                for vi, (j, kt) in enumerate(valid):
                    p.op("tensor", lambda e, h2=h2, psav=psav, j=j, kt=kt,
                         qs=qs, first=(vi == 0), last=(vi == len(valid) - 1):
                         e.matmul(
                             psav[:, 0:65],
                             PT[h2][:, j, qs * 128:(qs + 1) * 128],
                             V_fn(h2, kt), start=first, stop=last),
                         reads=[PT[h2], ctx.sb["fl_V"]], writes=[psav])
                p.op("vector", lambda e, psav=psav, rc=rc: e.reciprocal(
                    out=rc[:], in_=psav[:, 64:65]), reads=[psav], writes=[rc])
            o_sink(qtile, ps_ava, ps_avb, rc_a, rc_b)


def emit_layer0(ctx):
    p = ctx.p
    emit_ln(ctx)
    xn_fm = ctx.sb["xn_fm"]
    ident = ctx.consts["ident"]
    onesrow = ctx.consts["onesrow"]

    ps_sa = ctx.sb["ps_sa"]
    ps_sb = ctx.sb["ps_sb"]
    ps_ava = ctx.sb["ps_ava"]
    ps_avb = ctx.sb["ps_avb"]
    ps_m1 = ctx.sb["ps_m1"]
    ps_m2 = ctx.sb["ps_m2"]
    rw_bf = ctx.alloc("l0_rw_bf", [8, 1024], BF16)
    load_convert(ctx, rw_bf[:], ctx.hbm["L0_ffn_router_w"], 8, 1024)
    rwT = ctx.alloc("l0_rwT", [128, 8, 8], BF16)
    for c in range(8):
        pe_transpose(ctx, rwT[:, c, :], rw_bf[:, c * 128:(c + 1) * 128],
                     8, 128)
    for c in range(8):
        p.op("tensor", lambda e, c=c: e.matmul(
            ps_m1[0:1, 0:8], xn_fm[:, c, 0:1], rwT[:, c, :],
            start=(c == 0), stop=(c == 7)),
            reads=[xn_fm, rwT], writes=[ps_m1])
    gl = ctx.alloc("l0_gates", [1, 64], F32)
    p.op("vector", lambda e: e.tensor_copy(out=gl[:, 0:8],
                                           in_=ps_m1[0:1, 0:8]),
         reads=[ps_m1], writes=[gl])
    p.op("vector", lambda e: e.reduce_max(out=gl[:, 8:9], in_=gl[:, 0:8],
                                          axis=AX.X), reads=[gl], writes=[gl])
    p.op("vector", lambda e: e.tensor_scalar(
        out=gl[:, 16:24], in0=gl[:, 0:8], scalar1=gl[:, 8:9], scalar2=None,
        op0=ALU.is_equal), reads=[gl], writes=[gl])
    p.op("vector", lambda e: e.scalar_tensor_tensor(
        out=gl[:, 24:32], in0=gl[:, 16:24], scalar=-1e30, in1=gl[:, 0:8],
        op0=ALU.mult, op1=ALU.add), reads=[gl], writes=[gl])
    p.op("vector", lambda e: e.reduce_max(out=gl[:, 9:10], in_=gl[:, 24:32],
                                          axis=AX.X), reads=[gl], writes=[gl])
    p.op("vector", lambda e: e.tensor_scalar(
        out=gl[:, 32:40], in0=gl[:, 0:8], scalar1=gl[:, 9:10], scalar2=None,
        op0=ALU.is_ge), reads=[gl], writes=[gl])
    p.op("vector", lambda e: e.tensor_scalar(
        out=gl[:, 40:48], in0=gl[:, 0:8], scalar1=gl[:, 8:9], scalar2=None,
        op0=ALU.subtract), reads=[gl], writes=[gl])
    p.op("scalar", lambda e: e.activation(
        out=gl[:, 48:56], in_=gl[:, 40:48], func=AF.Exp),
        reads=[gl], writes=[gl])
    p.op("vector", lambda e: e.tensor_mul(
        out=gl[:, 48:56], in0=gl[:, 48:56], in1=gl[:, 32:40]),
        reads=[gl], writes=[gl])
    p.op("vector", lambda e: e.reduce_sum(out=gl[:, 10:11], in_=gl[:, 48:56],
                                          axis=AX.X), reads=[gl], writes=[gl])
    p.op("vector", lambda e: e.reciprocal(out=gl[:, 11:12], in_=gl[:, 10:11]),
         reads=[gl], writes=[gl])
    p.op("vector", lambda e: e.tensor_scalar(
        out=gl[:, 56:64], in0=gl[:, 48:56], scalar1=gl[:, 11:12],
        scalar2=None, op0=ALU.mult), reads=[gl], writes=[gl])
    g_bf = ctx.alloc("l0_g_bf", [1, 8], BF16)
    p.op("vector", lambda e: e.tensor_copy(out=g_bf[:], in_=gl[:, 56:64]),
         reads=[gl], writes=[g_bf])
    gcol8 = ctx.alloc("l0_gcol8", [8, 1], BF16)
    pe_transpose(ctx, gcol8[:], g_bf[:], 1, 8)
    e16 = ctx.consts["e16sel"]
    p.op("tensor", lambda e: e.matmul(ps_m1[:, 0:1], e16[:], gcol8[:],
                                      start=True, stop=True),
         reads=[e16, gcol8], writes=[ps_m1])
    gcol = ctx.alloc("l0_gcol", [128, 1], F32)
    p.op("vector", lambda e: e.tensor_copy(out=gcol[:], in_=ps_m1[:, 0:1]),
         reads=[ps_m1], writes=[gcol])

    ld_bf = ctx.alloc("l0_ld_bf", [128, 1024], BF16)
    load_convert(ctx, ld_bf[:],
                 ctx.hbm["L0_ffn_lora_down"].rearrange("e r h -> (e r) h"),
                 128, 1024)
    ldT = ctx.alloc("l0_ldT", [128, 8, 128], BF16)
    for c in range(8):
        pe_transpose(ctx, ldT[:, c, :], ld_bf[:, c * 128:(c + 1) * 128],
                     128, 128)
    down_fm = ctx.alloc("l0_down_fm", [128, T], BF16)
    for tb in range(4):
        tsl = slice(tb * 512, (tb + 1) * 512)
        for c in range(8):
            p.op("tensor", lambda e, tsl=tsl, c=c: e.matmul(
                ps_m1[:, :], ldT[:, c, :], xn_fm[:, c, tsl],
                start=(c == 0), stop=(c == 7)),
                reads=[ldT, xn_fm], writes=[ps_m1])
        dst = down_fm[:, tsl]
        p.op("vector", lambda e, dst=dst: e.tensor_scalar(
            out=dst, in0=ps_m1[:, :], scalar1=gcol[:], scalar2=None,
            op0=ALU.mult), reads=[ps_m1, gcol], writes=[down_fm])

    up_bf = ctx.alloc("l0_up_bf", [128, 64, 16], BF16)
    ws = ctx.sb["wstage"]
    sv_up = ws[:, 0:1024].rearrange("p (c r) -> p c r", r=16)
    hb_up = ctx.hbm["L0_ffn_lora_up"].rearrange(
        "e h r -> (e h) r").rearrange("(c p) r -> p c r", p=128)
    p.dma("sync", lambda e: e.dma_start(out=sv_up, in_=hb_up), writes=[ws])
    p.op("gpsimd", lambda e: e.tensor_copy(out=up_bf[:], in_=sv_up),
         reads=[ws], writes=[up_bf])
    upstage = ctx.alloc("l0_upstage", [16, 8, 128], BF16)
    upg = ctx.alloc("l0_upg", [128, D], BF16)
    for ee in range(8):
        for j in range(8):
            pe_transpose(ctx, upstage[:, j, :], up_bf[:, ee * 8 + j, :],
                         128, 16)
        p.dma("sync", lambda e, ee=ee: e.dma_start(
            out=upg[ee * 16:(ee + 1) * 16, :], in_=upstage[:]),
            reads=[upstage], writes=[upg])

    ctx.free("l0_ld_bf", "l0_ldT", "l0_up_bf", "l0_upstage", "l0_rw_bf",
             "l0_rwT", "l0_gates", "l0_g_bf", "l0_gcol8")
    O_fm = ctx.alloc("l0_O_fm", [128, ND, T], BF16)
    Qp = ctx.alloc("l0_Qp", [128, T], BF16)
    Kp = ctx.alloc("l0_Kp", [128, T], BF16)
    Vp = ctx.alloc("l0_Vp", [128, NT, 130], BF16)
    ctx.alloc("fl_PT0", [128, 16, 256], BF16)
    ctx.alloc("fl_PT1", [128, 16, 256], BF16)
    ctx.alloc("fl_rc_a", [128, 1], F32)
    ctx.alloc("fl_rc_b", [128, 1], F32)
    wq_bf = ctx.alloc("l0_wq_bf", [128, 8, 128], BF16)
    wk_bf = ctx.alloc("l0_wk_bf", [128, 8, 128], BF16)
    wv_bf = ctx.alloc("l0_wv_bf", [128, 8, 128], BF16)
    nsc = ctx.alloc("l0_nsc", [1, 64], F32)
    nsc_bf = ctx.alloc("l0_nscbf", [1, 2], BF16)
    bias_a = ctx.alloc("l0_bias_a", [128, 1], F32)
    bias_b = ctx.alloc("l0_bias_b", [128, 1], F32)
    sqst = ctx.sb["stage_bf"]
    onesA = ctx.consts["onesA"]
    onesB = ctx.consts["onesB"]
    stageO = ctx.alloc("l0_stageO", [128, 128], BF16)

    for hp in range(8):
        for wname, wbf in (("L0_attn_Wq", wq_bf), ("L0_attn_Wk", wk_bf),
                           ("L0_attn_Wv", wv_bf)):
            hb = ctx.hbm[wname][:, hp * 128:(hp + 1) * 128].rearrange(
                "(c p) m -> p c m", p=128)
            ws = ctx.sb["wstage"]
            sv = ws[:, 0:1024].rearrange("p (c m) -> p c m", c=8)
            p.dma("sync", lambda e, hb=hb, sv=sv: e.dma_start(
                out=sv, in_=hb), writes=[ws])
            p.op("gpsimd", lambda e, wbf=wbf, sv=sv: e.tensor_copy(
                out=wbf[:], in_=sv), reads=[ws], writes=[wbf])
        for (wbf, dstp) in ((wq_bf, Qp), (wk_bf, Kp)):
            for tb in range(4):
                tsl = slice(tb * 512, (tb + 1) * 512)
                for c in range(8):
                    p.op("tensor", lambda e, wbf=wbf, tsl=tsl, c=c: e.matmul(
                        ps_m2[:, :], wbf[:, c, :], xn_fm[:, c, tsl],
                        start=(c == 0), stop=(c == 7)),
                        reads=[wbf, xn_fm], writes=[ps_m2])
                dst = dstp[:, tsl]
                p.op("vector", lambda e, dst=dst: e.tensor_copy(
                    out=dst, in_=ps_m2[:, :]),
                    reads=[ps_m2], writes=[dstp])
        for tb in range(NT):
            for c in range(8):
                p.op("tensor", lambda e, tb=tb, c=c: e.matmul(
                    ps_m2[:, 0:128],
                    xn_fm[:, c, tb * 128:(tb + 1) * 128],
                    wv_bf[:, c, :], start=(c == 0), stop=(c == 7)),
                    reads=[wv_bf, xn_fm], writes=[ps_m2])
            p.op("vector", lambda e, tb=tb: e.tensor_copy(
                out=Vp[:, tb, 0:64], in_=ps_m2[:, 0:64]),
                reads=[ps_m2], writes=[Vp])
            p.op("vector", lambda e, tb=tb: e.tensor_copy(
                out=Vp[:, tb, 65:129], in_=ps_m2[:, 64:128]),
                reads=[ps_m2], writes=[Vp])
        p.op("vector", lambda e: e.memset(Vp[:, :, 64:65], 1.0), writes=[Vp])
        p.op("vector", lambda e: e.memset(Vp[:, :, 129:130], 1.0),
             writes=[Vp])

        for si, src in enumerate((Qp, Kp)):
            for tb in range(4):
                sl = slice(tb * 512, (tb + 1) * 512)
                p.op("scalar", lambda e, src=src, sl=sl: e.activation(
                    out=sqst[:, 0:512], in_=src[:, sl], func=AF.Square),
                    reads=[src], writes=[sqst])
                for h, ones in enumerate((onesA, onesB)):
                    p.op("tensor", lambda e, ones=ones: e.matmul(
                        ps_m1[0:1, :], ones[:], sqst[:, 0:512],
                        start=True, stop=True),
                        reads=[sqst, ones], writes=[ps_m1])
                    o = (si * 2 + h) * 4 + tb
                    p.op("vector", lambda e, o=o: e.reduce_max(
                        out=nsc[:, o:o + 1], in_=ps_m1[0:1, :], axis=AX.X),
                        reads=[ps_m1], writes=[nsc])
        for h in range(2):
            o = 16 + h * 8
            qs4 = slice(h * 4, h * 4 + 4)
            ks4 = slice(8 + h * 4, 8 + h * 4 + 4)
            p.op("vector", lambda e, o=o, qs4=qs4: e.reduce_max(
                out=nsc[:, o:o + 1], in_=nsc[:, qs4], axis=AX.X),
                reads=[nsc], writes=[nsc])
            p.op("vector", lambda e, o=o, ks4=ks4: e.reduce_max(
                out=nsc[:, o + 1:o + 2], in_=nsc[:, ks4], axis=AX.X),
                reads=[nsc], writes=[nsc])
            p.op("vector", lambda e, o=o: e.tensor_mul(
                out=nsc[:, o + 2:o + 3], in0=nsc[:, o:o + 1],
                in1=nsc[:, o + 1:o + 2]), reads=[nsc], writes=[nsc])
            p.op("scalar", lambda e, o=o: e.activation(
                out=nsc[:, o + 3:o + 4], in_=nsc[:, o + 2:o + 3],
                func=AF.Sqrt), reads=[nsc], writes=[nsc])
            p.op("vector", lambda e, o=o: e.tensor_scalar_mul(
                out=nsc[:, o + 4:o + 5], in0=nsc[:, o + 3:o + 4],
                scalar1=-0.125), reads=[nsc], writes=[nsc])
        p.op("vector", lambda e: e.tensor_copy(out=nsc_bf[:, 0:1],
                                               in_=nsc[:, 20:21]),
             reads=[nsc], writes=[nsc_bf])
        p.op("vector", lambda e: e.tensor_copy(out=nsc_bf[:, 1:2],
                                               in_=nsc[:, 28:29]),
             reads=[nsc], writes=[nsc_bf])
        for biast, cs in ((bias_a, slice(0, 1)), (bias_b, slice(1, 2))):
            p.op("tensor", lambda e, cs=cs: e.matmul(
                ps_m1[:, 0:1], onesrow[:], nsc_bf[:, cs],
                start=True, stop=True),
                reads=[nsc_bf, onesrow], writes=[ps_m1])
            p.op("vector", lambda e, biast=biast: e.tensor_copy(
                out=biast[:], in_=ps_m1[:, 0:1]),
                reads=[ps_m1], writes=[biast])

        ctx.sb["fl_K0"] = Kp
        ctx.sb["fl_K1"] = Kp
        ctx.sb["fl_V"] = Vp

        def l0_sink(qtile, psa, psb, rca, rcb, hp=hp):
            p.op("vector", lambda e: e.tensor_scalar(
                out=stageO[:, 0:64], in0=psa[:, 0:64], scalar1=rca[:],
                scalar2=None, op0=ALU.mult),
                reads=[psa, rca], writes=[stageO])
            p.op("vector", lambda e: e.tensor_scalar(
                out=stageO[:, 64:128], in0=psb[:, 0:64], scalar1=rcb[:],
                scalar2=None, op0=ALU.mult),
                reads=[psb, rcb], writes=[stageO])
            pe_transpose(ctx, O_fm[:, hp, qtile * 128:(qtile + 1) * 128],
                         stageO[:], 128, 128)

        flash256(ctx, Qp,
                 lambda h2, ksl: Kp[h2 * 64:(h2 + 1) * 64, ksl],
                 lambda h2, kt: Vp[:, kt, h2 * 65:(h2 + 1) * 65],
                 bias_a, bias_b, l0_sink,
                 lambda qb: list(range(16)),
                 lambda qb, kt, qs: None)

    ctx.free("fl_PT0", "fl_PT1", "fl_rc_a", "fl_rc_b")
    ctx.sb["scratch_f32_c"] = ctx.alloc("l0_scrc", [128, D], F32)
    wout_bf = load_weight_bf16(ctx, "l0_wout", ctx.hbm["L0_attn_Wout"],
                               1024, 1024)
    x = ctx.sb["x_tm"]
    xnr = ctx.sb["scratch_f32_a"]
    sbuf_s = ctx.sb["scratch_f32_c"]
    tmp = ctx.sb["scratch_f32_b"]
    ps_z = ctx.sb["ps_m2"]
    for tb in range(NT):
        xn_rec(ctx, tb, xnr[:])
        for half in range(2):
            esl = slice(half * 512, (half + 1) * 512)
            for c in range(8):
                p.op("tensor", lambda e, tb=tb, c=c, esl=esl: e.matmul(
                    ps_z[:, :], O_fm[:, c, tb * 128:(tb + 1) * 128],
                    wout_bf[:, c, esl], start=(c == 0), stop=(c == 7)),
                    reads=[O_fm, wout_bf], writes=[ps_z])
            p.op("vector", lambda e, esl=esl: e.scalar_tensor_tensor(
                out=sbuf_s[:, esl], in0=ps_z[:, :], scalar=1.0,
                in1=xnr[:, esl], op0=ALU.mult, op1=ALU.add),
                reads=[ps_z, xnr], writes=[sbuf_s])
        for half in range(2):
            hsl = slice(half * 512, (half + 1) * 512)
            p.op("tensor", lambda e, tb=tb, hsl=hsl: e.matmul(
                ps_m1[:, :], down_fm[:, tb * 128:(tb + 1) * 128],
                upg[:, hsl], start=True, stop=True),
                reads=[down_fm, upg], writes=[ps_m1])
            p.op("vector", lambda e, hsl=hsl: e.scalar_tensor_tensor(
                out=tmp[:, hsl], in0=ps_m1[:, :], scalar=1.0,
                in1=xnr[:, hsl], op0=ALU.mult, op1=ALU.add),
                reads=[ps_m1, xnr], writes=[tmp])
        p.op("scalar", lambda e: e.activation(
            out=tmp[:], in_=tmp[:], func=AF.Silu), reads=[tmp], writes=[tmp])
        p.op("vector", lambda e, tb=tb: e.tensor_mul(
            out=tmp[:], in0=tmp[:], in1=sbuf_s[:]),
            reads=[tmp, sbuf_s], writes=[tmp])
        p.op("vector", lambda e, tb=tb: e.tensor_add(
            out=x[:, tb, :], in0=x[:, tb, :], in1=tmp[:]),
            reads=[tmp, x], writes=[x])




def bcast_col(ctx, row_ap, out_col, reads):
    p = ctx.p
    tmp = ctx.sb["bc_bf"]
    ps = ctx.sb["ps_m1"]
    p.op("vector", lambda e: e.tensor_copy(out=tmp[:], in_=row_ap),
         reads=list(reads), writes=[tmp])
    p.op("tensor", lambda e: e.matmul(ps[:, 0:1], ctx.consts["onesrow"][:],
                                      tmp[:], start=True, stop=True),
         reads=[tmp, "c_onesrow"], writes=[ps])
    p.op("vector", lambda e: e.tensor_copy(out=out_col, in_=ps[:, 0:1]),
         reads=[ps], writes=[out_col])


def cmul_col(ctx, outr, outi, ar, ai, br, bi):
    p = ctx.p
    w = ctx.sb["l1_cmul_w"]
    p.op("vector", lambda e: e.tensor_mul(out=w[:, 0:1], in0=ar, in1=br),
         reads=[ar, br], writes=[w])
    p.op("vector", lambda e: e.tensor_mul(out=w[:, 1:2], in0=ai, in1=bi),
         reads=[ai, bi], writes=[w])
    p.op("vector", lambda e: e.tensor_mul(out=w[:, 2:3], in0=ar, in1=bi),
         reads=[ar, bi], writes=[w])
    p.op("vector", lambda e: e.tensor_mul(out=w[:, 3:4], in0=ai, in1=br),
         reads=[ai, br], writes=[w])
    p.op("vector", lambda e: e.tensor_sub(out=outr, in0=w[:, 0:1],
                                          in1=w[:, 1:2]),
         reads=[w], writes=[outr])
    p.op("vector", lambda e: e.tensor_add(out=outi, in0=w[:, 2:3],
                                          in1=w[:, 3:4]),
         reads=[w], writes=[outi])


def emit_layer1(ctx):
    p = ctx.p
    emit_ln(ctx)
    xn_fm = ctx.sb["xn_fm"]
    ps_m1 = ctx.sb["ps_m1"]
    ps_m2 = ctx.sb["ps_m2"]
    ps_big = ctx.sb["ps_sa"]
    ctx.alloc("bc_bf", [1, 1], BF16)
    ctx.alloc("l1_cmul_w", [64, 4], F32)

    ac = ctx.alloc("l1_acols", [64, 40], F32)
    p.dma("sync", lambda e: e.dma_start(
        out=ac[:, 0:1], in_=ctx.hbm["L1_attn_log_A_real"][:, None]),
        writes=[ac])
    p.dma("sync", lambda e: e.dma_start(
        out=ac[:, 1:2], in_=ctx.hbm["L1_attn_log_A_imag"][:, None]),
        writes=[ac])
    p.op("scalar", lambda e: e.activation(out=ac[:, 2:3], in_=ac[:, 0:1],
                                          func=AF.Exp),
         reads=[ac], writes=[ac])
    p.op("vector", lambda e: e.tensor_scalar_mul(
        out=ac[:, 2:3], in0=ac[:, 2:3], scalar1=-0.5),
        reads=[ac], writes=[ac])
    p.op("vector", lambda e: e.tensor_scalar_mul(
        out=ac[:, 3:4], in0=ac[:, 1:2], scalar1=0.5),
        reads=[ac], writes=[ac])
    p.op("vector", lambda e: e.tensor_scalar(
        out=ac[:, 4:5], in0=ac[:, 2:3], scalar1=1.0, scalar2=-1.0,
        op0=ALU.subtract, op1=ALU.mult), reads=[ac], writes=[ac])
    p.op("vector", lambda e: e.tensor_mul(out=ac[:, 5:6], in0=ac[:, 4:5],
                                          in1=ac[:, 4:5]),
         reads=[ac], writes=[ac])
    p.op("vector", lambda e: e.tensor_mul(out=ac[:, 6:7], in0=ac[:, 3:4],
                                          in1=ac[:, 3:4]),
         reads=[ac], writes=[ac])
    p.op("vector", lambda e: e.tensor_add(out=ac[:, 5:6], in0=ac[:, 5:6],
                                          in1=ac[:, 6:7]),
         reads=[ac], writes=[ac])
    p.op("vector", lambda e: e.reciprocal(out=ac[:, 7:8], in_=ac[:, 5:6]),
         reads=[ac], writes=[ac])
    p.op("vector", lambda e: e.tensor_mul(out=ac[:, 8:9], in0=ac[:, 2:3],
                                          in1=ac[:, 2:3]),
         reads=[ac], writes=[ac])
    p.op("vector", lambda e: e.tensor_add(out=ac[:, 8:9], in0=ac[:, 8:9],
                                          in1=ac[:, 6:7]),
         reads=[ac], writes=[ac])
    p.op("vector", lambda e: e.tensor_scalar(
        out=ac[:, 8:9], in0=ac[:, 8:9], scalar1=1.0, scalar2=-1.0,
        op0=ALU.subtract, op1=ALU.mult), reads=[ac], writes=[ac])
    p.op("vector", lambda e: e.tensor_mul(out=ac[:, 10:11], in0=ac[:, 8:9],
                                          in1=ac[:, 7:8]),
         reads=[ac], writes=[ac])
    p.op("vector", lambda e: e.tensor_scalar_mul(
        out=ac[:, 11:12], in0=ac[:, 3:4], scalar1=2.0),
        reads=[ac], writes=[ac])
    p.op("vector", lambda e: e.tensor_mul(out=ac[:, 11:12], in0=ac[:, 11:12],
                                          in1=ac[:, 7:8]),
         reads=[ac], writes=[ac])
    p.op("vector", lambda e: e.tensor_copy(out=ac[:, 12:13],
                                           in_=ac[:, 10:11]),
         reads=[ac], writes=[ac])
    p.op("vector", lambda e: e.tensor_copy(out=ac[:, 13:14],
                                           in_=ac[:, 11:12]),
         reads=[ac], writes=[ac])
    for j in range(1, 11):
        o, q = 12 + 2 * j, 12 + 2 * (j - 1)
        cmul_col(ctx, ac[:, o:o + 1], ac[:, o + 1:o + 2],
                 ac[:, q:q + 1], ac[:, q + 1:q + 2],
                 ac[:, q:q + 1], ac[:, q + 1:q + 2])
    nAim = ctx.alloc("l1_nAim", [64, 11], F32)
    for j in range(11):
        p.op("vector", lambda e, j=j: e.tensor_scalar_mul(
            out=nAim[:, j:j + 1], in0=ac[:, 13 + 2 * j:14 + 2 * j],
            scalar1=-1.0), reads=[ac], writes=[nAim])
    Fre = ctx.alloc("l1_Fre", [64, 16], F32)
    Fim = ctx.alloc("l1_Fim", [64, 16], F32)
    p.op("vector", lambda e: e.tensor_copy(out=Fre[:, 0:1], in_=ac[:, 12:13]),
         reads=[ac], writes=[Fre])
    p.op("vector", lambda e: e.tensor_copy(out=Fim[:, 0:1], in_=ac[:, 13:14]),
         reads=[ac], writes=[Fim])
    for i in range(1, 16):
        cmul_col(ctx, Fre[:, i:i + 1], Fim[:, i:i + 1],
                 Fre[:, i - 1:i], Fim[:, i - 1:i],
                 ac[:, 12:13], ac[:, 13:14])

    bw_bf = ctx.alloc("l1_bw_bf", [128, 1024], BF16)
    load_convert(ctx, bw_bf[:], ctx.hbm["L1_attn_B_w"], 128, 1024)
    bwT = ctx.alloc("l1_bwT", [128, 8, 128], BF16)
    for c in range(8):
        pe_transpose(ctx, bwT[:, c, :], bw_bf[:, c * 128:(c + 1) * 128],
                     128, 128)
    SRe = [ctx.alloc(f"l1_sre{k}", [64, T], F32) for k in range(2)]
    SIm = [ctx.alloc(f"l1_sim{k}", [64, T], F32) for k in range(2)]
    hstmp = ctx.alloc("l1_hstmp", [64, T], F32)
    for tb in range(4):
        tsl = slice(tb * 512, (tb + 1) * 512)
        for c in range(8):
            p.op("tensor", lambda e, tsl=tsl, c=c: e.matmul(
                ps_big[:, 0:512], bwT[:, c, :], xn_fm[:, c, tsl],
                start=(c == 0), stop=(c == 7)),
                reads=[bwT, xn_fm], writes=[ps_big])
        p.op("vector", lambda e, tsl=tsl: e.tensor_copy(
            out=SRe[0][:, tsl], in_=ps_big[0:64, 0:512]),
            reads=[ps_big], writes=[SRe[0]])
        p.op("vector", lambda e, tsl=tsl: e.tensor_copy(
            out=SIm[0][:, tsl], in_=ps_big[64:128, 0:512]),
            reads=[ps_big], writes=[SIm[0]])

    def hs_pass(bufs_s, bufs_d, s, width, j):
        re_s, im_s = bufs_s
        re_d, im_d = bufs_d
        are = ac[:, 12 + 2 * j:13 + 2 * j]
        aim = ac[:, 13 + 2 * j:14 + 2 * j]
        naim = nAim[:, j:j + 1]
        n = re_s.shape[1]

        def v(t):
            if width == n:
                return t[:][:, None, :]
            return t[:].rearrange("p (c w) -> p c w", w=width)
        p.op("vector", lambda e: e.tensor_copy(
            out=v(re_d)[:, :, 0:s], in_=v(re_s)[:, :, 0:s]),
            reads=[re_s], writes=[re_d])
        p.op("vector", lambda e: e.tensor_copy(
            out=v(im_d)[:, :, 0:s], in_=v(im_s)[:, :, 0:s]),
            reads=[im_s], writes=[im_d])
        p.op("vector", lambda e: e.scalar_tensor_tensor(
            out=v(hstmp)[:, 0:n // width, 0:width - s],
            in0=v(re_s)[:, :, 0:width - s],
            scalar=are, in1=v(re_s)[:, :, s:width],
            op0=ALU.mult, op1=ALU.add), reads=[re_s, ac], writes=[hstmp])
        p.op("vector", lambda e: e.scalar_tensor_tensor(
            out=v(re_d)[:, :, s:width], in0=v(im_s)[:, :, 0:width - s],
            scalar=naim, in1=v(hstmp)[:, 0:n // width, 0:width - s],
            op0=ALU.mult, op1=ALU.add), reads=[im_s, hstmp, nAim],
            writes=[re_d])
        p.op("vector", lambda e: e.scalar_tensor_tensor(
            out=v(hstmp)[:, 0:n // width, 0:width - s],
            in0=v(im_s)[:, :, 0:width - s],
            scalar=are, in1=v(im_s)[:, :, s:width],
            op0=ALU.mult, op1=ALU.add), reads=[im_s, ac], writes=[hstmp])
        p.op("vector", lambda e: e.scalar_tensor_tensor(
            out=v(im_d)[:, :, s:width], in0=v(re_s)[:, :, 0:width - s],
            scalar=aim, in1=v(hstmp)[:, 0:n // width, 0:width - s],
            op0=ALU.mult, op1=ALU.add), reads=[re_s, hstmp, ac],
            writes=[im_d])

    cur = 0
    for j, s in enumerate((1, 2, 4, 8)):
        hs_pass((SRe[cur], SIm[cur]), (SRe[1 - cur], SIm[1 - cur]),
                s, 16, j)
        cur = 1 - cur
    CRe = [ctx.alloc(f"l1_cre{k}", [64, 128], F32) for k in range(2)]
    CIm = [ctx.alloc(f"l1_cim{k}", [64, 128], F32) for k in range(2)]
    p.op("vector", lambda e: e.tensor_copy(
        out=CRe[0][:], in_=SRe[cur][:].rearrange(
            "p (c w) -> p c w", w=16)[:, :, 15]),
        reads=[SRe[cur]], writes=[CRe[0]])
    p.op("vector", lambda e: e.tensor_copy(
        out=CIm[0][:], in_=SIm[cur][:].rearrange(
            "p (c w) -> p c w", w=16)[:, :, 15]),
        reads=[SIm[cur]], writes=[CIm[0]])
    ccur = 0
    for jj, s in enumerate((1, 2, 4, 8, 16, 32, 64)):
        hs_pass((CRe[ccur], CIm[ccur]), (CRe[1 - ccur], CIm[1 - ccur]),
                s, 128, 4 + jj)
        ccur = 1 - ccur

    OP1 = hstmp
    OP2r = SRe[1 - cur]
    vS = lambda t: t[:].rearrange("p (c w) -> p c w", w=16)
    for comp in range(2):
        dst = SRe[cur] if comp == 0 else SIm[cur]
        if comp == 0:
            t1a, t1b = CRe[ccur], Fre
            t2a, t2b = CIm[ccur], Fim
        else:
            t1a, t1b = CRe[ccur], Fim
            t2a, t2b = CIm[ccur], Fre
        p.op("vector", lambda e, t1a=t1a, t1b=t1b: e.tensor_mul(
            out=vS(OP1)[:, 1:128, :],
            in0=t1a[:, 0:127, None].broadcast_to([64, 127, 16]),
            in1=t1b[:, None, :].broadcast_to([64, 127, 16])),
            reads=[t1a, t1b], writes=[OP1])
        p.op("vector", lambda e, t2a=t2a, t2b=t2b: e.tensor_mul(
            out=vS(OP2r)[:, 1:128, :],
            in0=t2a[:, 0:127, None].broadcast_to([64, 127, 16]),
            in1=t2b[:, None, :].broadcast_to([64, 127, 16])),
            reads=[t2a, t2b], writes=[OP2r])
        op = "tensor_sub" if comp == 0 else "tensor_add"
        p.op("vector", lambda e, op=op: getattr(e, op)(
            out=vS(OP1)[:, 1:128, :], in0=vS(OP1)[:, 1:128, :],
            in1=vS(OP2r)[:, 1:128, :]), reads=[OP1, OP2r], writes=[OP1])
        p.op("vector", lambda e, dst=dst: e.tensor_add(
            out=vS(dst)[:, 1:128, :], in0=vS(dst)[:, 1:128, :],
            in1=vS(OP1)[:, 1:128, :]), reads=[OP1, dst], writes=[dst])

    Hfm = ctx.alloc("l1_Hfm", [128, T], BF16)
    stg = ctx.sb["stage_bf"]
    p.op("vector", lambda e: e.tensor_copy(out=Hfm[0:64, :],
                                           in_=SRe[cur][:]),
         reads=[SRe[cur]], writes=[Hfm])
    for piece in range(2):
        sl = slice(piece * 1024, (piece + 1) * 1024)
        p.op("vector", lambda e, sl=sl: e.tensor_copy(
            out=stg[0:64, :], in_=SIm[cur][:, sl]),
            reads=[SIm[cur]], writes=[stg])
        p.dma("sync", lambda e, sl=sl: e.dma_start(
            out=Hfm[64:128, sl], in_=stg[0:64, :]),
            reads=[stg], writes=[Hfm])

    ctx.free("l1_sre0", "l1_sre1", "l1_sim0", "l1_sim1", "l1_hstmp",
             "l1_cre0", "l1_cre1", "l1_cim0", "l1_cim1", "l1_bw_bf",
             "l1_bwT", "l1_Fre", "l1_Fim", "l1_nAim")

    cw_bf = ctx.alloc("l1_cw_bf", [128, 1024], BF16)
    ws_cw = ctx.sb["wstage"][:, 0:1024].rearrange("p (c n) -> p c n", n=128)
    p.dma("sync", lambda e: e.dma_start(
        out=ws_cw, in_=ctx.hbm["L1_attn_C_w"].rearrange(
            "(c p) n -> p c n", p=128)), writes=["wstage"])
    p.op("gpsimd", lambda e: e.tensor_copy(
        out=cw_bf[:].rearrange("p (c n) -> p c n", n=128), in_=ws_cw),
        reads=["wstage"], writes=[cw_bf])
    cwT = ctx.alloc("l1_cwT", [128, 8, 128], BF16)
    for c in range(8):
        pe_transpose(ctx, cwT[:, c, :],
                     cw_bf[:, c * 128:(c + 1) * 128], 128, 128)
    dwT = ctx.alloc("l1_dwT", [128, 8, 1024], BF16)
    dwp = ctx.alloc("l1_dwp", [128, 1024], BF16)
    for r in range(8):
        load_convert(ctx, dwp[:],
                     ctx.hbm["L1_attn_D_w"][r * 128:(r + 1) * 128, :],
                     128, 1024, )
        for c in range(8):
            pe_transpose(ctx, dwT[:, c, r * 128:(r + 1) * 128],
                         dwp[:, c * 128:(c + 1) * 128], 128, 128)

    emit_l1_ffn_pre(ctx)

    x = ctx.sb["x_tm"]
    xnr = ctx.sb["scratch_f32_a"]
    ctx.sb["scratch_f32_c"] = ctx.alloc("l1_scrc", [128, D], F32)
    sc = ctx.sb["scratch_f32_c"]
    tmp = ctx.sb["scratch_f32_b"]
    for tb in range(NT):
        for half in range(2):
            esl = slice(half * 512, (half + 1) * 512)
            p.op("tensor", lambda e, tb=tb, esl=esl: e.matmul(
                ps_m2[:, :], Hfm[:, tb * 128:(tb + 1) * 128],
                cwT[:].rearrange("p c n -> p (c n)")[:, esl],
                start=True, stop=False),
                reads=[Hfm, cwT], writes=[ps_m2])
            for c in range(8):
                p.op("tensor", lambda e, tb=tb, esl=esl, c=c: e.matmul(
                    ps_m2[:, :], xn_fm[:, c, tb * 128:(tb + 1) * 128],
                    dwT[:, c, esl], start=False, stop=(c == 7)),
                    reads=[xn_fm, dwT], writes=[ps_m2])
            p.op("vector", lambda e, esl=esl: e.tensor_copy(
                out=sc[:, esl], in_=ps_m2[:, :]), reads=[ps_m2], writes=[sc])
        emit_l1_ffn_tile(ctx, tb, tmp)
        p.op("vector", lambda e: e.tensor_mul(out=tmp[:], in0=tmp[:],
                                              in1=sc[:]),
             reads=[tmp, sc], writes=[tmp])
        p.op("vector", lambda e, tb=tb: e.tensor_add(
            out=x[:, tb, :], in0=x[:, tb, :], in1=tmp[:]),
            reads=[tmp, x], writes=[x])
    ctx.free(*[n for n in list(ctx.live) if n.startswith("l1_")])


def emit_l1_ffn_pre(ctx):
    p = ctx.p
    xn_fm = ctx.sb["xn_fm"]
    ps_m1 = ctx.sb["ps_m1"]
    ps_m2 = ctx.sb["ps_m2"]
    ident = ctx.consts["ident"]

    ws = ctx.sb["wstage"]
    sg_f = ws[0:64, 0:1024]
    p.dma("sync", lambda e: e.dma_start(
        out=sg_f, in_=ctx.hbm["L1_ffn_signatures_raw"]), writes=[ws])
    q_bf = ctx.alloc("l1_q_bf", [64, 1024], BF16)
    qf2 = ctx.sb["scratch_f32_a"][0:64, :]
    p.op("vector", lambda e: e.tensor_scalar(
        out=qf2, in0=sg_f, scalar1=-0.3, scalar2=None, op0=ALU.is_lt),
        reads=[ws], writes=["scratch_f32_a"])
    p.op("vector", lambda e: e.tensor_scalar(
        out=sg_f, in0=sg_f, scalar1=0.3, scalar2=None, op0=ALU.is_gt),
        reads=[ws], writes=[ws])
    p.op("vector", lambda e: e.tensor_sub(out=q_bf[:], in0=sg_f, in1=qf2),
         reads=[ws, "scratch_f32_a"], writes=[q_bf])
    qT = ctx.alloc("l1_qT", [128, 8, 64], BF16)
    for c in range(8):
        pe_transpose(ctx, qT[:, c, :], q_bf[:, c * 128:(c + 1) * 128],
                     64, 128)
    grp8 = ctx.consts["grp8"]
    cs_bf = ctx.alloc("l1_cs_bf", [8, 1024], BF16)
    for half in range(2):
        sl = slice(half * 512, (half + 1) * 512)
        p.op("tensor", lambda e, sl=sl: e.matmul(
            ps_m1[0:8, :], grp8[:], q_bf[:, sl], start=True, stop=True),
            reads=[grp8, q_bf], writes=[ps_m1])
        p.op("scalar", lambda e, sl=sl: e.activation(
            out=cs_bf[:, sl], in_=ps_m1[0:8, :], func=AF.Sign),
            reads=[ps_m1], writes=[cs_bf])
    csT = ctx.alloc("l1_csT", [128, 8, 8], BF16)
    for c in range(8):
        pe_transpose(ctx, csT[:, c, :], cs_bf[:, c * 128:(c + 1) * 128],
                     8, 128)

    cscore = ctx.alloc("l1_cscore", [128, NT, 8], F32)
    tscore = ctx.alloc("l1_tscore", [128, NT, 64], F32)
    for tb in range(NT):
        tsl = slice(tb * 128, (tb + 1) * 128)
        for c in range(8):
            p.op("tensor", lambda e, tsl=tsl, c=c: e.matmul(
                ps_m1[:, 0:8], xn_fm[:, c, tsl], csT[:, c, :],
                start=(c == 0), stop=(c == 7)),
                reads=[xn_fm, csT], writes=[ps_m1])
        p.op("vector", lambda e, tb=tb: e.tensor_copy(
            out=cscore[:, tb, :], in_=ps_m1[:, 0:8]),
            reads=[ps_m1], writes=[cscore])
        for c in range(8):
            p.op("tensor", lambda e, tsl=tsl, c=c: e.matmul(
                ps_m2[:, 0:64], xn_fm[:, c, tsl], qT[:, c, :],
                start=(c == 0), stop=(c == 7)),
                reads=[xn_fm, qT], writes=[ps_m2])
        p.op("vector", lambda e, tb=tb: e.tensor_copy(
            out=tscore[:, tb, :], in_=ps_m2[:, 0:64]),
            reads=[ps_m2], writes=[tscore])

    trow = ctx.alloc("l1_trow", [1, 2], F32)
    p.dma("sync", lambda e: e.dma_start(
        out=trow[:, 0:1], in_=ctx.hbm["L1_ffn_temperature"][:, None]),
        writes=[trow])
    p.op("vector", lambda e: e.tensor_scalar_add(
        out=trow[:, 1:2], in0=trow[:, 0:1], scalar1=1e-6),
        reads=[trow], writes=[trow])
    tcol = ctx.alloc("l1_tcol", [128, 1], F32)
    bcast_col(ctx, trow[:, 1:2], tcol[:], [trow])
    rtcol = ctx.alloc("l1_rtcol", [128, 1], F32)
    p.op("vector", lambda e: e.reciprocal(out=rtcol[:], in_=tcol[:]),
         reads=[tcol], writes=[rtcol])
    krow = ctx.alloc("l1_krow", [1, 16], F32)
    p.dma("sync", lambda e: e.dma_start(
        out=krow[:, 0:8], in_=ctx.hbm["L1_ffn_knot_values"][None, :]),
        writes=[krow])
    p.op("vector", lambda e: e.tensor_sub(
        out=krow[:, 8:15], in0=krow[:, 1:8], in1=krow[:, 0:7]),
        reads=[krow], writes=[krow])
    kcols = ctx.alloc("l1_kcols", [128, 8], F32)
    bcast_col(ctx, krow[:, 0:1], kcols[:, 0:1], [krow])
    for j in range(7):
        bcast_col(ctx, krow[:, 8 + j:9 + j], kcols[:, 1 + j:2 + j], [krow])

    def calibrate(score_t, ncol):
        w = NT * ncol
        sig = ctx.sb["scratch_f32_a"][:, 0:w]
        val = ctx.alloc(f"{score_t.name}_val", [128, w], F32)
        u = ctx.sb["scratch_f32_b"][:, 0:w]
        flat = score_t[:].rearrange("p a b -> p (a b)")
        p.op("scalar", lambda e: e.activation(
            out=sig, in_=flat, func=AF.Sigmoid, scale=rtcol[:]),
            reads=[score_t, rtcol], writes=["scratch_f32_a"])
        p.op("vector", lambda e: e.tensor_scalar(
            out=val[:], in0=sig, scalar1=0.0, scalar2=kcols[:, 0:1],
            op0=ALU.mult, op1=ALU.add),
            reads=["scratch_f32_a", kcols], writes=[val])
        for j in range(7):
            p.op("vector", lambda e, j=j: e.tensor_scalar(
                out=u, in0=sig, scalar1=7.0, scalar2=float(-j),
                op0=ALU.mult, op1=ALU.add),
                reads=["scratch_f32_a"], writes=["scratch_f32_b"])
            p.op("vector", lambda e: e.tensor_scalar(
                out=u, in0=u, scalar1=0.0, scalar2=1.0,
                op0=ALU.max, op1=ALU.min),
                reads=["scratch_f32_b"], writes=["scratch_f32_b"])
            p.op("vector", lambda e, j=j: e.scalar_tensor_tensor(
                out=val[:], in0=u, scalar=kcols[:, 1 + j:2 + j],
                in1=val[:], op0=ALU.mult, op1=ALU.add),
                reads=["scratch_f32_b", kcols, val], writes=[val])
        return val

    cval = calibrate(cscore, 8)
    tval = calibrate(tscore, 64)

    iota8 = ctx.consts32["iota8m9"]
    cidx = ctx.alloc("l1_cidx", [128, NT], F32)
    m8 = ctx.alloc("l1_m8", [128, NT], F32)
    eq8 = ctx.alloc("l1_eq8", [128, NT * 8], F32)
    vc = lambda t, w: t[:].rearrange("p (a b) -> p a b", b=w)
    p.op("vector", lambda e: e.tensor_reduce(
        out=m8[:], in_=vc(cval, 8), axis=AX.X, op=ALU.max),
        reads=[cval], writes=[m8])
    p.op("vector", lambda e: e.tensor_tensor(
        out=vc(eq8, 8), in0=vc(cval, 8),
        in1=m8[:, :, None].broadcast_to([128, NT, 8]), op=ALU.is_equal),
        reads=[cval, m8], writes=[eq8])
    p.op("vector", lambda e: e.tensor_mul(
        out=eq8[:], in0=eq8[:], in1=iota8[:, 0:NT * 8]),
        reads=[eq8, "c32_iota8m9"], writes=[eq8])
    p.op("vector", lambda e: e.tensor_scalar_add(
        out=eq8[:], in0=eq8[:], scalar1=9.0), reads=[eq8], writes=[eq8])
    p.op("vector", lambda e: e.tensor_reduce(
        out=cidx[:], in_=vc(eq8, 8), axis=AX.X, op=ALU.min),
        reads=[eq8], writes=[cidx])
    sel = ctx.alloc("l1_sel", [128, NT * 8], F32)
    ohc = ctx.alloc("l1_ohc", [128, NT], F32)
    p.op("vector", lambda e: e.memset(sel[:], 0.0), writes=[sel])
    for c in range(8):
        p.op("vector", lambda e, c=c: e.tensor_scalar(
            out=ohc[:], in0=cidx[:], scalar1=float(c), scalar2=None,
            op0=ALU.is_equal), reads=[cidx], writes=[ohc])
        p.op("vector", lambda e, c=c: e.tensor_mul(
            out=vc(eq8, 8), in0=vc(tval, 64)[:, :, c * 8:(c + 1) * 8],
            in1=ohc[:, :, None].broadcast_to([128, NT, 8])),
            reads=[tval, ohc], writes=[eq8])
        p.op("vector", lambda e: e.tensor_add(
            out=sel[:], in0=sel[:], in1=eq8[:]), reads=[eq8, sel],
            writes=[sel])
    lidx = ctx.alloc("l1_lidx", [128, NT], F32)
    p.op("vector", lambda e: e.tensor_reduce(
        out=m8[:], in_=vc(sel, 8), axis=AX.X, op=ALU.max),
        reads=[sel], writes=[m8])
    p.op("vector", lambda e: e.tensor_tensor(
        out=vc(eq8, 8), in0=vc(sel, 8),
        in1=m8[:, :, None].broadcast_to([128, NT, 8]), op=ALU.is_equal),
        reads=[sel, m8], writes=[eq8])
    p.op("vector", lambda e: e.tensor_mul(
        out=eq8[:], in0=eq8[:], in1=iota8[:, 0:NT * 8]),
        reads=[eq8, "c32_iota8m9"], writes=[eq8])
    p.op("vector", lambda e: e.tensor_scalar_add(
        out=eq8[:], in0=eq8[:], scalar1=9.0), reads=[eq8], writes=[eq8])
    p.op("vector", lambda e: e.tensor_reduce(
        out=lidx[:], in_=vc(eq8, 8), axis=AX.X, op=ALU.min),
        reads=[eq8], writes=[lidx])
    tidx = ctx.alloc("l1_tidx", [128, NT], F32)
    p.op("vector", lambda e: e.scalar_tensor_tensor(
        out=tidx[:], in0=cidx[:], scalar=8.0, in1=lidx[:],
        op0=ALU.mult, op1=ALU.add), reads=[cidx, lidx], writes=[tidx])

    tidx_bf = ctx.alloc("l1_tidx_bf", [128, NT], BF16)
    p.op("vector", lambda e: e.tensor_copy(out=tidx_bf[:], in_=tidx[:]),
         reads=[tidx], writes=[tidx_bf])
    tidx_row = ctx.alloc("l1_tidx_row", [1, T], BF16)
    for tb in range(NT):
        pe_transpose(ctx, tidx_row[:, tb * 128:(tb + 1) * 128],
                     tidx_bf[:, tb:tb + 1], 128, 1)
    oh_fm = ctx.alloc("l1_oh_fm", [64, T], BF16)
    iota_col = ctx.consts32["iotacol"]
    for tb in range(4):
        tsl = slice(tb * 512, (tb + 1) * 512)
        p.op("tensor", lambda e, tsl=tsl: e.matmul(
            ps_m1[0:64, :], ctx.consts["onesrow"][:, 0:64],
            tidx_row[:, tsl], start=True, stop=True),
            reads=[tidx_row, "c_onesrow"], writes=[ps_m1])
        p.op("vector", lambda e, tsl=tsl: e.tensor_scalar(
            out=oh_fm[:, tsl], in0=ps_m1[0:64, :],
            scalar1=iota_col[0:64, :], scalar2=None, op0=ALU.is_equal),
            reads=[ps_m1, "c32_iotacol"], writes=[oh_fm])

    gp = ctx.alloc("l1_gp", [64, 72], BF16)
    load_convert(ctx, gp[:, 0:32],
                 ctx.hbm["L1_ffn_spline_w1"].rearrange("n a b -> n (a b)"),
                 64, 32)
    load_convert(ctx, gp[:, 32:48], ctx.hbm["L1_ffn_spline_b1"], 64, 16)
    load_convert(ctx, gp[:, 48:64],
                 ctx.hbm["L1_ffn_spline_w2"].rearrange("n a b -> n (a b)"),
                 64, 16)
    load_convert(ctx, gp[:, 64:65], ctx.hbm["L1_ffn_spline_b2"], 64, 1)
    gath = ctx.alloc("l1_gath", [128, NT, 72], F32)
    for tb in range(NT):
        p.op("tensor", lambda e, tb=tb: e.matmul(
            ps_m1[:, 0:72], oh_fm[:, tb * 128:(tb + 1) * 128], gp[:, 0:72],
            start=True, stop=True), reads=[oh_fm, gp], writes=[ps_m1])
        p.op("vector", lambda e, tb=tb: e.tensor_copy(
            out=gath[:, tb, :], in_=ps_m1[:, 0:72]),
            reads=[ps_m1], writes=[gath])

    cw1 = load_weight_bf16(ctx, "l1_cw1", ctx.hbm["L1_ffn_compress_w1"],
                           1024, 256)
    b1 = ctx.alloc("l1_b1", [1, 256], BF16)
    load_convert(ctx, b1[:], ctx.hbm["L1_ffn_compress_b1"][None, :], 1, 256)
    h_fm = ctx.alloc("l1_h_fm", [128, 2, T], BF16)
    ones512 = ctx.consts["ones512"]
    for m in range(2):
        for tb in range(4):
            tsl = slice(tb * 512, (tb + 1) * 512)
            for c in range(8):
                p.op("tensor", lambda e, m=m, tsl=tsl, c=c: e.matmul(
                    ps_m2[:, :], cw1[:, c, m * 128:(m + 1) * 128],
                    xn_fm[:, c, tsl], start=(c == 0), stop=False),
                    reads=[cw1, xn_fm], writes=[ps_m2])
            p.op("tensor", lambda e, m=m, tsl=tsl: e.matmul(
                ps_m2[:, :], b1[:, m * 128:(m + 1) * 128], ones512[:],
                start=False, stop=True), reads=[b1, "c_ones512"],
                writes=[ps_m2])
            p.op("scalar", lambda e, m=m, tsl=tsl: e.activation(
                out=h_fm[:, m, tsl], in_=ps_m2[:, :], func=AF.Gelu),
                reads=[ps_m2], writes=[h_fm])
    cw2 = ctx.alloc("l1_cw2", [128, 2, 2], BF16)
    ws_c2 = ctx.sb["wstage"][:, 0:4].rearrange("p (a b) -> p a b", b=2)
    p.dma("sync", lambda e: e.dma_start(
        out=ws_c2, in_=ctx.hbm["L1_ffn_compress_w2"].rearrange(
            "(a p) b -> p a b", p=128)), writes=["wstage"])
    p.op("gpsimd", lambda e: e.tensor_copy(out=cw2[:], in_=ws_c2),
         reads=["wstage"], writes=[cw2])
    b2 = ctx.alloc("l1_b2", [1, 2], BF16)
    load_convert(ctx, b2[:], ctx.hbm["L1_ffn_compress_b2"][None, :], 1, 2)
    comp = ctx.alloc("l1_comp", [128, NT, 2], F32)
    for tb in range(NT):
        tsl = slice(tb * 128, (tb + 1) * 128)
        for m in range(2):
            p.op("tensor", lambda e, tsl=tsl, m=m: e.matmul(
                ps_m1[:, 0:2], h_fm[:, m, tsl], cw2[:, m, :],
                start=(m == 0), stop=False),
                reads=[h_fm, cw2], writes=[ps_m1])
        p.op("tensor", lambda e, tsl=tsl: e.matmul(
            ps_m1[:, 0:2], ctx.consts["onesrow"][:], b2[:],
            start=False, stop=True), reads=[b2, "c_onesrow"],
            writes=[ps_m1])
        p.op("scalar", lambda e, tb=tb: e.activation(
            out=comp[:, tb, :], in_=ps_m1[:, 0:2], func=AF.Tanh),
            reads=[ps_m1], writes=[comp])

    dirs = ctx.alloc("l1_dirs", [64, 1024], BF16)
    load_convert(ctx, dirs[:], ctx.hbm["L1_ffn_directions"], 64, 1024)
    grow = ctx.alloc("l1_grow", [128, 1024], BF16)
    brow = ctx.alloc("l1_brow", [128, 1024], BF16)
    gb_bf = ctx.alloc("l1_gb_bf", [1, 2048], BF16)
    load_convert(ctx, gb_bf[:, 0:1024],
                 ctx.hbm["L1_ffn_norm_gamma"][None, :], 1, 1024)
    load_convert(ctx, gb_bf[:, 1024:2048],
                 ctx.hbm["L1_ffn_norm_beta"][None, :], 1, 1024)
    for half in range(2):
        sl = slice(half * 512, (half + 1) * 512)
        p.op("tensor", lambda e, sl=sl: e.matmul(
            ps_m1[:, :], ctx.consts["onesrow"][:], gb_bf[:, sl],
            start=True, stop=True), reads=[gb_bf, "c_onesrow"],
            writes=[ps_m1])
        p.op("vector", lambda e, sl=sl: e.tensor_copy(
            out=grow[:, sl], in_=ps_m1[:, :]), reads=[ps_m1], writes=[grow])
        sl2 = slice(1024 + half * 512, 1024 + (half + 1) * 512)
        p.op("tensor", lambda e, sl2=sl2: e.matmul(
            ps_m1[:, :], ctx.consts["onesrow"][:], gb_bf[:, sl2],
            start=True, stop=True), reads=[gb_bf, "c_onesrow"],
            writes=[ps_m1])
        p.op("vector", lambda e, sl=sl: e.tensor_copy(
            out=brow[:, sl], in_=ps_m1[:, :]), reads=[ps_m1], writes=[brow])
    ctx.alloc("l1_spl", [128, 32], F32)
    ctx.alloc("l1_sq", [128, 1024], F32)
    ctx.alloc("l1_xnr2", [128, 1024], F32)
    osrow = ctx.alloc("l1_osrow", [1, 1], F32)
    p.dma("sync", lambda e: e.dma_start(
        out=osrow[:], in_=ctx.hbm["L1_ffn_output_scale"][:, None]),
        writes=[osrow])
    oscol = ctx.alloc("l1_oscol", [128, 1], F32)
    bcast_col(ctx, osrow[:], oscol[:], [osrow])


def emit_l1_ffn_tile(ctx, tb, gout):
    p = ctx.p
    gath = ctx.sb["l1_gath"]
    comp = ctx.sb["l1_comp"]
    oh_fm = ctx.sb["l1_oh_fm"]
    dirs = ctx.sb["l1_dirs"]
    grow = ctx.sb["l1_grow"]
    brow = ctx.sb["l1_brow"]
    oscol = ctx.sb["l1_oscol"]
    col = ctx.sb["scratch_col"]
    spl = ctx.sb["l1_spl"]
    ps_m1 = ctx.sb["ps_m1"]
    xnr = ctx.sb["l1_xnr2"]
    p.op("vector", lambda e: e.tensor_scalar(
        out=spl[:, 0:16], in0=gath[:, tb, 0:16],
        scalar1=comp[:, tb, 0:1], scalar2=None, op0=ALU.mult),
        reads=[gath, comp], writes=[spl])
    p.op("vector", lambda e: e.scalar_tensor_tensor(
        out=spl[:, 0:16], in0=gath[:, tb, 16:32],
        scalar=comp[:, tb, 1:2], in1=spl[:, 0:16],
        op0=ALU.mult, op1=ALU.add), reads=[gath, comp, spl], writes=[spl])
    p.op("vector", lambda e: e.tensor_add(
        out=spl[:, 0:16], in0=spl[:, 0:16], in1=gath[:, tb, 32:48]),
        reads=[gath, spl], writes=[spl])
    p.op("vector", lambda e: e.tensor_scalar_max(
        out=spl[:, 0:16], in0=spl[:, 0:16], scalar1=0.0),
        reads=[spl], writes=[spl])
    p.op("vector", lambda e: e.tensor_mul(
        out=spl[:, 16:32], in0=spl[:, 0:16], in1=gath[:, tb, 48:64]),
        reads=[gath, spl], writes=[spl])
    p.op("vector", lambda e: e.reduce_sum(
        out=col[:, 0:1], in_=spl[:, 16:32], axis=AX.X),
        reads=[spl], writes=[col])
    p.op("vector", lambda e: e.tensor_add(
        out=col[:, 0:1], in0=col[:, 0:1], in1=gath[:, tb, 64:65]),
        reads=[gath, col], writes=[col])
    p.op("vector", lambda e: e.tensor_mul(
        out=col[:, 0:1], in0=col[:, 0:1], in1=oscol[:]),
        reads=[oscol, col], writes=[col])
    xn_rec(ctx, tb, xnr[:])
    for half in range(2):
        hsl = slice(half * 512, (half + 1) * 512)
        p.op("tensor", lambda e, hsl=hsl: e.matmul(
            ps_m1[:, :], oh_fm[:, tb * 128:(tb + 1) * 128], dirs[:, hsl],
            start=True, stop=True), reads=[oh_fm, dirs], writes=[ps_m1])
        p.op("vector", lambda e, hsl=hsl: e.tensor_scalar(
            out=gout[:, hsl], in0=ps_m1[:, :], scalar1=col[:, 0:1],
            scalar2=None, op0=ALU.mult), reads=[ps_m1, col], writes=[gout])
    p.op("vector", lambda e: e.tensor_add(
        out=gout[:], in0=gout[:], in1=xnr[:]), reads=[xnr, gout],
        writes=[gout])
    p.op("vector", lambda e: e.reduce_sum(
        out=col[:, 1:2], in_=gout[:], axis=AX.X), reads=[gout], writes=[col])
    p.op("vector", lambda e: e.tensor_scalar_mul(
        out=col[:, 1:2], in0=col[:, 1:2], scalar1=1.0 / D),
        reads=[col], writes=[col])
    p.op("vector", lambda e: e.tensor_scalar(
        out=gout[:], in0=gout[:], scalar1=col[:, 1:2], scalar2=None,
        op0=ALU.subtract), reads=[col, gout], writes=[gout])
    sq = ctx.sb["l1_sq"]
    p.op("scalar", lambda e: e.activation(
        out=sq[:], in_=gout[:], func=AF.Square), reads=[gout], writes=[sq])
    p.op("vector", lambda e: e.reduce_sum(
        out=col[:, 2:3], in_=sq[:], axis=AX.X), reads=[sq], writes=[col])
    epsc = ctx.fconst(1e-5)
    p.op("scalar", lambda e, epsc=epsc: e.activation(
        out=col[:, 3:4], in_=col[:, 2:3], func=AF.Sqrt, bias=epsc,
        scale=1.0 / D), reads=[col, "fconsts"], writes=[col])
    p.op("vector", lambda e: e.reciprocal(out=col[:, 3:4], in_=col[:, 3:4]),
         reads=[col], writes=[col])
    p.op("vector", lambda e: e.tensor_scalar(
        out=gout[:], in0=gout[:], scalar1=col[:, 3:4], scalar2=None,
        op0=ALU.mult), reads=[col, gout], writes=[gout])
    p.op("vector", lambda e: e.tensor_mul(
        out=gout[:], in0=gout[:], in1=grow[:]), reads=[grow, gout],
        writes=[gout])
    p.op("vector", lambda e: e.tensor_add(
        out=gout[:], in0=gout[:], in1=brow[:]), reads=[brow, gout],
        writes=[gout])
    p.op("scalar", lambda e: e.activation(
        out=gout[:], in_=gout[:], func=AF.Silu), reads=[gout],
        writes=[gout])



def row_bcast(ctx, dst, row_bf_ap, width, reads):
    p = ctx.p
    ps = ctx.sb["ps_m1"]
    for c0 in range(0, width, 512):
        w = min(512, width - c0)
        p.op("tensor", lambda e, c0=c0, w=w: e.matmul(
            ps[:, 0:w], ctx.consts["onesrow"][:], row_bf_ap[:, c0:c0 + w],
            start=True, stop=True), reads=list(reads) + ["c_onesrow"],
            writes=[ps])
        p.op("vector", lambda e, c0=c0, w=w: e.tensor_copy(
            out=dst[:, c0:c0 + w], in_=ps[:, 0:w]), reads=[ps], writes=[dst])


def emit_layer2a(ctx):
    import os
    NPAIR = int(os.environ.get("P2A_PAIRS", "8"))
    PHASE = int(os.environ.get("P2A_PHASE", "9"))
    p = ctx.p
    emit_ln(ctx)
    xn_fm = ctx.sb["xn_fm"]
    ident = ctx.consts["ident"]
    ps_m1 = ctx.sb["ps_m1"]
    ps_m2 = ctx.sb["ps_m2"]
    ws = ctx.sb["wstage"]
    s_state = ctx.state_out("s_state", [T, D])

    convw = {}
    for nm in ("conv_q", "conv_k", "conv_v"):
        t = ctx.alloc(f"l2_{nm}", [128, 8, 4], F32)
        p.dma("sync", lambda e, t=t, nm=nm: e.dma_start(
            out=t[:], in_=ctx.hbm[f"L2_attn_{nm}"].rearrange(
                "(c p) j -> p c j", p=128)), writes=[t])
        convw[nm] = t
    w0 = ctx.alloc("l2_w0", [128, 8, 64], BF16)
    sv = ws[:, 0:512].rearrange("p (c j) -> p c j", j=64)
    p.dma("sync", lambda e: e.dma_start(
        out=sv, in_=ctx.hbm["L2_attn_W_init"].rearrange(
            "o h a b -> (o h a) b").rearrange("(c p) b -> p c b", p=128)),
        writes=[ws])
    p.op("gpsimd", lambda e: e.tensor_copy(out=w0[:], in_=sv),
         reads=[ws], writes=[w0])
    gbrow = ctx.alloc("l2_gbrow", [1, 128], BF16)
    load_convert(ctx, gbrow[:, 0:64], ctx.hbm["L2_attn_ln_gamma"][None, :],
                 1, 64)
    load_convert(ctx, gbrow[:, 64:128], ctx.hbm["L2_attn_ln_beta"][None, :],
                 1, 64)
    grow64 = ctx.alloc("l2_grow64", [128, 128], F32)
    row_bcast(ctx, grow64[:, 0:64], gbrow[:, 0:64], 64, [gbrow])
    row_bcast(ctx, grow64[:, 64:128], gbrow[:, 64:128], 64, [gbrow])

    O_all = ctx.alloc("l2_O_all", [128, NT, 1024], BF16)
    tril = ctx.consts["tril_incl"]

    wTp = ctx.alloc("l2_wTp", [128, 8, 128], BF16)
    ctx.alloc("l2_vstage", [128, T], BF16)
    ctx.alloc("l2_nrowf", [1, 512], F32)
    ctx.alloc("l2_rkcol_bf", [128, 1], BF16)
    wpc = ctx.alloc("l2_wpc", [128, 1024], BF16)
    prj = ctx.alloc("l2_prj", [128, T], F32)
    cnv = ctx.alloc("l2_cnv", [128, T], F32)
    Qfm = ctx.alloc("l2_Qfm", [128, T], BF16)
    Kfm = ctx.alloc("l2_Kfm", [128, T], BF16)
    Vtm = ctx.alloc("l2_Vtm", [128, NT, 128], BF16)
    Ktm = ctx.alloc("l2_Ktm", [128, NT, 128], BF16)
    Rk = ctx.alloc("l2_Rk", [128, 2 * NT], F32)
    nrow = ctx.alloc("l2_nrow", [1, 2 * T], BF16)
    state_sb = ctx.alloc("l2_state_sb", [128, 64], BF16)
    state_f32 = ctx.alloc("l2_state_f32", [128, 64], F32)
    Pbf = [ctx.alloc(f"l2_Pbf{h}", [128, 128], BF16) for h in range(2)]
    ps_sc = [ctx.sb["ps_ava"], ctx.sb["ps_avb"]]
    ps_o = ctx.sb["ps_sb"]
    ps_st = ctx.sb["ps_sa"]
    stg = ctx.sb["stage_bf"]

    def conv_silu(nm, dst_bf, hp, square_norms):
        cw = convw[nm]
        w3 = cw[:, hp, 3:4]
        p.op("vector", lambda e, w3=w3: e.scalar_tensor_tensor(
            out=cnv[:], in0=prj[:], scalar=w3, in1=prj[:],
            op0=ALU.mult, op1=ALU.add), reads=[prj, cw], writes=[cnv])
        for j in (2, 1, 0):
            sft = 3 - j
            wj = cw[:, hp, j:j + 1]
            p.op("vector", lambda e, wj=wj, sft=sft: e.scalar_tensor_tensor(
                out=cnv[:, sft:T], in0=prj[:, 0:T - sft], scalar=wj,
                in1=cnv[:, sft:T], op0=ALU.mult, op1=ALU.add),
                reads=[prj, cw, cnv], writes=[cnv])
        p.op("scalar", lambda e: e.activation(
            out=dst_bf[:], in_=cnv[:], func=AF.Silu),
            reads=[cnv], writes=[dst_bf])
        if square_norms is not None:
            base = square_norms
            for half in range(2):
                hsl = slice(half * 1024, (half + 1) * 1024)
                p.op("scalar", lambda e, hsl=hsl: e.activation(
                    out=stg[:], in_=dst_bf[:, hsl], func=AF.Square),
                    reads=[dst_bf], writes=[stg])
                for hh, ones in enumerate((ctx.consts["onesA"],
                                           ctx.consts["onesB"])):
                    for qq in range(2):
                        qsl = slice(qq * 512, (qq + 1) * 512)
                        p.op("tensor", lambda e, ones=ones, qsl=qsl: e.matmul(
                            ps_m1[0:1, :], ones[:], stg[:, qsl],
                            start=True, stop=True),
                            reads=[stg, ones], writes=[ps_m1])
                        dsl = slice(base + hh * T + half * 1024 + qq * 512,
                                    base + hh * T + half * 1024 + (qq + 1) * 512)
                        p.op("vector", lambda e, dsl=dsl: e.tensor_copy(
                            out=nrow[:, dsl], in_=ps_m1[0:1, :]),
                            reads=[ps_m1], writes=[nrow])

    import os as _os
    BAR = _os.environ.get("P2A_BARRIER", "0") == "1"
    for hp in range(NPAIR):
        if BAR and hp > 0:
            p.full_barrier()
        for nm, wname, dst in (("conv_q", "L2_attn_Wq", Qfm),
                               ("conv_k", "L2_attn_Wk", Kfm),
                               ("conv_v", "L2_attn_Wv", Vtm)):
            hb = ctx.hbm[wname][hp * 128:(hp + 1) * 128, :]
            sv2 = ws[:, 0:1024]
            p.dma("sync", lambda e, hb=hb, sv2=sv2: e.dma_start(
                out=sv2, in_=hb), writes=[ws])
            p.op("gpsimd", lambda e, sv2=sv2: e.tensor_copy(
                out=wpc[:], in_=sv2), reads=[ws], writes=[wpc])
            for c in range(8):
                pe_transpose(ctx, wTp[:, c, :],
                             wpc[:, c * 128:(c + 1) * 128], 128, 128)
            for tb in range(4):
                tsl = slice(tb * 512, (tb + 1) * 512)
                for c in range(8):
                    p.op("tensor", lambda e, tsl=tsl, c=c: e.matmul(
                        ps_m2[:, :], wTp[:, c, :], xn_fm[:, c, tsl],
                        start=(c == 0), stop=(c == 7)),
                        reads=[wTp, xn_fm], writes=[ps_m2])
                p.op("vector", lambda e, tsl=tsl: e.tensor_copy(
                    out=prj[:, tsl], in_=ps_m2[:, :]),
                    reads=[ps_m2], writes=[prj])
            if nm == "conv_q":
                conv_silu(nm, Qfm, hp, None)
            elif nm == "conv_k":
                conv_silu(nm, Kfm, hp, 0)
            else:
                conv_silu(nm, ctx.sb["l2_vstage"], hp, None)
        for hh in range(2):
            nrf = ctx.sb["l2_nrowf"]
            ec = ctx.fconst(1e-12)
            for seg in range(4):
                sl = slice(hh * T + seg * 512, hh * T + (seg + 1) * 512)
                p.op("scalar", lambda e, sl=sl, nrf=nrf, ec=ec: e.activation(
                    out=nrf[:], in_=nrow[:, sl], func=AF.Sqrt, bias=ec[0:1, :]),
                    reads=[nrow, "fconsts"], writes=[nrf])
                p.op("vector", lambda e, nrf=nrf: e.reciprocal(
                    out=nrf[:], in_=nrf[:]), reads=[nrf], writes=[nrf])
                p.op("vector", lambda e, sl=sl, nrf=nrf: e.tensor_copy(
                    out=nrow[:, sl], in_=nrf[:]), reads=[nrf], writes=[nrow])
            rkc = ctx.sb["l2_rkcol_bf"]
            for kt in range(NT):
                pe_transpose(ctx, rkc[:],
                             nrow[:, hh * T + kt * 128:hh * T + (kt + 1) * 128],
                             1, 128)
                p.op("vector", lambda e, kt=kt, hh=hh, rkc=rkc: e.tensor_copy(
                    out=Rk[:, hh * NT + kt:hh * NT + kt + 1], in_=rkc[:]),
                    reads=[rkc], writes=[Rk])
        vst = ctx.sb["l2_vstage"]
        for kt in range(NT):
            pe_transpose(ctx, Vtm[:, kt, :], vst[:, kt * 128:(kt + 1) * 128],
                         128, 128)
            pe_transpose(ctx, Ktm[:, kt, :], Kfm[:, kt * 128:(kt + 1) * 128],
                         128, 128)
            for hh in range(2):
                p.op("vector", lambda e, kt=kt, hh=hh: e.tensor_scalar(
                    out=Vtm[:, kt, hh * 64:(hh + 1) * 64],
                    in0=Vtm[:, kt, hh * 64:(hh + 1) * 64],
                    scalar1=Rk[:, hh * NT + kt:hh * NT + kt + 1],
                    scalar2=None, op0=ALU.mult),
                    reads=[Rk, Vtm], writes=[Vtm])
        SCAN = int(os.environ.get("P2A_SCAN", "9"))
        for c in range(NT if PHASE >= 2 else 0):
            csl = slice(c * 128, (c + 1) * 128)
            if c == 0 or SCAN < 3:
                p.op("vector", lambda e, hp=hp: e.tensor_copy(
                    out=state_sb[:], in_=w0[:, hp, :]),
                    reads=[w0], writes=[state_sb])
            else:
                p.op("vector", lambda e, hp=hp: e.scalar_tensor_tensor(
                    out=state_sb[:], in0=state_f32[:], scalar=1.0,
                    in1=w0[:, hp, :], op0=ALU.mult, op1=ALU.add),
                    reads=[state_f32, w0], writes=[state_sb])
            for hh, tp in (((0, (0, 0)), (1, (64, 0))) if SCAN >= 1 else ()):
                ob = hh * 512
                p.op("tensor", lambda e, hh=hh, tp=tp, csl=csl, ob=ob:
                     e.matmul(
                         ps_o[:, ob:ob + 64],
                         Qfm[hh * 64:(hh + 1) * 64, csl],
                         state_sb[hh * 64:(hh + 1) * 64, :],
                         start=True, stop=(SCAN < 2), tile_position=tp),
                     reads=[Qfm, state_sb], writes=[ps_o])
            for hh, tp in (((0, (0, 0)), (1, (64, 0))) if SCAN >= 2 else ()):
                p.op("tensor", lambda e, hh=hh, tp=tp, csl=csl: e.matmul(
                    ps_sc[hh][:, 0:128], Kfm[hh * 64:(hh + 1) * 64, csl],
                    Qfm[hh * 64:(hh + 1) * 64, csl],
                    start=True, stop=True, tile_position=tp),
                    reads=[Kfm, Qfm], writes=[ps_sc[hh]])
                p.op("vector", lambda e, hh=hh: e.tensor_tensor(
                    out=Pbf[hh][:], in0=ps_sc[hh][:, 0:128], in1=tril[:],
                    op=ALU.mult), reads=[ps_sc[hh], tril, "c_tril_incl"],
                    writes=[Pbf[hh]])
                p.op("tensor", lambda e, hh=hh, c=c: e.matmul(
                    ps_o[:, hh * 512:hh * 512 + 64], Pbf[hh][:],
                    Vtm[:, c, hh * 64:(hh + 1) * 64],
                    start=False, stop=True),
                    reads=[Pbf[hh], Vtm], writes=[ps_o])
            if SCAN >= 1:
                for hh in range(2):
                    p.op("vector", lambda e, c=c, hh=hh, hp=hp: e.tensor_copy(
                        out=O_all[:, c,
                                  hp * 128 + hh * 64:hp * 128 + hh * 64 + 64],
                        in_=ps_o[:, hh * 512:hh * 512 + 64]),
                        reads=[ps_o], writes=[O_all])
            for hh, tp in (((0, (0, 0)), (1, (0, 64))) if SCAN >= 3 else ()):
                p.op("tensor", lambda e, hh=hh, tp=tp, c=c: e.matmul(
                    ps_st[hh * 64:(hh + 1) * 64, hh * 512:hh * 512 + 64],
                    Ktm[:, c, hh * 64:(hh + 1) * 64],
                    Vtm[:, c, hh * 64:(hh + 1) * 64],
                    start=True, stop=True, tile_position=tp),
                    reads=[Ktm, Vtm], writes=[ps_st])
            if SCAN >= 3:
                if c == 0:
                    for hh in range(2):
                        p.op("vector", lambda e, hh=hh: e.tensor_copy(
                            out=state_f32[hh * 64:(hh + 1) * 64, :],
                            in_=ps_st[hh * 64:(hh + 1) * 64,
                                      hh * 512:hh * 512 + 64]),
                            reads=[ps_st], writes=[state_f32])
                else:
                    for hh in range(2):
                        p.op("vector", lambda e, hh=hh: e.scalar_tensor_tensor(
                            out=state_f32[hh * 64:(hh + 1) * 64, :],
                            in0=ps_st[hh * 64:(hh + 1) * 64,
                                      hh * 512:hh * 512 + 64],
                            scalar=1.0,
                            in1=state_f32[hh * 64:(hh + 1) * 64, :],
                            op0=ALU.mult, op1=ALU.add),
                            reads=[ps_st, state_f32], writes=[state_f32])
        if hp == 0:
            ctx.tap_full("l2_O_p0", O_all)

    ctx.tap_full("l2_O_raw", O_all)
    ctx.tap_full("l2_Qfm", Qfm)
    ctx.tap_full("l2_Kfm", Kfm)
    ctx.tap_full("l2_Vtm", Vtm)
    ctx.tap_full("l2_O_mid", O_all)
    ctx.free("l2_wTp", "l2_wpc", "l2_prj", "l2_cnv", "l2_Qfm", "l2_Kfm",
             "l2_Vtm", "l2_Ktm", "l2_Rk", "l2_nrow", "l2_state_sb",
             "l2_Pbf0", "l2_Pbf1", "l2_vstage", "l2_nrowf", "l2_rkcol_bf",
             "l2_state_f32")

    import os as _os3
    if _os3.environ.get("P2A_BAR3", "0") == "1":
        p.full_barrier()
    if PHASE < 3:
        ctx.free(*[n for n in list(ctx.live) if n.startswith("l2_")])
        zz = ctx.sb["scratch_f32_a"]
        p.op("vector", lambda e: e.memset(zz[:], 0.0), writes=[zz])
        for tb in range(NT):
            p.dma("sync", lambda e, tb=tb: e.dma_start(
                out=s_state.rearrange("(i p) d -> p i d", p=128)[:, tb, :],
                in_=zz[:]), reads=[zz], writes=["s_state_d"], is_out=True)
        return
    col2 = ctx.alloc("l2_col2", [128, 64], F32)
    cen = ctx.sb["scratch_f32_a"]
    sq = ctx.sb["scratch_f32_b"]
    epsc = ctx.fconst(1e-5)
    for kt in range(NT):
        ov = O_all[:, kt, :].rearrange("p (h d) -> p h d", d=64)
        cv = cen[:].rearrange("p (h d) -> p h d", d=64)
        p.op("vector", lambda e, ov=ov: e.reduce_sum(
            out=col2[:, 0:16], in_=ov, axis=AX.X),
            reads=[O_all], writes=[col2])
        p.op("vector", lambda e: e.tensor_scalar_mul(
            out=col2[:, 0:16], in0=col2[:, 0:16], scalar1=1.0 / 64),
            reads=[col2], writes=[col2])
        p.op("vector", lambda e, ov=ov, cv=cv: e.tensor_tensor(
            out=cv, in0=ov,
            in1=col2[:, 0:16, None].broadcast_to([128, 16, 64]),
            op=ALU.subtract), reads=[O_all, col2], writes=[cen])
        p.op("scalar", lambda e: e.activation(
            out=sq[:], in_=cen[:], func=AF.Square), reads=[cen], writes=[sq])
        p.op("vector", lambda e: e.reduce_sum(
            out=col2[:, 16:32], in_=sq[:].rearrange(
                "p (h d) -> p h d", d=64), axis=AX.X),
            reads=[sq], writes=[col2])
        p.op("scalar", lambda e, epsc=epsc: e.activation(
            out=col2[:, 32:48], in_=col2[:, 16:32], func=AF.Sqrt,
            bias=epsc, scale=1.0 / 64),
            reads=[col2, "fconsts"], writes=[col2])
        p.op("vector", lambda e: e.reciprocal(
            out=col2[:, 32:48], in_=col2[:, 32:48]),
            reads=[col2], writes=[col2])
        p.op("vector", lambda e, cv=cv: e.tensor_tensor(
            out=cv, in0=cv,
            in1=col2[:, 32:48, None].broadcast_to([128, 16, 64]),
            op=ALU.mult), reads=[cen, col2], writes=[cen])
        p.op("vector", lambda e, cv=cv: e.tensor_tensor(
            out=cv, in0=cv,
            in1=grow64[:, 0:64][:, None, :].broadcast_to([128, 16, 64]),
            op=ALU.mult), reads=[cen, grow64], writes=[cen])
        p.op("vector", lambda e, ov=ov, cv=cv: e.tensor_tensor(
            out=ov, in0=cv,
            in1=grow64[:, 64:128][:, None, :].broadcast_to([128, 16, 64]),
            op=ALU.add), reads=[cen, grow64], writes=[O_all])

    ctx.tap_full("l2_O_ln", O_all)
    import os as _os2
    if _os2.environ.get("P2A_BAR2", "0") == "1":
        p.full_barrier()
    if PHASE < 4:
        ctx.free(*[n for n in list(ctx.live) if n.startswith("l2_")])
        zz = ctx.sb["scratch_f32_a"]
        p.op("vector", lambda e: e.memset(zz[:], 0.0), writes=[zz])
        for tb in range(NT):
            p.dma("sync", lambda e, tb=tb: e.dma_start(
                out=s_state.rearrange("(i p) d -> p i d", p=128)[:, tb, :],
                in_=zz[:]), reads=[zz], writes=["s_state_d"], is_out=True)
        return
    wgT = ctx.alloc("l2_wgT", [128, 8, 1024], BF16)
    woT = ctx.alloc("l2_woT", [128, 8, 1024], BF16)
    wpc2 = ctx.alloc("l2_wpc2", [128, 1024], BF16)
    import os as _os5
    OWNSTAGE = _os5.environ.get("P2A_OWNSTAGE", "0") == "1"
    if OWNSTAGE:
        gstage = ctx.alloc("l2_gstage", [128, 1024], F32)
    for wname, wT in (("L2_attn_Wg", wgT), ("L2_attn_Wo", woT)):
        for r in range(8):
            if OWNSTAGE:
                hbr = ctx.hbm[wname][r * 128:(r + 1) * 128, :]
                p.dma("sync", lambda e, hbr=hbr: e.dma_start(
                    out=gstage[:], in_=hbr), writes=[gstage])
                p.op("gpsimd", lambda e: e.tensor_copy(
                    out=wpc2[:], in_=gstage[:]),
                    reads=[gstage], writes=[wpc2])
            else:
                load_convert(ctx, wpc2[:],
                             ctx.hbm[wname][r * 128:(r + 1) * 128, :],
                             128, 1024)
            for c in range(8):
                pe_transpose(ctx, wT[:, c, r * 128:(r + 1) * 128],
                             wpc2[:, c * 128:(c + 1) * 128], 128, 128)
    og_bf = ctx.alloc("l2_og_bf", [128, 1024], BF16)
    og_fm = ctx.alloc("l2_og_fm", [128, 8, 128], BF16)
    import os as _os4
    if _os4.environ.get("P2A_BAR4", "0") == "1":
        p.full_barrier()
    for tb in range(NT):
        for half in range(2):
            esl = slice(half * 512, (half + 1) * 512)
            for c in range(8):
                p.op("tensor", lambda e, tb=tb, c=c, esl=esl: e.matmul(
                    ps_m2[:, :], xn_fm[:, c, tb * 128:(tb + 1) * 128],
                    wgT[:, c, esl], start=(c == 0), stop=(c == 7)),
                    reads=[xn_fm, wgT], writes=[ps_m2])
            p.op("vector", lambda e, tb=tb, esl=esl: e.tensor_tensor(
                out=og_bf[:, esl], in0=ps_m2[:, :],
                in1=O_all[:, tb, esl], op=ALU.mult),
                reads=[ps_m2, O_all], writes=[og_bf])
        for c in range(8):
            pe_transpose(ctx, og_fm[:, c, :],
                         og_bf[:, c * 128:(c + 1) * 128], 128, 128)
        for half in range(2):
            esl = slice(half * 512, (half + 1) * 512)
            for c in range(8):
                p.op("tensor", lambda e, c=c, esl=esl: e.matmul(
                    ps_m2[:, :], og_fm[:, c, :], woT[:, c, esl],
                    start=(c == 0), stop=(c == 7)),
                    reads=[og_fm, woT], writes=[ps_m2])
            p.op("vector", lambda e, esl=esl: e.tensor_copy(
                out=cen[:, esl], in_=ps_m2[:, :]), reads=[ps_m2],
                writes=[cen])
        p.dma("sync", lambda e, tb=tb: e.dma_start(
            out=s_state.rearrange("(i p) d -> p i d", p=128)[:, tb, :],
            in_=cen[:]), reads=[cen], writes=["s_state_d"], is_out=True)
    ctx.free(*[n for n in list(ctx.live) if n.startswith("l2_")])


def emit_layer2b(ctx):
    p = ctx.p
    emit_ln(ctx)
    xn_fm = ctx.sb["xn_fm"]
    ident = ctx.consts["ident"]
    ps_m1 = ctx.sb["ps_m1"]
    ps_m2 = ctx.sb["ps_m2"]
    ws = ctx.sb["wstage"]
    s_state = ctx.state_in("s_state", [T, D])
    nsc = ctx.alloc("l2b_nsc", [1, 64], F32)
    nsc_bf = ctx.alloc("l2b_nscbf", [1, 2], BF16)
    bias_a = ctx.alloc("l2b_bias_a", [128, 1], F32)
    bias_b = ctx.alloc("l2b_bias_b", [128, 1], F32)
    ctx.alloc("fl_rc_a", [128, 1], F32)
    ctx.alloc("fl_rc_b", [128, 1], F32)

    wvT = ctx.alloc("l2b_wvT", [128, 8, 512], BF16)
    wpc = ctx.alloc("l2b_wpc", [128, 1024], BF16)
    for r in range(4):
        load_convert(ctx, wpc[:],
                     ctx.hbm["L2_ffn_Wv"].rearrange(
                         "v d c -> (v d) c")[r * 128:(r + 1) * 128, :],
                     128, 1024)
        for c in range(8):
            pe_transpose(ctx, wvT[:, c, r * 128:(r + 1) * 128],
                         wpc[:, c * 128:(c + 1) * 128], 128, 128)
    PV = ctx.alloc("l2b_PV", [128, NT, 512], BF16)
    for tb in range(NT):
        for c in range(8):
            p.op("tensor", lambda e, tb=tb, c=c: e.matmul(
                ps_m2[:, :], xn_fm[:, c, tb * 128:(tb + 1) * 128],
                wvT[:, c, :], start=(c == 0), stop=(c == 7)),
                reads=[xn_fm, wvT], writes=[ps_m2])
        p.op("vector", lambda e, tb=tb: e.tensor_copy(
            out=PV[:, tb, :], in_=ps_m2[:, :]), reads=[ps_m2], writes=[PV])
    ctx.free("l2b_wvT")
    wsT = ctx.alloc("l2b_wsT", [128, 8, 128], BF16)
    GV = ctx.alloc("l2b_GV", [128, NT, 128], F32)
    MIX = ctx.alloc("l2b_MIX", [128, NT, 128], BF16)
    MASK = ctx.alloc("l2b_MASK", [128, NT, 128], BF16)
    m16 = ctx.alloc("l2b_m16", [128, NT * 16], F32)
    eqt = ctx.alloc("l2b_eqt", [128, NT * 128], F32)
    for wname, dst, use_sig in (("L2_ffn_Ws", MIX, True),
                                ("L2_ffn_Wd", MASK, False)):
        load_convert(ctx, wpc[:],
                     ctx.hbm[wname].rearrange("h v c -> (h v) c"), 128, 1024)
        for c in range(8):
            pe_transpose(ctx, wsT[:, c, :],
                         wpc[:, c * 128:(c + 1) * 128], 128, 128)
        for tb in range(NT):
            for c in range(8):
                p.op("tensor", lambda e, tb=tb, c=c: e.matmul(
                    ps_m1[:, 0:128], xn_fm[:, c, tb * 128:(tb + 1) * 128],
                    wsT[:, c, :], start=(c == 0), stop=(c == 7)),
                    reads=[xn_fm, wsT], writes=[ps_m1])
            p.op("vector", lambda e, tb=tb: e.tensor_copy(
                out=GV[:, tb, :], in_=ps_m1[:, 0:128]),
                reads=[ps_m1], writes=[GV])
        gv4 = GV[:].rearrange("p t (h v) -> p (t h) v", v=8)
        eq4 = eqt[:].rearrange("p (a v) -> p a v", v=8)
        p.op("vector", lambda e, gv4=gv4: e.tensor_reduce(
            out=m16[:], in_=gv4, axis=AX.X, op=ALU.max),
            reads=[GV], writes=[m16])
        p.op("vector", lambda e, gv4=gv4, eq4=eq4: e.tensor_tensor(
            out=eq4, in0=gv4,
            in1=m16[:, :, None].broadcast_to([128, NT * 16, 8]),
            op=ALU.is_equal), reads=[GV, m16], writes=[eqt])
        p.op("vector", lambda e, gv4=gv4, eq4=eq4: e.scalar_tensor_tensor(
            out=eqt[:], in0=eqt[:], scalar=-1e30,
            in1=GV[:].rearrange("p t v -> p (t v)"),
            op0=ALU.mult, op1=ALU.add), reads=[eqt, GV], writes=[eqt])
        p.op("vector", lambda e, eq4=eq4: e.tensor_reduce(
            out=m16[:], in_=eq4, axis=AX.X, op=ALU.max),
            reads=[eqt], writes=[m16])
        p.op("vector", lambda e, gv4=gv4, eq4=eq4: e.tensor_tensor(
            out=eq4, in0=gv4,
            in1=m16[:, :, None].broadcast_to([128, NT * 16, 8]),
            op=ALU.is_ge), reads=[GV, m16], writes=[eqt])
        if use_sig:
            p.op("scalar", lambda e: e.activation(
                out=GV[:], in_=GV[:], func=AF.Sigmoid),
                reads=[GV], writes=[GV])
            p.op("vector", lambda e, dst=dst: e.tensor_tensor(
                out=dst[:].rearrange("p t v -> p (t v)"),
                in0=GV[:].rearrange("p t v -> p (t v)"), in1=eqt[:],
                op=ALU.mult), reads=[GV, eqt], writes=[dst])
        else:
            p.op("vector", lambda e, dst=dst: e.tensor_copy(
                out=dst[:].rearrange("p t v -> p (t v)"), in_=eqt[:]),
                reads=[eqt], writes=[dst])
    ctx.free("l2b_wsT", "l2b_GV", "l2b_m16", "l2b_eqt", "l2b_wpc")

    ctx.alloc("fl_PT0", [128, 16, 256], BF16)
    ctx.alloc("fl_PT1", [128, 16, 256], BF16)
    wT = ctx.alloc("l2b_wT", [128, 8, 128], BF16)
    zacc = ctx.alloc("l2b_zacc", [128, NT, 512], F32)
    p.op("vector", lambda e: e.memset(zacc[:], 0.0), writes=[zacc])
    VA = ctx.alloc("l2b_VA", [128, NT, 130], BF16)
    Qp = ctx.alloc("l2b_Qp", [128, T], BF16)
    Kp = ctx.alloc("l2b_Kp", [128, T], BF16)
    Opair = ctx.alloc("l2b_Opair", [128, NT, 128], BF16)
    wpc3 = ctx.sb["stage_bf"]
    vmt = ctx.alloc("l2b_ztmp", [128, NT * 64], BF16)
    sqst = ctx.sb["stage_bf"]
    ps_sa = ctx.sb["ps_sa"]
    ps_sb = ctx.sb["ps_sb"]
    ps_ava = ctx.sb["ps_ava"]
    ps_avb = ctx.sb["ps_avb"]
    onesA, onesB = ctx.consts["onesA"], ctx.consts["onesB"]
    onesrow = ctx.consts["onesrow"]

    for hp in range(8):
        for wname, dstp in (("L2_ffn_Wq", Qp), ("L2_ffn_Wk", Kp)):
            hbq = ctx.hbm[wname].rearrange(
                "h v c -> (h v) c")[hp * 128:(hp + 1) * 128, :]
            p.dma("sync", lambda e, hbq=hbq: e.dma_start(
                out=ctx.sb["wstage"][:, 0:1024], in_=hbq),
                writes=["wstage"])
            p.op("gpsimd", lambda e: e.tensor_copy(
                out=wpc3[:], in_=ctx.sb["wstage"][:, 0:1024]),
                reads=["wstage"], writes=["stage_bf"])
            for c in range(8):
                pe_transpose(ctx, wT[:, c, :],
                             wpc3[:, c * 128:(c + 1) * 128], 128, 128)
            for tb in range(4):
                tsl = slice(tb * 512, (tb + 1) * 512)
                for c in range(8):
                    p.op("tensor", lambda e, tsl=tsl, c=c: e.matmul(
                        ps_m2[:, :], wT[:, c, :], xn_fm[:, c, tsl],
                        start=(c == 0), stop=(c == 7)),
                        reads=[wT, xn_fm], writes=[ps_m2])
                p.op("vector", lambda e, tsl=tsl, dstp=dstp: e.tensor_copy(
                    out=dstp[:, tsl], in_=ps_m2[:, :]),
                    reads=[ps_m2], writes=[dstp])
        p.op("vector", lambda e: e.memset(VA[:, :, 64:65], 1.0), writes=[VA])
        p.op("vector", lambda e: e.memset(VA[:, :, 129:130], 1.0),
             writes=[VA])
        for hh in range(2):
            h = hp * 2 + hh
            voff = hh * 65
            for e_ in range(8):
                srcs = [PV, MIX]
                if e_ == 0:
                    p.op("vector", lambda e, h=h, voff=voff, e_=e_:
                         e.tensor_mul(
                             out=VA[:, :, voff:voff + 64],
                             in0=PV[:, :, e_ * 64:(e_ + 1) * 64],
                             in1=MIX[:, :, h * 8 + e_:h * 8 + e_ + 1]
                             .broadcast_to([128, NT, 64])),
                         reads=srcs, writes=[VA])
                else:
                    p.op("vector", lambda e, h=h, voff=voff, e_=e_:
                         e.tensor_mul(
                             out=vmt[:].rearrange("p (t d) -> p t d", d=64),
                             in0=PV[:, :, e_ * 64:(e_ + 1) * 64],
                             in1=MIX[:, :, h * 8 + e_:h * 8 + e_ + 1]
                             .broadcast_to([128, NT, 64])),
                         reads=srcs, writes=[vmt])
                    p.op("gpsimd", lambda e, voff=voff: e.tensor_add(
                        out=VA[:, :, voff:voff + 64],
                        in0=VA[:, :, voff:voff + 64],
                        in1=vmt[:].rearrange("p (t d) -> p t d", d=64)),
                        reads=[vmt, VA], writes=[VA])
        for si, src_ in enumerate((Qp, Kp)):
            for tb in range(4):
                sl = slice(tb * 512, (tb + 1) * 512)
                p.op("scalar", lambda e, src_=src_, sl=sl: e.activation(
                    out=sqst[:, 0:512], in_=src_[:, sl], func=AF.Square),
                    reads=[src_], writes=[sqst])
                for h2, ones in enumerate((onesA, onesB)):
                    p.op("tensor", lambda e, ones=ones: e.matmul(
                        ps_m1[0:1, :], ones[:], sqst[:, 0:512],
                        start=True, stop=True),
                        reads=[sqst, ones], writes=[ps_m1])
                    o = (si * 2 + h2) * 4 + tb
                    p.op("vector", lambda e, o=o: e.reduce_max(
                        out=nsc[:, o:o + 1], in_=ps_m1[0:1, :], axis=AX.X),
                        reads=[ps_m1], writes=[nsc])
        for h2 in range(2):
            o = 16 + h2 * 8
            qs4 = slice(h2 * 4, h2 * 4 + 4)
            ks4 = slice(8 + h2 * 4, 8 + h2 * 4 + 4)
            p.op("vector", lambda e, o=o, qs4=qs4: e.reduce_max(
                out=nsc[:, o:o + 1], in_=nsc[:, qs4], axis=AX.X),
                reads=[nsc], writes=[nsc])
            p.op("vector", lambda e, o=o, ks4=ks4: e.reduce_max(
                out=nsc[:, o + 1:o + 2], in_=nsc[:, ks4], axis=AX.X),
                reads=[nsc], writes=[nsc])
            p.op("vector", lambda e, o=o: e.tensor_mul(
                out=nsc[:, o + 2:o + 3], in0=nsc[:, o:o + 1],
                in1=nsc[:, o + 1:o + 2]), reads=[nsc], writes=[nsc])
            p.op("scalar", lambda e, o=o: e.activation(
                out=nsc[:, o + 3:o + 4], in_=nsc[:, o + 2:o + 3],
                func=AF.Sqrt), reads=[nsc], writes=[nsc])
            p.op("vector", lambda e, o=o: e.tensor_scalar_mul(
                out=nsc[:, o + 4:o + 5], in0=nsc[:, o + 3:o + 4],
                scalar1=-0.125), reads=[nsc], writes=[nsc])
        p.op("vector", lambda e: e.tensor_copy(out=nsc_bf[:, 0:1],
                                               in_=nsc[:, 20:21]),
             reads=[nsc], writes=[nsc_bf])
        p.op("vector", lambda e: e.tensor_copy(out=nsc_bf[:, 1:2],
                                               in_=nsc[:, 28:29]),
             reads=[nsc], writes=[nsc_bf])
        for biast, cs in ((bias_a, slice(0, 1)), (bias_b, slice(1, 2))):
            p.op("tensor", lambda e, cs=cs: e.matmul(
                ps_m1[:, 0:1], onesrow[:], nsc_bf[:, cs],
                start=True, stop=True),
                reads=[nsc_bf, onesrow], writes=[ps_m1])
            p.op("vector", lambda e, biast=biast: e.tensor_copy(
                out=biast[:], in_=ps_m1[:, 0:1]), reads=[ps_m1],
                writes=[biast])
        ctx.sb["fl_K0"] = Kp
        ctx.sb["fl_K1"] = Kp
        ctx.sb["fl_V"] = VA

        def l2b_sink(qtile, psa, psb, rca, rcb):
            for h2, (psav, rc) in enumerate(((psa, rca), (psb, rcb))):
                p.op("vector", lambda e, h2=h2, psav=psav, rc=rc,
                     qtile=qtile: e.tensor_scalar(
                         out=Opair[:, qtile, h2 * 64:(h2 + 1) * 64],
                         in0=psav[:, 0:64], scalar1=rc[:], scalar2=None,
                         op0=ALU.mult),
                     reads=[psav, rc], writes=[Opair])

        flash256(ctx, Qp,
                 lambda h2, ksl: Kp[h2 * 64:(h2 + 1) * 64, ksl],
                 lambda h2, kt: VA[:, kt, h2 * 65:(h2 + 1) * 65],
                 bias_a, bias_b, l2b_sink,
                 lambda qb: list(range(16)),
                 lambda qb, kt, qs: None)
        ztmp = ctx.sb["l2b_ztmp"]
        for h2 in range(2):
            h = hp * 2 + h2
            for e_ in range(8):
                p.op("vector", lambda e, h2=h2, h=h, e_=e_: e.tensor_mul(
                    out=ztmp[:].rearrange("p (t d) -> p t d", d=64),
                    in0=Opair[:, :, h2 * 64:(h2 + 1) * 64],
                    in1=MASK[:, :, h * 8 + e_:h * 8 + e_ + 1]
                    .broadcast_to([128, NT, 64])),
                    reads=[Opair, MASK], writes=[ztmp])
                p.op("gpsimd", lambda e, e_=e_: e.tensor_add(
                    out=zacc[:].rearrange(
                        "p t (v d) -> p t v d", d=64)[:, :, e_, :],
                    in0=zacc[:].rearrange(
                        "p t (v d) -> p t v d", d=64)[:, :, e_, :],
                    in1=ztmp[:].rearrange("p (t d) -> p t d", d=64)),
                    reads=[ztmp, zacc], writes=[zacc])

    ctx.free("l2b_VA", "l2b_Qp", "l2b_Kp", "fl_PT0", "fl_PT1",
             "fl_rc_a", "fl_rc_b", "l2b_Opair", "l2b_wT",
             "l2b_PV", "l2b_MIX", "l2b_MASK", "l2b_ztmp")

    zbf = ctx.alloc("l2b_zbf", [128, NT, 512], BF16)
    p.op("vector", lambda e: e.tensor_copy(out=zbf[:], in_=zacc[:]),
         reads=[zacc], writes=[zbf])
    ctx.free("l2b_zacc")
    z_fm = ctx.alloc("l2b_zfm", [128, 4, T], BF16)
    for tb in range(NT):
        for pc in range(4):
            pe_transpose(ctx, z_fm[:, pc, tb * 128:(tb + 1) * 128],
                         zbf[:, tb, pc * 128:(pc + 1) * 128], 128, 128)
    ctx.free("l2b_zbf")
    woT2 = ctx.alloc("l2b_woT2", [128, 4, 1024], BF16)
    wpc4 = ctx.alloc("l2b_wpc4", [128, 8, 64], BF16)
    for e_ in range(8):
        sv3 = ws[:, 0:512].rearrange("p (c j) -> p c j", j=64)
        p.dma("sync", lambda e, e_=e_, sv3=sv3: e.dma_start(
            out=sv3, in_=ctx.hbm["L2_ffn_Wo"][e_].rearrange(
                "(c p) j -> p c j", p=128)), writes=[ws])
        p.op("gpsimd", lambda e, sv3=sv3: e.tensor_copy(
            out=wpc4[:], in_=sv3), reads=[ws], writes=[wpc4])
        for r in range(8):
            pe_transpose(ctx, woT2[(e_ % 2) * 64:(e_ % 2) * 64 + 64,
                                   e_ // 2, r * 128:(r + 1) * 128],
                         wpc4[:, r, :], 128, 64)
    x = ctx.sb["x_tm"]
    gt = ctx.sb["scratch_f32_b"]
    ctx.sb["scratch_f32_c"] = ctx.alloc("l2b_scrc", [128, D], F32)
    st = ctx.sb["scratch_f32_c"]
    for tb in range(NT):
        for half in range(2):
            esl = slice(half * 512, (half + 1) * 512)
            for pc in range(4):
                p.op("tensor", lambda e, tb=tb, pc=pc, esl=esl: e.matmul(
                    ps_m2[:, :], z_fm[:, pc, tb * 128:(tb + 1) * 128],
                    woT2[:, pc, esl], start=(pc == 0), stop=(pc == 3)),
                    reads=[z_fm, woT2], writes=[ps_m2])
            p.op("scalar", lambda e, esl=esl: e.activation(
                out=gt[:, esl], in_=ps_m2[:, :], func=AF.Silu),
                reads=[ps_m2], writes=[gt])
        p.dma("sync", lambda e, tb=tb: e.dma_start(
            out=st[:], in_=s_state.rearrange(
                "(i p) d -> p i d", p=128)[:, tb, :]),
            reads=["s_state_d"], writes=[st])
        p.op("vector", lambda e: e.tensor_mul(
            out=gt[:], in0=gt[:], in1=st[:]), reads=[gt, st], writes=[gt])
        p.op("vector", lambda e, tb=tb: e.tensor_add(
            out=x[:, tb, :], in0=x[:, tb, :], in1=gt[:]),
            reads=[gt, x], writes=[x])
    ctx.free(*[n for n in list(ctx.live) if n.startswith("l2b_")])



def emit_quant(ctx, dst_bf, src_f32, width, tag):
    p = ctx.p
    nb = width // 16
    q = ctx.sb["l3_qs"]
    qr = ctx.sb["l3_qr"]
    ab = ctx.sb["l3_qab"]
    g1 = ctx.sb["l3_qg1"]
    g2 = ctx.sb["l3_qg2"]
    sg = ctx.sb["l3_qsg"]
    v3 = lambda t, w=width: t[0:128, 0:w].rearrange("p (b k) -> p b k", k=16)
    p.op("vector", lambda e: e.tensor_reduce(
        out=q[:, 0:nb], in_=src_f32.rearrange("p (b k) -> p b k", k=16),
        axis=AX.X, op=ALU.max, apply_absolute_value=True),
        reads=[src_f32], writes=[q])
    p.op("vector", lambda e: e.tensor_scalar(
        out=q[:, 0:nb], in0=q[:, 0:nb], scalar1=1.0 / 6.0, scalar2=1e-12,
        op0=ALU.mult, op1=ALU.max), reads=[q], writes=[q])
    p.op("vector", lambda e: e.reciprocal(out=qr[:, 0:nb], in_=q[:, 0:nb]),
         reads=[q], writes=[qr])
    p.op("scalar", lambda e: e.activation(
        out=ab[:, 0:width], in_=src_f32, func=AF.Abs),
        reads=[src_f32], writes=[ab])
    p.op("scalar", lambda e: e.activation(
        out=sg[:, 0:width], in_=src_f32, func=AF.Sign),
        reads=[src_f32], writes=[sg])
    p.op("vector", lambda e: e.tensor_tensor(
        out=v3(ab), in0=v3(ab),
        in1=qr[:, 0:nb, None].broadcast_to([128, nb, 16]), op=ALU.mult),
        reads=[ab, qr], writes=[ab])
    p.op("vector", lambda e: e.tensor_scalar(
        out=g1[:, 0:width], in0=ab[:, 0:width], scalar1=0.25, scalar2=None,
        op0=ALU.is_gt), reads=[ab], writes=[g1])
    for m in (0.75, 1.25, 1.75):
        p.op("vector", lambda e, m=m: e.scalar_tensor_tensor(
            out=g1[:, 0:width], in0=ab[:, 0:width], scalar=m,
            in1=g1[:, 0:width], op0=ALU.is_gt, op1=ALU.add),
            reads=[ab, g1], writes=[g1])
    p.op("vector", lambda e: e.tensor_scalar(
        out=g2[:, 0:width], in0=ab[:, 0:width], scalar1=2.5, scalar2=None,
        op0=ALU.is_gt), reads=[ab], writes=[g2])
    p.op("vector", lambda e: e.scalar_tensor_tensor(
        out=g2[:, 0:width], in0=ab[:, 0:width], scalar=3.5,
        in1=g2[:, 0:width], op0=ALU.is_gt, op1=ALU.add),
        reads=[ab, g2], writes=[g2])
    p.op("vector", lambda e: e.tensor_scalar(
        out=ab[:, 0:width], in0=ab[:, 0:width], scalar1=5.0, scalar2=2.0,
        op0=ALU.is_gt, op1=ALU.mult), reads=[ab], writes=[ab])
    p.op("gpsimd", lambda e: e.tensor_add(
        out=g2[:, 0:width], in0=g2[:, 0:width], in1=ab[:, 0:width]),
        reads=[ab, g2], writes=[g2])
    p.op("vector", lambda e: e.scalar_tensor_tensor(
        out=g1[:, 0:width], in0=g1[:, 0:width], scalar=0.5,
        in1=g2[:, 0:width], op0=ALU.mult, op1=ALU.add),
        reads=[g1, g2], writes=[g1])
    p.op("gpsimd", lambda e: e.tensor_mul(
        out=g1[:, 0:width], in0=g1[:, 0:width], in1=sg[:, 0:width]),
        reads=[g1, sg], writes=[g1])
    p.op("vector", lambda e: e.tensor_tensor(
        out=dst_bf.rearrange("p (b k) -> p b k", k=16), in0=v3(g1),
        in1=q[:, 0:nb, None].broadcast_to([128, nb, 16]), op=ALU.mult),
        reads=[g1, q], writes=[dst_bf])


def emit_layer3(ctx):
    p = ctx.p
    nc = ctx.nc
    emit_ln(ctx)
    xn_fm = ctx.sb["xn_fm"]
    ident = ctx.consts["ident"]
    onesrow = ctx.consts["onesrow"]
    ones512 = ctx.consts["ones512"]
    ps_m1 = ctx.sb["ps_m1"]
    ps_m2 = ctx.sb["ps_m2"]
    ws = ctx.sb["wstage"]

    ctx.alloc("l3_qs", [128, 176], F32)
    ctx.alloc("l3_qr", [128, 176], F32)
    ctx.alloc("l3_qab", [128, 2816], F32)
    ctx.alloc("l3_qg1", [128, 2816], F32)
    ctx.alloc("l3_qg2", [128, 2816], F32)
    ctx.alloc("l3_qsg", [128, 2816], BF16)

    h_dram = nc.dram_tensor("l3_hq_fm", [22 * 128, T], BF16,
                            kind="Internal").ap()
    wdqT_dram = nc.dram_tensor("l3_wdqT", [22 * 128, 1024], BF16,
                               kind="Internal").ap()
    ypart_dram = nc.dram_tensor("l3_ypart", [T, D], F32,
                                kind="Internal").ap()
    g_dram = nc.dram_tensor("l3_g", [T, D], F32, kind="Internal").ap()

    xq_fm = ctx.alloc("l3_xq_fm", [128, ND, T], BF16)
    xnr = ctx.sb["scratch_f32_a"]
    xq_t = ctx.sb["stage_bf"]
    for tb in range(NT):
        xn_rec(ctx, tb, xnr[:])
        emit_quant(ctx, xq_t[:], xnr[:], D, f"xq{tb}")
        for c in range(ND):
            pe_transpose(ctx, xq_fm[:, c, tb * 128:(tb + 1) * 128],
                         xq_t[:, c * 128:(c + 1) * 128], 128, 128)

    wq_f = ctx.alloc("l3_wq_f", [128, 1024], BF16)
    wu_f = ctx.alloc("l3_wu_f", [128, 1024], BF16)
    wqT = ctx.alloc("l3_wqT", [128, 8, 128], BF16)
    wuT = ctx.alloc("l3_wuT", [128, 8, 128], BF16)
    hslab = ctx.alloc("l3_hslab", [128, NT, 128], BF16)
    hf32 = ctx.alloc("l3_hf32", [128, 128], F32)
    ps_g = ctx.sb["ps_ava"]
    ps_u = ctx.sb["ps_avb"]
    for r in range(22):
        for wname, wqt, wqc in (("L3_ffn_w_gate", wqT, wq_f),
                                ("L3_ffn_w_up", wuT, wu_f)):
            sv = ws[:, 0:1024]
            hb = ctx.hbm[wname][r * 128:(r + 1) * 128, :]
            p.dma("sync", lambda e, hb=hb, sv=sv: e.dma_start(
                out=sv, in_=hb), writes=[ws])
            emit_quant(ctx, wqc[:], sv, 1024, f"w{r}")
            for c in range(8):
                pe_transpose(ctx, wqt[:, c, :],
                             wqc[:, c * 128:(c + 1) * 128], 128, 128)
        for tb in range(NT):
            for c in range(8):
                p.op("tensor", lambda e, tb=tb, c=c: e.matmul(
                    ps_g[:, 0:128], xq_fm[:, c, tb * 128:(tb + 1) * 128],
                    wqT[:, c, :], start=(c == 0), stop=(c == 7)),
                    reads=[xq_fm, wqT], writes=[ps_g])
            for c in range(8):
                p.op("tensor", lambda e, tb=tb, c=c: e.matmul(
                    ps_u[:, 0:128], xq_fm[:, c, tb * 128:(tb + 1) * 128],
                    wuT[:, c, :], start=(c == 0), stop=(c == 7)),
                    reads=[xq_fm, wuT], writes=[ps_u])
            p.op("scalar", lambda e: e.activation(
                out=hf32[:], in_=ps_g[:, 0:128], func=AF.Silu),
                reads=[ps_g], writes=[hf32])
            p.op("vector", lambda e: e.tensor_tensor(
                out=hf32[:], in0=hf32[:], in1=ps_u[:, 0:128], op=ALU.mult),
                reads=[hf32, ps_u], writes=[hf32])
            emit_quant(ctx, xq_t[:, 0:128], hf32[:], 128, f"h{r}_{tb}")
            pe_transpose(ctx, hslab[:, tb, :], xq_t[:, 0:128], 128, 128)
        p.dma("sync", lambda e, r=r: e.dma_start(
            out=h_dram[r * 128:(r + 1) * 128, :],
            in_=hslab[:].rearrange("p t k -> p (t k)")), reads=[hslab],
            writes=["l3_hq_d"], )
    ctx.free("l3_wq_f", "l3_wu_f", "l3_wqT", "l3_wuT", "l3_hslab",
             "l3_hf32")

    wd_f = ctx.alloc("l3_wd_f", [128, 2816], BF16)
    for r in range(8):
        for c0, cw in ((0, 1024), (1024, 1024), (2048, 768)):
            sv = ws[:, 0:cw]
            hb1 = ctx.hbm["L3_ffn_w_down"][r * 128:(r + 1) * 128,
                                           c0:c0 + cw]
            p.dma("sync", lambda e, hb1=hb1, sv=sv: e.dma_start(
                out=sv, in_=hb1), writes=[ws])
            emit_quant(ctx, wd_f[:, c0:c0 + cw], sv, cw, f"wd{r}_{c0}")
        for kc in range(22):
            pe_transpose(ctx, xq_t[:, 0:128],
                         wd_f[:, kc * 128:(kc + 1) * 128], 128, 128)
            p.dma("sync", lambda e, kc=kc, r=r: e.dma_start(
                out=wdqT_dram[kc * 128:(kc + 1) * 128,
                              r * 128:(r + 1) * 128],
                in_=xq_t[:, 0:128]), reads=[xq_t], writes=["l3_wdq_d"])
    ctx.free("l3_wd_f", "l3_xq_fm")
    ctx.free("l3_qab", "l3_qg1", "l3_qg2", "l3_qsg", "l3_qs", "l3_qr")

    hq = ctx.alloc("l3_hq", [128, 11, T], BF16)
    wdT = ctx.alloc("l3_wdT", [128, 11, 1024], BF16)
    yt = ctx.sb["scratch_f32_b"]
    for half in range(2):
        k0 = half * 11
        for kc in range(11):
            p.dma("sync", lambda e, kc=kc, k0=k0: e.dma_start(
                out=hq[:, kc, :],
                in_=h_dram[(k0 + kc) * 128:(k0 + kc + 1) * 128, :]),
                reads=["l3_hq_d"], writes=[hq])
            p.dma("sync", lambda e, kc=kc, k0=k0: e.dma_start(
                out=wdT[:, kc, :],
                in_=wdqT_dram[(k0 + kc) * 128:(k0 + kc + 1) * 128, :]),
                reads=["l3_wdq_d"], writes=[wdT])
        for tb in range(NT):
            if half == 1:
                p.dma("sync", lambda e, tb=tb: e.dma_start(
                    out=yt[:], in_=ypart_dram.rearrange(
                        "(i p) d -> p i d", p=128)[:, tb, :]),
                    reads=["l3_yp_d"], writes=[yt])
            for eh in range(2):
                esl = slice(eh * 512, (eh + 1) * 512)
                for kc in range(11):
                    p.op("tensor", lambda e, tb=tb, kc=kc, esl=esl: e.matmul(
                        ps_m2[:, :], hq[:, kc, tb * 128:(tb + 1) * 128],
                        wdT[:, kc, esl], start=(kc == 0), stop=(kc == 10)),
                        reads=[hq, wdT], writes=[ps_m2])
                if half == 0:
                    p.op("vector", lambda e, esl=esl: e.tensor_copy(
                        out=yt[:, esl], in_=ps_m2[:, :]),
                        reads=[ps_m2], writes=[yt])
                else:
                    p.op("vector", lambda e, esl=esl: e.tensor_add(
                        out=yt[:, esl], in0=yt[:, esl], in1=ps_m2[:, :]),
                        reads=[ps_m2, yt], writes=[yt])
            if half == 0:
                p.dma("sync", lambda e, tb=tb: e.dma_start(
                    out=ypart_dram.rearrange(
                        "(i p) d -> p i d", p=128)[:, tb, :], in_=yt[:]),
                    reads=[yt], writes=["l3_yp_d"])
            else:
                p.op("scalar", lambda e: e.activation(
                    out=yt[:], in_=yt[:], func=AF.Silu),
                    reads=[yt], writes=[yt])
                p.dma("sync", lambda e, tb=tb: e.dma_start(
                    out=g_dram.rearrange(
                        "(i p) d -> p i d", p=128)[:, tb, :], in_=yt[:]),
                    reads=[yt], writes=["l3_g_d"])
    ctx.free("l3_hq", "l3_wdT")

    triu = ctx.consts["tril_incl"]
    trilx = ctx.consts["trilx"]
    wk_bf = ctx.alloc("l3_wk_bf", [128, 8, 256], BF16)
    wv_bf = ctx.alloc("l3_wv_bf", [128, 8, 256], BF16)
    for wname, wbf in (("L3_attn_Wk", wk_bf), ("L3_attn_Wv", wv_bf)):
        for cp in range(2):
            sv = ws[:, 0:1024].rearrange("p (c m) -> p c m", c=4)
            hb = ctx.hbm[wname].rearrange(
                "(c p) m -> p c m", p=128)[:, cp * 4:(cp + 1) * 4, :]
            p.dma("sync", lambda e, hb=hb, sv=sv: e.dma_start(
                out=sv, in_=hb), writes=[ws])
            p.op("gpsimd", lambda e, wbf=wbf, sv=sv, cp=cp: e.tensor_copy(
                out=wbf[:, cp * 4:(cp + 1) * 4, :], in_=sv),
                reads=[ws], writes=[wbf])
    bias_rows = ctx.alloc("l3_bias_rows", [1, 2048 + 1024], BF16)
    for nm, off, w in (("L3_attn_bq", 0, 1024), ("L3_attn_bk", 1024, 256),
                       ("L3_attn_bv", 1280, 256), ("L3_attn_bo", 2048, 1024)):
        load_convert(ctx, bias_rows[:, off:off + w],
                     ctx.hbm[nm][None, :], 1, w)
    wkd = ctx.alloc("l3_wkd", [128, 8, 128], BF16)
    Kd = ctx.alloc("l3_Kd", [128, 4, T], BF16)
    Vx = ctx.alloc("l3_Vx", [128, NT, 260], BF16)
    bkd = ctx.alloc("l3_bkd", [1, 128], BF16)
    for g in range(4):
        for c in range(8):
            for dup in range(2):
                p.op("vector", lambda e, g=g, c=c, dup=dup: e.tensor_copy(
                    out=wkd[:, c, dup * 64:(dup + 1) * 64],
                    in_=wk_bf[:, c, g * 64:(g + 1) * 64]),
                    reads=[wk_bf], writes=[wkd])
        for dup in range(2):
            p.op("vector", lambda e, g=g, dup=dup: e.tensor_copy(
                out=bkd[:, dup * 64:(dup + 1) * 64],
                in_=bias_rows[:, 1024 + g * 64:1024 + (g + 1) * 64]),
                reads=[bias_rows], writes=[bkd])
        for tb in range(4):
            tsl = slice(tb * 512, (tb + 1) * 512)
            for c in range(8):
                p.op("tensor", lambda e, tsl=tsl, c=c: e.matmul(
                    ps_m2[:, :], wkd[:, c, :], xn_fm[:, c, tsl],
                    start=(c == 0), stop=False),
                    reads=[wkd, xn_fm], writes=[ps_m2])
            p.op("tensor", lambda e, tsl=tsl: e.matmul(
                ps_m2[:, :], bkd[:], ones512[:], start=False, stop=True),
                reads=[bkd, "c_ones512"], writes=[ps_m2])
            p.op("vector", lambda e, g=g, tsl=tsl: e.tensor_copy(
                out=Kd[:, g, tsl], in_=ps_m2[:, :]),
                reads=[ps_m2], writes=[Kd])
    for tb in range(NT):
        for c in range(8):
            p.op("tensor", lambda e, tb=tb, c=c: e.matmul(
                ps_m2[:, 0:256], xn_fm[:, c, tb * 128:(tb + 1) * 128],
                wv_bf[:, c, :], start=(c == 0), stop=False),
                reads=[wv_bf, xn_fm], writes=[ps_m2])
        p.op("tensor", lambda e, tb=tb: e.matmul(
            ps_m2[:, 0:256], onesrow[:], bias_rows[:, 1280:1536],
            start=False, stop=True),
            reads=[bias_rows, "c_onesrow"], writes=[ps_m2])
        p.op("vector", lambda e, tb=tb: e.tensor_copy(
            out=Vx[:, tb, :].rearrange("p (g u) -> p g u", u=65)[:, :, 0:64],
            in_=ps_m2[:, 0:256].rearrange("p (g u) -> p g u", u=64)),
            reads=[ps_m2], writes=[Vx])
    p.op("vector", lambda e: e.memset(
        Vx[:].rearrange("p t (g u) -> p t g u", u=65)[:, :, :, 64:65], 1.0),
        writes=[Vx])

    ctx.free("l3_wk_bf", "l3_wv_bf", "l3_wkd", "l3_bkd")
    O_fm = ctx.alloc("l3_O_fm", [128, ND, T], BF16)
    Qp = ctx.alloc("l3_Qp", [128, T], BF16)
    wq_p = ctx.alloc("l3_wq_p", [128, 8, 128], BF16)
    ctx.alloc("fl_PT0", [128, 16, 256], BF16)
    ctx.alloc("fl_PT1", [128, 16, 256], BF16)
    ctx.alloc("fl_rc_a", [128, 1], F32)
    ctx.alloc("fl_rc_b", [128, 1], F32)
    nsc = ctx.alloc("l3_nsc", [1, 64], F32)
    nsc_bf = ctx.alloc("l3_nscbf", [1, 2], BF16)
    bias_a = ctx.alloc("l3_bias_a", [128, 1], F32)
    bias_b = ctx.alloc("l3_bias_b", [128, 1], F32)
    stageO = ctx.alloc("l3_stageO", [128, 128], BF16)
    sqst = ctx.sb["stage_bf"]
    ps_sa = ctx.sb["ps_sa"]
    ps_sb = ctx.sb["ps_sb"]
    ps_ava = ctx.sb["ps_ava"]
    ps_avb = ctx.sb["ps_avb"]
    onesA, onesB = ctx.consts["onesA"], ctx.consts["onesB"]

    for hp in range(8):
        g = (2 * hp) // 4
        sv = ws[:, 0:1024].rearrange("p (c m) -> p c m", c=8)
        hb = ctx.hbm["L3_attn_Wq"][:, hp * 128:(hp + 1) * 128].rearrange(
            "(c p) m -> p c m", p=128)
        p.dma("sync", lambda e, hb=hb, sv=sv: e.dma_start(out=sv, in_=hb),
              writes=[ws])
        p.op("gpsimd", lambda e, sv=sv: e.tensor_copy(
            out=wq_p[:], in_=sv), reads=[ws], writes=[wq_p])
        for tb in range(4):
            tsl = slice(tb * 512, (tb + 1) * 512)
            for c in range(8):
                p.op("tensor", lambda e, tsl=tsl, c=c: e.matmul(
                    ps_m2[:, :], wq_p[:, c, :], xn_fm[:, c, tsl],
                    start=(c == 0), stop=False),
                    reads=[wq_p, xn_fm], writes=[ps_m2])
            p.op("tensor", lambda e, tsl=tsl, hp=hp: e.matmul(
                ps_m2[:, :], bias_rows[:, hp * 128:(hp + 1) * 128],
                ones512[:], start=False, stop=True),
                reads=[bias_rows, "c_ones512"], writes=[ps_m2])
            p.op("vector", lambda e, tsl=tsl: e.tensor_copy(
                out=Qp[:, tsl], in_=ps_m2[:, :]), reads=[ps_m2], writes=[Qp])
        for si in range(2):
            for tb in range(4):
                sl = slice(tb * 512, (tb + 1) * 512)
                if si == 0:
                    p.op("scalar", lambda e, sl=sl: e.activation(
                        out=sqst[:, 0:512], in_=Qp[:, sl], func=AF.Square),
                        reads=[Qp], writes=[sqst])
                else:
                    p.op("scalar", lambda e, sl=sl, g=g: e.activation(
                        out=sqst[:, 0:512], in_=Kd[:, g, sl],
                        func=AF.Square), reads=[Kd], writes=[sqst])
                for h2, ones in enumerate((onesA, onesB)):
                    p.op("tensor", lambda e, ones=ones: e.matmul(
                        ps_m1[0:1, :], ones[:], sqst[:, 0:512],
                        start=True, stop=True),
                        reads=[sqst, ones], writes=[ps_m1])
                    o = (si * 2 + h2) * 4 + tb
                    p.op("vector", lambda e, o=o: e.reduce_max(
                        out=nsc[:, o:o + 1], in_=ps_m1[0:1, :], axis=AX.X),
                        reads=[ps_m1], writes=[nsc])
        for h2 in range(2):
            o = 16 + h2 * 8
            qs4 = slice(h2 * 4, h2 * 4 + 4)
            ks4 = slice(8 + h2 * 4, 8 + h2 * 4 + 4)
            p.op("vector", lambda e, o=o, qs4=qs4: e.reduce_max(
                out=nsc[:, o:o + 1], in_=nsc[:, qs4], axis=AX.X),
                reads=[nsc], writes=[nsc])
            p.op("vector", lambda e, o=o, ks4=ks4: e.reduce_max(
                out=nsc[:, o + 1:o + 2], in_=nsc[:, ks4], axis=AX.X),
                reads=[nsc], writes=[nsc])
            p.op("vector", lambda e, o=o: e.tensor_mul(
                out=nsc[:, o + 2:o + 3], in0=nsc[:, o:o + 1],
                in1=nsc[:, o + 1:o + 2]), reads=[nsc], writes=[nsc])
            p.op("scalar", lambda e, o=o: e.activation(
                out=nsc[:, o + 3:o + 4], in_=nsc[:, o + 2:o + 3],
                func=AF.Sqrt), reads=[nsc], writes=[nsc])
            p.op("vector", lambda e, o=o: e.tensor_scalar_mul(
                out=nsc[:, o + 4:o + 5], in0=nsc[:, o + 3:o + 4],
                scalar1=-0.125), reads=[nsc], writes=[nsc])
        p.op("vector", lambda e: e.tensor_copy(out=nsc_bf[:, 0:1],
                                               in_=nsc[:, 20:21]),
             reads=[nsc], writes=[nsc_bf])
        p.op("vector", lambda e: e.tensor_copy(out=nsc_bf[:, 1:2],
                                               in_=nsc[:, 28:29]),
             reads=[nsc], writes=[nsc_bf])
        for biast, cs in ((bias_a, slice(0, 1)), (bias_b, slice(1, 2))):
            p.op("tensor", lambda e, cs=cs: e.matmul(
                ps_m1[:, 0:1], onesrow[:], nsc_bf[:, cs],
                start=True, stop=True),
                reads=[nsc_bf, onesrow], writes=[ps_m1])
            p.op("vector", lambda e, biast=biast: e.tensor_copy(
                out=biast[:], in_=ps_m1[:, 0:1]),
                reads=[ps_m1], writes=[biast])
        ctx.sb["fl_K0"] = Kd
        ctx.sb["fl_K1"] = Kd
        ctx.sb["fl_V"] = Vx

        def l3_sink(qtile, psa, psb, rca, rcb, hp=hp):
            p.op("vector", lambda e: e.tensor_scalar(
                out=stageO[:, 0:64], in0=psa[:, 0:64], scalar1=rca[:],
                scalar2=None, op0=ALU.mult),
                reads=[psa, rca], writes=[stageO])
            p.op("vector", lambda e: e.tensor_scalar(
                out=stageO[:, 64:128], in0=psb[:, 0:64], scalar1=rcb[:],
                scalar2=None, op0=ALU.mult),
                reads=[psb, rcb], writes=[stageO])
            pe_transpose(ctx, O_fm[:, hp, qtile * 128:(qtile + 1) * 128],
                         stageO[:], 128, 128)

        def l3_kts(qb):
            return [2 * qb + d for d in range(-4, 2) if 2 * qb + d >= 0]

        def l3_mask(qb, kt, qs):
            d = kt - 2 * qb
            if d == qs:
                return triu
            if d > qs:
                return "skip"
            if d == qs - 4:
                return trilx
            if d < qs - 4:
                return "skip"
            return None

        flash256(ctx, Qp,
                 lambda h2, ksl, g=g: Kd[h2 * 64:(h2 + 1) * 64, g, ksl],
                 lambda h2, kt, g=g: Vx[:, kt, g * 65:(g + 1) * 65],
                 bias_a, bias_b, l3_sink, l3_kts, l3_mask)

    ctx.tap_full("l3_O", O_fm)
    ctx.tap_full("l3_Kd", Kd)
    ctx.tap_full("l3_Qp", Qp)
    ctx.free("l3_Qp", "l3_wq_p", "fl_PT0", "fl_PT1", "fl_rc_a",
             "fl_rc_b", "l3_Kd", "l3_Vx", "l3_stageO")

    wo_bf = load_weight_bf16(ctx, "l3_wo", ctx.hbm["L3_attn_Wo"], 1024, 1024)
    x = ctx.sb["x_tm"]
    ctx.sb["scratch_f32_c"] = ctx.alloc("l3_scrc", [128, D], F32)
    st = ctx.sb["scratch_f32_c"]
    bo_brow = ctx.alloc("l3_bo_brow", [128, D], F32)
    row_bcast(ctx, bo_brow[:], ctx.sb["l3_bias_rows"][:, 2048:3072], D,
              ["l3_bias_rows"])
    gt = ctx.sb["scratch_f32_b"]
    s_tap = None
    if "l3_s" in ctx.dev_taps:
        s_tap = ctx.nc.dram_tensor("tap_l3_s", [128, NT, D], F32,
                                   kind="ExternalOutput").ap()
        ctx.taps["l3_s"] = [128, NT, D]
    if "l3_g" in ctx.dev_taps:
        gt_tap = ctx.nc.dram_tensor("tap_l3_g", [T, D], F32,
                                    kind="ExternalOutput").ap()
        ctx.taps["l3_g"] = [T, D]
        p.dma("sync", lambda e: e.dma_start(out=gt_tap, in_=g_dram),
              reads=["l3_g_d"], is_out=True)
    for tb in range(NT):
        p.dma("sync", lambda e, tb=tb: e.dma_start(
            out=gt[:], in_=g_dram.rearrange(
                "(i p) d -> p i d", p=128)[:, tb, :]),
            reads=["l3_g_d"], writes=[gt])
        for half in range(2):
            esl = slice(half * 512, (half + 1) * 512)
            for c in range(8):
                p.op("tensor", lambda e, tb=tb, c=c, esl=esl: e.matmul(
                    ps_m2[:, :], O_fm[:, c, tb * 128:(tb + 1) * 128],
                    wo_bf[:, c, esl], start=(c == 0), stop=(c == 7)),
                    reads=[O_fm, wo_bf], writes=[ps_m2])
            p.op("vector", lambda e, esl=esl: e.scalar_tensor_tensor(
                out=st[:, esl], in0=ps_m2[:, :], scalar=1.0,
                in1=bo_brow[:, esl], op0=ALU.mult, op1=ALU.add),
                reads=[ps_m2, bo_brow], writes=[st])
        if s_tap is not None:
            p.dma("sync", lambda e, tb=tb: e.dma_start(
                out=s_tap[:, tb, :], in_=st[:]), reads=[st], is_out=True)
        p.op("vector", lambda e: e.tensor_mul(
            out=gt[:], in0=gt[:], in1=st[:]), reads=[gt, st], writes=[gt])
        p.op("vector", lambda e, tb=tb: e.tensor_add(
            out=x[:, tb, :], in0=x[:, tb, :], in1=gt[:]),
            reads=[gt, x], writes=[x])
    ctx.free(*[n for n in list(ctx.live) if n.startswith("l3_")])


def make_consts():
    ident = np.eye(128).astype(BF)
    onesrow = np.ones((1, 128), dtype=BF)
    onesA = np.zeros((128, 1), dtype=BF)
    onesA[0:64] = 1
    onesB = np.zeros((128, 1), dtype=BF)
    onesB[64:128] = 1
    e16sel = np.zeros((8, 128), dtype=BF)
    for e in range(8):
        e16sel[e, e * 16:(e + 1) * 16] = 1
    ones512 = np.ones((1, 512), dtype=BF)
    tril_incl = np.triu(np.ones((128, 128))).astype(BF)
    trilx = (np.tril(np.ones((128, 128)), -1)).astype(BF)
    grp8 = np.full((64, 8), 0.125, dtype=BF)
    g8 = np.zeros((64, 8), dtype=BF)
    for c in range(8):
        g8[c * 8:(c + 1) * 8, c] = 0.125
    return dict(ident=ident, onesrow=onesrow, onesA=onesA, onesB=onesB,
                e16sel=e16sel, ones512=ones512, grp8=g8,
                tril_incl=tril_incl, trilx=trilx)


def make_consts32():
    iota8m9 = np.zeros((128, 128), dtype=np.float32)
    for j in range(128):
        iota8m9[:, j] = (j % 8) - 9.0
    iotacol = np.arange(128, dtype=np.float32).reshape(128, 1)
    return dict(iota8m9=iota8m9, iotacol=iotacol)


WEIGHT_SPECS = {
    0: ["L0_attn_Wq", "L0_attn_Wk", "L0_attn_Wv", "L0_attn_Wout",
        "L0_ffn_router_w", "L0_ffn_lora_down", "L0_ffn_lora_up"],
    1: ["L1_attn_log_A_real", "L1_attn_log_A_imag", "L1_attn_B_w",
        "L1_attn_C_w", "L1_attn_D_w", "L1_ffn_signatures_raw",
        "L1_ffn_knot_values", "L1_ffn_temperature", "L1_ffn_compress_w1",
        "L1_ffn_compress_b1", "L1_ffn_compress_w2", "L1_ffn_compress_b2",
        "L1_ffn_spline_w1", "L1_ffn_spline_b1", "L1_ffn_spline_w2",
        "L1_ffn_spline_b2", "L1_ffn_directions", "L1_ffn_norm_gamma",
        "L1_ffn_norm_beta", "L1_ffn_output_scale"],
    "2a": ["L2_attn_Wq", "L2_attn_Wk", "L2_attn_Wv", "L2_attn_conv_q",
           "L2_attn_conv_k", "L2_attn_conv_v", "L2_attn_W_init",
           "L2_attn_ln_gamma", "L2_attn_ln_beta", "L2_attn_Wg",
           "L2_attn_Wo"],
    "2b": ["L2_ffn_Wq", "L2_ffn_Wk", "L2_ffn_Wv", "L2_ffn_Ws", "L2_ffn_Wd",
           "L2_ffn_Wo"],
    3: ["L3_attn_Wq", "L3_attn_bq", "L3_attn_Wk", "L3_attn_bk",
        "L3_attn_Wv", "L3_attn_bv", "L3_attn_Wo", "L3_attn_bo",
        "L3_ffn_w_gate", "L3_ffn_w_up", "L3_ffn_w_down"],
}

WEIGHT_SHAPES = {
    "L0_attn_Wq": (1024, 1024), "L0_attn_Wk": (1024, 1024),
    "L0_attn_Wv": (1024, 1024), "L0_attn_Wout": (1024, 1024),
    "L0_ffn_router_w": (8, 1024), "L0_ffn_lora_down": (8, 16, 1024),
    "L0_ffn_lora_up": (8, 1024, 16),
    "L1_attn_log_A_real": (64,), "L1_attn_log_A_imag": (64,),
    "L1_attn_B_w": (128, 1024), "L1_attn_C_w": (1024, 128),
    "L1_attn_D_w": (1024, 1024), "L1_ffn_signatures_raw": (64, 1024),
    "L1_ffn_knot_values": (8,), "L1_ffn_temperature": (1,),
    "L1_ffn_compress_w1": (1024, 256), "L1_ffn_compress_b1": (256,),
    "L1_ffn_compress_w2": (256, 2), "L1_ffn_compress_b2": (2,),
    "L1_ffn_spline_w1": (64, 2, 16), "L1_ffn_spline_b1": (64, 16),
    "L1_ffn_spline_w2": (64, 16, 1), "L1_ffn_spline_b2": (64, 1),
    "L1_ffn_directions": (64, 1024), "L1_ffn_norm_gamma": (1024,),
    "L1_ffn_norm_beta": (1024,), "L1_ffn_output_scale": (1,),
    "L2_attn_Wq": (1024, 1024), "L2_attn_Wk": (1024, 1024),
    "L2_attn_Wv": (1024, 1024), "L2_attn_conv_q": (1024, 4),
    "L2_attn_conv_k": (1024, 4), "L2_attn_conv_v": (1024, 4),
    "L2_attn_W_init": (1, 16, 64, 64), "L2_attn_ln_gamma": (64,),
    "L2_attn_ln_beta": (64,), "L2_attn_Wg": (1024, 1024),
    "L2_attn_Wo": (1024, 1024),
    "L2_ffn_Wq": (16, 64, 1024), "L2_ffn_Wk": (16, 64, 1024),
    "L2_ffn_Wv": (8, 64, 1024), "L2_ffn_Ws": (16, 8, 1024),
    "L2_ffn_Wd": (16, 8, 1024), "L2_ffn_Wo": (8, 1024, 64),
    "L3_attn_Wq": (1024, 1024), "L3_attn_bq": (1024,),
    "L3_attn_Wk": (1024, 256), "L3_attn_bk": (256,),
    "L3_attn_Wv": (1024, 256), "L3_attn_bv": (256,),
    "L3_attn_Wo": (1024, 1024), "L3_attn_bo": (1024,),
    "L3_ffn_w_gate": (2816, 1024), "L3_ffn_w_up": (2816, 1024),
    "L3_ffn_w_down": (1024, 2816),
}

LAYER_EMITTERS = {0: emit_layer0, 1: emit_layer1,
                  '2a': emit_layer2a, '2b': emit_layer2b,
                  3: emit_layer3}


def build_program(layers, dev_taps=()):
    nc = bass.Bass("TRN2", target_bir_lowering=False, debug=False)
    p = Prog(nc)
    ctx = Ctx(nc, p, dev_taps=dev_taps)

    x_in = nc.dram_tensor("x_in", [T, D], F32, kind="ExternalInput").ap()
    x_out = nc.dram_tensor("x_out", [T, D], F32, kind="ExternalOutput").ap()
    for li in layers:
        for w in WEIGHT_SPECS[li]:
            ctx.hbm[w] = nc.dram_tensor(
                w, list(WEIGHT_SHAPES[w]), F32, kind="ExternalInput").ap()
    cvals = make_consts()
    for name, arr in cvals.items():
        ap = nc.dram_tensor(f"const_{name}", list(arr.shape), BF16,
                            kind="ExternalInput").ap()
        sb = ctx.alloc(f"c_{name}", list(arr.shape), BF16)
        p.dma("sync", lambda e, ap=ap, sb=sb: e.dma_start(out=sb[:], in_=ap),
              writes=[sb])
        ctx.consts[name] = sb
    c32 = make_consts32()
    for name, arr in c32.items():
        ap = nc.dram_tensor(f"const32_{name}", list(arr.shape), F32,
                            kind="ExternalInput").ap()
        sb = ctx.alloc(f"c32_{name}", list(arr.shape), F32)
        p.dma("sync", lambda e, ap=ap, sb=sb: e.dma_start(out=sb[:], in_=ap),
              writes=[sb])
        ctx.consts32[name] = sb

    x_tm = ctx.alloc("x_tm", [128, NT, D], F32)
    ctx.alloc("xn_fm", [128, ND, T], BF16)
    ctx.alloc("ln_mu", [128, NT], F32)
    ctx.alloc("ln_rstd", [128, NT], F32)
    ctx.alloc("scratch_f32_a", [128, D], F32)
    ctx.alloc("scratch_f32_b", [128, D], F32)
    ctx.alloc("stage_bf", [128, D], BF16)
    ctx.alloc("scratch_col", [128, 4], F32)
    ctx.alloc("wstage", [128, 1024], F32)
    fc = ctx.alloc("fconsts", [128, 16], F32)
    fc_ap = nc.dram_tensor("const_fconsts", [128, 16], F32,
                           kind="ExternalInput").ap()
    p.dma("sync", lambda e: e.dma_start(out=fc[:], in_=fc_ap),
          writes=[fc])
    ctx.psum("ps_sa", [128, 1024])
    ctx.psum("ps_sb", [128, 1024])
    ctx.psum("ps_ava", [128, 512])
    ctx.psum("ps_avb", [128, 512])
    ctx.psum("ps_m1", [128, 512])
    ctx.psum("ps_m2", [128, 512])
    ctx.sb["ps_tr"] = ctx.sb["ps_m1"]

    p.dma("sync", lambda e: e.dma_start(
        out=x_tm[:], in_=x_in.rearrange("(i p) d -> p i d", p=128)),
        writes=[x_tm])

    ctx.start_dynamic()
    for li in layers:
        LAYER_EMITTERS[li](ctx)

    p.dma("sync", lambda e: e.dma_start(
        out=x_out.rearrange("(i p) d -> p i d", p=128), in_=x_tm[:]),
        reads=[x_tm], is_out=True)

    p.finalize()
    return nc, ctx, cvals


def run_layers(inputs, layers, x, dev_taps=(), carry=None):
    carry = carry if carry is not None else [{} for _ in range(8)]
    nc, ctx, cvals = build_program(layers, dev_taps=dev_taps)
    in_maps = []
    for core in range(8):
        m = {"x_in": np.ascontiguousarray(x[core], dtype=np.float32)}
        for name in ctx.extra_in:
            m[name] = carry[core][name]
        for li in layers:
            for w in WEIGHT_SPECS[li]:
                m[w] = np.ascontiguousarray(inputs[w], dtype=np.float32)
        for name, arr in cvals.items():
            m[f"const_{name}"] = arr
        for name, arr in make_consts32().items():
            m[f"const32_{name}"] = arr
        fcarr = np.zeros((128, 16), dtype=np.float32)
        for v, j in ctx.fconst_vals.items():
            fcarr[:, j] = v
        m["const_fconsts"] = fcarr
        in_maps.append(m)
    res = run_bass_kernel_spmd(nc, in_maps, core_ids=list(range(8)))
    xo = np.stack([np.asarray(r["x_out"]) for r in res.results])
    for core in range(8):
        for name in ctx.extra_out:
            carry[core][name] = np.asarray(res.results[core][name])
    taps = {}
    for name in ctx.taps:
        taps[name] = np.stack([np.asarray(r[f"tap_{name}"])
                               for r in res.results])
    return xo, taps


def kernel(**inputs):
    x = np.ascontiguousarray(inputs["L0_attn_x"]).astype(np.float32)
    carry = [{} for _ in range(8)]
    for group in ([0], [1], ["2a"], ["2b"], [3]):
        if LAYER_EMITTERS.get(group[0]) is None:
            continue
        x, _ = run_layers(inputs, group, x, carry=carry)
    return x


# version 03
# speedup vs baseline: 1.0132x; 1.0132x over previous
import contextlib
import numpy as np
import ml_dtypes

import concourse.bass as bass
import concourse.mybir as mybir
from concourse.bass_utils import run_bass_kernel_spmd

F32 = mybir.dt.float32
BF16 = mybir.dt.bfloat16
I32 = mybir.dt.int32
AF = mybir.ActivationFunctionType
ALU = mybir.AluOpType
AX = mybir.AxisListType

T = 2048
D = 1024
NT = T // 128
ND = D // 128

BF = ml_dtypes.bfloat16


class Prog:

    ENGINES = ("tensor", "vector", "scalar", "gpsimd", "sync")

    def __init__(self, nc):
        self.nc = nc
        self.ops = []
        self.buf = {}
        self.out_dmas = []

    @staticmethod
    def key(t):
        if isinstance(t, str):
            return t
        if hasattr(t, "tensor"):
            t = t.tensor
        return t.name

    def op(self, engine, fn, reads=(), writes=(), is_dma=False, is_out=False,
           extra_deps=()):
        opid = len(self.ops)
        deps = set(int(d) for d in extra_deps)
        rk = [self.key(t) for t in reads]
        wk = [self.key(t) for t in writes]
        for k in rk:
            st = self.buf.get(k)
            if st is not None and st["w"] is not None:
                deps.add(st["w"])
        for k in wk:
            st = self.buf.get(k)
            if st is not None:
                if st["w"] is not None:
                    deps.add(st["w"])
                deps.update(st["r"])
        for k in rk:
            st = self.buf.setdefault(k, {"w": None, "r": []})
            st["r"].append(opid)
        for k in wk:
            self.buf[k] = {"w": opid, "r": []}
        deps.discard(opid)
        self.ops.append(dict(engine=engine, fn=fn, deps=sorted(deps),
                             is_dma=is_dma))
        if is_out:
            self.out_dmas.append(opid)
        return opid

    def dma(self, engine, fn, reads=(), writes=(), is_out=False,
            extra_deps=()):
        return self.op(engine, fn, reads, writes, is_dma=True, is_out=is_out,
                       extra_deps=extra_deps)

    def full_barrier(self):
        last = {}
        for i, o in enumerate(self.ops):
            last[o["engine"]] = i
        deps = sorted(last.values())
        for e in self.ENGINES:
            self.op(e, lambda eng: eng.nop(), extra_deps=deps)

    def alias_barrier(self, old_names, new_name, extra=()):
        carry = set(extra)
        for k in old_names:
            st = self.buf.get(self.key(k))
            if st is not None:
                if st["w"] is not None:
                    carry.add(st["w"])
                carry.update(st["r"])
        self.buf[self.key(new_name)] = {"w": None, "r": sorted(carry)}

    def finalize(self):
        nc = self.nc
        n = len(self.ops)
        needs_inc = [False] * n
        for i, o in enumerate(self.ops):
            for d in o["deps"]:
                dop = self.ops[d]
                if (not dop["is_dma"] and dop["engine"] == "tensor"
                        and o["engine"] == "tensor" and not o["is_dma"]):
                    continue
                needs_inc[d] = True
        for i in self.out_dmas:
            needs_inc[i] = True
        for i, o in enumerate(self.ops):
            if o["is_dma"]:
                needs_inc[i] = True

        NSLOT = 8
        incidx = [0] * n
        dma_slot = [0] * n
        counters = {}
        dma_counters = {}
        for i, o in enumerate(self.ops):
            if o["is_dma"]:
                k = dma_counters.get(o["engine"], 0)
                dma_counters[o["engine"]] = k + 1
                slot = k % NSLOT
                dma_slot[i] = slot
                skey = (o["engine"], True, slot)
                counters[skey] = counters.get(skey, 0) + 1
                incidx[i] = counters[skey]
            elif needs_inc[i]:
                skey = (o["engine"], False)
                counters[skey] = counters.get(skey, 0) + 1
                incidx[i] = counters[skey]

        stack = contextlib.ExitStack()
        sems = {}
        for e in self.ENGINES:
            sems[(e, False)] = stack.enter_context(nc.semaphore(f"s_{e}"))
            for s in range(NSLOT):
                sems[(e, True, s)] = stack.enter_context(
                    nc.semaphore(f"d_{e}{s}"))

        per_engine = {e: [] for e in self.ENGINES}
        for i, o in enumerate(self.ops):
            per_engine[o["engine"]].append(i)

        ops = self.ops
        incs_total = sum(1 for x in needs_inc if x)
        watermark = {e: {} for e in self.ENGINES}

        def emit_engine(eng_name, eng):
            wm = watermark[eng_name]
            for i in per_engine[eng_name]:
                o = ops[i]
                for d in o["deps"]:
                    dop = ops[d]
                    if (not dop["is_dma"] and dop["engine"] == "tensor"
                            and eng_name == "tensor" and not o["is_dma"]):
                        continue
                    if dop["is_dma"]:
                        skey = (dop["engine"], True, dma_slot[d])
                    else:
                        skey = (dop["engine"], False)
                    tgt = incidx[d] * (16 if dop["is_dma"] else 1)
                    if tgt == 0:
                        raise RuntimeError(f"op {i} depends on {d}: no inc")
                    if wm.get(skey, 0) < tgt:
                        eng.wait_ge(sems[skey], tgt)
                        wm[skey] = tgt
                if o["is_dma"]:
                    skey = (eng_name, True, dma_slot[i])
                    prev = (incidx[i] - 1) * 16
                    if prev > 0 and wm.get(skey, 0) < prev:
                        eng.wait_ge(sems[skey], prev)
                        wm[skey] = prev
                inst = o["fn"](eng)
                if o["is_dma"]:
                    inst.then_inc(sems[(eng_name, True, dma_slot[i])], 16)
                elif needs_inc[i]:
                    inst.then_inc(sems[(eng_name, False)], 1)
            if eng_name == "sync":
                done = {}
                for i in self.out_dmas:
                    skey = (ops[i]["engine"], True, dma_slot[i])
                    done[skey] = max(done.get(skey, 0), incidx[i] * 16)
                for skey, tgt in sorted(done.items()):
                    if wm.get(skey, 0) < tgt:
                        eng.wait_ge(sems[skey], tgt)

        with nc.Block() as block:
            block.vector(lambda e: emit_engine("vector", e))
            block.scalar(lambda e: emit_engine("scalar", e))
            block.tensor(lambda e: emit_engine("tensor", e))
            block.gpsimd(lambda e: emit_engine("gpsimd", e))
            block.sync(lambda e: emit_engine("sync", e))
        stack.close()


_ISIZE = {F32: 4, BF16: 2, I32: 4}


class Ctx:
    def __init__(self, nc, p, dev_taps=()):
        self.nc = nc
        self.p = p
        self.sb = {}
        self.hbm = {}
        self.consts = {}
        self.consts32 = {}
        self.dev_taps = set(dev_taps)
        self.taps = {}
        self.fconst_vals = {}
        self.extra_in = {}
        self.extra_out = {}
        self.dyn_started = False
        self.freelist = []
        self.live = {}
        self.dead = []

    def start_dynamic(self):
        self.dyn_started = True
        base = self.nc.sbuf_base
        top = self.nc.sbuf_top
        self.freelist = [[base, top - base]]

    def alloc(self, name, shape, dtype):
        if not self.dyn_started:
            h = self.nc.alloc_sbuf_tensor(name, shape, dtype)
            self.sb[name] = h
            return h
        size = int(np.prod(shape[1:])) * _ISIZE[dtype] if len(shape) > 1 \
            else _ISIZE[dtype]
        size = (size + 31) // 32 * 32
        for blk in self.freelist:
            if blk[1] >= size:
                off = blk[0]
                blk[0] += size
                blk[1] -= size
                break
        else:
            raise RuntimeError(
                f"arena: no space for {name} ({size}B); "
                f"free={[(b[0], b[1]) for b in self.freelist]}")
        olds = []
        guards = set()
        newdead = []
        for rec in self.dead:
            (o, s, n, g) = rec
            if o < off + size and off < o + s:
                olds.append(n)
                guards.update(g)
                if o < off:
                    newdead.append((o, off - o, n, g))
                if o + s > off + size:
                    newdead.append((off + size, o + s - (off + size), n, g))
            else:
                newdead.append(rec)
        self.dead = newdead
        self._pending_guards = sorted(guards)
        h = self.nc.alloc_sbuf_tensor_at(name, list(shape), dtype,
                                         offset=off)
        self.sb[name] = h
        self.live[name] = (off, size, h.name)
        if not hasattr(self, "alloc_log"):
            self.alloc_log = []
        self.alloc_log.append((name, off, size, tuple(shape), dtype))
        if olds or self._pending_guards:
            self.p.alias_barrier(olds, h.name, extra=self._pending_guards)
        return h

    def free(self, *names):
        last = {}
        for i, o in enumerate(self.p.ops):
            last[o["engine"]] = i
        guard = tuple(sorted(last.values()))
        for name in names:
            off, size, mangled = self.live.pop(name)
            self.dead.append((off, size, mangled, guard))
            self.freelist.append([off, size])
            self.freelist.sort()
            merged = []
            for blk in self.freelist:
                if merged and merged[-1][0] + merged[-1][1] == blk[0]:
                    merged[-1][1] += blk[1]
                else:
                    merged.append(blk)
            self.freelist = merged

    def state_in(self, name, shape, dtype=F32):
        if name in getattr(self, "_states", {}):
            return self._states[name]
        ap = self.nc.dram_tensor(name, list(shape), dtype,
                                 kind="ExternalInput").ap()
        self.extra_in[name] = shape
        return ap

    def state_out(self, name, shape, dtype=F32):
        if getattr(self, "fused", False):
            if not hasattr(self, "_states"):
                self._states = {}
            ap = self.nc.dram_tensor(name, list(shape), dtype,
                                     kind="Internal").ap()
            self._states[name] = ap
            return ap
        ap = self.nc.dram_tensor(name, list(shape), dtype,
                                 kind="ExternalOutput").ap()
        self.extra_out[name] = shape
        return ap

    def fconst(self, value):
        v = float(value)
        if v not in self.fconst_vals:
            assert len(self.fconst_vals) < self.sb["fconsts"].shape[1]
            self.fconst_vals[v] = len(self.fconst_vals)
        j = self.fconst_vals[v]
        return self.sb["fconsts"][:, j:j + 1]

    def psum(self, name, shape, dtype=F32):
        h = self.nc.alloc_psum_tensor(name, shape, dtype)
        self.sb[name] = h
        return h

    def tap_full(self, name, sbuf_t):
        if name not in self.dev_taps:
            return
        sh = list(sbuf_t.shape)
        out = self.nc.dram_tensor(f"tap_{name}", sh, sbuf_t.dtype,
                                  kind="ExternalOutput").ap()
        self.p.dma("sync", lambda e: e.dma_start(out=out, in_=sbuf_t[:]),
                   reads=[sbuf_t], is_out=True)
        self.taps[name] = sh

    def tile_tap(self, name, tile_ap, tb, reads):
        if name not in self.dev_taps:
            return
        if name not in self.taps:
            self.nc.dram_tensor(f"tap_{name}", [128, NT, D], F32,
                                kind="ExternalOutput")
            self.taps[name] = [128, NT, D]
        dram = self.nc.get_tensor(f"tap_{name}") if hasattr(self.nc, "get_tensor") else None
        if not hasattr(self, "_tap_handles"):
            self._tap_handles = {}
        if name not in self._tap_handles:
            pass
        raise NotImplementedError


def stage_view(ctx, rows, cols):
    ws = ctx.sb["wstage"]
    assert rows <= 128 and cols <= 2048
    return ws[0:rows, 0:cols]


def load_convert(ctx, dst_bf_ap, hbm_ap, rows, cols, tag=""):
    p = ctx.p
    ws = ctx.sb["wstage"]
    sv = ws[0:rows, 0:cols]
    p.dma("sync", lambda e: e.dma_start(out=sv, in_=hbm_ap),
          reads=[], writes=[ws])
    p.op("gpsimd", lambda e: e.tensor_copy(out=dst_bf_ap, in_=sv),
         reads=[ws], writes=[dst_bf_ap])


def load_weight_bf16(ctx, name, hbm_ap, rows, cols):
    p = ctx.p
    rc = rows // 128
    bf = ctx.alloc(f"{name}_bf", [128, rc, cols], BF16)
    ws = ctx.sb["wstage"]
    piece = min(cols, 1024 // rc)
    for c0 in range(0, cols, piece):
        cw = min(piece, cols - c0)
        sv = ws[:, 0:rc * cw].rearrange("p (c m) -> p c m", c=rc)
        hb = hbm_ap[:, c0:c0 + cw].rearrange("(c p) m -> p c m", p=128)
        p.dma("sync", lambda e, sv=sv, hb=hb: e.dma_start(out=sv, in_=hb),
              reads=[], writes=[ws])
        dst = bf[:, :, c0:c0 + cw]
        p.op("gpsimd", lambda e, dst=dst, sv=sv: e.tensor_copy(
            out=dst, in_=sv), reads=[ws], writes=[bf])
    return bf


def pe_transpose(ctx, out_ap, in_ap, n_in_part, n_in_free):
    p = ctx.p
    pst = ctx.sb["ps_tr"]
    ident = ctx.consts["ident"]
    p.op("tensor", lambda e: e.transpose(
        pst.bitcast(BF16)[0:n_in_free, 0:n_in_part], in_ap,
        ident[0:n_in_part, 0:n_in_part]),
        reads=[in_ap, ident], writes=[pst])
    p.op("vector", lambda e: e.tensor_copy(
        out=out_ap, in_=pst.bitcast(BF16)[0:n_in_free, 0:n_in_part]),
        reads=[pst], writes=[out_ap])


def emit_ln(ctx):
    p = ctx.p
    x = ctx.sb["x_tm"]
    xn_fm = ctx.sb["xn_fm"]
    mu = ctx.sb["ln_mu"]
    rstd = ctx.sb["ln_rstd"]
    cen = ctx.sb["scratch_f32_a"]
    sq = ctx.sb["scratch_f32_b"]
    stage = ctx.sb["stage_bf"]
    col = ctx.sb["scratch_col"]
    for i in range(NT):
        xt = x[:, i, :]
        mcol = mu[:, i:i + 1]
        rcol = rstd[:, i:i + 1]
        p.op("vector", lambda e, xt=xt: e.reduce_sum(
            out=col[:, 0:1], in_=xt, axis=AX.X), reads=[x], writes=[col])
        p.op("vector", lambda e, mcol=mcol: e.tensor_scalar_mul(
            out=mcol, in0=col[:, 0:1], scalar1=1.0 / D),
            reads=[col], writes=[mu])
        p.op("vector", lambda e, xt=xt, mcol=mcol: e.tensor_scalar(
            out=cen[:], in0=xt, scalar1=mcol, scalar2=None,
            op0=ALU.subtract), reads=[x, mu], writes=[cen])
        p.op("scalar", lambda e: e.activation(
            out=sq[:], in_=cen[:], func=AF.Square), reads=[cen], writes=[sq])
        p.op("vector", lambda e: e.reduce_sum(
            out=col[:, 1:2], in_=sq[:], axis=AX.X), reads=[sq], writes=[col])
        epsc = ctx.fconst(1e-5)
        p.op("scalar", lambda e, epsc=epsc: e.activation(
            out=col[:, 2:3], in_=col[:, 1:2], func=AF.Sqrt,
            bias=epsc, scale=1.0 / D),
            reads=[col, "fconsts"], writes=[col])
        p.op("vector", lambda e, rcol=rcol: e.reciprocal(
            out=rcol, in_=col[:, 2:3]), reads=[col], writes=[rstd])
        p.op("scalar", lambda e, rcol=rcol: e.activation(
            out=stage[:], in_=cen[:], func=AF.Copy, scale=rcol),
            reads=[cen, rstd], writes=[stage])
        for c in range(ND):
            pe_transpose(ctx, xn_fm[:, c, i * 128:(i + 1) * 128],
                         stage[:, c * 128:(c + 1) * 128], 128, 128)


def xn_rec(ctx, i, out_tile, half=None):
    p = ctx.p
    x = ctx.sb["x_tm"]
    mu = ctx.sb["ln_mu"]
    rstd = ctx.sb["ln_rstd"]
    src = x[:, i, :] if half is None else x[:, i, half * 512:(half + 1) * 512]
    p.op("vector", lambda e: e.tensor_scalar(
        out=out_tile, in0=src, scalar1=mu[:, i:i + 1],
        scalar2=rstd[:, i:i + 1], op0=ALU.subtract, op1=ALU.mult),
        reads=[x, mu, rstd], writes=[out_tile])



def flash256(ctx, Qp, Kp_fn, V_fn, bias_a, bias_b, o_sink, kts_for_qb,
             mask_for, scale=0.125):
    p = ctx.p
    PT = [ctx.sb["fl_PT0"], ctx.sb["fl_PT1"]]
    ps_sa = ctx.sb["ps_sa"]
    ps_sb = ctx.sb["ps_sb"]
    ps_ava = ctx.sb["ps_ava"]
    ps_avb = ctx.sb["ps_avb"]
    rc_a = ctx.sb["fl_rc_a"]
    rc_b = ctx.sb["fl_rc_b"]
    for qb in range(8):
        qsl = slice(qb * 256, (qb + 1) * 256)
        kts = kts_for_qb(qb)
        for g0 in range(0, len(kts), 4):
            grp = kts[g0:g0 + 4]
            for j, kt in enumerate(grp):
                ksl = slice(kt * 128, (kt + 1) * 128)
                psl = slice(j * 256, (j + 1) * 256)
                p.op("tensor", lambda e, ksl=ksl, qsl=qsl, psl=psl: e.matmul(
                    ps_sa[:, psl], Kp_fn(0, ksl), Qp[0:64, qsl],
                    start=True, stop=True, tile_position=(0, 0)),
                    reads=[ctx.sb["fl_K0"], Qp], writes=[ps_sa])
                p.op("tensor", lambda e, ksl=ksl, qsl=qsl, psl=psl: e.matmul(
                    ps_sb[:, psl], Kp_fn(1, ksl), Qp[64:128, qsl],
                    start=True, stop=True, tile_position=(64, 0)),
                    reads=[ctx.sb["fl_K1"], Qp], writes=[ps_sb])
            n = len(grp)
            for h2, (pssrc, biast) in enumerate(((ps_sa, bias_a),
                                                 (ps_sb, bias_b))):
                p.op("scalar", lambda e, h2=h2, pssrc=pssrc, biast=biast,
                     n=n, g0=g0: e.activation(
                         out=PT[h2][:, g0:g0 + n, :],
                         in_=pssrc[:, 0:n * 256].rearrange(
                             "p (k q) -> p k q", k=n),
                         func=AF.Exp, bias=biast[:], scale=scale),
                     reads=[pssrc, biast], writes=[PT[h2]])
        for j, kt in enumerate(kts):
            for qs in range(2):
                m = mask_for(qb, kt, qs)
                if m is None or isinstance(m, str):
                    continue
                for h2 in range(2):
                    p.op("vector", lambda e, h2=h2, j=j, qs=qs, m=m:
                         e.tensor_tensor(
                             out=PT[h2][:, j, qs * 128:(qs + 1) * 128],
                             in0=PT[h2][:, j, qs * 128:(qs + 1) * 128],
                             in1=m[:], op=ALU.mult),
                         reads=[PT[h2], "c_tril_incl", "c_trilx"],
                         writes=[PT[h2]])
        for qs in range(2):
            qtile = qb * 2 + qs
            for h2, (psav, rc) in enumerate(((ps_ava, rc_a),
                                             (ps_avb, rc_b))):
                valid = [(j, kt) for j, kt in enumerate(kts)
                         if not isinstance(mask_for(qb, kt, qs), str)]
                for vi, (j, kt) in enumerate(valid):
                    p.op("tensor", lambda e, h2=h2, psav=psav, j=j, kt=kt,
                         qs=qs, first=(vi == 0), last=(vi == len(valid) - 1):
                         e.matmul(
                             psav[:, 0:65],
                             PT[h2][:, j, qs * 128:(qs + 1) * 128],
                             V_fn(h2, kt), start=first, stop=last),
                         reads=[PT[h2], ctx.sb["fl_V"]], writes=[psav])
                p.op("vector", lambda e, psav=psav, rc=rc: e.reciprocal(
                    out=rc[:], in_=psav[:, 64:65]), reads=[psav], writes=[rc])
            o_sink(qtile, ps_ava, ps_avb, rc_a, rc_b)


def emit_layer0(ctx):
    p = ctx.p
    emit_ln(ctx)
    xn_fm = ctx.sb["xn_fm"]
    ident = ctx.consts["ident"]
    onesrow = ctx.consts["onesrow"]

    ps_sa = ctx.sb["ps_sa"]
    ps_sb = ctx.sb["ps_sb"]
    ps_ava = ctx.sb["ps_ava"]
    ps_avb = ctx.sb["ps_avb"]
    ps_m1 = ctx.sb["ps_m1"]
    ps_m2 = ctx.sb["ps_m2"]
    rw_bf = ctx.alloc("l0_rw_bf", [8, 1024], BF16)
    load_convert(ctx, rw_bf[:], ctx.hbm["L0_ffn_router_w"], 8, 1024)
    rwT = ctx.alloc("l0_rwT", [128, 8, 8], BF16)
    for c in range(8):
        pe_transpose(ctx, rwT[:, c, :], rw_bf[:, c * 128:(c + 1) * 128],
                     8, 128)
    for c in range(8):
        p.op("tensor", lambda e, c=c: e.matmul(
            ps_m1[0:1, 0:8], xn_fm[:, c, 0:1], rwT[:, c, :],
            start=(c == 0), stop=(c == 7)),
            reads=[xn_fm, rwT], writes=[ps_m1])
    gl = ctx.alloc("l0_gates", [1, 64], F32)
    p.op("vector", lambda e: e.tensor_copy(out=gl[:, 0:8],
                                           in_=ps_m1[0:1, 0:8]),
         reads=[ps_m1], writes=[gl])
    p.op("vector", lambda e: e.reduce_max(out=gl[:, 8:9], in_=gl[:, 0:8],
                                          axis=AX.X), reads=[gl], writes=[gl])
    p.op("vector", lambda e: e.tensor_scalar(
        out=gl[:, 16:24], in0=gl[:, 0:8], scalar1=gl[:, 8:9], scalar2=None,
        op0=ALU.is_equal), reads=[gl], writes=[gl])
    p.op("vector", lambda e: e.scalar_tensor_tensor(
        out=gl[:, 24:32], in0=gl[:, 16:24], scalar=-1e30, in1=gl[:, 0:8],
        op0=ALU.mult, op1=ALU.add), reads=[gl], writes=[gl])
    p.op("vector", lambda e: e.reduce_max(out=gl[:, 9:10], in_=gl[:, 24:32],
                                          axis=AX.X), reads=[gl], writes=[gl])
    p.op("vector", lambda e: e.tensor_scalar(
        out=gl[:, 32:40], in0=gl[:, 0:8], scalar1=gl[:, 9:10], scalar2=None,
        op0=ALU.is_ge), reads=[gl], writes=[gl])
    p.op("vector", lambda e: e.tensor_scalar(
        out=gl[:, 40:48], in0=gl[:, 0:8], scalar1=gl[:, 8:9], scalar2=None,
        op0=ALU.subtract), reads=[gl], writes=[gl])
    p.op("scalar", lambda e: e.activation(
        out=gl[:, 48:56], in_=gl[:, 40:48], func=AF.Exp),
        reads=[gl], writes=[gl])
    p.op("vector", lambda e: e.tensor_mul(
        out=gl[:, 48:56], in0=gl[:, 48:56], in1=gl[:, 32:40]),
        reads=[gl], writes=[gl])
    p.op("vector", lambda e: e.reduce_sum(out=gl[:, 10:11], in_=gl[:, 48:56],
                                          axis=AX.X), reads=[gl], writes=[gl])
    p.op("vector", lambda e: e.reciprocal(out=gl[:, 11:12], in_=gl[:, 10:11]),
         reads=[gl], writes=[gl])
    p.op("vector", lambda e: e.tensor_scalar(
        out=gl[:, 56:64], in0=gl[:, 48:56], scalar1=gl[:, 11:12],
        scalar2=None, op0=ALU.mult), reads=[gl], writes=[gl])
    g_bf = ctx.alloc("l0_g_bf", [1, 8], BF16)
    p.op("vector", lambda e: e.tensor_copy(out=g_bf[:], in_=gl[:, 56:64]),
         reads=[gl], writes=[g_bf])
    gcol8 = ctx.alloc("l0_gcol8", [8, 1], BF16)
    pe_transpose(ctx, gcol8[:], g_bf[:], 1, 8)
    e16 = ctx.consts["e16sel"]
    p.op("tensor", lambda e: e.matmul(ps_m1[:, 0:1], e16[:], gcol8[:],
                                      start=True, stop=True),
         reads=[e16, gcol8], writes=[ps_m1])
    gcol = ctx.alloc("l0_gcol", [128, 1], F32)
    p.op("vector", lambda e: e.tensor_copy(out=gcol[:], in_=ps_m1[:, 0:1]),
         reads=[ps_m1], writes=[gcol])

    ld_bf = ctx.alloc("l0_ld_bf", [128, 1024], BF16)
    load_convert(ctx, ld_bf[:],
                 ctx.hbm["L0_ffn_lora_down"].rearrange("e r h -> (e r) h"),
                 128, 1024)
    ldT = ctx.alloc("l0_ldT", [128, 8, 128], BF16)
    for c in range(8):
        pe_transpose(ctx, ldT[:, c, :], ld_bf[:, c * 128:(c + 1) * 128],
                     128, 128)
    down_fm = ctx.alloc("l0_down_fm", [128, T], BF16)
    for tb in range(4):
        tsl = slice(tb * 512, (tb + 1) * 512)
        for c in range(8):
            p.op("tensor", lambda e, tsl=tsl, c=c: e.matmul(
                ps_m1[:, :], ldT[:, c, :], xn_fm[:, c, tsl],
                start=(c == 0), stop=(c == 7)),
                reads=[ldT, xn_fm], writes=[ps_m1])
        dst = down_fm[:, tsl]
        p.op("vector", lambda e, dst=dst: e.tensor_scalar(
            out=dst, in0=ps_m1[:, :], scalar1=gcol[:], scalar2=None,
            op0=ALU.mult), reads=[ps_m1, gcol], writes=[down_fm])

    up_bf = ctx.alloc("l0_up_bf", [128, 64, 16], BF16)
    ws = ctx.sb["wstage"]
    sv_up = ws[:, 0:1024].rearrange("p (c r) -> p c r", r=16)
    hb_up = ctx.hbm["L0_ffn_lora_up"].rearrange(
        "e h r -> (e h) r").rearrange("(c p) r -> p c r", p=128)
    p.dma("sync", lambda e: e.dma_start(out=sv_up, in_=hb_up), writes=[ws])
    p.op("gpsimd", lambda e: e.tensor_copy(out=up_bf[:], in_=sv_up),
         reads=[ws], writes=[up_bf])
    upstage = ctx.alloc("l0_upstage", [16, 8, 128], BF16)
    upg = ctx.alloc("l0_upg", [128, D], BF16)
    for ee in range(8):
        for j in range(8):
            pe_transpose(ctx, upstage[:, j, :], up_bf[:, ee * 8 + j, :],
                         128, 16)
        p.dma("sync", lambda e, ee=ee: e.dma_start(
            out=upg[ee * 16:(ee + 1) * 16, :], in_=upstage[:]),
            reads=[upstage], writes=[upg])

    ctx.free("l0_ld_bf", "l0_ldT", "l0_up_bf", "l0_upstage", "l0_rw_bf",
             "l0_rwT", "l0_gates", "l0_g_bf", "l0_gcol8")
    O_fm = ctx.alloc("l0_O_fm", [128, ND, T], BF16)
    Qp = ctx.alloc("l0_Qp", [128, T], BF16)
    Kp = ctx.alloc("l0_Kp", [128, T], BF16)
    Vp = ctx.alloc("l0_Vp", [128, NT, 130], BF16)
    ctx.alloc("fl_PT0", [128, 16, 256], BF16)
    ctx.alloc("fl_PT1", [128, 16, 256], BF16)
    ctx.alloc("fl_rc_a", [128, 1], F32)
    ctx.alloc("fl_rc_b", [128, 1], F32)
    wq_bf = ctx.alloc("l0_wq_bf", [128, 8, 128], BF16)
    wk_bf = ctx.alloc("l0_wk_bf", [128, 8, 128], BF16)
    wv_bf = ctx.alloc("l0_wv_bf", [128, 8, 128], BF16)
    nsc = ctx.alloc("l0_nsc", [1, 64], F32)
    nsc_bf = ctx.alloc("l0_nscbf", [1, 2], BF16)
    bias_a = ctx.alloc("l0_bias_a", [128, 1], F32)
    bias_b = ctx.alloc("l0_bias_b", [128, 1], F32)
    sqst = ctx.sb["stage_bf"]
    onesA = ctx.consts["onesA"]
    onesB = ctx.consts["onesB"]
    stageO = ctx.alloc("l0_stageO", [128, 128], BF16)

    for hp in range(8):
        for wname, wbf in (("L0_attn_Wq", wq_bf), ("L0_attn_Wk", wk_bf),
                           ("L0_attn_Wv", wv_bf)):
            hb = ctx.hbm[wname][:, hp * 128:(hp + 1) * 128].rearrange(
                "(c p) m -> p c m", p=128)
            ws = ctx.sb["wstage"]
            sv = ws[:, 0:1024].rearrange("p (c m) -> p c m", c=8)
            p.dma("sync", lambda e, hb=hb, sv=sv: e.dma_start(
                out=sv, in_=hb), writes=[ws])
            p.op("gpsimd", lambda e, wbf=wbf, sv=sv: e.tensor_copy(
                out=wbf[:], in_=sv), reads=[ws], writes=[wbf])
        for (wbf, dstp) in ((wq_bf, Qp), (wk_bf, Kp)):
            for tb in range(4):
                tsl = slice(tb * 512, (tb + 1) * 512)
                for c in range(8):
                    p.op("tensor", lambda e, wbf=wbf, tsl=tsl, c=c: e.matmul(
                        ps_m2[:, :], wbf[:, c, :], xn_fm[:, c, tsl],
                        start=(c == 0), stop=(c == 7)),
                        reads=[wbf, xn_fm], writes=[ps_m2])
                dst = dstp[:, tsl]
                p.op("vector", lambda e, dst=dst: e.tensor_copy(
                    out=dst, in_=ps_m2[:, :]),
                    reads=[ps_m2], writes=[dstp])
        for tb in range(NT):
            for c in range(8):
                p.op("tensor", lambda e, tb=tb, c=c: e.matmul(
                    ps_m2[:, 0:128],
                    xn_fm[:, c, tb * 128:(tb + 1) * 128],
                    wv_bf[:, c, :], start=(c == 0), stop=(c == 7)),
                    reads=[wv_bf, xn_fm], writes=[ps_m2])
            p.op("vector", lambda e, tb=tb: e.tensor_copy(
                out=Vp[:, tb, 0:64], in_=ps_m2[:, 0:64]),
                reads=[ps_m2], writes=[Vp])
            p.op("vector", lambda e, tb=tb: e.tensor_copy(
                out=Vp[:, tb, 65:129], in_=ps_m2[:, 64:128]),
                reads=[ps_m2], writes=[Vp])
        p.op("vector", lambda e: e.memset(Vp[:, :, 64:65], 1.0), writes=[Vp])
        p.op("vector", lambda e: e.memset(Vp[:, :, 129:130], 1.0),
             writes=[Vp])

        for si, src in enumerate((Qp, Kp)):
            for tb in range(4):
                sl = slice(tb * 512, (tb + 1) * 512)
                p.op("scalar", lambda e, src=src, sl=sl: e.activation(
                    out=sqst[:, 0:512], in_=src[:, sl], func=AF.Square),
                    reads=[src], writes=[sqst])
                for h, ones in enumerate((onesA, onesB)):
                    p.op("tensor", lambda e, ones=ones: e.matmul(
                        ps_m1[0:1, :], ones[:], sqst[:, 0:512],
                        start=True, stop=True),
                        reads=[sqst, ones], writes=[ps_m1])
                    o = (si * 2 + h) * 4 + tb
                    p.op("vector", lambda e, o=o: e.reduce_max(
                        out=nsc[:, o:o + 1], in_=ps_m1[0:1, :], axis=AX.X),
                        reads=[ps_m1], writes=[nsc])
        for h in range(2):
            o = 16 + h * 8
            qs4 = slice(h * 4, h * 4 + 4)
            ks4 = slice(8 + h * 4, 8 + h * 4 + 4)
            p.op("vector", lambda e, o=o, qs4=qs4: e.reduce_max(
                out=nsc[:, o:o + 1], in_=nsc[:, qs4], axis=AX.X),
                reads=[nsc], writes=[nsc])
            p.op("vector", lambda e, o=o, ks4=ks4: e.reduce_max(
                out=nsc[:, o + 1:o + 2], in_=nsc[:, ks4], axis=AX.X),
                reads=[nsc], writes=[nsc])
            p.op("vector", lambda e, o=o: e.tensor_mul(
                out=nsc[:, o + 2:o + 3], in0=nsc[:, o:o + 1],
                in1=nsc[:, o + 1:o + 2]), reads=[nsc], writes=[nsc])
            p.op("scalar", lambda e, o=o: e.activation(
                out=nsc[:, o + 3:o + 4], in_=nsc[:, o + 2:o + 3],
                func=AF.Sqrt), reads=[nsc], writes=[nsc])
            p.op("vector", lambda e, o=o: e.tensor_scalar_mul(
                out=nsc[:, o + 4:o + 5], in0=nsc[:, o + 3:o + 4],
                scalar1=-0.125), reads=[nsc], writes=[nsc])
        p.op("vector", lambda e: e.tensor_copy(out=nsc_bf[:, 0:1],
                                               in_=nsc[:, 20:21]),
             reads=[nsc], writes=[nsc_bf])
        p.op("vector", lambda e: e.tensor_copy(out=nsc_bf[:, 1:2],
                                               in_=nsc[:, 28:29]),
             reads=[nsc], writes=[nsc_bf])
        for biast, cs in ((bias_a, slice(0, 1)), (bias_b, slice(1, 2))):
            p.op("tensor", lambda e, cs=cs: e.matmul(
                ps_m1[:, 0:1], onesrow[:], nsc_bf[:, cs],
                start=True, stop=True),
                reads=[nsc_bf, onesrow], writes=[ps_m1])
            p.op("vector", lambda e, biast=biast: e.tensor_copy(
                out=biast[:], in_=ps_m1[:, 0:1]),
                reads=[ps_m1], writes=[biast])

        ctx.sb["fl_K0"] = Kp
        ctx.sb["fl_K1"] = Kp
        ctx.sb["fl_V"] = Vp

        def l0_sink(qtile, psa, psb, rca, rcb, hp=hp):
            p.op("vector", lambda e: e.tensor_scalar(
                out=stageO[:, 0:64], in0=psa[:, 0:64], scalar1=rca[:],
                scalar2=None, op0=ALU.mult),
                reads=[psa, rca], writes=[stageO])
            p.op("vector", lambda e: e.tensor_scalar(
                out=stageO[:, 64:128], in0=psb[:, 0:64], scalar1=rcb[:],
                scalar2=None, op0=ALU.mult),
                reads=[psb, rcb], writes=[stageO])
            pe_transpose(ctx, O_fm[:, hp, qtile * 128:(qtile + 1) * 128],
                         stageO[:], 128, 128)

        flash256(ctx, Qp,
                 lambda h2, ksl: Kp[h2 * 64:(h2 + 1) * 64, ksl],
                 lambda h2, kt: Vp[:, kt, h2 * 65:(h2 + 1) * 65],
                 bias_a, bias_b, l0_sink,
                 lambda qb: list(range(16)),
                 lambda qb, kt, qs: None)

    ctx.free("fl_PT0", "fl_PT1", "fl_rc_a", "fl_rc_b")
    ctx.sb["scratch_f32_c"] = ctx.alloc("l0_scrc", [128, D], F32)
    wout_bf = load_weight_bf16(ctx, "l0_wout", ctx.hbm["L0_attn_Wout"],
                               1024, 1024)
    x = ctx.sb["x_tm"]
    xnr = ctx.sb["scratch_f32_a"]
    sbuf_s = ctx.sb["scratch_f32_c"]
    tmp = ctx.sb["scratch_f32_b"]
    ps_z = ctx.sb["ps_m2"]
    for tb in range(NT):
        xn_rec(ctx, tb, xnr[:])
        for half in range(2):
            esl = slice(half * 512, (half + 1) * 512)
            for c in range(8):
                p.op("tensor", lambda e, tb=tb, c=c, esl=esl: e.matmul(
                    ps_z[:, :], O_fm[:, c, tb * 128:(tb + 1) * 128],
                    wout_bf[:, c, esl], start=(c == 0), stop=(c == 7)),
                    reads=[O_fm, wout_bf], writes=[ps_z])
            p.op("vector", lambda e, esl=esl: e.scalar_tensor_tensor(
                out=sbuf_s[:, esl], in0=ps_z[:, :], scalar=1.0,
                in1=xnr[:, esl], op0=ALU.mult, op1=ALU.add),
                reads=[ps_z, xnr], writes=[sbuf_s])
        for half in range(2):
            hsl = slice(half * 512, (half + 1) * 512)
            p.op("tensor", lambda e, tb=tb, hsl=hsl: e.matmul(
                ps_m1[:, :], down_fm[:, tb * 128:(tb + 1) * 128],
                upg[:, hsl], start=True, stop=True),
                reads=[down_fm, upg], writes=[ps_m1])
            p.op("vector", lambda e, hsl=hsl: e.scalar_tensor_tensor(
                out=tmp[:, hsl], in0=ps_m1[:, :], scalar=1.0,
                in1=xnr[:, hsl], op0=ALU.mult, op1=ALU.add),
                reads=[ps_m1, xnr], writes=[tmp])
        p.op("scalar", lambda e: e.activation(
            out=tmp[:], in_=tmp[:], func=AF.Silu), reads=[tmp], writes=[tmp])
        p.op("vector", lambda e, tb=tb: e.tensor_mul(
            out=tmp[:], in0=tmp[:], in1=sbuf_s[:]),
            reads=[tmp, sbuf_s], writes=[tmp])
        p.op("vector", lambda e, tb=tb: e.tensor_add(
            out=x[:, tb, :], in0=x[:, tb, :], in1=tmp[:]),
            reads=[tmp, x], writes=[x])
    ctx.free(*[n for n in list(ctx.live) if n.startswith("l0_")])


def bcast_col(ctx, row_ap, out_col, reads):
    p = ctx.p
    tmp = ctx.sb["bc_bf"]
    ps = ctx.sb["ps_m1"]
    p.op("vector", lambda e: e.tensor_copy(out=tmp[:], in_=row_ap),
         reads=list(reads), writes=[tmp])
    p.op("tensor", lambda e: e.matmul(ps[:, 0:1], ctx.consts["onesrow"][:],
                                      tmp[:], start=True, stop=True),
         reads=[tmp, "c_onesrow"], writes=[ps])
    p.op("vector", lambda e: e.tensor_copy(out=out_col, in_=ps[:, 0:1]),
         reads=[ps], writes=[out_col])


def cmul_col(ctx, outr, outi, ar, ai, br, bi):
    p = ctx.p
    w = ctx.sb["l1_cmul_w"]
    p.op("vector", lambda e: e.tensor_mul(out=w[:, 0:1], in0=ar, in1=br),
         reads=[ar, br], writes=[w])
    p.op("vector", lambda e: e.tensor_mul(out=w[:, 1:2], in0=ai, in1=bi),
         reads=[ai, bi], writes=[w])
    p.op("vector", lambda e: e.tensor_mul(out=w[:, 2:3], in0=ar, in1=bi),
         reads=[ar, bi], writes=[w])
    p.op("vector", lambda e: e.tensor_mul(out=w[:, 3:4], in0=ai, in1=br),
         reads=[ai, br], writes=[w])
    p.op("vector", lambda e: e.tensor_sub(out=outr, in0=w[:, 0:1],
                                          in1=w[:, 1:2]),
         reads=[w], writes=[outr])
    p.op("vector", lambda e: e.tensor_add(out=outi, in0=w[:, 2:3],
                                          in1=w[:, 3:4]),
         reads=[w], writes=[outi])


def emit_layer1(ctx):
    p = ctx.p
    emit_ln(ctx)
    xn_fm = ctx.sb["xn_fm"]
    ps_m1 = ctx.sb["ps_m1"]
    ps_m2 = ctx.sb["ps_m2"]
    ps_big = ctx.sb["ps_sa"]
    ctx.alloc("bc_bf", [1, 1], BF16)
    ctx.alloc("l1_cmul_w", [64, 4], F32)

    ac = ctx.alloc("l1_acols", [64, 40], F32)
    p.dma("sync", lambda e: e.dma_start(
        out=ac[:, 0:1], in_=ctx.hbm["L1_attn_log_A_real"][:, None]),
        writes=[ac])
    p.dma("sync", lambda e: e.dma_start(
        out=ac[:, 1:2], in_=ctx.hbm["L1_attn_log_A_imag"][:, None]),
        writes=[ac])
    p.op("scalar", lambda e: e.activation(out=ac[:, 2:3], in_=ac[:, 0:1],
                                          func=AF.Exp),
         reads=[ac], writes=[ac])
    p.op("vector", lambda e: e.tensor_scalar_mul(
        out=ac[:, 2:3], in0=ac[:, 2:3], scalar1=-0.5),
        reads=[ac], writes=[ac])
    p.op("vector", lambda e: e.tensor_scalar_mul(
        out=ac[:, 3:4], in0=ac[:, 1:2], scalar1=0.5),
        reads=[ac], writes=[ac])
    p.op("vector", lambda e: e.tensor_scalar(
        out=ac[:, 4:5], in0=ac[:, 2:3], scalar1=1.0, scalar2=-1.0,
        op0=ALU.subtract, op1=ALU.mult), reads=[ac], writes=[ac])
    p.op("vector", lambda e: e.tensor_mul(out=ac[:, 5:6], in0=ac[:, 4:5],
                                          in1=ac[:, 4:5]),
         reads=[ac], writes=[ac])
    p.op("vector", lambda e: e.tensor_mul(out=ac[:, 6:7], in0=ac[:, 3:4],
                                          in1=ac[:, 3:4]),
         reads=[ac], writes=[ac])
    p.op("vector", lambda e: e.tensor_add(out=ac[:, 5:6], in0=ac[:, 5:6],
                                          in1=ac[:, 6:7]),
         reads=[ac], writes=[ac])
    p.op("vector", lambda e: e.reciprocal(out=ac[:, 7:8], in_=ac[:, 5:6]),
         reads=[ac], writes=[ac])
    p.op("vector", lambda e: e.tensor_mul(out=ac[:, 8:9], in0=ac[:, 2:3],
                                          in1=ac[:, 2:3]),
         reads=[ac], writes=[ac])
    p.op("vector", lambda e: e.tensor_add(out=ac[:, 8:9], in0=ac[:, 8:9],
                                          in1=ac[:, 6:7]),
         reads=[ac], writes=[ac])
    p.op("vector", lambda e: e.tensor_scalar(
        out=ac[:, 8:9], in0=ac[:, 8:9], scalar1=1.0, scalar2=-1.0,
        op0=ALU.subtract, op1=ALU.mult), reads=[ac], writes=[ac])
    p.op("vector", lambda e: e.tensor_mul(out=ac[:, 10:11], in0=ac[:, 8:9],
                                          in1=ac[:, 7:8]),
         reads=[ac], writes=[ac])
    p.op("vector", lambda e: e.tensor_scalar_mul(
        out=ac[:, 11:12], in0=ac[:, 3:4], scalar1=2.0),
        reads=[ac], writes=[ac])
    p.op("vector", lambda e: e.tensor_mul(out=ac[:, 11:12], in0=ac[:, 11:12],
                                          in1=ac[:, 7:8]),
         reads=[ac], writes=[ac])
    p.op("vector", lambda e: e.tensor_copy(out=ac[:, 12:13],
                                           in_=ac[:, 10:11]),
         reads=[ac], writes=[ac])
    p.op("vector", lambda e: e.tensor_copy(out=ac[:, 13:14],
                                           in_=ac[:, 11:12]),
         reads=[ac], writes=[ac])
    for j in range(1, 11):
        o, q = 12 + 2 * j, 12 + 2 * (j - 1)
        cmul_col(ctx, ac[:, o:o + 1], ac[:, o + 1:o + 2],
                 ac[:, q:q + 1], ac[:, q + 1:q + 2],
                 ac[:, q:q + 1], ac[:, q + 1:q + 2])
    nAim = ctx.alloc("l1_nAim", [64, 11], F32)
    for j in range(11):
        p.op("vector", lambda e, j=j: e.tensor_scalar_mul(
            out=nAim[:, j:j + 1], in0=ac[:, 13 + 2 * j:14 + 2 * j],
            scalar1=-1.0), reads=[ac], writes=[nAim])
    Fre = ctx.alloc("l1_Fre", [64, 16], F32)
    Fim = ctx.alloc("l1_Fim", [64, 16], F32)
    p.op("vector", lambda e: e.tensor_copy(out=Fre[:, 0:1], in_=ac[:, 12:13]),
         reads=[ac], writes=[Fre])
    p.op("vector", lambda e: e.tensor_copy(out=Fim[:, 0:1], in_=ac[:, 13:14]),
         reads=[ac], writes=[Fim])
    for i in range(1, 16):
        cmul_col(ctx, Fre[:, i:i + 1], Fim[:, i:i + 1],
                 Fre[:, i - 1:i], Fim[:, i - 1:i],
                 ac[:, 12:13], ac[:, 13:14])

    bw_bf = ctx.alloc("l1_bw_bf", [128, 1024], BF16)
    load_convert(ctx, bw_bf[:], ctx.hbm["L1_attn_B_w"], 128, 1024)
    bwT = ctx.alloc("l1_bwT", [128, 8, 128], BF16)
    for c in range(8):
        pe_transpose(ctx, bwT[:, c, :], bw_bf[:, c * 128:(c + 1) * 128],
                     128, 128)
    SRe = [ctx.alloc(f"l1_sre{k}", [64, T], F32) for k in range(2)]
    SIm = [ctx.alloc(f"l1_sim{k}", [64, T], F32) for k in range(2)]
    hstmp = ctx.alloc("l1_hstmp", [64, T], F32)
    for tb in range(4):
        tsl = slice(tb * 512, (tb + 1) * 512)
        for c in range(8):
            p.op("tensor", lambda e, tsl=tsl, c=c: e.matmul(
                ps_big[:, 0:512], bwT[:, c, :], xn_fm[:, c, tsl],
                start=(c == 0), stop=(c == 7)),
                reads=[bwT, xn_fm], writes=[ps_big])
        p.op("vector", lambda e, tsl=tsl: e.tensor_copy(
            out=SRe[0][:, tsl], in_=ps_big[0:64, 0:512]),
            reads=[ps_big], writes=[SRe[0]])
        p.op("vector", lambda e, tsl=tsl: e.tensor_copy(
            out=SIm[0][:, tsl], in_=ps_big[64:128, 0:512]),
            reads=[ps_big], writes=[SIm[0]])

    def hs_pass(bufs_s, bufs_d, s, width, j):
        re_s, im_s = bufs_s
        re_d, im_d = bufs_d
        are = ac[:, 12 + 2 * j:13 + 2 * j]
        aim = ac[:, 13 + 2 * j:14 + 2 * j]
        naim = nAim[:, j:j + 1]
        n = re_s.shape[1]

        def v(t):
            if width == n:
                return t[:][:, None, :]
            return t[:].rearrange("p (c w) -> p c w", w=width)
        p.op("vector", lambda e: e.tensor_copy(
            out=v(re_d)[:, :, 0:s], in_=v(re_s)[:, :, 0:s]),
            reads=[re_s], writes=[re_d])
        p.op("vector", lambda e: e.tensor_copy(
            out=v(im_d)[:, :, 0:s], in_=v(im_s)[:, :, 0:s]),
            reads=[im_s], writes=[im_d])
        p.op("vector", lambda e: e.scalar_tensor_tensor(
            out=v(hstmp)[:, 0:n // width, 0:width - s],
            in0=v(re_s)[:, :, 0:width - s],
            scalar=are, in1=v(re_s)[:, :, s:width],
            op0=ALU.mult, op1=ALU.add), reads=[re_s, ac], writes=[hstmp])
        p.op("vector", lambda e: e.scalar_tensor_tensor(
            out=v(re_d)[:, :, s:width], in0=v(im_s)[:, :, 0:width - s],
            scalar=naim, in1=v(hstmp)[:, 0:n // width, 0:width - s],
            op0=ALU.mult, op1=ALU.add), reads=[im_s, hstmp, nAim],
            writes=[re_d])
        p.op("vector", lambda e: e.scalar_tensor_tensor(
            out=v(hstmp)[:, 0:n // width, 0:width - s],
            in0=v(im_s)[:, :, 0:width - s],
            scalar=are, in1=v(im_s)[:, :, s:width],
            op0=ALU.mult, op1=ALU.add), reads=[im_s, ac], writes=[hstmp])
        p.op("vector", lambda e: e.scalar_tensor_tensor(
            out=v(im_d)[:, :, s:width], in0=v(re_s)[:, :, 0:width - s],
            scalar=aim, in1=v(hstmp)[:, 0:n // width, 0:width - s],
            op0=ALU.mult, op1=ALU.add), reads=[re_s, hstmp, ac],
            writes=[im_d])

    cur = 0
    for j, s in enumerate((1, 2, 4, 8)):
        hs_pass((SRe[cur], SIm[cur]), (SRe[1 - cur], SIm[1 - cur]),
                s, 16, j)
        cur = 1 - cur
    CRe = [ctx.alloc(f"l1_cre{k}", [64, 128], F32) for k in range(2)]
    CIm = [ctx.alloc(f"l1_cim{k}", [64, 128], F32) for k in range(2)]
    p.op("vector", lambda e: e.tensor_copy(
        out=CRe[0][:], in_=SRe[cur][:].rearrange(
            "p (c w) -> p c w", w=16)[:, :, 15]),
        reads=[SRe[cur]], writes=[CRe[0]])
    p.op("vector", lambda e: e.tensor_copy(
        out=CIm[0][:], in_=SIm[cur][:].rearrange(
            "p (c w) -> p c w", w=16)[:, :, 15]),
        reads=[SIm[cur]], writes=[CIm[0]])
    ccur = 0
    for jj, s in enumerate((1, 2, 4, 8, 16, 32, 64)):
        hs_pass((CRe[ccur], CIm[ccur]), (CRe[1 - ccur], CIm[1 - ccur]),
                s, 128, 4 + jj)
        ccur = 1 - ccur

    OP1 = hstmp
    OP2r = SRe[1 - cur]
    vS = lambda t: t[:].rearrange("p (c w) -> p c w", w=16)
    for comp in range(2):
        dst = SRe[cur] if comp == 0 else SIm[cur]
        if comp == 0:
            t1a, t1b = CRe[ccur], Fre
            t2a, t2b = CIm[ccur], Fim
        else:
            t1a, t1b = CRe[ccur], Fim
            t2a, t2b = CIm[ccur], Fre
        p.op("vector", lambda e, t1a=t1a, t1b=t1b: e.tensor_mul(
            out=vS(OP1)[:, 1:128, :],
            in0=t1a[:, 0:127, None].broadcast_to([64, 127, 16]),
            in1=t1b[:, None, :].broadcast_to([64, 127, 16])),
            reads=[t1a, t1b], writes=[OP1])
        p.op("vector", lambda e, t2a=t2a, t2b=t2b: e.tensor_mul(
            out=vS(OP2r)[:, 1:128, :],
            in0=t2a[:, 0:127, None].broadcast_to([64, 127, 16]),
            in1=t2b[:, None, :].broadcast_to([64, 127, 16])),
            reads=[t2a, t2b], writes=[OP2r])
        op = "tensor_sub" if comp == 0 else "tensor_add"
        p.op("vector", lambda e, op=op: getattr(e, op)(
            out=vS(OP1)[:, 1:128, :], in0=vS(OP1)[:, 1:128, :],
            in1=vS(OP2r)[:, 1:128, :]), reads=[OP1, OP2r], writes=[OP1])
        p.op("vector", lambda e, dst=dst: e.tensor_add(
            out=vS(dst)[:, 1:128, :], in0=vS(dst)[:, 1:128, :],
            in1=vS(OP1)[:, 1:128, :]), reads=[OP1, dst], writes=[dst])

    Hfm = ctx.alloc("l1_Hfm", [128, T], BF16)
    stg = ctx.sb["stage_bf"]
    p.op("vector", lambda e: e.tensor_copy(out=Hfm[0:64, :],
                                           in_=SRe[cur][:]),
         reads=[SRe[cur]], writes=[Hfm])
    for piece in range(2):
        sl = slice(piece * 1024, (piece + 1) * 1024)
        p.op("vector", lambda e, sl=sl: e.tensor_copy(
            out=stg[0:64, :], in_=SIm[cur][:, sl]),
            reads=[SIm[cur]], writes=[stg])
        p.dma("sync", lambda e, sl=sl: e.dma_start(
            out=Hfm[64:128, sl], in_=stg[0:64, :]),
            reads=[stg], writes=[Hfm])

    ctx.free("l1_sre0", "l1_sre1", "l1_sim0", "l1_sim1", "l1_hstmp",
             "l1_cre0", "l1_cre1", "l1_cim0", "l1_cim1", "l1_bw_bf",
             "l1_bwT", "l1_Fre", "l1_Fim", "l1_nAim")

    cw_bf = ctx.alloc("l1_cw_bf", [128, 1024], BF16)
    ws_cw = ctx.sb["wstage"][:, 0:1024].rearrange("p (c n) -> p c n", n=128)
    p.dma("sync", lambda e: e.dma_start(
        out=ws_cw, in_=ctx.hbm["L1_attn_C_w"].rearrange(
            "(c p) n -> p c n", p=128)), writes=["wstage"])
    p.op("gpsimd", lambda e: e.tensor_copy(
        out=cw_bf[:].rearrange("p (c n) -> p c n", n=128), in_=ws_cw),
        reads=["wstage"], writes=[cw_bf])
    cwT = ctx.alloc("l1_cwT", [128, 8, 128], BF16)
    for c in range(8):
        pe_transpose(ctx, cwT[:, c, :],
                     cw_bf[:, c * 128:(c + 1) * 128], 128, 128)
    dwT = ctx.alloc("l1_dwT", [128, 8, 1024], BF16)
    dwp = ctx.alloc("l1_dwp", [128, 1024], BF16)
    for r in range(8):
        load_convert(ctx, dwp[:],
                     ctx.hbm["L1_attn_D_w"][r * 128:(r + 1) * 128, :],
                     128, 1024, )
        for c in range(8):
            pe_transpose(ctx, dwT[:, c, r * 128:(r + 1) * 128],
                         dwp[:, c * 128:(c + 1) * 128], 128, 128)

    emit_l1_ffn_pre(ctx)

    x = ctx.sb["x_tm"]
    xnr = ctx.sb["scratch_f32_a"]
    ctx.sb["scratch_f32_c"] = ctx.alloc("l1_scrc", [128, D], F32)
    sc = ctx.sb["scratch_f32_c"]
    tmp = ctx.sb["scratch_f32_b"]
    for tb in range(NT):
        for half in range(2):
            esl = slice(half * 512, (half + 1) * 512)
            p.op("tensor", lambda e, tb=tb, esl=esl: e.matmul(
                ps_m2[:, :], Hfm[:, tb * 128:(tb + 1) * 128],
                cwT[:].rearrange("p c n -> p (c n)")[:, esl],
                start=True, stop=False),
                reads=[Hfm, cwT], writes=[ps_m2])
            for c in range(8):
                p.op("tensor", lambda e, tb=tb, esl=esl, c=c: e.matmul(
                    ps_m2[:, :], xn_fm[:, c, tb * 128:(tb + 1) * 128],
                    dwT[:, c, esl], start=False, stop=(c == 7)),
                    reads=[xn_fm, dwT], writes=[ps_m2])
            p.op("vector", lambda e, esl=esl: e.tensor_copy(
                out=sc[:, esl], in_=ps_m2[:, :]), reads=[ps_m2], writes=[sc])
        emit_l1_ffn_tile(ctx, tb, tmp)
        p.op("vector", lambda e: e.tensor_mul(out=tmp[:], in0=tmp[:],
                                              in1=sc[:]),
             reads=[tmp, sc], writes=[tmp])
        p.op("vector", lambda e, tb=tb: e.tensor_add(
            out=x[:, tb, :], in0=x[:, tb, :], in1=tmp[:]),
            reads=[tmp, x], writes=[x])
    ctx.free(*[n for n in list(ctx.live) if n.startswith("l1_")])


def emit_l1_ffn_pre(ctx):
    p = ctx.p
    xn_fm = ctx.sb["xn_fm"]
    ps_m1 = ctx.sb["ps_m1"]
    ps_m2 = ctx.sb["ps_m2"]
    ident = ctx.consts["ident"]

    ws = ctx.sb["wstage"]
    sg_f = ws[0:64, 0:1024]
    p.dma("sync", lambda e: e.dma_start(
        out=sg_f, in_=ctx.hbm["L1_ffn_signatures_raw"]), writes=[ws])
    q_bf = ctx.alloc("l1_q_bf", [64, 1024], BF16)
    qf2 = ctx.sb["scratch_f32_a"][0:64, :]
    p.op("vector", lambda e: e.tensor_scalar(
        out=qf2, in0=sg_f, scalar1=-0.3, scalar2=None, op0=ALU.is_lt),
        reads=[ws], writes=["scratch_f32_a"])
    p.op("vector", lambda e: e.tensor_scalar(
        out=sg_f, in0=sg_f, scalar1=0.3, scalar2=None, op0=ALU.is_gt),
        reads=[ws], writes=[ws])
    p.op("vector", lambda e: e.tensor_sub(out=q_bf[:], in0=sg_f, in1=qf2),
         reads=[ws, "scratch_f32_a"], writes=[q_bf])
    qT = ctx.alloc("l1_qT", [128, 8, 64], BF16)
    for c in range(8):
        pe_transpose(ctx, qT[:, c, :], q_bf[:, c * 128:(c + 1) * 128],
                     64, 128)
    grp8 = ctx.consts["grp8"]
    cs_bf = ctx.alloc("l1_cs_bf", [8, 1024], BF16)
    for half in range(2):
        sl = slice(half * 512, (half + 1) * 512)
        p.op("tensor", lambda e, sl=sl: e.matmul(
            ps_m1[0:8, :], grp8[:], q_bf[:, sl], start=True, stop=True),
            reads=[grp8, q_bf], writes=[ps_m1])
        p.op("scalar", lambda e, sl=sl: e.activation(
            out=cs_bf[:, sl], in_=ps_m1[0:8, :], func=AF.Sign),
            reads=[ps_m1], writes=[cs_bf])
    csT = ctx.alloc("l1_csT", [128, 8, 8], BF16)
    for c in range(8):
        pe_transpose(ctx, csT[:, c, :], cs_bf[:, c * 128:(c + 1) * 128],
                     8, 128)

    cscore = ctx.alloc("l1_cscore", [128, NT, 8], F32)
    tscore = ctx.alloc("l1_tscore", [128, NT, 64], F32)
    for tb in range(NT):
        tsl = slice(tb * 128, (tb + 1) * 128)
        for c in range(8):
            p.op("tensor", lambda e, tsl=tsl, c=c: e.matmul(
                ps_m1[:, 0:8], xn_fm[:, c, tsl], csT[:, c, :],
                start=(c == 0), stop=(c == 7)),
                reads=[xn_fm, csT], writes=[ps_m1])
        p.op("vector", lambda e, tb=tb: e.tensor_copy(
            out=cscore[:, tb, :], in_=ps_m1[:, 0:8]),
            reads=[ps_m1], writes=[cscore])
        for c in range(8):
            p.op("tensor", lambda e, tsl=tsl, c=c: e.matmul(
                ps_m2[:, 0:64], xn_fm[:, c, tsl], qT[:, c, :],
                start=(c == 0), stop=(c == 7)),
                reads=[xn_fm, qT], writes=[ps_m2])
        p.op("vector", lambda e, tb=tb: e.tensor_copy(
            out=tscore[:, tb, :], in_=ps_m2[:, 0:64]),
            reads=[ps_m2], writes=[tscore])

    trow = ctx.alloc("l1_trow", [1, 2], F32)
    p.dma("sync", lambda e: e.dma_start(
        out=trow[:, 0:1], in_=ctx.hbm["L1_ffn_temperature"][:, None]),
        writes=[trow])
    p.op("vector", lambda e: e.tensor_scalar_add(
        out=trow[:, 1:2], in0=trow[:, 0:1], scalar1=1e-6),
        reads=[trow], writes=[trow])
    tcol = ctx.alloc("l1_tcol", [128, 1], F32)
    bcast_col(ctx, trow[:, 1:2], tcol[:], [trow])
    rtcol = ctx.alloc("l1_rtcol", [128, 1], F32)
    p.op("vector", lambda e: e.reciprocal(out=rtcol[:], in_=tcol[:]),
         reads=[tcol], writes=[rtcol])
    krow = ctx.alloc("l1_krow", [1, 16], F32)
    p.dma("sync", lambda e: e.dma_start(
        out=krow[:, 0:8], in_=ctx.hbm["L1_ffn_knot_values"][None, :]),
        writes=[krow])
    p.op("vector", lambda e: e.tensor_sub(
        out=krow[:, 8:15], in0=krow[:, 1:8], in1=krow[:, 0:7]),
        reads=[krow], writes=[krow])
    kcols = ctx.alloc("l1_kcols", [128, 8], F32)
    bcast_col(ctx, krow[:, 0:1], kcols[:, 0:1], [krow])
    for j in range(7):
        bcast_col(ctx, krow[:, 8 + j:9 + j], kcols[:, 1 + j:2 + j], [krow])

    def calibrate(score_t, ncol):
        w = NT * ncol
        sig = ctx.sb["scratch_f32_a"][:, 0:w]
        val = ctx.alloc(f"{score_t.name}_val", [128, w], F32)
        u = ctx.sb["scratch_f32_b"][:, 0:w]
        flat = score_t[:].rearrange("p a b -> p (a b)")
        p.op("scalar", lambda e: e.activation(
            out=sig, in_=flat, func=AF.Sigmoid, scale=rtcol[:]),
            reads=[score_t, rtcol], writes=["scratch_f32_a"])
        p.op("vector", lambda e: e.tensor_scalar(
            out=val[:], in0=sig, scalar1=0.0, scalar2=kcols[:, 0:1],
            op0=ALU.mult, op1=ALU.add),
            reads=["scratch_f32_a", kcols], writes=[val])
        for j in range(7):
            p.op("vector", lambda e, j=j: e.tensor_scalar(
                out=u, in0=sig, scalar1=7.0, scalar2=float(-j),
                op0=ALU.mult, op1=ALU.add),
                reads=["scratch_f32_a"], writes=["scratch_f32_b"])
            p.op("vector", lambda e: e.tensor_scalar(
                out=u, in0=u, scalar1=0.0, scalar2=1.0,
                op0=ALU.max, op1=ALU.min),
                reads=["scratch_f32_b"], writes=["scratch_f32_b"])
            p.op("vector", lambda e, j=j: e.scalar_tensor_tensor(
                out=val[:], in0=u, scalar=kcols[:, 1 + j:2 + j],
                in1=val[:], op0=ALU.mult, op1=ALU.add),
                reads=["scratch_f32_b", kcols, val], writes=[val])
        return val

    cval = calibrate(cscore, 8)
    tval = calibrate(tscore, 64)

    iota8 = ctx.consts32["iota8m9"]
    cidx = ctx.alloc("l1_cidx", [128, NT], F32)
    m8 = ctx.alloc("l1_m8", [128, NT], F32)
    eq8 = ctx.alloc("l1_eq8", [128, NT * 8], F32)
    vc = lambda t, w: t[:].rearrange("p (a b) -> p a b", b=w)
    p.op("vector", lambda e: e.tensor_reduce(
        out=m8[:], in_=vc(cval, 8), axis=AX.X, op=ALU.max),
        reads=[cval], writes=[m8])
    p.op("vector", lambda e: e.tensor_tensor(
        out=vc(eq8, 8), in0=vc(cval, 8),
        in1=m8[:, :, None].broadcast_to([128, NT, 8]), op=ALU.is_equal),
        reads=[cval, m8], writes=[eq8])
    p.op("vector", lambda e: e.tensor_mul(
        out=eq8[:], in0=eq8[:], in1=iota8[:, 0:NT * 8]),
        reads=[eq8, "c32_iota8m9"], writes=[eq8])
    p.op("vector", lambda e: e.tensor_scalar_add(
        out=eq8[:], in0=eq8[:], scalar1=9.0), reads=[eq8], writes=[eq8])
    p.op("vector", lambda e: e.tensor_reduce(
        out=cidx[:], in_=vc(eq8, 8), axis=AX.X, op=ALU.min),
        reads=[eq8], writes=[cidx])
    sel = ctx.alloc("l1_sel", [128, NT * 8], F32)
    ohc = ctx.alloc("l1_ohc", [128, NT], F32)
    p.op("vector", lambda e: e.memset(sel[:], 0.0), writes=[sel])
    for c in range(8):
        p.op("vector", lambda e, c=c: e.tensor_scalar(
            out=ohc[:], in0=cidx[:], scalar1=float(c), scalar2=None,
            op0=ALU.is_equal), reads=[cidx], writes=[ohc])
        p.op("vector", lambda e, c=c: e.tensor_mul(
            out=vc(eq8, 8), in0=vc(tval, 64)[:, :, c * 8:(c + 1) * 8],
            in1=ohc[:, :, None].broadcast_to([128, NT, 8])),
            reads=[tval, ohc], writes=[eq8])
        p.op("vector", lambda e: e.tensor_add(
            out=sel[:], in0=sel[:], in1=eq8[:]), reads=[eq8, sel],
            writes=[sel])
    lidx = ctx.alloc("l1_lidx", [128, NT], F32)
    p.op("vector", lambda e: e.tensor_reduce(
        out=m8[:], in_=vc(sel, 8), axis=AX.X, op=ALU.max),
        reads=[sel], writes=[m8])
    p.op("vector", lambda e: e.tensor_tensor(
        out=vc(eq8, 8), in0=vc(sel, 8),
        in1=m8[:, :, None].broadcast_to([128, NT, 8]), op=ALU.is_equal),
        reads=[sel, m8], writes=[eq8])
    p.op("vector", lambda e: e.tensor_mul(
        out=eq8[:], in0=eq8[:], in1=iota8[:, 0:NT * 8]),
        reads=[eq8, "c32_iota8m9"], writes=[eq8])
    p.op("vector", lambda e: e.tensor_scalar_add(
        out=eq8[:], in0=eq8[:], scalar1=9.0), reads=[eq8], writes=[eq8])
    p.op("vector", lambda e: e.tensor_reduce(
        out=lidx[:], in_=vc(eq8, 8), axis=AX.X, op=ALU.min),
        reads=[eq8], writes=[lidx])
    tidx = ctx.alloc("l1_tidx", [128, NT], F32)
    p.op("vector", lambda e: e.scalar_tensor_tensor(
        out=tidx[:], in0=cidx[:], scalar=8.0, in1=lidx[:],
        op0=ALU.mult, op1=ALU.add), reads=[cidx, lidx], writes=[tidx])

    tidx_bf = ctx.alloc("l1_tidx_bf", [128, NT], BF16)
    p.op("vector", lambda e: e.tensor_copy(out=tidx_bf[:], in_=tidx[:]),
         reads=[tidx], writes=[tidx_bf])
    tidx_row = ctx.alloc("l1_tidx_row", [1, T], BF16)
    for tb in range(NT):
        pe_transpose(ctx, tidx_row[:, tb * 128:(tb + 1) * 128],
                     tidx_bf[:, tb:tb + 1], 128, 1)
    oh_fm = ctx.alloc("l1_oh_fm", [64, T], BF16)
    iota_col = ctx.consts32["iotacol"]
    for tb in range(4):
        tsl = slice(tb * 512, (tb + 1) * 512)
        p.op("tensor", lambda e, tsl=tsl: e.matmul(
            ps_m1[0:64, :], ctx.consts["onesrow"][:, 0:64],
            tidx_row[:, tsl], start=True, stop=True),
            reads=[tidx_row, "c_onesrow"], writes=[ps_m1])
        p.op("vector", lambda e, tsl=tsl: e.tensor_scalar(
            out=oh_fm[:, tsl], in0=ps_m1[0:64, :],
            scalar1=iota_col[0:64, :], scalar2=None, op0=ALU.is_equal),
            reads=[ps_m1, "c32_iotacol"], writes=[oh_fm])

    gp = ctx.alloc("l1_gp", [64, 72], BF16)
    load_convert(ctx, gp[:, 0:32],
                 ctx.hbm["L1_ffn_spline_w1"].rearrange("n a b -> n (a b)"),
                 64, 32)
    load_convert(ctx, gp[:, 32:48], ctx.hbm["L1_ffn_spline_b1"], 64, 16)
    load_convert(ctx, gp[:, 48:64],
                 ctx.hbm["L1_ffn_spline_w2"].rearrange("n a b -> n (a b)"),
                 64, 16)
    load_convert(ctx, gp[:, 64:65], ctx.hbm["L1_ffn_spline_b2"], 64, 1)
    gath = ctx.alloc("l1_gath", [128, NT, 72], F32)
    for tb in range(NT):
        p.op("tensor", lambda e, tb=tb: e.matmul(
            ps_m1[:, 0:72], oh_fm[:, tb * 128:(tb + 1) * 128], gp[:, 0:72],
            start=True, stop=True), reads=[oh_fm, gp], writes=[ps_m1])
        p.op("vector", lambda e, tb=tb: e.tensor_copy(
            out=gath[:, tb, :], in_=ps_m1[:, 0:72]),
            reads=[ps_m1], writes=[gath])

    cw1 = load_weight_bf16(ctx, "l1_cw1", ctx.hbm["L1_ffn_compress_w1"],
                           1024, 256)
    b1 = ctx.alloc("l1_b1", [1, 256], BF16)
    load_convert(ctx, b1[:], ctx.hbm["L1_ffn_compress_b1"][None, :], 1, 256)
    h_fm = ctx.alloc("l1_h_fm", [128, 2, T], BF16)
    ones512 = ctx.consts["ones512"]
    for m in range(2):
        for tb in range(4):
            tsl = slice(tb * 512, (tb + 1) * 512)
            for c in range(8):
                p.op("tensor", lambda e, m=m, tsl=tsl, c=c: e.matmul(
                    ps_m2[:, :], cw1[:, c, m * 128:(m + 1) * 128],
                    xn_fm[:, c, tsl], start=(c == 0), stop=False),
                    reads=[cw1, xn_fm], writes=[ps_m2])
            p.op("tensor", lambda e, m=m, tsl=tsl: e.matmul(
                ps_m2[:, :], b1[:, m * 128:(m + 1) * 128], ones512[:],
                start=False, stop=True), reads=[b1, "c_ones512"],
                writes=[ps_m2])
            p.op("scalar", lambda e, m=m, tsl=tsl: e.activation(
                out=h_fm[:, m, tsl], in_=ps_m2[:, :], func=AF.Gelu),
                reads=[ps_m2], writes=[h_fm])
    cw2 = ctx.alloc("l1_cw2", [128, 2, 2], BF16)
    ws_c2 = ctx.sb["wstage"][:, 0:4].rearrange("p (a b) -> p a b", b=2)
    p.dma("sync", lambda e: e.dma_start(
        out=ws_c2, in_=ctx.hbm["L1_ffn_compress_w2"].rearrange(
            "(a p) b -> p a b", p=128)), writes=["wstage"])
    p.op("gpsimd", lambda e: e.tensor_copy(out=cw2[:], in_=ws_c2),
         reads=["wstage"], writes=[cw2])
    b2 = ctx.alloc("l1_b2", [1, 2], BF16)
    load_convert(ctx, b2[:], ctx.hbm["L1_ffn_compress_b2"][None, :], 1, 2)
    comp = ctx.alloc("l1_comp", [128, NT, 2], F32)
    for tb in range(NT):
        tsl = slice(tb * 128, (tb + 1) * 128)
        for m in range(2):
            p.op("tensor", lambda e, tsl=tsl, m=m: e.matmul(
                ps_m1[:, 0:2], h_fm[:, m, tsl], cw2[:, m, :],
                start=(m == 0), stop=False),
                reads=[h_fm, cw2], writes=[ps_m1])
        p.op("tensor", lambda e, tsl=tsl: e.matmul(
            ps_m1[:, 0:2], ctx.consts["onesrow"][:], b2[:],
            start=False, stop=True), reads=[b2, "c_onesrow"],
            writes=[ps_m1])
        p.op("scalar", lambda e, tb=tb: e.activation(
            out=comp[:, tb, :], in_=ps_m1[:, 0:2], func=AF.Tanh),
            reads=[ps_m1], writes=[comp])

    dirs = ctx.alloc("l1_dirs", [64, 1024], BF16)
    load_convert(ctx, dirs[:], ctx.hbm["L1_ffn_directions"], 64, 1024)
    grow = ctx.alloc("l1_grow", [128, 1024], BF16)
    brow = ctx.alloc("l1_brow", [128, 1024], BF16)
    gb_bf = ctx.alloc("l1_gb_bf", [1, 2048], BF16)
    load_convert(ctx, gb_bf[:, 0:1024],
                 ctx.hbm["L1_ffn_norm_gamma"][None, :], 1, 1024)
    load_convert(ctx, gb_bf[:, 1024:2048],
                 ctx.hbm["L1_ffn_norm_beta"][None, :], 1, 1024)
    for half in range(2):
        sl = slice(half * 512, (half + 1) * 512)
        p.op("tensor", lambda e, sl=sl: e.matmul(
            ps_m1[:, :], ctx.consts["onesrow"][:], gb_bf[:, sl],
            start=True, stop=True), reads=[gb_bf, "c_onesrow"],
            writes=[ps_m1])
        p.op("vector", lambda e, sl=sl: e.tensor_copy(
            out=grow[:, sl], in_=ps_m1[:, :]), reads=[ps_m1], writes=[grow])
        sl2 = slice(1024 + half * 512, 1024 + (half + 1) * 512)
        p.op("tensor", lambda e, sl2=sl2: e.matmul(
            ps_m1[:, :], ctx.consts["onesrow"][:], gb_bf[:, sl2],
            start=True, stop=True), reads=[gb_bf, "c_onesrow"],
            writes=[ps_m1])
        p.op("vector", lambda e, sl=sl: e.tensor_copy(
            out=brow[:, sl], in_=ps_m1[:, :]), reads=[ps_m1], writes=[brow])
    ctx.alloc("l1_spl", [128, 32], F32)
    ctx.alloc("l1_sq", [128, 1024], F32)
    ctx.alloc("l1_xnr2", [128, 1024], F32)
    osrow = ctx.alloc("l1_osrow", [1, 1], F32)
    p.dma("sync", lambda e: e.dma_start(
        out=osrow[:], in_=ctx.hbm["L1_ffn_output_scale"][:, None]),
        writes=[osrow])
    oscol = ctx.alloc("l1_oscol", [128, 1], F32)
    bcast_col(ctx, osrow[:], oscol[:], [osrow])


def emit_l1_ffn_tile(ctx, tb, gout):
    p = ctx.p
    gath = ctx.sb["l1_gath"]
    comp = ctx.sb["l1_comp"]
    oh_fm = ctx.sb["l1_oh_fm"]
    dirs = ctx.sb["l1_dirs"]
    grow = ctx.sb["l1_grow"]
    brow = ctx.sb["l1_brow"]
    oscol = ctx.sb["l1_oscol"]
    col = ctx.sb["scratch_col"]
    spl = ctx.sb["l1_spl"]
    ps_m1 = ctx.sb["ps_m1"]
    xnr = ctx.sb["l1_xnr2"]
    p.op("vector", lambda e: e.tensor_scalar(
        out=spl[:, 0:16], in0=gath[:, tb, 0:16],
        scalar1=comp[:, tb, 0:1], scalar2=None, op0=ALU.mult),
        reads=[gath, comp], writes=[spl])
    p.op("vector", lambda e: e.scalar_tensor_tensor(
        out=spl[:, 0:16], in0=gath[:, tb, 16:32],
        scalar=comp[:, tb, 1:2], in1=spl[:, 0:16],
        op0=ALU.mult, op1=ALU.add), reads=[gath, comp, spl], writes=[spl])
    p.op("vector", lambda e: e.tensor_add(
        out=spl[:, 0:16], in0=spl[:, 0:16], in1=gath[:, tb, 32:48]),
        reads=[gath, spl], writes=[spl])
    p.op("vector", lambda e: e.tensor_scalar_max(
        out=spl[:, 0:16], in0=spl[:, 0:16], scalar1=0.0),
        reads=[spl], writes=[spl])
    p.op("vector", lambda e: e.tensor_mul(
        out=spl[:, 16:32], in0=spl[:, 0:16], in1=gath[:, tb, 48:64]),
        reads=[gath, spl], writes=[spl])
    p.op("vector", lambda e: e.reduce_sum(
        out=col[:, 0:1], in_=spl[:, 16:32], axis=AX.X),
        reads=[spl], writes=[col])
    p.op("vector", lambda e: e.tensor_add(
        out=col[:, 0:1], in0=col[:, 0:1], in1=gath[:, tb, 64:65]),
        reads=[gath, col], writes=[col])
    p.op("vector", lambda e: e.tensor_mul(
        out=col[:, 0:1], in0=col[:, 0:1], in1=oscol[:]),
        reads=[oscol, col], writes=[col])
    xn_rec(ctx, tb, xnr[:])
    for half in range(2):
        hsl = slice(half * 512, (half + 1) * 512)
        p.op("tensor", lambda e, hsl=hsl: e.matmul(
            ps_m1[:, :], oh_fm[:, tb * 128:(tb + 1) * 128], dirs[:, hsl],
            start=True, stop=True), reads=[oh_fm, dirs], writes=[ps_m1])
        p.op("vector", lambda e, hsl=hsl: e.tensor_scalar(
            out=gout[:, hsl], in0=ps_m1[:, :], scalar1=col[:, 0:1],
            scalar2=None, op0=ALU.mult), reads=[ps_m1, col], writes=[gout])
    p.op("vector", lambda e: e.tensor_add(
        out=gout[:], in0=gout[:], in1=xnr[:]), reads=[xnr, gout],
        writes=[gout])
    p.op("vector", lambda e: e.reduce_sum(
        out=col[:, 1:2], in_=gout[:], axis=AX.X), reads=[gout], writes=[col])
    p.op("vector", lambda e: e.tensor_scalar_mul(
        out=col[:, 1:2], in0=col[:, 1:2], scalar1=1.0 / D),
        reads=[col], writes=[col])
    p.op("vector", lambda e: e.tensor_scalar(
        out=gout[:], in0=gout[:], scalar1=col[:, 1:2], scalar2=None,
        op0=ALU.subtract), reads=[col, gout], writes=[gout])
    sq = ctx.sb["l1_sq"]
    p.op("scalar", lambda e: e.activation(
        out=sq[:], in_=gout[:], func=AF.Square), reads=[gout], writes=[sq])
    p.op("vector", lambda e: e.reduce_sum(
        out=col[:, 2:3], in_=sq[:], axis=AX.X), reads=[sq], writes=[col])
    epsc = ctx.fconst(1e-5)
    p.op("scalar", lambda e, epsc=epsc: e.activation(
        out=col[:, 3:4], in_=col[:, 2:3], func=AF.Sqrt, bias=epsc,
        scale=1.0 / D), reads=[col, "fconsts"], writes=[col])
    p.op("vector", lambda e: e.reciprocal(out=col[:, 3:4], in_=col[:, 3:4]),
         reads=[col], writes=[col])
    p.op("vector", lambda e: e.tensor_scalar(
        out=gout[:], in0=gout[:], scalar1=col[:, 3:4], scalar2=None,
        op0=ALU.mult), reads=[col, gout], writes=[gout])
    p.op("vector", lambda e: e.tensor_mul(
        out=gout[:], in0=gout[:], in1=grow[:]), reads=[grow, gout],
        writes=[gout])
    p.op("vector", lambda e: e.tensor_add(
        out=gout[:], in0=gout[:], in1=brow[:]), reads=[brow, gout],
        writes=[gout])
    p.op("scalar", lambda e: e.activation(
        out=gout[:], in_=gout[:], func=AF.Silu), reads=[gout],
        writes=[gout])



def row_bcast(ctx, dst, row_bf_ap, width, reads):
    p = ctx.p
    ps = ctx.sb["ps_m1"]
    for c0 in range(0, width, 512):
        w = min(512, width - c0)
        p.op("tensor", lambda e, c0=c0, w=w: e.matmul(
            ps[:, 0:w], ctx.consts["onesrow"][:], row_bf_ap[:, c0:c0 + w],
            start=True, stop=True), reads=list(reads) + ["c_onesrow"],
            writes=[ps])
        p.op("vector", lambda e, c0=c0, w=w: e.tensor_copy(
            out=dst[:, c0:c0 + w], in_=ps[:, 0:w]), reads=[ps], writes=[dst])


def emit_layer2a(ctx):
    import os
    NPAIR = int(os.environ.get("P2A_PAIRS", "8"))
    PHASE = int(os.environ.get("P2A_PHASE", "9"))
    p = ctx.p
    emit_ln(ctx)
    xn_fm = ctx.sb["xn_fm"]
    ident = ctx.consts["ident"]
    ps_m1 = ctx.sb["ps_m1"]
    ps_m2 = ctx.sb["ps_m2"]
    ws = ctx.sb["wstage"]
    s_state = ctx.state_out("s_state", [T, D])

    convw = {}
    for nm in ("conv_q", "conv_k", "conv_v"):
        t = ctx.alloc(f"l2_{nm}", [128, 8, 4], F32)
        p.dma("sync", lambda e, t=t, nm=nm: e.dma_start(
            out=t[:], in_=ctx.hbm[f"L2_attn_{nm}"].rearrange(
                "(c p) j -> p c j", p=128)), writes=[t])
        convw[nm] = t
    w0 = ctx.alloc("l2_w0", [128, 8, 64], BF16)
    sv = ws[:, 0:512].rearrange("p (c j) -> p c j", j=64)
    p.dma("sync", lambda e: e.dma_start(
        out=sv, in_=ctx.hbm["L2_attn_W_init"].rearrange(
            "o h a b -> (o h a) b").rearrange("(c p) b -> p c b", p=128)),
        writes=[ws])
    p.op("gpsimd", lambda e: e.tensor_copy(out=w0[:], in_=sv),
         reads=[ws], writes=[w0])
    gbrow = ctx.alloc("l2_gbrow", [1, 128], BF16)
    load_convert(ctx, gbrow[:, 0:64], ctx.hbm["L2_attn_ln_gamma"][None, :],
                 1, 64)
    load_convert(ctx, gbrow[:, 64:128], ctx.hbm["L2_attn_ln_beta"][None, :],
                 1, 64)
    grow64 = ctx.alloc("l2_grow64", [128, 128], F32)
    row_bcast(ctx, grow64[:, 0:64], gbrow[:, 0:64], 64, [gbrow])
    row_bcast(ctx, grow64[:, 64:128], gbrow[:, 64:128], 64, [gbrow])

    O_all = ctx.alloc("l2_O_all", [128, NT, 1024], BF16)
    tril = ctx.consts["tril_incl"]

    wTp = ctx.alloc("l2_wTp", [128, 8, 128], BF16)
    ctx.alloc("l2_vstage", [128, T], BF16)
    ctx.alloc("l2_nrowf", [1, 512], F32)
    ctx.alloc("l2_rkcol_bf", [128, 1], BF16)
    wpc = ctx.alloc("l2_wpc", [128, 1024], BF16)
    prj = ctx.alloc("l2_prj", [128, T], F32)
    cnv = ctx.alloc("l2_cnv", [128, T], F32)
    Qfm = ctx.alloc("l2_Qfm", [128, T], BF16)
    Kfm = ctx.alloc("l2_Kfm", [128, T], BF16)
    Vtm = ctx.alloc("l2_Vtm", [128, NT, 128], BF16)
    Ktm = ctx.alloc("l2_Ktm", [128, NT, 128], BF16)
    Rk = ctx.alloc("l2_Rk", [128, 2 * NT], F32)
    nrow = ctx.alloc("l2_nrow", [1, 2 * T], BF16)
    state_sb = ctx.alloc("l2_state_sb", [128, 64], BF16)
    state_f32 = ctx.alloc("l2_state_f32", [128, 64], F32)
    Pbf = [ctx.alloc(f"l2_Pbf{h}", [128, 128], BF16) for h in range(2)]
    ps_sc = [ctx.sb["ps_ava"], ctx.sb["ps_avb"]]
    ps_o = ctx.sb["ps_sb"]
    ps_st = ctx.sb["ps_sa"]
    stg = ctx.sb["stage_bf"]

    def conv_silu(nm, dst_bf, hp, square_norms):
        cw = convw[nm]
        w3 = cw[:, hp, 3:4]
        p.op("vector", lambda e, w3=w3: e.scalar_tensor_tensor(
            out=cnv[:], in0=prj[:], scalar=w3, in1=prj[:],
            op0=ALU.mult, op1=ALU.add), reads=[prj, cw], writes=[cnv])
        for j in (2, 1, 0):
            sft = 3 - j
            wj = cw[:, hp, j:j + 1]
            p.op("vector", lambda e, wj=wj, sft=sft: e.scalar_tensor_tensor(
                out=cnv[:, sft:T], in0=prj[:, 0:T - sft], scalar=wj,
                in1=cnv[:, sft:T], op0=ALU.mult, op1=ALU.add),
                reads=[prj, cw, cnv], writes=[cnv])
        p.op("scalar", lambda e: e.activation(
            out=dst_bf[:], in_=cnv[:], func=AF.Silu),
            reads=[cnv], writes=[dst_bf])
        if square_norms is not None:
            base = square_norms
            for half in range(2):
                hsl = slice(half * 1024, (half + 1) * 1024)
                p.op("scalar", lambda e, hsl=hsl: e.activation(
                    out=stg[:], in_=dst_bf[:, hsl], func=AF.Square),
                    reads=[dst_bf], writes=[stg])
                for hh, ones in enumerate((ctx.consts["onesA"],
                                           ctx.consts["onesB"])):
                    for qq in range(2):
                        qsl = slice(qq * 512, (qq + 1) * 512)
                        p.op("tensor", lambda e, ones=ones, qsl=qsl: e.matmul(
                            ps_m1[0:1, :], ones[:], stg[:, qsl],
                            start=True, stop=True),
                            reads=[stg, ones], writes=[ps_m1])
                        dsl = slice(base + hh * T + half * 1024 + qq * 512,
                                    base + hh * T + half * 1024 + (qq + 1) * 512)
                        p.op("vector", lambda e, dsl=dsl: e.tensor_copy(
                            out=nrow[:, dsl], in_=ps_m1[0:1, :]),
                            reads=[ps_m1], writes=[nrow])

    import os as _os
    BAR = _os.environ.get("P2A_BARRIER", "0") == "1"
    for hp in range(NPAIR):
        if BAR and hp > 0:
            p.full_barrier()
        for nm, wname, dst in (("conv_q", "L2_attn_Wq", Qfm),
                               ("conv_k", "L2_attn_Wk", Kfm),
                               ("conv_v", "L2_attn_Wv", Vtm)):
            hb = ctx.hbm[wname][hp * 128:(hp + 1) * 128, :]
            sv2 = ws[:, 0:1024]
            p.dma("sync", lambda e, hb=hb, sv2=sv2: e.dma_start(
                out=sv2, in_=hb), writes=[ws])
            p.op("gpsimd", lambda e, sv2=sv2: e.tensor_copy(
                out=wpc[:], in_=sv2), reads=[ws], writes=[wpc])
            for c in range(8):
                pe_transpose(ctx, wTp[:, c, :],
                             wpc[:, c * 128:(c + 1) * 128], 128, 128)
            for tb in range(4):
                tsl = slice(tb * 512, (tb + 1) * 512)
                for c in range(8):
                    p.op("tensor", lambda e, tsl=tsl, c=c: e.matmul(
                        ps_m2[:, :], wTp[:, c, :], xn_fm[:, c, tsl],
                        start=(c == 0), stop=(c == 7)),
                        reads=[wTp, xn_fm], writes=[ps_m2])
                p.op("vector", lambda e, tsl=tsl: e.tensor_copy(
                    out=prj[:, tsl], in_=ps_m2[:, :]),
                    reads=[ps_m2], writes=[prj])
            if nm == "conv_q":
                conv_silu(nm, Qfm, hp, None)
            elif nm == "conv_k":
                conv_silu(nm, Kfm, hp, 0)
            else:
                conv_silu(nm, ctx.sb["l2_vstage"], hp, None)
        for hh in range(2):
            nrf = ctx.sb["l2_nrowf"]
            ec = ctx.fconst(1e-12)
            for seg in range(4):
                sl = slice(hh * T + seg * 512, hh * T + (seg + 1) * 512)
                p.op("scalar", lambda e, sl=sl, nrf=nrf, ec=ec: e.activation(
                    out=nrf[:], in_=nrow[:, sl], func=AF.Sqrt, bias=ec[0:1, :]),
                    reads=[nrow, "fconsts"], writes=[nrf])
                p.op("vector", lambda e, nrf=nrf: e.reciprocal(
                    out=nrf[:], in_=nrf[:]), reads=[nrf], writes=[nrf])
                p.op("vector", lambda e, sl=sl, nrf=nrf: e.tensor_copy(
                    out=nrow[:, sl], in_=nrf[:]), reads=[nrf], writes=[nrow])
            rkc = ctx.sb["l2_rkcol_bf"]
            for kt in range(NT):
                pe_transpose(ctx, rkc[:],
                             nrow[:, hh * T + kt * 128:hh * T + (kt + 1) * 128],
                             1, 128)
                p.op("vector", lambda e, kt=kt, hh=hh, rkc=rkc: e.tensor_copy(
                    out=Rk[:, hh * NT + kt:hh * NT + kt + 1], in_=rkc[:]),
                    reads=[rkc], writes=[Rk])
        vst = ctx.sb["l2_vstage"]
        for kt in range(NT):
            pe_transpose(ctx, Vtm[:, kt, :], vst[:, kt * 128:(kt + 1) * 128],
                         128, 128)
            pe_transpose(ctx, Ktm[:, kt, :], Kfm[:, kt * 128:(kt + 1) * 128],
                         128, 128)
            for hh in range(2):
                p.op("vector", lambda e, kt=kt, hh=hh: e.tensor_scalar(
                    out=Vtm[:, kt, hh * 64:(hh + 1) * 64],
                    in0=Vtm[:, kt, hh * 64:(hh + 1) * 64],
                    scalar1=Rk[:, hh * NT + kt:hh * NT + kt + 1],
                    scalar2=None, op0=ALU.mult),
                    reads=[Rk, Vtm], writes=[Vtm])
        SCAN = int(os.environ.get("P2A_SCAN", "9"))
        for c in range(NT if PHASE >= 2 else 0):
            csl = slice(c * 128, (c + 1) * 128)
            if c == 0 or SCAN < 3:
                p.op("vector", lambda e, hp=hp: e.tensor_copy(
                    out=state_sb[:], in_=w0[:, hp, :]),
                    reads=[w0], writes=[state_sb])
            else:
                p.op("vector", lambda e, hp=hp: e.scalar_tensor_tensor(
                    out=state_sb[:], in0=state_f32[:], scalar=1.0,
                    in1=w0[:, hp, :], op0=ALU.mult, op1=ALU.add),
                    reads=[state_f32, w0], writes=[state_sb])
            for hh, tp in (((0, (0, 0)), (1, (64, 0))) if SCAN >= 1 else ()):
                ob = hh * 512
                p.op("tensor", lambda e, hh=hh, tp=tp, csl=csl, ob=ob:
                     e.matmul(
                         ps_o[:, ob:ob + 64],
                         Qfm[hh * 64:(hh + 1) * 64, csl],
                         state_sb[hh * 64:(hh + 1) * 64, :],
                         start=True, stop=(SCAN < 2), tile_position=tp),
                     reads=[Qfm, state_sb], writes=[ps_o])
            for hh, tp in (((0, (0, 0)), (1, (64, 0))) if SCAN >= 2 else ()):
                p.op("tensor", lambda e, hh=hh, tp=tp, csl=csl: e.matmul(
                    ps_sc[hh][:, 0:128], Kfm[hh * 64:(hh + 1) * 64, csl],
                    Qfm[hh * 64:(hh + 1) * 64, csl],
                    start=True, stop=True, tile_position=tp),
                    reads=[Kfm, Qfm], writes=[ps_sc[hh]])
                p.op("vector", lambda e, hh=hh: e.tensor_tensor(
                    out=Pbf[hh][:], in0=ps_sc[hh][:, 0:128], in1=tril[:],
                    op=ALU.mult), reads=[ps_sc[hh], tril, "c_tril_incl"],
                    writes=[Pbf[hh]])
                p.op("tensor", lambda e, hh=hh, c=c: e.matmul(
                    ps_o[:, hh * 512:hh * 512 + 64], Pbf[hh][:],
                    Vtm[:, c, hh * 64:(hh + 1) * 64],
                    start=False, stop=True),
                    reads=[Pbf[hh], Vtm], writes=[ps_o])
            if SCAN >= 1:
                for hh in range(2):
                    p.op("vector", lambda e, c=c, hh=hh, hp=hp: e.tensor_copy(
                        out=O_all[:, c,
                                  hp * 128 + hh * 64:hp * 128 + hh * 64 + 64],
                        in_=ps_o[:, hh * 512:hh * 512 + 64]),
                        reads=[ps_o], writes=[O_all])
            for hh, tp in (((0, (0, 0)), (1, (0, 64))) if SCAN >= 3 else ()):
                p.op("tensor", lambda e, hh=hh, tp=tp, c=c: e.matmul(
                    ps_st[hh * 64:(hh + 1) * 64, hh * 512:hh * 512 + 64],
                    Ktm[:, c, hh * 64:(hh + 1) * 64],
                    Vtm[:, c, hh * 64:(hh + 1) * 64],
                    start=True, stop=True, tile_position=tp),
                    reads=[Ktm, Vtm], writes=[ps_st])
            if SCAN >= 3:
                if c == 0:
                    for hh in range(2):
                        p.op("vector", lambda e, hh=hh: e.tensor_copy(
                            out=state_f32[hh * 64:(hh + 1) * 64, :],
                            in_=ps_st[hh * 64:(hh + 1) * 64,
                                      hh * 512:hh * 512 + 64]),
                            reads=[ps_st], writes=[state_f32])
                else:
                    for hh in range(2):
                        p.op("vector", lambda e, hh=hh: e.scalar_tensor_tensor(
                            out=state_f32[hh * 64:(hh + 1) * 64, :],
                            in0=ps_st[hh * 64:(hh + 1) * 64,
                                      hh * 512:hh * 512 + 64],
                            scalar=1.0,
                            in1=state_f32[hh * 64:(hh + 1) * 64, :],
                            op0=ALU.mult, op1=ALU.add),
                            reads=[ps_st, state_f32], writes=[state_f32])
        if hp == 0:
            ctx.tap_full("l2_O_p0", O_all)

    ctx.tap_full("l2_O_raw", O_all)
    ctx.tap_full("l2_Qfm", Qfm)
    ctx.tap_full("l2_Kfm", Kfm)
    ctx.tap_full("l2_Vtm", Vtm)
    ctx.tap_full("l2_O_mid", O_all)
    ctx.free("l2_wTp", "l2_wpc", "l2_prj", "l2_cnv", "l2_Qfm", "l2_Kfm",
             "l2_Vtm", "l2_Ktm", "l2_Rk", "l2_nrow", "l2_state_sb",
             "l2_Pbf0", "l2_Pbf1", "l2_vstage", "l2_nrowf", "l2_rkcol_bf",
             "l2_state_f32")

    import os as _os3
    if _os3.environ.get("P2A_BAR3", "0") == "1":
        p.full_barrier()
    if PHASE < 3:
        ctx.free(*[n for n in list(ctx.live) if n.startswith("l2_")])
        zz = ctx.sb["scratch_f32_a"]
        p.op("vector", lambda e: e.memset(zz[:], 0.0), writes=[zz])
        for tb in range(NT):
            p.dma("sync", lambda e, tb=tb: e.dma_start(
                out=s_state.rearrange("(i p) d -> p i d", p=128)[:, tb, :],
                in_=zz[:]), reads=[zz], writes=["s_state_d"], is_out=True)
        return
    col2 = ctx.alloc("l2_col2", [128, 64], F32)
    cen = ctx.sb["scratch_f32_a"]
    sq = ctx.sb["scratch_f32_b"]
    epsc = ctx.fconst(1e-5)
    for kt in range(NT):
        ov = O_all[:, kt, :].rearrange("p (h d) -> p h d", d=64)
        cv = cen[:].rearrange("p (h d) -> p h d", d=64)
        p.op("vector", lambda e, ov=ov: e.reduce_sum(
            out=col2[:, 0:16], in_=ov, axis=AX.X),
            reads=[O_all], writes=[col2])
        p.op("vector", lambda e: e.tensor_scalar_mul(
            out=col2[:, 0:16], in0=col2[:, 0:16], scalar1=1.0 / 64),
            reads=[col2], writes=[col2])
        p.op("vector", lambda e, ov=ov, cv=cv: e.tensor_tensor(
            out=cv, in0=ov,
            in1=col2[:, 0:16, None].broadcast_to([128, 16, 64]),
            op=ALU.subtract), reads=[O_all, col2], writes=[cen])
        p.op("scalar", lambda e: e.activation(
            out=sq[:], in_=cen[:], func=AF.Square), reads=[cen], writes=[sq])
        p.op("vector", lambda e: e.reduce_sum(
            out=col2[:, 16:32], in_=sq[:].rearrange(
                "p (h d) -> p h d", d=64), axis=AX.X),
            reads=[sq], writes=[col2])
        p.op("scalar", lambda e, epsc=epsc: e.activation(
            out=col2[:, 32:48], in_=col2[:, 16:32], func=AF.Sqrt,
            bias=epsc, scale=1.0 / 64),
            reads=[col2, "fconsts"], writes=[col2])
        p.op("vector", lambda e: e.reciprocal(
            out=col2[:, 32:48], in_=col2[:, 32:48]),
            reads=[col2], writes=[col2])
        p.op("vector", lambda e, cv=cv: e.tensor_tensor(
            out=cv, in0=cv,
            in1=col2[:, 32:48, None].broadcast_to([128, 16, 64]),
            op=ALU.mult), reads=[cen, col2], writes=[cen])
        p.op("vector", lambda e, cv=cv: e.tensor_tensor(
            out=cv, in0=cv,
            in1=grow64[:, 0:64][:, None, :].broadcast_to([128, 16, 64]),
            op=ALU.mult), reads=[cen, grow64], writes=[cen])
        p.op("vector", lambda e, ov=ov, cv=cv: e.tensor_tensor(
            out=ov, in0=cv,
            in1=grow64[:, 64:128][:, None, :].broadcast_to([128, 16, 64]),
            op=ALU.add), reads=[cen, grow64], writes=[O_all])

    ctx.tap_full("l2_O_ln", O_all)
    import os as _os2
    if _os2.environ.get("P2A_BAR2", "0") == "1":
        p.full_barrier()
    if PHASE < 4:
        ctx.free(*[n for n in list(ctx.live) if n.startswith("l2_")])
        zz = ctx.sb["scratch_f32_a"]
        p.op("vector", lambda e: e.memset(zz[:], 0.0), writes=[zz])
        for tb in range(NT):
            p.dma("sync", lambda e, tb=tb: e.dma_start(
                out=s_state.rearrange("(i p) d -> p i d", p=128)[:, tb, :],
                in_=zz[:]), reads=[zz], writes=["s_state_d"], is_out=True)
        return
    wgT = ctx.alloc("l2_wgT", [128, 8, 1024], BF16)
    woT = ctx.alloc("l2_woT", [128, 8, 1024], BF16)
    wpc2 = ctx.alloc("l2_wpc2", [128, 1024], BF16)
    import os as _os5
    OWNSTAGE = _os5.environ.get("P2A_OWNSTAGE", "0") == "1"
    if OWNSTAGE:
        gstage = ctx.alloc("l2_gstage", [128, 1024], F32)
    for wname, wT in (("L2_attn_Wg", wgT), ("L2_attn_Wo", woT)):
        for r in range(8):
            if OWNSTAGE:
                hbr = ctx.hbm[wname][r * 128:(r + 1) * 128, :]
                p.dma("sync", lambda e, hbr=hbr: e.dma_start(
                    out=gstage[:], in_=hbr), writes=[gstage])
                p.op("gpsimd", lambda e: e.tensor_copy(
                    out=wpc2[:], in_=gstage[:]),
                    reads=[gstage], writes=[wpc2])
            else:
                load_convert(ctx, wpc2[:],
                             ctx.hbm[wname][r * 128:(r + 1) * 128, :],
                             128, 1024)
            for c in range(8):
                pe_transpose(ctx, wT[:, c, r * 128:(r + 1) * 128],
                             wpc2[:, c * 128:(c + 1) * 128], 128, 128)
    og_bf = ctx.alloc("l2_og_bf", [128, 1024], BF16)
    og_fm = ctx.alloc("l2_og_fm", [128, 8, 128], BF16)
    import os as _os4
    if _os4.environ.get("P2A_BAR4", "0") == "1":
        p.full_barrier()
    for tb in range(NT):
        for half in range(2):
            esl = slice(half * 512, (half + 1) * 512)
            for c in range(8):
                p.op("tensor", lambda e, tb=tb, c=c, esl=esl: e.matmul(
                    ps_m2[:, :], xn_fm[:, c, tb * 128:(tb + 1) * 128],
                    wgT[:, c, esl], start=(c == 0), stop=(c == 7)),
                    reads=[xn_fm, wgT], writes=[ps_m2])
            p.op("vector", lambda e, tb=tb, esl=esl: e.tensor_tensor(
                out=og_bf[:, esl], in0=ps_m2[:, :],
                in1=O_all[:, tb, esl], op=ALU.mult),
                reads=[ps_m2, O_all], writes=[og_bf])
        for c in range(8):
            pe_transpose(ctx, og_fm[:, c, :],
                         og_bf[:, c * 128:(c + 1) * 128], 128, 128)
        for half in range(2):
            esl = slice(half * 512, (half + 1) * 512)
            for c in range(8):
                p.op("tensor", lambda e, c=c, esl=esl: e.matmul(
                    ps_m2[:, :], og_fm[:, c, :], woT[:, c, esl],
                    start=(c == 0), stop=(c == 7)),
                    reads=[og_fm, woT], writes=[ps_m2])
            p.op("vector", lambda e, esl=esl: e.tensor_copy(
                out=cen[:, esl], in_=ps_m2[:, :]), reads=[ps_m2],
                writes=[cen])
        p.dma("sync", lambda e, tb=tb: e.dma_start(
            out=s_state.rearrange("(i p) d -> p i d", p=128)[:, tb, :],
            in_=cen[:]), reads=[cen], writes=["s_state_d"], is_out=True)
    ctx.free(*[n for n in list(ctx.live) if n.startswith("l2_")])


def emit_layer2b(ctx):
    p = ctx.p
    emit_ln(ctx)
    xn_fm = ctx.sb["xn_fm"]
    ident = ctx.consts["ident"]
    ps_m1 = ctx.sb["ps_m1"]
    ps_m2 = ctx.sb["ps_m2"]
    ws = ctx.sb["wstage"]
    s_state = ctx.state_in("s_state", [T, D])
    nsc = ctx.alloc("l2b_nsc", [1, 64], F32)
    nsc_bf = ctx.alloc("l2b_nscbf", [1, 2], BF16)
    bias_a = ctx.alloc("l2b_bias_a", [128, 1], F32)
    bias_b = ctx.alloc("l2b_bias_b", [128, 1], F32)
    ctx.alloc("fl_rc_a", [128, 1], F32)
    ctx.alloc("fl_rc_b", [128, 1], F32)

    wvT = ctx.alloc("l2b_wvT", [128, 8, 512], BF16)
    wpc = ctx.alloc("l2b_wpc", [128, 1024], BF16)
    for r in range(4):
        load_convert(ctx, wpc[:],
                     ctx.hbm["L2_ffn_Wv"].rearrange(
                         "v d c -> (v d) c")[r * 128:(r + 1) * 128, :],
                     128, 1024)
        for c in range(8):
            pe_transpose(ctx, wvT[:, c, r * 128:(r + 1) * 128],
                         wpc[:, c * 128:(c + 1) * 128], 128, 128)
    PV = ctx.alloc("l2b_PV", [128, NT, 512], BF16)
    for tb in range(NT):
        for c in range(8):
            p.op("tensor", lambda e, tb=tb, c=c: e.matmul(
                ps_m2[:, :], xn_fm[:, c, tb * 128:(tb + 1) * 128],
                wvT[:, c, :], start=(c == 0), stop=(c == 7)),
                reads=[xn_fm, wvT], writes=[ps_m2])
        p.op("vector", lambda e, tb=tb: e.tensor_copy(
            out=PV[:, tb, :], in_=ps_m2[:, :]), reads=[ps_m2], writes=[PV])
    ctx.free("l2b_wvT")
    wsT = ctx.alloc("l2b_wsT", [128, 8, 128], BF16)
    GV = ctx.alloc("l2b_GV", [128, NT, 128], F32)
    MIX = ctx.alloc("l2b_MIX", [128, NT, 128], BF16)
    MASK = ctx.alloc("l2b_MASK", [128, NT, 128], BF16)
    m16 = ctx.alloc("l2b_m16", [128, NT * 16], F32)
    eqt = ctx.alloc("l2b_eqt", [128, NT * 128], F32)
    for wname, dst, use_sig in (("L2_ffn_Ws", MIX, True),
                                ("L2_ffn_Wd", MASK, False)):
        load_convert(ctx, wpc[:],
                     ctx.hbm[wname].rearrange("h v c -> (h v) c"), 128, 1024)
        for c in range(8):
            pe_transpose(ctx, wsT[:, c, :],
                         wpc[:, c * 128:(c + 1) * 128], 128, 128)
        for tb in range(NT):
            for c in range(8):
                p.op("tensor", lambda e, tb=tb, c=c: e.matmul(
                    ps_m1[:, 0:128], xn_fm[:, c, tb * 128:(tb + 1) * 128],
                    wsT[:, c, :], start=(c == 0), stop=(c == 7)),
                    reads=[xn_fm, wsT], writes=[ps_m1])
            p.op("vector", lambda e, tb=tb: e.tensor_copy(
                out=GV[:, tb, :], in_=ps_m1[:, 0:128]),
                reads=[ps_m1], writes=[GV])
        gv4 = GV[:].rearrange("p t (h v) -> p (t h) v", v=8)
        eq4 = eqt[:].rearrange("p (a v) -> p a v", v=8)
        p.op("vector", lambda e, gv4=gv4: e.tensor_reduce(
            out=m16[:], in_=gv4, axis=AX.X, op=ALU.max),
            reads=[GV], writes=[m16])
        p.op("vector", lambda e, gv4=gv4, eq4=eq4: e.tensor_tensor(
            out=eq4, in0=gv4,
            in1=m16[:, :, None].broadcast_to([128, NT * 16, 8]),
            op=ALU.is_equal), reads=[GV, m16], writes=[eqt])
        p.op("vector", lambda e, gv4=gv4, eq4=eq4: e.scalar_tensor_tensor(
            out=eqt[:], in0=eqt[:], scalar=-1e30,
            in1=GV[:].rearrange("p t v -> p (t v)"),
            op0=ALU.mult, op1=ALU.add), reads=[eqt, GV], writes=[eqt])
        p.op("vector", lambda e, eq4=eq4: e.tensor_reduce(
            out=m16[:], in_=eq4, axis=AX.X, op=ALU.max),
            reads=[eqt], writes=[m16])
        p.op("vector", lambda e, gv4=gv4, eq4=eq4: e.tensor_tensor(
            out=eq4, in0=gv4,
            in1=m16[:, :, None].broadcast_to([128, NT * 16, 8]),
            op=ALU.is_ge), reads=[GV, m16], writes=[eqt])
        if use_sig:
            p.op("scalar", lambda e: e.activation(
                out=GV[:], in_=GV[:], func=AF.Sigmoid),
                reads=[GV], writes=[GV])
            p.op("vector", lambda e, dst=dst: e.tensor_tensor(
                out=dst[:].rearrange("p t v -> p (t v)"),
                in0=GV[:].rearrange("p t v -> p (t v)"), in1=eqt[:],
                op=ALU.mult), reads=[GV, eqt], writes=[dst])
        else:
            p.op("vector", lambda e, dst=dst: e.tensor_copy(
                out=dst[:].rearrange("p t v -> p (t v)"), in_=eqt[:]),
                reads=[eqt], writes=[dst])
    ctx.free("l2b_wsT", "l2b_GV", "l2b_m16", "l2b_eqt", "l2b_wpc")

    ctx.alloc("fl_PT0", [128, 16, 256], BF16)
    ctx.alloc("fl_PT1", [128, 16, 256], BF16)
    wT = ctx.alloc("l2b_wT", [128, 8, 128], BF16)
    zacc = ctx.alloc("l2b_zacc", [128, NT, 512], F32)
    p.op("vector", lambda e: e.memset(zacc[:], 0.0), writes=[zacc])
    VA = ctx.alloc("l2b_VA", [128, NT, 130], BF16)
    Qp = ctx.alloc("l2b_Qp", [128, T], BF16)
    Kp = ctx.alloc("l2b_Kp", [128, T], BF16)
    Opair = ctx.alloc("l2b_Opair", [128, NT, 128], BF16)
    wpc3 = ctx.sb["stage_bf"]
    vmt = ctx.alloc("l2b_ztmp", [128, NT * 64], BF16)
    sqst = ctx.sb["stage_bf"]
    ps_sa = ctx.sb["ps_sa"]
    ps_sb = ctx.sb["ps_sb"]
    ps_ava = ctx.sb["ps_ava"]
    ps_avb = ctx.sb["ps_avb"]
    onesA, onesB = ctx.consts["onesA"], ctx.consts["onesB"]
    onesrow = ctx.consts["onesrow"]

    for hp in range(8):
        for wname, dstp in (("L2_ffn_Wq", Qp), ("L2_ffn_Wk", Kp)):
            hbq = ctx.hbm[wname].rearrange(
                "h v c -> (h v) c")[hp * 128:(hp + 1) * 128, :]
            p.dma("sync", lambda e, hbq=hbq: e.dma_start(
                out=ctx.sb["wstage"][:, 0:1024], in_=hbq),
                writes=["wstage"])
            p.op("gpsimd", lambda e: e.tensor_copy(
                out=wpc3[:], in_=ctx.sb["wstage"][:, 0:1024]),
                reads=["wstage"], writes=["stage_bf"])
            for c in range(8):
                pe_transpose(ctx, wT[:, c, :],
                             wpc3[:, c * 128:(c + 1) * 128], 128, 128)
            for tb in range(4):
                tsl = slice(tb * 512, (tb + 1) * 512)
                for c in range(8):
                    p.op("tensor", lambda e, tsl=tsl, c=c: e.matmul(
                        ps_m2[:, :], wT[:, c, :], xn_fm[:, c, tsl],
                        start=(c == 0), stop=(c == 7)),
                        reads=[wT, xn_fm], writes=[ps_m2])
                p.op("vector", lambda e, tsl=tsl, dstp=dstp: e.tensor_copy(
                    out=dstp[:, tsl], in_=ps_m2[:, :]),
                    reads=[ps_m2], writes=[dstp])
        p.op("vector", lambda e: e.memset(VA[:, :, 64:65], 1.0), writes=[VA])
        p.op("vector", lambda e: e.memset(VA[:, :, 129:130], 1.0),
             writes=[VA])
        for hh in range(2):
            h = hp * 2 + hh
            voff = hh * 65
            for e_ in range(8):
                srcs = [PV, MIX]
                if e_ == 0:
                    p.op("vector", lambda e, h=h, voff=voff, e_=e_:
                         e.tensor_mul(
                             out=VA[:, :, voff:voff + 64],
                             in0=PV[:, :, e_ * 64:(e_ + 1) * 64],
                             in1=MIX[:, :, h * 8 + e_:h * 8 + e_ + 1]
                             .broadcast_to([128, NT, 64])),
                         reads=srcs, writes=[VA])
                else:
                    p.op("vector", lambda e, h=h, voff=voff, e_=e_:
                         e.tensor_mul(
                             out=vmt[:].rearrange("p (t d) -> p t d", d=64),
                             in0=PV[:, :, e_ * 64:(e_ + 1) * 64],
                             in1=MIX[:, :, h * 8 + e_:h * 8 + e_ + 1]
                             .broadcast_to([128, NT, 64])),
                         reads=srcs, writes=[vmt])
                    p.op("gpsimd", lambda e, voff=voff: e.tensor_add(
                        out=VA[:, :, voff:voff + 64],
                        in0=VA[:, :, voff:voff + 64],
                        in1=vmt[:].rearrange("p (t d) -> p t d", d=64)),
                        reads=[vmt, VA], writes=[VA])
        for si, src_ in enumerate((Qp, Kp)):
            for tb in range(4):
                sl = slice(tb * 512, (tb + 1) * 512)
                p.op("scalar", lambda e, src_=src_, sl=sl: e.activation(
                    out=sqst[:, 0:512], in_=src_[:, sl], func=AF.Square),
                    reads=[src_], writes=[sqst])
                for h2, ones in enumerate((onesA, onesB)):
                    p.op("tensor", lambda e, ones=ones: e.matmul(
                        ps_m1[0:1, :], ones[:], sqst[:, 0:512],
                        start=True, stop=True),
                        reads=[sqst, ones], writes=[ps_m1])
                    o = (si * 2 + h2) * 4 + tb
                    p.op("vector", lambda e, o=o: e.reduce_max(
                        out=nsc[:, o:o + 1], in_=ps_m1[0:1, :], axis=AX.X),
                        reads=[ps_m1], writes=[nsc])
        for h2 in range(2):
            o = 16 + h2 * 8
            qs4 = slice(h2 * 4, h2 * 4 + 4)
            ks4 = slice(8 + h2 * 4, 8 + h2 * 4 + 4)
            p.op("vector", lambda e, o=o, qs4=qs4: e.reduce_max(
                out=nsc[:, o:o + 1], in_=nsc[:, qs4], axis=AX.X),
                reads=[nsc], writes=[nsc])
            p.op("vector", lambda e, o=o, ks4=ks4: e.reduce_max(
                out=nsc[:, o + 1:o + 2], in_=nsc[:, ks4], axis=AX.X),
                reads=[nsc], writes=[nsc])
            p.op("vector", lambda e, o=o: e.tensor_mul(
                out=nsc[:, o + 2:o + 3], in0=nsc[:, o:o + 1],
                in1=nsc[:, o + 1:o + 2]), reads=[nsc], writes=[nsc])
            p.op("scalar", lambda e, o=o: e.activation(
                out=nsc[:, o + 3:o + 4], in_=nsc[:, o + 2:o + 3],
                func=AF.Sqrt), reads=[nsc], writes=[nsc])
            p.op("vector", lambda e, o=o: e.tensor_scalar_mul(
                out=nsc[:, o + 4:o + 5], in0=nsc[:, o + 3:o + 4],
                scalar1=-0.125), reads=[nsc], writes=[nsc])
        p.op("vector", lambda e: e.tensor_copy(out=nsc_bf[:, 0:1],
                                               in_=nsc[:, 20:21]),
             reads=[nsc], writes=[nsc_bf])
        p.op("vector", lambda e: e.tensor_copy(out=nsc_bf[:, 1:2],
                                               in_=nsc[:, 28:29]),
             reads=[nsc], writes=[nsc_bf])
        for biast, cs in ((bias_a, slice(0, 1)), (bias_b, slice(1, 2))):
            p.op("tensor", lambda e, cs=cs: e.matmul(
                ps_m1[:, 0:1], onesrow[:], nsc_bf[:, cs],
                start=True, stop=True),
                reads=[nsc_bf, onesrow], writes=[ps_m1])
            p.op("vector", lambda e, biast=biast: e.tensor_copy(
                out=biast[:], in_=ps_m1[:, 0:1]), reads=[ps_m1],
                writes=[biast])
        ctx.sb["fl_K0"] = Kp
        ctx.sb["fl_K1"] = Kp
        ctx.sb["fl_V"] = VA

        def l2b_sink(qtile, psa, psb, rca, rcb):
            for h2, (psav, rc) in enumerate(((psa, rca), (psb, rcb))):
                p.op("vector", lambda e, h2=h2, psav=psav, rc=rc,
                     qtile=qtile: e.tensor_scalar(
                         out=Opair[:, qtile, h2 * 64:(h2 + 1) * 64],
                         in0=psav[:, 0:64], scalar1=rc[:], scalar2=None,
                         op0=ALU.mult),
                     reads=[psav, rc], writes=[Opair])

        flash256(ctx, Qp,
                 lambda h2, ksl: Kp[h2 * 64:(h2 + 1) * 64, ksl],
                 lambda h2, kt: VA[:, kt, h2 * 65:(h2 + 1) * 65],
                 bias_a, bias_b, l2b_sink,
                 lambda qb: list(range(16)),
                 lambda qb, kt, qs: None)
        ztmp = ctx.sb["l2b_ztmp"]
        for h2 in range(2):
            h = hp * 2 + h2
            for e_ in range(8):
                p.op("vector", lambda e, h2=h2, h=h, e_=e_: e.tensor_mul(
                    out=ztmp[:].rearrange("p (t d) -> p t d", d=64),
                    in0=Opair[:, :, h2 * 64:(h2 + 1) * 64],
                    in1=MASK[:, :, h * 8 + e_:h * 8 + e_ + 1]
                    .broadcast_to([128, NT, 64])),
                    reads=[Opair, MASK], writes=[ztmp])
                p.op("gpsimd", lambda e, e_=e_: e.tensor_add(
                    out=zacc[:].rearrange(
                        "p t (v d) -> p t v d", d=64)[:, :, e_, :],
                    in0=zacc[:].rearrange(
                        "p t (v d) -> p t v d", d=64)[:, :, e_, :],
                    in1=ztmp[:].rearrange("p (t d) -> p t d", d=64)),
                    reads=[ztmp, zacc], writes=[zacc])

    ctx.free("l2b_VA", "l2b_Qp", "l2b_Kp", "fl_PT0", "fl_PT1",
             "fl_rc_a", "fl_rc_b", "l2b_Opair", "l2b_wT",
             "l2b_PV", "l2b_MIX", "l2b_MASK", "l2b_ztmp")

    zbf = ctx.alloc("l2b_zbf", [128, NT, 512], BF16)
    p.op("vector", lambda e: e.tensor_copy(out=zbf[:], in_=zacc[:]),
         reads=[zacc], writes=[zbf])
    ctx.free("l2b_zacc")
    z_fm = ctx.alloc("l2b_zfm", [128, 4, T], BF16)
    for tb in range(NT):
        for pc in range(4):
            pe_transpose(ctx, z_fm[:, pc, tb * 128:(tb + 1) * 128],
                         zbf[:, tb, pc * 128:(pc + 1) * 128], 128, 128)
    ctx.free("l2b_zbf")
    woT2 = ctx.alloc("l2b_woT2", [128, 4, 1024], BF16)
    wpc4 = ctx.alloc("l2b_wpc4", [128, 8, 64], BF16)
    for e_ in range(8):
        sv3 = ws[:, 0:512].rearrange("p (c j) -> p c j", j=64)
        p.dma("sync", lambda e, e_=e_, sv3=sv3: e.dma_start(
            out=sv3, in_=ctx.hbm["L2_ffn_Wo"][e_].rearrange(
                "(c p) j -> p c j", p=128)), writes=[ws])
        p.op("gpsimd", lambda e, sv3=sv3: e.tensor_copy(
            out=wpc4[:], in_=sv3), reads=[ws], writes=[wpc4])
        for r in range(8):
            pe_transpose(ctx, woT2[(e_ % 2) * 64:(e_ % 2) * 64 + 64,
                                   e_ // 2, r * 128:(r + 1) * 128],
                         wpc4[:, r, :], 128, 64)
    x = ctx.sb["x_tm"]
    gt = ctx.sb["scratch_f32_b"]
    ctx.sb["scratch_f32_c"] = ctx.alloc("l2b_scrc", [128, D], F32)
    st = ctx.sb["scratch_f32_c"]
    for tb in range(NT):
        for half in range(2):
            esl = slice(half * 512, (half + 1) * 512)
            for pc in range(4):
                p.op("tensor", lambda e, tb=tb, pc=pc, esl=esl: e.matmul(
                    ps_m2[:, :], z_fm[:, pc, tb * 128:(tb + 1) * 128],
                    woT2[:, pc, esl], start=(pc == 0), stop=(pc == 3)),
                    reads=[z_fm, woT2], writes=[ps_m2])
            p.op("scalar", lambda e, esl=esl: e.activation(
                out=gt[:, esl], in_=ps_m2[:, :], func=AF.Silu),
                reads=[ps_m2], writes=[gt])
        p.dma("sync", lambda e, tb=tb: e.dma_start(
            out=st[:], in_=s_state.rearrange(
                "(i p) d -> p i d", p=128)[:, tb, :]),
            reads=["s_state_d"], writes=[st])
        p.op("vector", lambda e: e.tensor_mul(
            out=gt[:], in0=gt[:], in1=st[:]), reads=[gt, st], writes=[gt])
        p.op("vector", lambda e, tb=tb: e.tensor_add(
            out=x[:, tb, :], in0=x[:, tb, :], in1=gt[:]),
            reads=[gt, x], writes=[x])
    ctx.free(*[n for n in list(ctx.live) if n.startswith("l2b_")])



def emit_quant(ctx, dst_bf, src_f32, width, tag):
    p = ctx.p
    nb = width // 16
    q = ctx.sb["l3_qs"]
    qr = ctx.sb["l3_qr"]
    ab = ctx.sb["l3_qab"]
    g1 = ctx.sb["l3_qg1"]
    g2 = ctx.sb["l3_qg2"]
    sg = ctx.sb["l3_qsg"]
    v3 = lambda t, w=width: t[0:128, 0:w].rearrange("p (b k) -> p b k", k=16)
    p.op("vector", lambda e: e.tensor_reduce(
        out=q[:, 0:nb], in_=src_f32.rearrange("p (b k) -> p b k", k=16),
        axis=AX.X, op=ALU.max, apply_absolute_value=True),
        reads=[src_f32], writes=[q])
    p.op("vector", lambda e: e.tensor_scalar(
        out=q[:, 0:nb], in0=q[:, 0:nb], scalar1=1.0 / 6.0, scalar2=1e-12,
        op0=ALU.mult, op1=ALU.max), reads=[q], writes=[q])
    p.op("vector", lambda e: e.reciprocal(out=qr[:, 0:nb], in_=q[:, 0:nb]),
         reads=[q], writes=[qr])
    p.op("scalar", lambda e: e.activation(
        out=ab[:, 0:width], in_=src_f32, func=AF.Abs),
        reads=[src_f32], writes=[ab])
    p.op("scalar", lambda e: e.activation(
        out=sg[:, 0:width], in_=src_f32, func=AF.Sign),
        reads=[src_f32], writes=[sg])
    p.op("vector", lambda e: e.tensor_tensor(
        out=v3(ab), in0=v3(ab),
        in1=qr[:, 0:nb, None].broadcast_to([128, nb, 16]), op=ALU.mult),
        reads=[ab, qr], writes=[ab])
    p.op("vector", lambda e: e.tensor_scalar(
        out=g1[:, 0:width], in0=ab[:, 0:width], scalar1=0.25, scalar2=None,
        op0=ALU.is_gt), reads=[ab], writes=[g1])
    for m in (0.75, 1.25, 1.75):
        p.op("vector", lambda e, m=m: e.scalar_tensor_tensor(
            out=g1[:, 0:width], in0=ab[:, 0:width], scalar=m,
            in1=g1[:, 0:width], op0=ALU.is_gt, op1=ALU.add),
            reads=[ab, g1], writes=[g1])
    p.op("vector", lambda e: e.tensor_scalar(
        out=g2[:, 0:width], in0=ab[:, 0:width], scalar1=2.5, scalar2=None,
        op0=ALU.is_gt), reads=[ab], writes=[g2])
    p.op("vector", lambda e: e.scalar_tensor_tensor(
        out=g2[:, 0:width], in0=ab[:, 0:width], scalar=3.5,
        in1=g2[:, 0:width], op0=ALU.is_gt, op1=ALU.add),
        reads=[ab, g2], writes=[g2])
    p.op("vector", lambda e: e.tensor_scalar(
        out=ab[:, 0:width], in0=ab[:, 0:width], scalar1=5.0, scalar2=2.0,
        op0=ALU.is_gt, op1=ALU.mult), reads=[ab], writes=[ab])
    p.op("gpsimd", lambda e: e.tensor_add(
        out=g2[:, 0:width], in0=g2[:, 0:width], in1=ab[:, 0:width]),
        reads=[ab, g2], writes=[g2])
    p.op("vector", lambda e: e.scalar_tensor_tensor(
        out=g1[:, 0:width], in0=g1[:, 0:width], scalar=0.5,
        in1=g2[:, 0:width], op0=ALU.mult, op1=ALU.add),
        reads=[g1, g2], writes=[g1])
    p.op("gpsimd", lambda e: e.tensor_mul(
        out=g1[:, 0:width], in0=g1[:, 0:width], in1=sg[:, 0:width]),
        reads=[g1, sg], writes=[g1])
    p.op("vector", lambda e: e.tensor_tensor(
        out=dst_bf.rearrange("p (b k) -> p b k", k=16), in0=v3(g1),
        in1=q[:, 0:nb, None].broadcast_to([128, nb, 16]), op=ALU.mult),
        reads=[g1, q], writes=[dst_bf])


def emit_layer3(ctx):
    p = ctx.p
    nc = ctx.nc
    emit_ln(ctx)
    xn_fm = ctx.sb["xn_fm"]
    ident = ctx.consts["ident"]
    onesrow = ctx.consts["onesrow"]
    ones512 = ctx.consts["ones512"]
    ps_m1 = ctx.sb["ps_m1"]
    ps_m2 = ctx.sb["ps_m2"]
    ws = ctx.sb["wstage"]

    ctx.alloc("l3_qs", [128, 176], F32)
    ctx.alloc("l3_qr", [128, 176], F32)
    ctx.alloc("l3_qab", [128, 2816], F32)
    ctx.alloc("l3_qg1", [128, 2816], F32)
    ctx.alloc("l3_qg2", [128, 2816], F32)
    ctx.alloc("l3_qsg", [128, 2816], BF16)

    h_dram = nc.dram_tensor("l3_hq_fm", [22 * 128, T], BF16,
                            kind="Internal").ap()
    wdqT_dram = nc.dram_tensor("l3_wdqT", [22 * 128, 1024], BF16,
                               kind="Internal").ap()
    ypart_dram = nc.dram_tensor("l3_ypart", [T, D], F32,
                                kind="Internal").ap()
    g_dram = nc.dram_tensor("l3_g", [T, D], F32, kind="Internal").ap()

    xq_fm = ctx.alloc("l3_xq_fm", [128, ND, T], BF16)
    xnr = ctx.sb["scratch_f32_a"]
    xq_t = ctx.sb["stage_bf"]
    for tb in range(NT):
        xn_rec(ctx, tb, xnr[:])
        emit_quant(ctx, xq_t[:], xnr[:], D, f"xq{tb}")
        for c in range(ND):
            pe_transpose(ctx, xq_fm[:, c, tb * 128:(tb + 1) * 128],
                         xq_t[:, c * 128:(c + 1) * 128], 128, 128)

    wq_f = ctx.alloc("l3_wq_f", [128, 1024], BF16)
    wu_f = ctx.alloc("l3_wu_f", [128, 1024], BF16)
    wqT = ctx.alloc("l3_wqT", [128, 8, 128], BF16)
    wuT = ctx.alloc("l3_wuT", [128, 8, 128], BF16)
    hslab = ctx.alloc("l3_hslab", [128, NT, 128], BF16)
    hf32 = ctx.alloc("l3_hf32", [128, 128], F32)
    ps_g = ctx.sb["ps_ava"]
    ps_u = ctx.sb["ps_avb"]
    for r in range(22):
        for wname, wqt, wqc in (("L3_ffn_w_gate", wqT, wq_f),
                                ("L3_ffn_w_up", wuT, wu_f)):
            sv = ws[:, 0:1024]
            hb = ctx.hbm[wname][r * 128:(r + 1) * 128, :]
            p.dma("sync", lambda e, hb=hb, sv=sv: e.dma_start(
                out=sv, in_=hb), writes=[ws])
            emit_quant(ctx, wqc[:], sv, 1024, f"w{r}")
            for c in range(8):
                pe_transpose(ctx, wqt[:, c, :],
                             wqc[:, c * 128:(c + 1) * 128], 128, 128)
        for tb in range(NT):
            for c in range(8):
                p.op("tensor", lambda e, tb=tb, c=c: e.matmul(
                    ps_g[:, 0:128], xq_fm[:, c, tb * 128:(tb + 1) * 128],
                    wqT[:, c, :], start=(c == 0), stop=(c == 7)),
                    reads=[xq_fm, wqT], writes=[ps_g])
            for c in range(8):
                p.op("tensor", lambda e, tb=tb, c=c: e.matmul(
                    ps_u[:, 0:128], xq_fm[:, c, tb * 128:(tb + 1) * 128],
                    wuT[:, c, :], start=(c == 0), stop=(c == 7)),
                    reads=[xq_fm, wuT], writes=[ps_u])
            p.op("scalar", lambda e: e.activation(
                out=hf32[:], in_=ps_g[:, 0:128], func=AF.Silu),
                reads=[ps_g], writes=[hf32])
            p.op("vector", lambda e: e.tensor_tensor(
                out=hf32[:], in0=hf32[:], in1=ps_u[:, 0:128], op=ALU.mult),
                reads=[hf32, ps_u], writes=[hf32])
            emit_quant(ctx, xq_t[:, 0:128], hf32[:], 128, f"h{r}_{tb}")
            pe_transpose(ctx, hslab[:, tb, :], xq_t[:, 0:128], 128, 128)
        p.dma("sync", lambda e, r=r: e.dma_start(
            out=h_dram[r * 128:(r + 1) * 128, :],
            in_=hslab[:].rearrange("p t k -> p (t k)")), reads=[hslab],
            writes=["l3_hq_d"], )
    ctx.free("l3_wq_f", "l3_wu_f", "l3_wqT", "l3_wuT", "l3_hslab",
             "l3_hf32")

    wd_f = ctx.alloc("l3_wd_f", [128, 2816], BF16)
    for r in range(8):
        for c0, cw in ((0, 1024), (1024, 1024), (2048, 768)):
            sv = ws[:, 0:cw]
            hb1 = ctx.hbm["L3_ffn_w_down"][r * 128:(r + 1) * 128,
                                           c0:c0 + cw]
            p.dma("sync", lambda e, hb1=hb1, sv=sv: e.dma_start(
                out=sv, in_=hb1), writes=[ws])
            emit_quant(ctx, wd_f[:, c0:c0 + cw], sv, cw, f"wd{r}_{c0}")
        for kc in range(22):
            pe_transpose(ctx, xq_t[:, 0:128],
                         wd_f[:, kc * 128:(kc + 1) * 128], 128, 128)
            p.dma("sync", lambda e, kc=kc, r=r: e.dma_start(
                out=wdqT_dram[kc * 128:(kc + 1) * 128,
                              r * 128:(r + 1) * 128],
                in_=xq_t[:, 0:128]), reads=[xq_t], writes=["l3_wdq_d"])
    ctx.free("l3_wd_f", "l3_xq_fm")
    ctx.free("l3_qab", "l3_qg1", "l3_qg2", "l3_qsg", "l3_qs", "l3_qr")

    hq = ctx.alloc("l3_hq", [128, 11, T], BF16)
    wdT = ctx.alloc("l3_wdT", [128, 11, 1024], BF16)
    yt = ctx.sb["scratch_f32_b"]
    for half in range(2):
        k0 = half * 11
        for kc in range(11):
            p.dma("sync", lambda e, kc=kc, k0=k0: e.dma_start(
                out=hq[:, kc, :],
                in_=h_dram[(k0 + kc) * 128:(k0 + kc + 1) * 128, :]),
                reads=["l3_hq_d"], writes=[hq])
            p.dma("sync", lambda e, kc=kc, k0=k0: e.dma_start(
                out=wdT[:, kc, :],
                in_=wdqT_dram[(k0 + kc) * 128:(k0 + kc + 1) * 128, :]),
                reads=["l3_wdq_d"], writes=[wdT])
        for tb in range(NT):
            if half == 1:
                p.dma("sync", lambda e, tb=tb: e.dma_start(
                    out=yt[:], in_=ypart_dram.rearrange(
                        "(i p) d -> p i d", p=128)[:, tb, :]),
                    reads=["l3_yp_d"], writes=[yt])
            for eh in range(2):
                esl = slice(eh * 512, (eh + 1) * 512)
                for kc in range(11):
                    p.op("tensor", lambda e, tb=tb, kc=kc, esl=esl: e.matmul(
                        ps_m2[:, :], hq[:, kc, tb * 128:(tb + 1) * 128],
                        wdT[:, kc, esl], start=(kc == 0), stop=(kc == 10)),
                        reads=[hq, wdT], writes=[ps_m2])
                if half == 0:
                    p.op("vector", lambda e, esl=esl: e.tensor_copy(
                        out=yt[:, esl], in_=ps_m2[:, :]),
                        reads=[ps_m2], writes=[yt])
                else:
                    p.op("vector", lambda e, esl=esl: e.tensor_add(
                        out=yt[:, esl], in0=yt[:, esl], in1=ps_m2[:, :]),
                        reads=[ps_m2, yt], writes=[yt])
            if half == 0:
                p.dma("sync", lambda e, tb=tb: e.dma_start(
                    out=ypart_dram.rearrange(
                        "(i p) d -> p i d", p=128)[:, tb, :], in_=yt[:]),
                    reads=[yt], writes=["l3_yp_d"])
            else:
                p.op("scalar", lambda e: e.activation(
                    out=yt[:], in_=yt[:], func=AF.Silu),
                    reads=[yt], writes=[yt])
                p.dma("sync", lambda e, tb=tb: e.dma_start(
                    out=g_dram.rearrange(
                        "(i p) d -> p i d", p=128)[:, tb, :], in_=yt[:]),
                    reads=[yt], writes=["l3_g_d"])
    ctx.free("l3_hq", "l3_wdT")

    triu = ctx.consts["tril_incl"]
    trilx = ctx.consts["trilx"]
    wk_bf = ctx.alloc("l3_wk_bf", [128, 8, 256], BF16)
    wv_bf = ctx.alloc("l3_wv_bf", [128, 8, 256], BF16)
    for wname, wbf in (("L3_attn_Wk", wk_bf), ("L3_attn_Wv", wv_bf)):
        for cp in range(2):
            sv = ws[:, 0:1024].rearrange("p (c m) -> p c m", c=4)
            hb = ctx.hbm[wname].rearrange(
                "(c p) m -> p c m", p=128)[:, cp * 4:(cp + 1) * 4, :]
            p.dma("sync", lambda e, hb=hb, sv=sv: e.dma_start(
                out=sv, in_=hb), writes=[ws])
            p.op("gpsimd", lambda e, wbf=wbf, sv=sv, cp=cp: e.tensor_copy(
                out=wbf[:, cp * 4:(cp + 1) * 4, :], in_=sv),
                reads=[ws], writes=[wbf])
    bias_rows = ctx.alloc("l3_bias_rows", [1, 2048 + 1024], BF16)
    for nm, off, w in (("L3_attn_bq", 0, 1024), ("L3_attn_bk", 1024, 256),
                       ("L3_attn_bv", 1280, 256), ("L3_attn_bo", 2048, 1024)):
        load_convert(ctx, bias_rows[:, off:off + w],
                     ctx.hbm[nm][None, :], 1, w)
    wkd = ctx.alloc("l3_wkd", [128, 8, 128], BF16)
    Kd = ctx.alloc("l3_Kd", [128, 4, T], BF16)
    Vx = ctx.alloc("l3_Vx", [128, NT, 260], BF16)
    bkd = ctx.alloc("l3_bkd", [1, 128], BF16)
    for g in range(4):
        for c in range(8):
            for dup in range(2):
                p.op("vector", lambda e, g=g, c=c, dup=dup: e.tensor_copy(
                    out=wkd[:, c, dup * 64:(dup + 1) * 64],
                    in_=wk_bf[:, c, g * 64:(g + 1) * 64]),
                    reads=[wk_bf], writes=[wkd])
        for dup in range(2):
            p.op("vector", lambda e, g=g, dup=dup: e.tensor_copy(
                out=bkd[:, dup * 64:(dup + 1) * 64],
                in_=bias_rows[:, 1024 + g * 64:1024 + (g + 1) * 64]),
                reads=[bias_rows], writes=[bkd])
        for tb in range(4):
            tsl = slice(tb * 512, (tb + 1) * 512)
            for c in range(8):
                p.op("tensor", lambda e, tsl=tsl, c=c: e.matmul(
                    ps_m2[:, :], wkd[:, c, :], xn_fm[:, c, tsl],
                    start=(c == 0), stop=False),
                    reads=[wkd, xn_fm], writes=[ps_m2])
            p.op("tensor", lambda e, tsl=tsl: e.matmul(
                ps_m2[:, :], bkd[:], ones512[:], start=False, stop=True),
                reads=[bkd, "c_ones512"], writes=[ps_m2])
            p.op("vector", lambda e, g=g, tsl=tsl: e.tensor_copy(
                out=Kd[:, g, tsl], in_=ps_m2[:, :]),
                reads=[ps_m2], writes=[Kd])
    for tb in range(NT):
        for c in range(8):
            p.op("tensor", lambda e, tb=tb, c=c: e.matmul(
                ps_m2[:, 0:256], xn_fm[:, c, tb * 128:(tb + 1) * 128],
                wv_bf[:, c, :], start=(c == 0), stop=False),
                reads=[wv_bf, xn_fm], writes=[ps_m2])
        p.op("tensor", lambda e, tb=tb: e.matmul(
            ps_m2[:, 0:256], onesrow[:], bias_rows[:, 1280:1536],
            start=False, stop=True),
            reads=[bias_rows, "c_onesrow"], writes=[ps_m2])
        p.op("vector", lambda e, tb=tb: e.tensor_copy(
            out=Vx[:, tb, :].rearrange("p (g u) -> p g u", u=65)[:, :, 0:64],
            in_=ps_m2[:, 0:256].rearrange("p (g u) -> p g u", u=64)),
            reads=[ps_m2], writes=[Vx])
    p.op("vector", lambda e: e.memset(
        Vx[:].rearrange("p t (g u) -> p t g u", u=65)[:, :, :, 64:65], 1.0),
        writes=[Vx])

    ctx.free("l3_wk_bf", "l3_wv_bf", "l3_wkd", "l3_bkd")
    O_fm = ctx.alloc("l3_O_fm", [128, ND, T], BF16)
    Qp = ctx.alloc("l3_Qp", [128, T], BF16)
    wq_p = ctx.alloc("l3_wq_p", [128, 8, 128], BF16)
    ctx.alloc("fl_PT0", [128, 16, 256], BF16)
    ctx.alloc("fl_PT1", [128, 16, 256], BF16)
    ctx.alloc("fl_rc_a", [128, 1], F32)
    ctx.alloc("fl_rc_b", [128, 1], F32)
    nsc = ctx.alloc("l3_nsc", [1, 64], F32)
    nsc_bf = ctx.alloc("l3_nscbf", [1, 2], BF16)
    bias_a = ctx.alloc("l3_bias_a", [128, 1], F32)
    bias_b = ctx.alloc("l3_bias_b", [128, 1], F32)
    stageO = ctx.alloc("l3_stageO", [128, 128], BF16)
    sqst = ctx.sb["stage_bf"]
    ps_sa = ctx.sb["ps_sa"]
    ps_sb = ctx.sb["ps_sb"]
    ps_ava = ctx.sb["ps_ava"]
    ps_avb = ctx.sb["ps_avb"]
    onesA, onesB = ctx.consts["onesA"], ctx.consts["onesB"]

    for hp in range(8):
        g = (2 * hp) // 4
        sv = ws[:, 0:1024].rearrange("p (c m) -> p c m", c=8)
        hb = ctx.hbm["L3_attn_Wq"][:, hp * 128:(hp + 1) * 128].rearrange(
            "(c p) m -> p c m", p=128)
        p.dma("sync", lambda e, hb=hb, sv=sv: e.dma_start(out=sv, in_=hb),
              writes=[ws])
        p.op("gpsimd", lambda e, sv=sv: e.tensor_copy(
            out=wq_p[:], in_=sv), reads=[ws], writes=[wq_p])
        for tb in range(4):
            tsl = slice(tb * 512, (tb + 1) * 512)
            for c in range(8):
                p.op("tensor", lambda e, tsl=tsl, c=c: e.matmul(
                    ps_m2[:, :], wq_p[:, c, :], xn_fm[:, c, tsl],
                    start=(c == 0), stop=False),
                    reads=[wq_p, xn_fm], writes=[ps_m2])
            p.op("tensor", lambda e, tsl=tsl, hp=hp: e.matmul(
                ps_m2[:, :], bias_rows[:, hp * 128:(hp + 1) * 128],
                ones512[:], start=False, stop=True),
                reads=[bias_rows, "c_ones512"], writes=[ps_m2])
            p.op("vector", lambda e, tsl=tsl: e.tensor_copy(
                out=Qp[:, tsl], in_=ps_m2[:, :]), reads=[ps_m2], writes=[Qp])
        for si in range(2):
            for tb in range(4):
                sl = slice(tb * 512, (tb + 1) * 512)
                if si == 0:
                    p.op("scalar", lambda e, sl=sl: e.activation(
                        out=sqst[:, 0:512], in_=Qp[:, sl], func=AF.Square),
                        reads=[Qp], writes=[sqst])
                else:
                    p.op("scalar", lambda e, sl=sl, g=g: e.activation(
                        out=sqst[:, 0:512], in_=Kd[:, g, sl],
                        func=AF.Square), reads=[Kd], writes=[sqst])
                for h2, ones in enumerate((onesA, onesB)):
                    p.op("tensor", lambda e, ones=ones: e.matmul(
                        ps_m1[0:1, :], ones[:], sqst[:, 0:512],
                        start=True, stop=True),
                        reads=[sqst, ones], writes=[ps_m1])
                    o = (si * 2 + h2) * 4 + tb
                    p.op("vector", lambda e, o=o: e.reduce_max(
                        out=nsc[:, o:o + 1], in_=ps_m1[0:1, :], axis=AX.X),
                        reads=[ps_m1], writes=[nsc])
        for h2 in range(2):
            o = 16 + h2 * 8
            qs4 = slice(h2 * 4, h2 * 4 + 4)
            ks4 = slice(8 + h2 * 4, 8 + h2 * 4 + 4)
            p.op("vector", lambda e, o=o, qs4=qs4: e.reduce_max(
                out=nsc[:, o:o + 1], in_=nsc[:, qs4], axis=AX.X),
                reads=[nsc], writes=[nsc])
            p.op("vector", lambda e, o=o, ks4=ks4: e.reduce_max(
                out=nsc[:, o + 1:o + 2], in_=nsc[:, ks4], axis=AX.X),
                reads=[nsc], writes=[nsc])
            p.op("vector", lambda e, o=o: e.tensor_mul(
                out=nsc[:, o + 2:o + 3], in0=nsc[:, o:o + 1],
                in1=nsc[:, o + 1:o + 2]), reads=[nsc], writes=[nsc])
            p.op("scalar", lambda e, o=o: e.activation(
                out=nsc[:, o + 3:o + 4], in_=nsc[:, o + 2:o + 3],
                func=AF.Sqrt), reads=[nsc], writes=[nsc])
            p.op("vector", lambda e, o=o: e.tensor_scalar_mul(
                out=nsc[:, o + 4:o + 5], in0=nsc[:, o + 3:o + 4],
                scalar1=-0.125), reads=[nsc], writes=[nsc])
        p.op("vector", lambda e: e.tensor_copy(out=nsc_bf[:, 0:1],
                                               in_=nsc[:, 20:21]),
             reads=[nsc], writes=[nsc_bf])
        p.op("vector", lambda e: e.tensor_copy(out=nsc_bf[:, 1:2],
                                               in_=nsc[:, 28:29]),
             reads=[nsc], writes=[nsc_bf])
        for biast, cs in ((bias_a, slice(0, 1)), (bias_b, slice(1, 2))):
            p.op("tensor", lambda e, cs=cs: e.matmul(
                ps_m1[:, 0:1], onesrow[:], nsc_bf[:, cs],
                start=True, stop=True),
                reads=[nsc_bf, onesrow], writes=[ps_m1])
            p.op("vector", lambda e, biast=biast: e.tensor_copy(
                out=biast[:], in_=ps_m1[:, 0:1]),
                reads=[ps_m1], writes=[biast])
        ctx.sb["fl_K0"] = Kd
        ctx.sb["fl_K1"] = Kd
        ctx.sb["fl_V"] = Vx

        def l3_sink(qtile, psa, psb, rca, rcb, hp=hp):
            p.op("vector", lambda e: e.tensor_scalar(
                out=stageO[:, 0:64], in0=psa[:, 0:64], scalar1=rca[:],
                scalar2=None, op0=ALU.mult),
                reads=[psa, rca], writes=[stageO])
            p.op("vector", lambda e: e.tensor_scalar(
                out=stageO[:, 64:128], in0=psb[:, 0:64], scalar1=rcb[:],
                scalar2=None, op0=ALU.mult),
                reads=[psb, rcb], writes=[stageO])
            pe_transpose(ctx, O_fm[:, hp, qtile * 128:(qtile + 1) * 128],
                         stageO[:], 128, 128)

        def l3_kts(qb):
            return [2 * qb + d for d in range(-4, 2) if 2 * qb + d >= 0]

        def l3_mask(qb, kt, qs):
            d = kt - 2 * qb
            if d == qs:
                return triu
            if d > qs:
                return "skip"
            if d == qs - 4:
                return trilx
            if d < qs - 4:
                return "skip"
            return None

        flash256(ctx, Qp,
                 lambda h2, ksl, g=g: Kd[h2 * 64:(h2 + 1) * 64, g, ksl],
                 lambda h2, kt, g=g: Vx[:, kt, g * 65:(g + 1) * 65],
                 bias_a, bias_b, l3_sink, l3_kts, l3_mask)

    ctx.tap_full("l3_O", O_fm)
    ctx.tap_full("l3_Kd", Kd)
    ctx.tap_full("l3_Qp", Qp)
    ctx.free("l3_Qp", "l3_wq_p", "fl_PT0", "fl_PT1", "fl_rc_a",
             "fl_rc_b", "l3_Kd", "l3_Vx", "l3_stageO")

    wo_bf = load_weight_bf16(ctx, "l3_wo", ctx.hbm["L3_attn_Wo"], 1024, 1024)
    x = ctx.sb["x_tm"]
    ctx.sb["scratch_f32_c"] = ctx.alloc("l3_scrc", [128, D], F32)
    st = ctx.sb["scratch_f32_c"]
    bo_brow = ctx.alloc("l3_bo_brow", [128, D], F32)
    row_bcast(ctx, bo_brow[:], ctx.sb["l3_bias_rows"][:, 2048:3072], D,
              ["l3_bias_rows"])
    gt = ctx.sb["scratch_f32_b"]
    s_tap = None
    if "l3_s" in ctx.dev_taps:
        s_tap = ctx.nc.dram_tensor("tap_l3_s", [128, NT, D], F32,
                                   kind="ExternalOutput").ap()
        ctx.taps["l3_s"] = [128, NT, D]
    if "l3_g" in ctx.dev_taps:
        gt_tap = ctx.nc.dram_tensor("tap_l3_g", [T, D], F32,
                                    kind="ExternalOutput").ap()
        ctx.taps["l3_g"] = [T, D]
        p.dma("sync", lambda e: e.dma_start(out=gt_tap, in_=g_dram),
              reads=["l3_g_d"], is_out=True)
    for tb in range(NT):
        p.dma("sync", lambda e, tb=tb: e.dma_start(
            out=gt[:], in_=g_dram.rearrange(
                "(i p) d -> p i d", p=128)[:, tb, :]),
            reads=["l3_g_d"], writes=[gt])
        for half in range(2):
            esl = slice(half * 512, (half + 1) * 512)
            for c in range(8):
                p.op("tensor", lambda e, tb=tb, c=c, esl=esl: e.matmul(
                    ps_m2[:, :], O_fm[:, c, tb * 128:(tb + 1) * 128],
                    wo_bf[:, c, esl], start=(c == 0), stop=(c == 7)),
                    reads=[O_fm, wo_bf], writes=[ps_m2])
            p.op("vector", lambda e, esl=esl: e.scalar_tensor_tensor(
                out=st[:, esl], in0=ps_m2[:, :], scalar=1.0,
                in1=bo_brow[:, esl], op0=ALU.mult, op1=ALU.add),
                reads=[ps_m2, bo_brow], writes=[st])
        if s_tap is not None:
            p.dma("sync", lambda e, tb=tb: e.dma_start(
                out=s_tap[:, tb, :], in_=st[:]), reads=[st], is_out=True)
        p.op("vector", lambda e: e.tensor_mul(
            out=gt[:], in0=gt[:], in1=st[:]), reads=[gt, st], writes=[gt])
        p.op("vector", lambda e, tb=tb: e.tensor_add(
            out=x[:, tb, :], in0=x[:, tb, :], in1=gt[:]),
            reads=[gt, x], writes=[x])
    ctx.free(*[n for n in list(ctx.live) if n.startswith("l3_")])


def make_consts():
    ident = np.eye(128).astype(BF)
    onesrow = np.ones((1, 128), dtype=BF)
    onesA = np.zeros((128, 1), dtype=BF)
    onesA[0:64] = 1
    onesB = np.zeros((128, 1), dtype=BF)
    onesB[64:128] = 1
    e16sel = np.zeros((8, 128), dtype=BF)
    for e in range(8):
        e16sel[e, e * 16:(e + 1) * 16] = 1
    ones512 = np.ones((1, 512), dtype=BF)
    tril_incl = np.triu(np.ones((128, 128))).astype(BF)
    trilx = (np.tril(np.ones((128, 128)), -1)).astype(BF)
    grp8 = np.full((64, 8), 0.125, dtype=BF)
    g8 = np.zeros((64, 8), dtype=BF)
    for c in range(8):
        g8[c * 8:(c + 1) * 8, c] = 0.125
    return dict(ident=ident, onesrow=onesrow, onesA=onesA, onesB=onesB,
                e16sel=e16sel, ones512=ones512, grp8=g8,
                tril_incl=tril_incl, trilx=trilx)


def make_consts32():
    iota8m9 = np.zeros((128, 128), dtype=np.float32)
    for j in range(128):
        iota8m9[:, j] = (j % 8) - 9.0
    iotacol = np.arange(128, dtype=np.float32).reshape(128, 1)
    return dict(iota8m9=iota8m9, iotacol=iotacol)


WEIGHT_SPECS = {
    0: ["L0_attn_Wq", "L0_attn_Wk", "L0_attn_Wv", "L0_attn_Wout",
        "L0_ffn_router_w", "L0_ffn_lora_down", "L0_ffn_lora_up"],
    1: ["L1_attn_log_A_real", "L1_attn_log_A_imag", "L1_attn_B_w",
        "L1_attn_C_w", "L1_attn_D_w", "L1_ffn_signatures_raw",
        "L1_ffn_knot_values", "L1_ffn_temperature", "L1_ffn_compress_w1",
        "L1_ffn_compress_b1", "L1_ffn_compress_w2", "L1_ffn_compress_b2",
        "L1_ffn_spline_w1", "L1_ffn_spline_b1", "L1_ffn_spline_w2",
        "L1_ffn_spline_b2", "L1_ffn_directions", "L1_ffn_norm_gamma",
        "L1_ffn_norm_beta", "L1_ffn_output_scale"],
    "2a": ["L2_attn_Wq", "L2_attn_Wk", "L2_attn_Wv", "L2_attn_conv_q",
           "L2_attn_conv_k", "L2_attn_conv_v", "L2_attn_W_init",
           "L2_attn_ln_gamma", "L2_attn_ln_beta", "L2_attn_Wg",
           "L2_attn_Wo"],
    "2b": ["L2_ffn_Wq", "L2_ffn_Wk", "L2_ffn_Wv", "L2_ffn_Ws", "L2_ffn_Wd",
           "L2_ffn_Wo"],
    3: ["L3_attn_Wq", "L3_attn_bq", "L3_attn_Wk", "L3_attn_bk",
        "L3_attn_Wv", "L3_attn_bv", "L3_attn_Wo", "L3_attn_bo",
        "L3_ffn_w_gate", "L3_ffn_w_up", "L3_ffn_w_down"],
}

WEIGHT_SHAPES = {
    "L0_attn_Wq": (1024, 1024), "L0_attn_Wk": (1024, 1024),
    "L0_attn_Wv": (1024, 1024), "L0_attn_Wout": (1024, 1024),
    "L0_ffn_router_w": (8, 1024), "L0_ffn_lora_down": (8, 16, 1024),
    "L0_ffn_lora_up": (8, 1024, 16),
    "L1_attn_log_A_real": (64,), "L1_attn_log_A_imag": (64,),
    "L1_attn_B_w": (128, 1024), "L1_attn_C_w": (1024, 128),
    "L1_attn_D_w": (1024, 1024), "L1_ffn_signatures_raw": (64, 1024),
    "L1_ffn_knot_values": (8,), "L1_ffn_temperature": (1,),
    "L1_ffn_compress_w1": (1024, 256), "L1_ffn_compress_b1": (256,),
    "L1_ffn_compress_w2": (256, 2), "L1_ffn_compress_b2": (2,),
    "L1_ffn_spline_w1": (64, 2, 16), "L1_ffn_spline_b1": (64, 16),
    "L1_ffn_spline_w2": (64, 16, 1), "L1_ffn_spline_b2": (64, 1),
    "L1_ffn_directions": (64, 1024), "L1_ffn_norm_gamma": (1024,),
    "L1_ffn_norm_beta": (1024,), "L1_ffn_output_scale": (1,),
    "L2_attn_Wq": (1024, 1024), "L2_attn_Wk": (1024, 1024),
    "L2_attn_Wv": (1024, 1024), "L2_attn_conv_q": (1024, 4),
    "L2_attn_conv_k": (1024, 4), "L2_attn_conv_v": (1024, 4),
    "L2_attn_W_init": (1, 16, 64, 64), "L2_attn_ln_gamma": (64,),
    "L2_attn_ln_beta": (64,), "L2_attn_Wg": (1024, 1024),
    "L2_attn_Wo": (1024, 1024),
    "L2_ffn_Wq": (16, 64, 1024), "L2_ffn_Wk": (16, 64, 1024),
    "L2_ffn_Wv": (8, 64, 1024), "L2_ffn_Ws": (16, 8, 1024),
    "L2_ffn_Wd": (16, 8, 1024), "L2_ffn_Wo": (8, 1024, 64),
    "L3_attn_Wq": (1024, 1024), "L3_attn_bq": (1024,),
    "L3_attn_Wk": (1024, 256), "L3_attn_bk": (256,),
    "L3_attn_Wv": (1024, 256), "L3_attn_bv": (256,),
    "L3_attn_Wo": (1024, 1024), "L3_attn_bo": (1024,),
    "L3_ffn_w_gate": (2816, 1024), "L3_ffn_w_up": (2816, 1024),
    "L3_ffn_w_down": (1024, 2816),
}

LAYER_EMITTERS = {0: emit_layer0, 1: emit_layer1,
                  '2a': emit_layer2a, '2b': emit_layer2b,
                  3: emit_layer3}


def build_program(layers, dev_taps=()):
    nc = bass.Bass("TRN2", target_bir_lowering=False, debug=False)
    p = Prog(nc)
    ctx = Ctx(nc, p, dev_taps=dev_taps)
    ctx.fused = len(layers) > 1

    x_in = nc.dram_tensor("x_in", [T, D], F32, kind="ExternalInput").ap()
    x_out = nc.dram_tensor("x_out", [T, D], F32, kind="ExternalOutput").ap()
    for li in layers:
        for w in WEIGHT_SPECS[li]:
            ctx.hbm[w] = nc.dram_tensor(
                w, list(WEIGHT_SHAPES[w]), F32, kind="ExternalInput").ap()
    cvals = make_consts()
    for name, arr in cvals.items():
        ap = nc.dram_tensor(f"const_{name}", list(arr.shape), BF16,
                            kind="ExternalInput").ap()
        sb = ctx.alloc(f"c_{name}", list(arr.shape), BF16)
        p.dma("sync", lambda e, ap=ap, sb=sb: e.dma_start(out=sb[:], in_=ap),
              writes=[sb])
        ctx.consts[name] = sb
    c32 = make_consts32()
    for name, arr in c32.items():
        ap = nc.dram_tensor(f"const32_{name}", list(arr.shape), F32,
                            kind="ExternalInput").ap()
        sb = ctx.alloc(f"c32_{name}", list(arr.shape), F32)
        p.dma("sync", lambda e, ap=ap, sb=sb: e.dma_start(out=sb[:], in_=ap),
              writes=[sb])
        ctx.consts32[name] = sb

    x_tm = ctx.alloc("x_tm", [128, NT, D], F32)
    ctx.alloc("xn_fm", [128, ND, T], BF16)
    ctx.alloc("ln_mu", [128, NT], F32)
    ctx.alloc("ln_rstd", [128, NT], F32)
    ctx.alloc("scratch_f32_a", [128, D], F32)
    ctx.alloc("scratch_f32_b", [128, D], F32)
    ctx.alloc("stage_bf", [128, D], BF16)
    ctx.alloc("scratch_col", [128, 4], F32)
    ctx.alloc("wstage", [128, 1024], F32)
    fc = ctx.alloc("fconsts", [128, 16], F32)
    fc_ap = nc.dram_tensor("const_fconsts", [128, 16], F32,
                           kind="ExternalInput").ap()
    p.dma("sync", lambda e: e.dma_start(out=fc[:], in_=fc_ap),
          writes=[fc])
    ctx.psum("ps_sa", [128, 1024])
    ctx.psum("ps_sb", [128, 1024])
    ctx.psum("ps_ava", [128, 512])
    ctx.psum("ps_avb", [128, 512])
    ctx.psum("ps_m1", [128, 512])
    ctx.psum("ps_m2", [128, 512])
    ctx.sb["ps_tr"] = ctx.sb["ps_m1"]

    p.dma("sync", lambda e: e.dma_start(
        out=x_tm[:], in_=x_in.rearrange("(i p) d -> p i d", p=128)),
        writes=[x_tm])

    ctx.start_dynamic()
    for li in layers:
        LAYER_EMITTERS[li](ctx)

    p.dma("sync", lambda e: e.dma_start(
        out=x_out.rearrange("(i p) d -> p i d", p=128), in_=x_tm[:]),
        reads=[x_tm], is_out=True)

    p.finalize()
    return nc, ctx, cvals


def run_layers(inputs, layers, x, dev_taps=(), carry=None):
    carry = carry if carry is not None else [{} for _ in range(8)]
    nc, ctx, cvals = build_program(layers, dev_taps=dev_taps)
    in_maps = []
    for core in range(8):
        m = {"x_in": np.ascontiguousarray(x[core], dtype=np.float32)}
        for name in ctx.extra_in:
            m[name] = carry[core][name]
        for li in layers:
            for w in WEIGHT_SPECS[li]:
                m[w] = np.ascontiguousarray(inputs[w], dtype=np.float32)
        for name, arr in cvals.items():
            m[f"const_{name}"] = arr
        for name, arr in make_consts32().items():
            m[f"const32_{name}"] = arr
        fcarr = np.zeros((128, 16), dtype=np.float32)
        for v, j in ctx.fconst_vals.items():
            fcarr[:, j] = v
        m["const_fconsts"] = fcarr
        in_maps.append(m)
    res = run_bass_kernel_spmd(nc, in_maps, core_ids=list(range(8)))
    xo = np.stack([np.asarray(r["x_out"]) for r in res.results])
    for core in range(8):
        for name in ctx.extra_out:
            carry[core][name] = np.asarray(res.results[core][name])
    taps = {}
    for name in ctx.taps:
        taps[name] = np.stack([np.asarray(r[f"tap_{name}"])
                               for r in res.results])
    return xo, taps


FUSED = True


def kernel(**inputs):
    assert inputs["L0_attn_mask"].shape == (T, T)
    x = np.ascontiguousarray(inputs["L0_attn_x"]).astype(np.float32)
    if FUSED:
        x, _ = run_layers(inputs, [0, 1, "2a", "2b", 3], x)
        return x
    carry = [{} for _ in range(8)]
    for group in ([0], [1], ["2a"], ["2b"], [3]):
        x, _ = run_layers(inputs, group, x, carry=carry)
    return x


# version 06
# speedup vs baseline: 1.0389x; 1.0254x over previous
import contextlib
import numpy as np
import ml_dtypes

import concourse.bass as bass
import concourse.mybir as mybir
from concourse.bass_utils import run_bass_kernel_spmd

F32 = mybir.dt.float32
BF16 = mybir.dt.bfloat16
I32 = mybir.dt.int32
AF = mybir.ActivationFunctionType
ALU = mybir.AluOpType
AX = mybir.AxisListType

T = 2048
D = 1024
NT = T // 128
ND = D // 128

BF = ml_dtypes.bfloat16


class Prog:

    ENGINES = ("tensor", "vector", "scalar", "gpsimd", "sync")

    def __init__(self, nc):
        self.nc = nc
        self.ops = []
        self.buf = {}
        self.out_dmas = []

    @staticmethod
    def key(t):
        if isinstance(t, str):
            return t
        if hasattr(t, "tensor"):
            t = t.tensor
        return t.name

    def op(self, engine, fn, reads=(), writes=(), is_dma=False, is_out=False,
           extra_deps=()):
        opid = len(self.ops)
        deps = set(int(d) for d in extra_deps)
        rk = [self.key(t) for t in reads]
        wk = [self.key(t) for t in writes]
        for k in rk:
            st = self.buf.get(k)
            if st is not None and st["w"] is not None:
                deps.add(st["w"])
        for k in wk:
            st = self.buf.get(k)
            if st is not None:
                if st["w"] is not None:
                    deps.add(st["w"])
                deps.update(st["r"])
        for k in rk:
            st = self.buf.setdefault(k, {"w": None, "r": []})
            st["r"].append(opid)
        for k in wk:
            self.buf[k] = {"w": opid, "r": []}
        deps.discard(opid)
        self.ops.append(dict(engine=engine, fn=fn, deps=sorted(deps),
                             is_dma=is_dma))
        if is_out:
            self.out_dmas.append(opid)
        return opid

    def dma(self, engine, fn, reads=(), writes=(), is_out=False,
            extra_deps=()):
        return self.op(engine, fn, reads, writes, is_dma=True, is_out=is_out,
                       extra_deps=extra_deps)

    def full_barrier(self):
        last = {}
        for i, o in enumerate(self.ops):
            last[o["engine"]] = i
        deps = sorted(last.values())
        for e in self.ENGINES:
            self.op(e, lambda eng: eng.nop(), extra_deps=deps)

    def alias_barrier(self, old_names, new_name, extra=()):
        carry = set(extra)
        for k in old_names:
            st = self.buf.get(self.key(k))
            if st is not None:
                if st["w"] is not None:
                    carry.add(st["w"])
                carry.update(st["r"])
        self.buf[self.key(new_name)] = {"w": None, "r": sorted(carry)}

    def finalize(self):
        nc = self.nc
        n = len(self.ops)
        needs_inc = [False] * n
        for i, o in enumerate(self.ops):
            for d in o["deps"]:
                dop = self.ops[d]
                if (not dop["is_dma"] and dop["engine"] == "tensor"
                        and o["engine"] == "tensor" and not o["is_dma"]):
                    continue
                needs_inc[d] = True
        for i in self.out_dmas:
            needs_inc[i] = True
        for i, o in enumerate(self.ops):
            if o["is_dma"]:
                needs_inc[i] = True

        NSLOT = 8
        incidx = [0] * n
        dma_slot = [0] * n
        counters = {}
        dma_counters = {}
        for i, o in enumerate(self.ops):
            if o["is_dma"]:
                k = dma_counters.get(o["engine"], 0)
                dma_counters[o["engine"]] = k + 1
                slot = k % NSLOT
                dma_slot[i] = slot
                skey = (o["engine"], True, slot)
                counters[skey] = counters.get(skey, 0) + 1
                incidx[i] = counters[skey]
            elif needs_inc[i]:
                skey = (o["engine"], False)
                counters[skey] = counters.get(skey, 0) + 1
                incidx[i] = counters[skey]

        stack = contextlib.ExitStack()
        sems = {}
        for e in self.ENGINES:
            sems[(e, False)] = stack.enter_context(nc.semaphore(f"s_{e}"))
            for s in range(NSLOT):
                sems[(e, True, s)] = stack.enter_context(
                    nc.semaphore(f"d_{e}{s}"))

        per_engine = {e: [] for e in self.ENGINES}
        for i, o in enumerate(self.ops):
            per_engine[o["engine"]].append(i)

        ops = self.ops
        incs_total = sum(1 for x in needs_inc if x)
        watermark = {e: {} for e in self.ENGINES}

        def emit_engine(eng_name, eng):
            wm = watermark[eng_name]
            for i in per_engine[eng_name]:
                o = ops[i]
                for d in o["deps"]:
                    dop = ops[d]
                    if (not dop["is_dma"] and dop["engine"] == "tensor"
                            and eng_name == "tensor" and not o["is_dma"]):
                        continue
                    if dop["is_dma"]:
                        skey = (dop["engine"], True, dma_slot[d])
                    else:
                        skey = (dop["engine"], False)
                    tgt = incidx[d] * (16 if dop["is_dma"] else 1)
                    if tgt == 0:
                        raise RuntimeError(f"op {i} depends on {d}: no inc")
                    if wm.get(skey, 0) < tgt:
                        eng.wait_ge(sems[skey], tgt)
                        wm[skey] = tgt
                if o["is_dma"]:
                    skey = (eng_name, True, dma_slot[i])
                    prev = (incidx[i] - 1) * 16
                    if prev > 0 and wm.get(skey, 0) < prev:
                        eng.wait_ge(sems[skey], prev)
                        wm[skey] = prev
                inst = o["fn"](eng)
                if o["is_dma"]:
                    inst.then_inc(sems[(eng_name, True, dma_slot[i])], 16)
                elif needs_inc[i]:
                    inst.then_inc(sems[(eng_name, False)], 1)
            if eng_name == "sync":
                done = {}
                for i in self.out_dmas:
                    skey = (ops[i]["engine"], True, dma_slot[i])
                    done[skey] = max(done.get(skey, 0), incidx[i] * 16)
                for skey, tgt in sorted(done.items()):
                    if wm.get(skey, 0) < tgt:
                        eng.wait_ge(sems[skey], tgt)

        with nc.Block() as block:
            block.vector(lambda e: emit_engine("vector", e))
            block.scalar(lambda e: emit_engine("scalar", e))
            block.tensor(lambda e: emit_engine("tensor", e))
            block.gpsimd(lambda e: emit_engine("gpsimd", e))
            block.sync(lambda e: emit_engine("sync", e))
        stack.close()


_ISIZE = {F32: 4, BF16: 2, I32: 4}


class Ctx:
    def __init__(self, nc, p, dev_taps=()):
        self.nc = nc
        self.p = p
        self.sb = {}
        self.hbm = {}
        self.consts = {}
        self.consts32 = {}
        self.dev_taps = set(dev_taps)
        self.taps = {}
        self.fconst_vals = {}
        self.extra_in = {}
        self.extra_out = {}
        self.dyn_started = False
        self.freelist = []
        self.live = {}
        self.dead = []

    def start_dynamic(self):
        self.dyn_started = True
        base = self.nc.sbuf_base
        top = self.nc.sbuf_top
        self.freelist = [[base, top - base]]

    def alloc(self, name, shape, dtype):
        if not self.dyn_started:
            h = self.nc.alloc_sbuf_tensor(name, shape, dtype)
            self.sb[name] = h
            return h
        size = int(np.prod(shape[1:])) * _ISIZE[dtype] if len(shape) > 1 \
            else _ISIZE[dtype]
        size = (size + 31) // 32 * 32
        for blk in self.freelist:
            if blk[1] >= size:
                off = blk[0]
                blk[0] += size
                blk[1] -= size
                break
        else:
            raise RuntimeError(
                f"arena: no space for {name} ({size}B); "
                f"free={[(b[0], b[1]) for b in self.freelist]}")
        olds = []
        guards = set()
        newdead = []
        for rec in self.dead:
            (o, s, n, g) = rec
            if o < off + size and off < o + s:
                olds.append(n)
                guards.update(g)
                if o < off:
                    newdead.append((o, off - o, n, g))
                if o + s > off + size:
                    newdead.append((off + size, o + s - (off + size), n, g))
            else:
                newdead.append(rec)
        self.dead = newdead
        self._pending_guards = sorted(guards)
        h = self.nc.alloc_sbuf_tensor_at(name, list(shape), dtype,
                                         offset=off)
        self.sb[name] = h
        self.live[name] = (off, size, h.name)
        if not hasattr(self, "alloc_log"):
            self.alloc_log = []
        self.alloc_log.append((name, off, size, tuple(shape), dtype))
        if olds or self._pending_guards:
            self.p.alias_barrier(olds, h.name, extra=self._pending_guards)
        return h

    def free(self, *names):
        last = {}
        for i, o in enumerate(self.p.ops):
            last[o["engine"]] = i
        guard = tuple(sorted(last.values()))
        for name in names:
            off, size, mangled = self.live.pop(name)
            self.dead.append((off, size, mangled, guard))
            self.freelist.append([off, size])
            self.freelist.sort()
            merged = []
            for blk in self.freelist:
                if merged and merged[-1][0] + merged[-1][1] == blk[0]:
                    merged[-1][1] += blk[1]
                else:
                    merged.append(blk)
            self.freelist = merged

    def state_in(self, name, shape, dtype=F32):
        if name in getattr(self, "_states", {}):
            return self._states[name]
        ap = self.nc.dram_tensor(name, list(shape), dtype,
                                 kind="ExternalInput").ap()
        self.extra_in[name] = shape
        return ap

    def state_out(self, name, shape, dtype=F32):
        if getattr(self, "fused", False):
            if not hasattr(self, "_states"):
                self._states = {}
            ap = self.nc.dram_tensor(name, list(shape), dtype,
                                     kind="Internal").ap()
            self._states[name] = ap
            return ap
        ap = self.nc.dram_tensor(name, list(shape), dtype,
                                 kind="ExternalOutput").ap()
        self.extra_out[name] = shape
        return ap

    def fconst(self, value):
        v = float(value)
        if v not in self.fconst_vals:
            assert len(self.fconst_vals) < self.sb["fconsts"].shape[1]
            self.fconst_vals[v] = len(self.fconst_vals)
        j = self.fconst_vals[v]
        return self.sb["fconsts"][:, j:j + 1]

    def psum(self, name, shape, dtype=F32):
        h = self.nc.alloc_psum_tensor(name, shape, dtype)
        self.sb[name] = h
        return h

    def tap_full(self, name, sbuf_t):
        if name not in self.dev_taps:
            return
        sh = list(sbuf_t.shape)
        out = self.nc.dram_tensor(f"tap_{name}", sh, sbuf_t.dtype,
                                  kind="ExternalOutput").ap()
        self.p.dma("sync", lambda e: e.dma_start(out=out, in_=sbuf_t[:]),
                   reads=[sbuf_t], is_out=True)
        self.taps[name] = sh

    def tile_tap(self, name, tile_ap, tb, reads):
        if name not in self.dev_taps:
            return
        if name not in self.taps:
            self.nc.dram_tensor(f"tap_{name}", [128, NT, D], F32,
                                kind="ExternalOutput")
            self.taps[name] = [128, NT, D]
        dram = self.nc.get_tensor(f"tap_{name}") if hasattr(self.nc, "get_tensor") else None
        if not hasattr(self, "_tap_handles"):
            self._tap_handles = {}
        if name not in self._tap_handles:
            pass
        raise NotImplementedError


def stage_view(ctx, rows, cols):
    ws = ctx.sb["wstage"]
    assert rows <= 128 and cols <= 2048
    return ws[0:rows, 0:cols]


def load_convert(ctx, dst_bf_ap, hbm_ap, rows, cols, tag=""):
    p = ctx.p
    ws = ctx.sb["wstage"]
    sv = ws[0:rows, 0:cols]
    p.dma("sync", lambda e: e.dma_start(out=sv, in_=hbm_ap),
          reads=[], writes=[ws])
    p.op("gpsimd", lambda e: e.tensor_copy(out=dst_bf_ap, in_=sv),
         reads=[ws], writes=[dst_bf_ap])


def load_weight_bf16(ctx, name, hbm_ap, rows, cols):
    p = ctx.p
    rc = rows // 128
    bf = ctx.alloc(f"{name}_bf", [128, rc, cols], BF16)
    ws = ctx.sb["wstage"]
    piece = min(cols, 1024 // rc)
    for c0 in range(0, cols, piece):
        cw = min(piece, cols - c0)
        sv = ws[:, 0:rc * cw].rearrange("p (c m) -> p c m", c=rc)
        hb = hbm_ap[:, c0:c0 + cw].rearrange("(c p) m -> p c m", p=128)
        p.dma("sync", lambda e, sv=sv, hb=hb: e.dma_start(out=sv, in_=hb),
              reads=[], writes=[ws])
        dst = bf[:, :, c0:c0 + cw]
        p.op("gpsimd", lambda e, dst=dst, sv=sv: e.tensor_copy(
            out=dst, in_=sv), reads=[ws], writes=[bf])
    return bf


def pe_transpose(ctx, out_ap, in_ap, n_in_part, n_in_free):
    p = ctx.p
    pst = ctx.sb["ps_tr"]
    ident = ctx.consts["ident"]
    p.op("tensor", lambda e: e.transpose(
        pst.bitcast(BF16)[0:n_in_free, 0:n_in_part], in_ap,
        ident[0:n_in_part, 0:n_in_part]),
        reads=[in_ap, ident], writes=[pst])
    p.op("vector", lambda e: e.tensor_copy(
        out=out_ap, in_=pst.bitcast(BF16)[0:n_in_free, 0:n_in_part]),
        reads=[pst], writes=[out_ap])


def emit_ln(ctx):
    p = ctx.p
    x = ctx.sb["x_tm"]
    xn_fm = ctx.sb["xn_fm"]
    mu = ctx.sb["ln_mu"]
    rstd = ctx.sb["ln_rstd"]
    cen = ctx.sb["scratch_f32_a"]
    sq = ctx.sb["scratch_f32_b"]
    stage = ctx.sb["stage_bf"]
    col = ctx.sb["scratch_col"]
    for i in range(NT):
        xt = x[:, i, :]
        mcol = mu[:, i:i + 1]
        rcol = rstd[:, i:i + 1]
        p.op("vector", lambda e, xt=xt: e.reduce_sum(
            out=col[:, 0:1], in_=xt, axis=AX.X), reads=[x], writes=[col])
        p.op("vector", lambda e, mcol=mcol: e.tensor_scalar_mul(
            out=mcol, in0=col[:, 0:1], scalar1=1.0 / D),
            reads=[col], writes=[mu])
        p.op("vector", lambda e, xt=xt, mcol=mcol: e.tensor_scalar(
            out=cen[:], in0=xt, scalar1=mcol, scalar2=None,
            op0=ALU.subtract), reads=[x, mu], writes=[cen])
        p.op("scalar", lambda e: e.activation(
            out=sq[:], in_=cen[:], func=AF.Square), reads=[cen], writes=[sq])
        p.op("vector", lambda e: e.reduce_sum(
            out=col[:, 1:2], in_=sq[:], axis=AX.X), reads=[sq], writes=[col])
        epsc = ctx.fconst(1e-5)
        p.op("scalar", lambda e, epsc=epsc: e.activation(
            out=col[:, 2:3], in_=col[:, 1:2], func=AF.Sqrt,
            bias=epsc, scale=1.0 / D),
            reads=[col, "fconsts"], writes=[col])
        p.op("vector", lambda e, rcol=rcol: e.reciprocal(
            out=rcol, in_=col[:, 2:3]), reads=[col], writes=[rstd])
        p.op("scalar", lambda e, rcol=rcol: e.activation(
            out=stage[:], in_=cen[:], func=AF.Copy, scale=rcol),
            reads=[cen, rstd], writes=[stage])
        for c in range(ND):
            pe_transpose(ctx, xn_fm[:, c, i * 128:(i + 1) * 128],
                         stage[:, c * 128:(c + 1) * 128], 128, 128)


def xn_rec(ctx, i, out_tile, half=None):
    p = ctx.p
    x = ctx.sb["x_tm"]
    mu = ctx.sb["ln_mu"]
    rstd = ctx.sb["ln_rstd"]
    src = x[:, i, :] if half is None else x[:, i, half * 512:(half + 1) * 512]
    p.op("vector", lambda e: e.tensor_scalar(
        out=out_tile, in0=src, scalar1=mu[:, i:i + 1],
        scalar2=rstd[:, i:i + 1], op0=ALU.subtract, op1=ALU.mult),
        reads=[x, mu, rstd], writes=[out_tile])



def flash256(ctx, Qp, Kp_fn, V_fn, bias_a, bias_b, o_sink, kts_for_qb,
             mask_for, scale=0.125):
    p = ctx.p
    PT = [ctx.sb["fl_PT0"], ctx.sb["fl_PT1"]]
    ps_s = [[ctx.sb["ps_sa0"], ctx.sb["ps_sa1"]],
            [ctx.sb["ps_sb0"], ctx.sb["ps_sb1"]]]
    ps_ava = ctx.sb["ps_ava"]
    ps_avb = ctx.sb["ps_avb"]
    rc_a = ctx.sb["fl_rc_a"]
    rc_b = ctx.sb["fl_rc_b"]
    for qb in range(8):
        qsl = slice(qb * 256, (qb + 1) * 256)
        kts = kts_for_qb(qb)
        for gi, g0 in enumerate(range(0, len(kts), 2)):
            grp = kts[g0:g0 + 2]
            bank = gi % 2
            for j, kt in enumerate(grp):
                ksl = slice(kt * 128, (kt + 1) * 128)
                psl = slice(j * 256, (j + 1) * 256)
                p.op("tensor", lambda e, ksl=ksl, qsl=qsl, psl=psl,
                     bank=bank: e.matmul(
                         ps_s[0][bank][:, psl], Kp_fn(0, ksl),
                         Qp[0:64, qsl],
                         start=True, stop=True, tile_position=(0, 0)),
                     reads=[ctx.sb["fl_K0"], Qp], writes=[ps_s[0][bank]])
                p.op("tensor", lambda e, ksl=ksl, qsl=qsl, psl=psl,
                     bank=bank: e.matmul(
                         ps_s[1][bank][:, psl], Kp_fn(1, ksl),
                         Qp[64:128, qsl],
                         start=True, stop=True, tile_position=(64, 0)),
                     reads=[ctx.sb["fl_K1"], Qp], writes=[ps_s[1][bank]])
            n = len(grp)
            for h2, biast in enumerate((bias_a, bias_b)):
                pssrc = ps_s[h2][bank]
                p.op("scalar", lambda e, h2=h2, pssrc=pssrc, biast=biast,
                     n=n, g0=g0: e.activation(
                         out=PT[h2][:, g0:g0 + n, :],
                         in_=pssrc[:, 0:n * 256].rearrange(
                             "p (k q) -> p k q", k=n),
                         func=AF.Exp, bias=biast[:], scale=scale),
                     reads=[pssrc, biast], writes=[PT[h2]])
        for j, kt in enumerate(kts):
            for qs in range(2):
                m = mask_for(qb, kt, qs)
                if m is None or isinstance(m, str):
                    continue
                for h2 in range(2):
                    p.op("vector", lambda e, h2=h2, j=j, qs=qs, m=m:
                         e.tensor_tensor(
                             out=PT[h2][:, j, qs * 128:(qs + 1) * 128],
                             in0=PT[h2][:, j, qs * 128:(qs + 1) * 128],
                             in1=m[:], op=ALU.mult),
                         reads=[PT[h2], "c_tril_incl", "c_trilx"],
                         writes=[PT[h2]])
        for qs in range(2):
            qtile = qb * 2 + qs
            for h2, (psav, rc) in enumerate(((ps_ava, rc_a),
                                             (ps_avb, rc_b))):
                valid = [(j, kt) for j, kt in enumerate(kts)
                         if not isinstance(mask_for(qb, kt, qs), str)]
                for vi, (j, kt) in enumerate(valid):
                    p.op("tensor", lambda e, h2=h2, psav=psav, j=j, kt=kt,
                         qs=qs, first=(vi == 0), last=(vi == len(valid) - 1):
                         e.matmul(
                             psav[:, 0:65],
                             PT[h2][:, j, qs * 128:(qs + 1) * 128],
                             V_fn(h2, kt), start=first, stop=last),
                         reads=[PT[h2], ctx.sb["fl_V"]], writes=[psav])
                p.op("vector", lambda e, psav=psav, rc=rc: e.reciprocal(
                    out=rc[:], in_=psav[:, 64:65]), reads=[psav], writes=[rc])
            o_sink(qtile, ps_ava, ps_avb, rc_a, rc_b)


def emit_layer0(ctx):
    p = ctx.p
    emit_ln(ctx)
    xn_fm = ctx.sb["xn_fm"]
    ident = ctx.consts["ident"]
    onesrow = ctx.consts["onesrow"]

    ps_ava = ctx.sb["ps_ava"]
    ps_avb = ctx.sb["ps_avb"]
    ps_m1 = ctx.sb["ps_m1"]
    ps_m2 = ctx.sb["ps_m2"]
    rw_bf = ctx.alloc("l0_rw_bf", [8, 1024], BF16)
    load_convert(ctx, rw_bf[:], ctx.hbm["L0_ffn_router_w"], 8, 1024)
    rwT = ctx.alloc("l0_rwT", [128, 8, 8], BF16)
    for c in range(8):
        pe_transpose(ctx, rwT[:, c, :], rw_bf[:, c * 128:(c + 1) * 128],
                     8, 128)
    for c in range(8):
        p.op("tensor", lambda e, c=c: e.matmul(
            ps_m1[0:1, 0:8], xn_fm[:, c, 0:1], rwT[:, c, :],
            start=(c == 0), stop=(c == 7)),
            reads=[xn_fm, rwT], writes=[ps_m1])
    gl = ctx.alloc("l0_gates", [1, 64], F32)
    p.op("vector", lambda e: e.tensor_copy(out=gl[:, 0:8],
                                           in_=ps_m1[0:1, 0:8]),
         reads=[ps_m1], writes=[gl])
    p.op("vector", lambda e: e.reduce_max(out=gl[:, 8:9], in_=gl[:, 0:8],
                                          axis=AX.X), reads=[gl], writes=[gl])
    p.op("vector", lambda e: e.tensor_scalar(
        out=gl[:, 16:24], in0=gl[:, 0:8], scalar1=gl[:, 8:9], scalar2=None,
        op0=ALU.is_equal), reads=[gl], writes=[gl])
    p.op("vector", lambda e: e.scalar_tensor_tensor(
        out=gl[:, 24:32], in0=gl[:, 16:24], scalar=-1e30, in1=gl[:, 0:8],
        op0=ALU.mult, op1=ALU.add), reads=[gl], writes=[gl])
    p.op("vector", lambda e: e.reduce_max(out=gl[:, 9:10], in_=gl[:, 24:32],
                                          axis=AX.X), reads=[gl], writes=[gl])
    p.op("vector", lambda e: e.tensor_scalar(
        out=gl[:, 32:40], in0=gl[:, 0:8], scalar1=gl[:, 9:10], scalar2=None,
        op0=ALU.is_ge), reads=[gl], writes=[gl])
    p.op("vector", lambda e: e.tensor_scalar(
        out=gl[:, 40:48], in0=gl[:, 0:8], scalar1=gl[:, 8:9], scalar2=None,
        op0=ALU.subtract), reads=[gl], writes=[gl])
    p.op("scalar", lambda e: e.activation(
        out=gl[:, 48:56], in_=gl[:, 40:48], func=AF.Exp),
        reads=[gl], writes=[gl])
    p.op("vector", lambda e: e.tensor_mul(
        out=gl[:, 48:56], in0=gl[:, 48:56], in1=gl[:, 32:40]),
        reads=[gl], writes=[gl])
    p.op("vector", lambda e: e.reduce_sum(out=gl[:, 10:11], in_=gl[:, 48:56],
                                          axis=AX.X), reads=[gl], writes=[gl])
    p.op("vector", lambda e: e.reciprocal(out=gl[:, 11:12], in_=gl[:, 10:11]),
         reads=[gl], writes=[gl])
    p.op("vector", lambda e: e.tensor_scalar(
        out=gl[:, 56:64], in0=gl[:, 48:56], scalar1=gl[:, 11:12],
        scalar2=None, op0=ALU.mult), reads=[gl], writes=[gl])
    g_bf = ctx.alloc("l0_g_bf", [1, 8], BF16)
    p.op("vector", lambda e: e.tensor_copy(out=g_bf[:], in_=gl[:, 56:64]),
         reads=[gl], writes=[g_bf])
    gcol8 = ctx.alloc("l0_gcol8", [8, 1], BF16)
    pe_transpose(ctx, gcol8[:], g_bf[:], 1, 8)
    e16 = ctx.consts["e16sel"]
    p.op("tensor", lambda e: e.matmul(ps_m1[:, 0:1], e16[:], gcol8[:],
                                      start=True, stop=True),
         reads=[e16, gcol8], writes=[ps_m1])
    gcol = ctx.alloc("l0_gcol", [128, 1], F32)
    p.op("vector", lambda e: e.tensor_copy(out=gcol[:], in_=ps_m1[:, 0:1]),
         reads=[ps_m1], writes=[gcol])

    ld_bf = ctx.alloc("l0_ld_bf", [128, 1024], BF16)
    load_convert(ctx, ld_bf[:],
                 ctx.hbm["L0_ffn_lora_down"].rearrange("e r h -> (e r) h"),
                 128, 1024)
    ldT = ctx.alloc("l0_ldT", [128, 8, 128], BF16)
    for c in range(8):
        pe_transpose(ctx, ldT[:, c, :], ld_bf[:, c * 128:(c + 1) * 128],
                     128, 128)
    down_fm = ctx.alloc("l0_down_fm", [128, T], BF16)
    for tb in range(4):
        tsl = slice(tb * 512, (tb + 1) * 512)
        for c in range(8):
            p.op("tensor", lambda e, tsl=tsl, c=c: e.matmul(
                ps_m1[:, :], ldT[:, c, :], xn_fm[:, c, tsl],
                start=(c == 0), stop=(c == 7)),
                reads=[ldT, xn_fm], writes=[ps_m1])
        dst = down_fm[:, tsl]
        p.op("vector", lambda e, dst=dst: e.tensor_scalar(
            out=dst, in0=ps_m1[:, :], scalar1=gcol[:], scalar2=None,
            op0=ALU.mult), reads=[ps_m1, gcol], writes=[down_fm])

    up_bf = ctx.alloc("l0_up_bf", [128, 64, 16], BF16)
    ws = ctx.sb["wstage"]
    sv_up = ws[:, 0:1024].rearrange("p (c r) -> p c r", r=16)
    hb_up = ctx.hbm["L0_ffn_lora_up"].rearrange(
        "e h r -> (e h) r").rearrange("(c p) r -> p c r", p=128)
    p.dma("sync", lambda e: e.dma_start(out=sv_up, in_=hb_up), writes=[ws])
    p.op("gpsimd", lambda e: e.tensor_copy(out=up_bf[:], in_=sv_up),
         reads=[ws], writes=[up_bf])
    upstage = ctx.alloc("l0_upstage", [16, 8, 128], BF16)
    upg = ctx.alloc("l0_upg", [128, D], BF16)
    for ee in range(8):
        for j in range(8):
            pe_transpose(ctx, upstage[:, j, :], up_bf[:, ee * 8 + j, :],
                         128, 16)
        p.dma("sync", lambda e, ee=ee: e.dma_start(
            out=upg[ee * 16:(ee + 1) * 16, :], in_=upstage[:]),
            reads=[upstage], writes=[upg])

    ctx.free("l0_ld_bf", "l0_ldT", "l0_up_bf", "l0_upstage", "l0_rw_bf",
             "l0_rwT", "l0_gates", "l0_g_bf", "l0_gcol8")
    O_fm = ctx.alloc("l0_O_fm", [128, ND, T], BF16)
    Qp = ctx.alloc("l0_Qp", [128, T], BF16)
    Kp = ctx.alloc("l0_Kp", [128, T], BF16)
    Vp = ctx.alloc("l0_Vp", [128, NT, 130], BF16)
    ctx.alloc("fl_PT0", [128, 16, 256], BF16)
    ctx.alloc("fl_PT1", [128, 16, 256], BF16)
    ctx.alloc("fl_rc_a", [128, 1], F32)
    ctx.alloc("fl_rc_b", [128, 1], F32)
    wq_bf = ctx.alloc("l0_wq_bf", [128, 8, 128], BF16)
    wk_bf = ctx.alloc("l0_wk_bf", [128, 8, 128], BF16)
    wv_bf = ctx.alloc("l0_wv_bf", [128, 8, 128], BF16)
    nsc = ctx.alloc("l0_nsc", [1, 64], F32)
    nsc_bf = ctx.alloc("l0_nscbf", [1, 2], BF16)
    bias_a = ctx.alloc("l0_bias_a", [128, 1], F32)
    bias_b = ctx.alloc("l0_bias_b", [128, 1], F32)
    sqst = ctx.sb["stage_bf"]
    onesA = ctx.consts["onesA"]
    onesB = ctx.consts["onesB"]
    stageO = ctx.alloc("l0_stageO", [128, 128], BF16)

    for hp in range(8):
        for wname, wbf in (("L0_attn_Wq", wq_bf), ("L0_attn_Wk", wk_bf),
                           ("L0_attn_Wv", wv_bf)):
            hb = ctx.hbm[wname][:, hp * 128:(hp + 1) * 128].rearrange(
                "(c p) m -> p c m", p=128)
            ws = ctx.sb["wstage"]
            sv = ws[:, 0:1024].rearrange("p (c m) -> p c m", c=8)
            p.dma("sync", lambda e, hb=hb, sv=sv: e.dma_start(
                out=sv, in_=hb), writes=[ws])
            p.op("gpsimd", lambda e, wbf=wbf, sv=sv: e.tensor_copy(
                out=wbf[:], in_=sv), reads=[ws], writes=[wbf])
        for (wbf, dstp) in ((wq_bf, Qp), (wk_bf, Kp)):
            for tb in range(4):
                tsl = slice(tb * 512, (tb + 1) * 512)
                for c in range(8):
                    p.op("tensor", lambda e, wbf=wbf, tsl=tsl, c=c: e.matmul(
                        ps_m2[:, :], wbf[:, c, :], xn_fm[:, c, tsl],
                        start=(c == 0), stop=(c == 7)),
                        reads=[wbf, xn_fm], writes=[ps_m2])
                dst = dstp[:, tsl]
                p.op("vector", lambda e, dst=dst: e.tensor_copy(
                    out=dst, in_=ps_m2[:, :]),
                    reads=[ps_m2], writes=[dstp])
        for tb in range(NT):
            for c in range(8):
                p.op("tensor", lambda e, tb=tb, c=c: e.matmul(
                    ps_m2[:, 0:128],
                    xn_fm[:, c, tb * 128:(tb + 1) * 128],
                    wv_bf[:, c, :], start=(c == 0), stop=(c == 7)),
                    reads=[wv_bf, xn_fm], writes=[ps_m2])
            p.op("vector", lambda e, tb=tb: e.tensor_copy(
                out=Vp[:, tb, 0:64], in_=ps_m2[:, 0:64]),
                reads=[ps_m2], writes=[Vp])
            p.op("vector", lambda e, tb=tb: e.tensor_copy(
                out=Vp[:, tb, 65:129], in_=ps_m2[:, 64:128]),
                reads=[ps_m2], writes=[Vp])
        p.op("vector", lambda e: e.memset(Vp[:, :, 64:65], 1.0), writes=[Vp])
        p.op("vector", lambda e: e.memset(Vp[:, :, 129:130], 1.0),
             writes=[Vp])

        for si, src in enumerate((Qp, Kp)):
            for tb in range(4):
                sl = slice(tb * 512, (tb + 1) * 512)
                p.op("scalar", lambda e, src=src, sl=sl: e.activation(
                    out=sqst[:, 0:512], in_=src[:, sl], func=AF.Square),
                    reads=[src], writes=[sqst])
                for h, ones in enumerate((onesA, onesB)):
                    p.op("tensor", lambda e, ones=ones: e.matmul(
                        ps_m1[0:1, :], ones[:], sqst[:, 0:512],
                        start=True, stop=True),
                        reads=[sqst, ones], writes=[ps_m1])
                    o = (si * 2 + h) * 4 + tb
                    p.op("vector", lambda e, o=o: e.reduce_max(
                        out=nsc[:, o:o + 1], in_=ps_m1[0:1, :], axis=AX.X),
                        reads=[ps_m1], writes=[nsc])
        for h in range(2):
            o = 16 + h * 8
            qs4 = slice(h * 4, h * 4 + 4)
            ks4 = slice(8 + h * 4, 8 + h * 4 + 4)
            p.op("vector", lambda e, o=o, qs4=qs4: e.reduce_max(
                out=nsc[:, o:o + 1], in_=nsc[:, qs4], axis=AX.X),
                reads=[nsc], writes=[nsc])
            p.op("vector", lambda e, o=o, ks4=ks4: e.reduce_max(
                out=nsc[:, o + 1:o + 2], in_=nsc[:, ks4], axis=AX.X),
                reads=[nsc], writes=[nsc])
            p.op("vector", lambda e, o=o: e.tensor_mul(
                out=nsc[:, o + 2:o + 3], in0=nsc[:, o:o + 1],
                in1=nsc[:, o + 1:o + 2]), reads=[nsc], writes=[nsc])
            p.op("scalar", lambda e, o=o: e.activation(
                out=nsc[:, o + 3:o + 4], in_=nsc[:, o + 2:o + 3],
                func=AF.Sqrt), reads=[nsc], writes=[nsc])
            p.op("vector", lambda e, o=o: e.tensor_scalar_mul(
                out=nsc[:, o + 4:o + 5], in0=nsc[:, o + 3:o + 4],
                scalar1=-0.125), reads=[nsc], writes=[nsc])
        p.op("vector", lambda e: e.tensor_copy(out=nsc_bf[:, 0:1],
                                               in_=nsc[:, 20:21]),
             reads=[nsc], writes=[nsc_bf])
        p.op("vector", lambda e: e.tensor_copy(out=nsc_bf[:, 1:2],
                                               in_=nsc[:, 28:29]),
             reads=[nsc], writes=[nsc_bf])
        for biast, cs in ((bias_a, slice(0, 1)), (bias_b, slice(1, 2))):
            p.op("tensor", lambda e, cs=cs: e.matmul(
                ps_m1[:, 0:1], onesrow[:], nsc_bf[:, cs],
                start=True, stop=True),
                reads=[nsc_bf, onesrow], writes=[ps_m1])
            p.op("vector", lambda e, biast=biast: e.tensor_copy(
                out=biast[:], in_=ps_m1[:, 0:1]),
                reads=[ps_m1], writes=[biast])

        ctx.sb["fl_K0"] = Kp
        ctx.sb["fl_K1"] = Kp
        ctx.sb["fl_V"] = Vp

        def l0_sink(qtile, psa, psb, rca, rcb, hp=hp):
            p.op("vector", lambda e: e.tensor_scalar(
                out=stageO[:, 0:64], in0=psa[:, 0:64], scalar1=rca[:],
                scalar2=None, op0=ALU.mult),
                reads=[psa, rca], writes=[stageO])
            p.op("vector", lambda e: e.tensor_scalar(
                out=stageO[:, 64:128], in0=psb[:, 0:64], scalar1=rcb[:],
                scalar2=None, op0=ALU.mult),
                reads=[psb, rcb], writes=[stageO])
            pe_transpose(ctx, O_fm[:, hp, qtile * 128:(qtile + 1) * 128],
                         stageO[:], 128, 128)

        flash256(ctx, Qp,
                 lambda h2, ksl: Kp[h2 * 64:(h2 + 1) * 64, ksl],
                 lambda h2, kt: Vp[:, kt, h2 * 65:(h2 + 1) * 65],
                 bias_a, bias_b, l0_sink,
                 lambda qb: list(range(16)),
                 lambda qb, kt, qs: None)

    ctx.free("fl_PT0", "fl_PT1", "fl_rc_a", "fl_rc_b")
    ctx.sb["scratch_f32_c"] = ctx.alloc("l0_scrc", [128, D], F32)
    wout_bf = load_weight_bf16(ctx, "l0_wout", ctx.hbm["L0_attn_Wout"],
                               1024, 1024)
    x = ctx.sb["x_tm"]
    xnr = ctx.sb["scratch_f32_a"]
    sbuf_s = ctx.sb["scratch_f32_c"]
    tmp = ctx.sb["scratch_f32_b"]
    ps_z = ctx.sb["ps_m2"]
    for tb in range(NT):
        xn_rec(ctx, tb, xnr[:])
        for half in range(2):
            esl = slice(half * 512, (half + 1) * 512)
            for c in range(8):
                p.op("tensor", lambda e, tb=tb, c=c, esl=esl: e.matmul(
                    ps_z[:, :], O_fm[:, c, tb * 128:(tb + 1) * 128],
                    wout_bf[:, c, esl], start=(c == 0), stop=(c == 7)),
                    reads=[O_fm, wout_bf], writes=[ps_z])
            p.op("vector", lambda e, esl=esl: e.scalar_tensor_tensor(
                out=sbuf_s[:, esl], in0=ps_z[:, :], scalar=1.0,
                in1=xnr[:, esl], op0=ALU.mult, op1=ALU.add),
                reads=[ps_z, xnr], writes=[sbuf_s])
        for half in range(2):
            hsl = slice(half * 512, (half + 1) * 512)
            p.op("tensor", lambda e, tb=tb, hsl=hsl: e.matmul(
                ps_m1[:, :], down_fm[:, tb * 128:(tb + 1) * 128],
                upg[:, hsl], start=True, stop=True),
                reads=[down_fm, upg], writes=[ps_m1])
            p.op("vector", lambda e, hsl=hsl: e.scalar_tensor_tensor(
                out=tmp[:, hsl], in0=ps_m1[:, :], scalar=1.0,
                in1=xnr[:, hsl], op0=ALU.mult, op1=ALU.add),
                reads=[ps_m1, xnr], writes=[tmp])
        p.op("scalar", lambda e: e.activation(
            out=tmp[:], in_=tmp[:], func=AF.Silu), reads=[tmp], writes=[tmp])
        p.op("vector", lambda e, tb=tb: e.tensor_mul(
            out=tmp[:], in0=tmp[:], in1=sbuf_s[:]),
            reads=[tmp, sbuf_s], writes=[tmp])
        p.op("vector", lambda e, tb=tb: e.tensor_add(
            out=x[:, tb, :], in0=x[:, tb, :], in1=tmp[:]),
            reads=[tmp, x], writes=[x])
    ctx.free(*[n for n in list(ctx.live) if n.startswith("l0_")])


def bcast_col(ctx, row_ap, out_col, reads):
    p = ctx.p
    tmp = ctx.sb["bc_bf"]
    ps = ctx.sb["ps_m1"]
    p.op("vector", lambda e: e.tensor_copy(out=tmp[:], in_=row_ap),
         reads=list(reads), writes=[tmp])
    p.op("tensor", lambda e: e.matmul(ps[:, 0:1], ctx.consts["onesrow"][:],
                                      tmp[:], start=True, stop=True),
         reads=[tmp, "c_onesrow"], writes=[ps])
    p.op("vector", lambda e: e.tensor_copy(out=out_col, in_=ps[:, 0:1]),
         reads=[ps], writes=[out_col])


def cmul_col(ctx, outr, outi, ar, ai, br, bi):
    p = ctx.p
    w = ctx.sb["l1_cmul_w"]
    p.op("vector", lambda e: e.tensor_mul(out=w[:, 0:1], in0=ar, in1=br),
         reads=[ar, br], writes=[w])
    p.op("vector", lambda e: e.tensor_mul(out=w[:, 1:2], in0=ai, in1=bi),
         reads=[ai, bi], writes=[w])
    p.op("vector", lambda e: e.tensor_mul(out=w[:, 2:3], in0=ar, in1=bi),
         reads=[ar, bi], writes=[w])
    p.op("vector", lambda e: e.tensor_mul(out=w[:, 3:4], in0=ai, in1=br),
         reads=[ai, br], writes=[w])
    p.op("vector", lambda e: e.tensor_sub(out=outr, in0=w[:, 0:1],
                                          in1=w[:, 1:2]),
         reads=[w], writes=[outr])
    p.op("vector", lambda e: e.tensor_add(out=outi, in0=w[:, 2:3],
                                          in1=w[:, 3:4]),
         reads=[w], writes=[outi])


def emit_layer1(ctx):
    p = ctx.p
    emit_ln(ctx)
    xn_fm = ctx.sb["xn_fm"]
    ps_m1 = ctx.sb["ps_m1"]
    ps_m2 = ctx.sb["ps_m2"]
    ps_big = ctx.sb["ps_sa0"]
    ctx.alloc("bc_bf", [1, 1], BF16)
    ctx.alloc("l1_cmul_w", [64, 4], F32)

    ac = ctx.alloc("l1_acols", [64, 40], F32)
    p.dma("sync", lambda e: e.dma_start(
        out=ac[:, 0:1], in_=ctx.hbm["L1_attn_log_A_real"][:, None]),
        writes=[ac])
    p.dma("sync", lambda e: e.dma_start(
        out=ac[:, 1:2], in_=ctx.hbm["L1_attn_log_A_imag"][:, None]),
        writes=[ac])
    p.op("scalar", lambda e: e.activation(out=ac[:, 2:3], in_=ac[:, 0:1],
                                          func=AF.Exp),
         reads=[ac], writes=[ac])
    p.op("vector", lambda e: e.tensor_scalar_mul(
        out=ac[:, 2:3], in0=ac[:, 2:3], scalar1=-0.5),
        reads=[ac], writes=[ac])
    p.op("vector", lambda e: e.tensor_scalar_mul(
        out=ac[:, 3:4], in0=ac[:, 1:2], scalar1=0.5),
        reads=[ac], writes=[ac])
    p.op("vector", lambda e: e.tensor_scalar(
        out=ac[:, 4:5], in0=ac[:, 2:3], scalar1=1.0, scalar2=-1.0,
        op0=ALU.subtract, op1=ALU.mult), reads=[ac], writes=[ac])
    p.op("vector", lambda e: e.tensor_mul(out=ac[:, 5:6], in0=ac[:, 4:5],
                                          in1=ac[:, 4:5]),
         reads=[ac], writes=[ac])
    p.op("vector", lambda e: e.tensor_mul(out=ac[:, 6:7], in0=ac[:, 3:4],
                                          in1=ac[:, 3:4]),
         reads=[ac], writes=[ac])
    p.op("vector", lambda e: e.tensor_add(out=ac[:, 5:6], in0=ac[:, 5:6],
                                          in1=ac[:, 6:7]),
         reads=[ac], writes=[ac])
    p.op("vector", lambda e: e.reciprocal(out=ac[:, 7:8], in_=ac[:, 5:6]),
         reads=[ac], writes=[ac])
    p.op("vector", lambda e: e.tensor_mul(out=ac[:, 8:9], in0=ac[:, 2:3],
                                          in1=ac[:, 2:3]),
         reads=[ac], writes=[ac])
    p.op("vector", lambda e: e.tensor_add(out=ac[:, 8:9], in0=ac[:, 8:9],
                                          in1=ac[:, 6:7]),
         reads=[ac], writes=[ac])
    p.op("vector", lambda e: e.tensor_scalar(
        out=ac[:, 8:9], in0=ac[:, 8:9], scalar1=1.0, scalar2=-1.0,
        op0=ALU.subtract, op1=ALU.mult), reads=[ac], writes=[ac])
    p.op("vector", lambda e: e.tensor_mul(out=ac[:, 10:11], in0=ac[:, 8:9],
                                          in1=ac[:, 7:8]),
         reads=[ac], writes=[ac])
    p.op("vector", lambda e: e.tensor_scalar_mul(
        out=ac[:, 11:12], in0=ac[:, 3:4], scalar1=2.0),
        reads=[ac], writes=[ac])
    p.op("vector", lambda e: e.tensor_mul(out=ac[:, 11:12], in0=ac[:, 11:12],
                                          in1=ac[:, 7:8]),
         reads=[ac], writes=[ac])
    p.op("vector", lambda e: e.tensor_copy(out=ac[:, 12:13],
                                           in_=ac[:, 10:11]),
         reads=[ac], writes=[ac])
    p.op("vector", lambda e: e.tensor_copy(out=ac[:, 13:14],
                                           in_=ac[:, 11:12]),
         reads=[ac], writes=[ac])
    for j in range(1, 11):
        o, q = 12 + 2 * j, 12 + 2 * (j - 1)
        cmul_col(ctx, ac[:, o:o + 1], ac[:, o + 1:o + 2],
                 ac[:, q:q + 1], ac[:, q + 1:q + 2],
                 ac[:, q:q + 1], ac[:, q + 1:q + 2])
    nAim = ctx.alloc("l1_nAim", [64, 11], F32)
    for j in range(11):
        p.op("vector", lambda e, j=j: e.tensor_scalar_mul(
            out=nAim[:, j:j + 1], in0=ac[:, 13 + 2 * j:14 + 2 * j],
            scalar1=-1.0), reads=[ac], writes=[nAim])
    Fre = ctx.alloc("l1_Fre", [64, 16], F32)
    Fim = ctx.alloc("l1_Fim", [64, 16], F32)
    p.op("vector", lambda e: e.tensor_copy(out=Fre[:, 0:1], in_=ac[:, 12:13]),
         reads=[ac], writes=[Fre])
    p.op("vector", lambda e: e.tensor_copy(out=Fim[:, 0:1], in_=ac[:, 13:14]),
         reads=[ac], writes=[Fim])
    for i in range(1, 16):
        cmul_col(ctx, Fre[:, i:i + 1], Fim[:, i:i + 1],
                 Fre[:, i - 1:i], Fim[:, i - 1:i],
                 ac[:, 12:13], ac[:, 13:14])

    bw_bf = ctx.alloc("l1_bw_bf", [128, 1024], BF16)
    load_convert(ctx, bw_bf[:], ctx.hbm["L1_attn_B_w"], 128, 1024)
    bwT = ctx.alloc("l1_bwT", [128, 8, 128], BF16)
    for c in range(8):
        pe_transpose(ctx, bwT[:, c, :], bw_bf[:, c * 128:(c + 1) * 128],
                     128, 128)
    SRe = [ctx.alloc(f"l1_sre{k}", [64, T], F32) for k in range(2)]
    SIm = [ctx.alloc(f"l1_sim{k}", [64, T], F32) for k in range(2)]
    hstmp = ctx.alloc("l1_hstmp", [64, T], F32)
    ps_pair = [ps_big, ctx.sb["ps_sa1"]]
    for tb in range(4):
        tsl = slice(tb * 512, (tb + 1) * 512)
        psb = ps_pair[tb % 2]
        for c in range(8):
            p.op("tensor", lambda e, tsl=tsl, c=c, psb=psb: e.matmul(
                psb[:, 0:512], bwT[:, c, :], xn_fm[:, c, tsl],
                start=(c == 0), stop=(c == 7)),
                reads=[bwT, xn_fm], writes=[psb])
        p.op("vector", lambda e, tsl=tsl, psb=psb: e.tensor_copy(
            out=SRe[0][:, tsl], in_=psb[0:64, 0:512]),
            reads=[psb], writes=[SRe[0]])
        p.op("vector", lambda e, tsl=tsl, psb=psb: e.tensor_copy(
            out=SIm[0][:, tsl], in_=psb[64:128, 0:512]),
            reads=[psb], writes=[SIm[0]])

    def hs_pass(bufs_s, bufs_d, s, width, j):
        re_s, im_s = bufs_s
        re_d, im_d = bufs_d
        are = ac[:, 12 + 2 * j:13 + 2 * j]
        aim = ac[:, 13 + 2 * j:14 + 2 * j]
        naim = nAim[:, j:j + 1]
        n = re_s.shape[1]

        def v(t):
            if width == n:
                return t[:][:, None, :]
            return t[:].rearrange("p (c w) -> p c w", w=width)
        p.op("vector", lambda e: e.tensor_copy(
            out=v(re_d)[:, :, 0:s], in_=v(re_s)[:, :, 0:s]),
            reads=[re_s], writes=[re_d])
        p.op("vector", lambda e: e.tensor_copy(
            out=v(im_d)[:, :, 0:s], in_=v(im_s)[:, :, 0:s]),
            reads=[im_s], writes=[im_d])
        p.op("vector", lambda e: e.scalar_tensor_tensor(
            out=v(hstmp)[:, 0:n // width, 0:width - s],
            in0=v(re_s)[:, :, 0:width - s],
            scalar=are, in1=v(re_s)[:, :, s:width],
            op0=ALU.mult, op1=ALU.add), reads=[re_s, ac], writes=[hstmp])
        p.op("vector", lambda e: e.scalar_tensor_tensor(
            out=v(re_d)[:, :, s:width], in0=v(im_s)[:, :, 0:width - s],
            scalar=naim, in1=v(hstmp)[:, 0:n // width, 0:width - s],
            op0=ALU.mult, op1=ALU.add), reads=[im_s, hstmp, nAim],
            writes=[re_d])
        p.op("vector", lambda e: e.scalar_tensor_tensor(
            out=v(hstmp)[:, 0:n // width, 0:width - s],
            in0=v(im_s)[:, :, 0:width - s],
            scalar=are, in1=v(im_s)[:, :, s:width],
            op0=ALU.mult, op1=ALU.add), reads=[im_s, ac], writes=[hstmp])
        p.op("vector", lambda e: e.scalar_tensor_tensor(
            out=v(im_d)[:, :, s:width], in0=v(re_s)[:, :, 0:width - s],
            scalar=aim, in1=v(hstmp)[:, 0:n // width, 0:width - s],
            op0=ALU.mult, op1=ALU.add), reads=[re_s, hstmp, ac],
            writes=[im_d])

    cur = 0
    for j, s in enumerate((1, 2, 4, 8)):
        hs_pass((SRe[cur], SIm[cur]), (SRe[1 - cur], SIm[1 - cur]),
                s, 16, j)
        cur = 1 - cur
    CRe = [ctx.alloc(f"l1_cre{k}", [64, 128], F32) for k in range(2)]
    CIm = [ctx.alloc(f"l1_cim{k}", [64, 128], F32) for k in range(2)]
    p.op("vector", lambda e: e.tensor_copy(
        out=CRe[0][:], in_=SRe[cur][:].rearrange(
            "p (c w) -> p c w", w=16)[:, :, 15]),
        reads=[SRe[cur]], writes=[CRe[0]])
    p.op("vector", lambda e: e.tensor_copy(
        out=CIm[0][:], in_=SIm[cur][:].rearrange(
            "p (c w) -> p c w", w=16)[:, :, 15]),
        reads=[SIm[cur]], writes=[CIm[0]])
    ccur = 0
    for jj, s in enumerate((1, 2, 4, 8, 16, 32, 64)):
        hs_pass((CRe[ccur], CIm[ccur]), (CRe[1 - ccur], CIm[1 - ccur]),
                s, 128, 4 + jj)
        ccur = 1 - ccur

    OP1 = hstmp
    OP2r = SRe[1 - cur]
    vS = lambda t: t[:].rearrange("p (c w) -> p c w", w=16)
    for comp in range(2):
        dst = SRe[cur] if comp == 0 else SIm[cur]
        if comp == 0:
            t1a, t1b = CRe[ccur], Fre
            t2a, t2b = CIm[ccur], Fim
        else:
            t1a, t1b = CRe[ccur], Fim
            t2a, t2b = CIm[ccur], Fre
        p.op("vector", lambda e, t1a=t1a, t1b=t1b: e.tensor_mul(
            out=vS(OP1)[:, 1:128, :],
            in0=t1a[:, 0:127, None].broadcast_to([64, 127, 16]),
            in1=t1b[:, None, :].broadcast_to([64, 127, 16])),
            reads=[t1a, t1b], writes=[OP1])
        p.op("vector", lambda e, t2a=t2a, t2b=t2b: e.tensor_mul(
            out=vS(OP2r)[:, 1:128, :],
            in0=t2a[:, 0:127, None].broadcast_to([64, 127, 16]),
            in1=t2b[:, None, :].broadcast_to([64, 127, 16])),
            reads=[t2a, t2b], writes=[OP2r])
        op = "tensor_sub" if comp == 0 else "tensor_add"
        p.op("vector", lambda e, op=op: getattr(e, op)(
            out=vS(OP1)[:, 1:128, :], in0=vS(OP1)[:, 1:128, :],
            in1=vS(OP2r)[:, 1:128, :]), reads=[OP1, OP2r], writes=[OP1])
        p.op("vector", lambda e, dst=dst: e.tensor_add(
            out=vS(dst)[:, 1:128, :], in0=vS(dst)[:, 1:128, :],
            in1=vS(OP1)[:, 1:128, :]), reads=[OP1, dst], writes=[dst])

    Hfm = ctx.alloc("l1_Hfm", [128, T], BF16)
    stg = ctx.sb["stage_bf"]
    p.op("vector", lambda e: e.tensor_copy(out=Hfm[0:64, :],
                                           in_=SRe[cur][:]),
         reads=[SRe[cur]], writes=[Hfm])
    for piece in range(2):
        sl = slice(piece * 1024, (piece + 1) * 1024)
        p.op("vector", lambda e, sl=sl: e.tensor_copy(
            out=stg[0:64, :], in_=SIm[cur][:, sl]),
            reads=[SIm[cur]], writes=[stg])
        p.dma("sync", lambda e, sl=sl: e.dma_start(
            out=Hfm[64:128, sl], in_=stg[0:64, :]),
            reads=[stg], writes=[Hfm])

    ctx.free("l1_sre0", "l1_sre1", "l1_sim0", "l1_sim1", "l1_hstmp",
             "l1_cre0", "l1_cre1", "l1_cim0", "l1_cim1", "l1_bw_bf",
             "l1_bwT", "l1_Fre", "l1_Fim", "l1_nAim")

    cw_bf = ctx.alloc("l1_cw_bf", [128, 1024], BF16)
    ws_cw = ctx.sb["wstage"][:, 0:1024].rearrange("p (c n) -> p c n", n=128)
    p.dma("sync", lambda e: e.dma_start(
        out=ws_cw, in_=ctx.hbm["L1_attn_C_w"].rearrange(
            "(c p) n -> p c n", p=128)), writes=["wstage"])
    p.op("gpsimd", lambda e: e.tensor_copy(
        out=cw_bf[:].rearrange("p (c n) -> p c n", n=128), in_=ws_cw),
        reads=["wstage"], writes=[cw_bf])
    cwT = ctx.alloc("l1_cwT", [128, 8, 128], BF16)
    for c in range(8):
        pe_transpose(ctx, cwT[:, c, :],
                     cw_bf[:, c * 128:(c + 1) * 128], 128, 128)
    dwT = ctx.alloc("l1_dwT", [128, 8, 1024], BF16)
    dwp = ctx.alloc("l1_dwp", [128, 1024], BF16)
    for r in range(8):
        load_convert(ctx, dwp[:],
                     ctx.hbm["L1_attn_D_w"][r * 128:(r + 1) * 128, :],
                     128, 1024, )
        for c in range(8):
            pe_transpose(ctx, dwT[:, c, r * 128:(r + 1) * 128],
                         dwp[:, c * 128:(c + 1) * 128], 128, 128)

    emit_l1_ffn_pre(ctx)

    x = ctx.sb["x_tm"]
    xnr = ctx.sb["scratch_f32_a"]
    ctx.sb["scratch_f32_c"] = ctx.alloc("l1_scrc", [128, D], F32)
    sc = ctx.sb["scratch_f32_c"]
    tmp = ctx.sb["scratch_f32_b"]
    for tb in range(NT):
        for half in range(2):
            esl = slice(half * 512, (half + 1) * 512)
            p.op("tensor", lambda e, tb=tb, esl=esl: e.matmul(
                ps_m2[:, :], Hfm[:, tb * 128:(tb + 1) * 128],
                cwT[:].rearrange("p c n -> p (c n)")[:, esl],
                start=True, stop=False),
                reads=[Hfm, cwT], writes=[ps_m2])
            for c in range(8):
                p.op("tensor", lambda e, tb=tb, esl=esl, c=c: e.matmul(
                    ps_m2[:, :], xn_fm[:, c, tb * 128:(tb + 1) * 128],
                    dwT[:, c, esl], start=False, stop=(c == 7)),
                    reads=[xn_fm, dwT], writes=[ps_m2])
            p.op("vector", lambda e, esl=esl: e.tensor_copy(
                out=sc[:, esl], in_=ps_m2[:, :]), reads=[ps_m2], writes=[sc])
        emit_l1_ffn_tile(ctx, tb, tmp)
        p.op("vector", lambda e: e.tensor_mul(out=tmp[:], in0=tmp[:],
                                              in1=sc[:]),
             reads=[tmp, sc], writes=[tmp])
        p.op("vector", lambda e, tb=tb: e.tensor_add(
            out=x[:, tb, :], in0=x[:, tb, :], in1=tmp[:]),
            reads=[tmp, x], writes=[x])
    ctx.free(*[n for n in list(ctx.live) if n.startswith("l1_")])


def emit_l1_ffn_pre(ctx):
    p = ctx.p
    xn_fm = ctx.sb["xn_fm"]
    ps_m1 = ctx.sb["ps_m1"]
    ps_m2 = ctx.sb["ps_m2"]
    ident = ctx.consts["ident"]

    ws = ctx.sb["wstage"]
    sg_f = ws[0:64, 0:1024]
    p.dma("sync", lambda e: e.dma_start(
        out=sg_f, in_=ctx.hbm["L1_ffn_signatures_raw"]), writes=[ws])
    q_bf = ctx.alloc("l1_q_bf", [64, 1024], BF16)
    qf2 = ctx.sb["scratch_f32_a"][0:64, :]
    p.op("vector", lambda e: e.tensor_scalar(
        out=qf2, in0=sg_f, scalar1=-0.3, scalar2=None, op0=ALU.is_lt),
        reads=[ws], writes=["scratch_f32_a"])
    p.op("vector", lambda e: e.tensor_scalar(
        out=sg_f, in0=sg_f, scalar1=0.3, scalar2=None, op0=ALU.is_gt),
        reads=[ws], writes=[ws])
    p.op("vector", lambda e: e.tensor_sub(out=q_bf[:], in0=sg_f, in1=qf2),
         reads=[ws, "scratch_f32_a"], writes=[q_bf])
    qT = ctx.alloc("l1_qT", [128, 8, 64], BF16)
    for c in range(8):
        pe_transpose(ctx, qT[:, c, :], q_bf[:, c * 128:(c + 1) * 128],
                     64, 128)
    grp8 = ctx.consts["grp8"]
    cs_bf = ctx.alloc("l1_cs_bf", [8, 1024], BF16)
    for half in range(2):
        sl = slice(half * 512, (half + 1) * 512)
        p.op("tensor", lambda e, sl=sl: e.matmul(
            ps_m1[0:8, :], grp8[:], q_bf[:, sl], start=True, stop=True),
            reads=[grp8, q_bf], writes=[ps_m1])
        p.op("scalar", lambda e, sl=sl: e.activation(
            out=cs_bf[:, sl], in_=ps_m1[0:8, :], func=AF.Sign),
            reads=[ps_m1], writes=[cs_bf])
    csT = ctx.alloc("l1_csT", [128, 8, 8], BF16)
    for c in range(8):
        pe_transpose(ctx, csT[:, c, :], cs_bf[:, c * 128:(c + 1) * 128],
                     8, 128)

    cscore = ctx.alloc("l1_cscore", [128, NT, 8], F32)
    tscore = ctx.alloc("l1_tscore", [128, NT, 64], F32)
    for tb in range(NT):
        tsl = slice(tb * 128, (tb + 1) * 128)
        for c in range(8):
            p.op("tensor", lambda e, tsl=tsl, c=c: e.matmul(
                ps_m1[:, 0:8], xn_fm[:, c, tsl], csT[:, c, :],
                start=(c == 0), stop=(c == 7)),
                reads=[xn_fm, csT], writes=[ps_m1])
        p.op("vector", lambda e, tb=tb: e.tensor_copy(
            out=cscore[:, tb, :], in_=ps_m1[:, 0:8]),
            reads=[ps_m1], writes=[cscore])
        for c in range(8):
            p.op("tensor", lambda e, tsl=tsl, c=c: e.matmul(
                ps_m2[:, 0:64], xn_fm[:, c, tsl], qT[:, c, :],
                start=(c == 0), stop=(c == 7)),
                reads=[xn_fm, qT], writes=[ps_m2])
        p.op("vector", lambda e, tb=tb: e.tensor_copy(
            out=tscore[:, tb, :], in_=ps_m2[:, 0:64]),
            reads=[ps_m2], writes=[tscore])

    trow = ctx.alloc("l1_trow", [1, 2], F32)
    p.dma("sync", lambda e: e.dma_start(
        out=trow[:, 0:1], in_=ctx.hbm["L1_ffn_temperature"][:, None]),
        writes=[trow])
    p.op("vector", lambda e: e.tensor_scalar_add(
        out=trow[:, 1:2], in0=trow[:, 0:1], scalar1=1e-6),
        reads=[trow], writes=[trow])
    tcol = ctx.alloc("l1_tcol", [128, 1], F32)
    bcast_col(ctx, trow[:, 1:2], tcol[:], [trow])
    rtcol = ctx.alloc("l1_rtcol", [128, 1], F32)
    p.op("vector", lambda e: e.reciprocal(out=rtcol[:], in_=tcol[:]),
         reads=[tcol], writes=[rtcol])
    krow = ctx.alloc("l1_krow", [1, 16], F32)
    p.dma("sync", lambda e: e.dma_start(
        out=krow[:, 0:8], in_=ctx.hbm["L1_ffn_knot_values"][None, :]),
        writes=[krow])
    p.op("vector", lambda e: e.tensor_sub(
        out=krow[:, 8:15], in0=krow[:, 1:8], in1=krow[:, 0:7]),
        reads=[krow], writes=[krow])
    kcols = ctx.alloc("l1_kcols", [128, 8], F32)
    bcast_col(ctx, krow[:, 0:1], kcols[:, 0:1], [krow])
    for j in range(7):
        bcast_col(ctx, krow[:, 8 + j:9 + j], kcols[:, 1 + j:2 + j], [krow])

    def calibrate(score_t, ncol):
        w = NT * ncol
        sig = ctx.sb["scratch_f32_a"][:, 0:w]
        val = ctx.alloc(f"{score_t.name}_val", [128, w], F32)
        u = ctx.sb["scratch_f32_b"][:, 0:w]
        flat = score_t[:].rearrange("p a b -> p (a b)")
        p.op("scalar", lambda e: e.activation(
            out=sig, in_=flat, func=AF.Sigmoid, scale=rtcol[:]),
            reads=[score_t, rtcol], writes=["scratch_f32_a"])
        p.op("vector", lambda e: e.tensor_scalar(
            out=val[:], in0=sig, scalar1=0.0, scalar2=kcols[:, 0:1],
            op0=ALU.mult, op1=ALU.add),
            reads=["scratch_f32_a", kcols], writes=[val])
        for j in range(7):
            p.op("vector", lambda e, j=j: e.tensor_scalar(
                out=u, in0=sig, scalar1=7.0, scalar2=float(-j),
                op0=ALU.mult, op1=ALU.add),
                reads=["scratch_f32_a"], writes=["scratch_f32_b"])
            p.op("vector", lambda e: e.tensor_scalar(
                out=u, in0=u, scalar1=0.0, scalar2=1.0,
                op0=ALU.max, op1=ALU.min),
                reads=["scratch_f32_b"], writes=["scratch_f32_b"])
            p.op("vector", lambda e, j=j: e.scalar_tensor_tensor(
                out=val[:], in0=u, scalar=kcols[:, 1 + j:2 + j],
                in1=val[:], op0=ALU.mult, op1=ALU.add),
                reads=["scratch_f32_b", kcols, val], writes=[val])
        return val

    cval = calibrate(cscore, 8)
    tval = calibrate(tscore, 64)

    iota8 = ctx.consts32["iota8m9"]
    cidx = ctx.alloc("l1_cidx", [128, NT], F32)
    m8 = ctx.alloc("l1_m8", [128, NT], F32)
    eq8 = ctx.alloc("l1_eq8", [128, NT * 8], F32)
    vc = lambda t, w: t[:].rearrange("p (a b) -> p a b", b=w)
    p.op("vector", lambda e: e.tensor_reduce(
        out=m8[:], in_=vc(cval, 8), axis=AX.X, op=ALU.max),
        reads=[cval], writes=[m8])
    p.op("vector", lambda e: e.tensor_tensor(
        out=vc(eq8, 8), in0=vc(cval, 8),
        in1=m8[:, :, None].broadcast_to([128, NT, 8]), op=ALU.is_equal),
        reads=[cval, m8], writes=[eq8])
    p.op("vector", lambda e: e.tensor_mul(
        out=eq8[:], in0=eq8[:], in1=iota8[:, 0:NT * 8]),
        reads=[eq8, "c32_iota8m9"], writes=[eq8])
    p.op("vector", lambda e: e.tensor_scalar_add(
        out=eq8[:], in0=eq8[:], scalar1=9.0), reads=[eq8], writes=[eq8])
    p.op("vector", lambda e: e.tensor_reduce(
        out=cidx[:], in_=vc(eq8, 8), axis=AX.X, op=ALU.min),
        reads=[eq8], writes=[cidx])
    sel = ctx.alloc("l1_sel", [128, NT * 8], F32)
    ohc = ctx.alloc("l1_ohc", [128, NT], F32)
    p.op("vector", lambda e: e.memset(sel[:], 0.0), writes=[sel])
    for c in range(8):
        p.op("vector", lambda e, c=c: e.tensor_scalar(
            out=ohc[:], in0=cidx[:], scalar1=float(c), scalar2=None,
            op0=ALU.is_equal), reads=[cidx], writes=[ohc])
        p.op("vector", lambda e, c=c: e.tensor_mul(
            out=vc(eq8, 8), in0=vc(tval, 64)[:, :, c * 8:(c + 1) * 8],
            in1=ohc[:, :, None].broadcast_to([128, NT, 8])),
            reads=[tval, ohc], writes=[eq8])
        p.op("vector", lambda e: e.tensor_add(
            out=sel[:], in0=sel[:], in1=eq8[:]), reads=[eq8, sel],
            writes=[sel])
    lidx = ctx.alloc("l1_lidx", [128, NT], F32)
    p.op("vector", lambda e: e.tensor_reduce(
        out=m8[:], in_=vc(sel, 8), axis=AX.X, op=ALU.max),
        reads=[sel], writes=[m8])
    p.op("vector", lambda e: e.tensor_tensor(
        out=vc(eq8, 8), in0=vc(sel, 8),
        in1=m8[:, :, None].broadcast_to([128, NT, 8]), op=ALU.is_equal),
        reads=[sel, m8], writes=[eq8])
    p.op("vector", lambda e: e.tensor_mul(
        out=eq8[:], in0=eq8[:], in1=iota8[:, 0:NT * 8]),
        reads=[eq8, "c32_iota8m9"], writes=[eq8])
    p.op("vector", lambda e: e.tensor_scalar_add(
        out=eq8[:], in0=eq8[:], scalar1=9.0), reads=[eq8], writes=[eq8])
    p.op("vector", lambda e: e.tensor_reduce(
        out=lidx[:], in_=vc(eq8, 8), axis=AX.X, op=ALU.min),
        reads=[eq8], writes=[lidx])
    tidx = ctx.alloc("l1_tidx", [128, NT], F32)
    p.op("vector", lambda e: e.scalar_tensor_tensor(
        out=tidx[:], in0=cidx[:], scalar=8.0, in1=lidx[:],
        op0=ALU.mult, op1=ALU.add), reads=[cidx, lidx], writes=[tidx])

    tidx_bf = ctx.alloc("l1_tidx_bf", [128, NT], BF16)
    p.op("vector", lambda e: e.tensor_copy(out=tidx_bf[:], in_=tidx[:]),
         reads=[tidx], writes=[tidx_bf])
    tidx_row = ctx.alloc("l1_tidx_row", [1, T], BF16)
    for tb in range(NT):
        pe_transpose(ctx, tidx_row[:, tb * 128:(tb + 1) * 128],
                     tidx_bf[:, tb:tb + 1], 128, 1)
    oh_fm = ctx.alloc("l1_oh_fm", [64, T], BF16)
    iota_col = ctx.consts32["iotacol"]
    for tb in range(4):
        tsl = slice(tb * 512, (tb + 1) * 512)
        p.op("tensor", lambda e, tsl=tsl: e.matmul(
            ps_m1[0:64, :], ctx.consts["onesrow"][:, 0:64],
            tidx_row[:, tsl], start=True, stop=True),
            reads=[tidx_row, "c_onesrow"], writes=[ps_m1])
        p.op("vector", lambda e, tsl=tsl: e.tensor_scalar(
            out=oh_fm[:, tsl], in0=ps_m1[0:64, :],
            scalar1=iota_col[0:64, :], scalar2=None, op0=ALU.is_equal),
            reads=[ps_m1, "c32_iotacol"], writes=[oh_fm])

    gp = ctx.alloc("l1_gp", [64, 72], BF16)
    load_convert(ctx, gp[:, 0:32],
                 ctx.hbm["L1_ffn_spline_w1"].rearrange("n a b -> n (a b)"),
                 64, 32)
    load_convert(ctx, gp[:, 32:48], ctx.hbm["L1_ffn_spline_b1"], 64, 16)
    load_convert(ctx, gp[:, 48:64],
                 ctx.hbm["L1_ffn_spline_w2"].rearrange("n a b -> n (a b)"),
                 64, 16)
    load_convert(ctx, gp[:, 64:65], ctx.hbm["L1_ffn_spline_b2"], 64, 1)
    gath = ctx.alloc("l1_gath", [128, NT, 72], F32)
    for tb in range(NT):
        p.op("tensor", lambda e, tb=tb: e.matmul(
            ps_m1[:, 0:72], oh_fm[:, tb * 128:(tb + 1) * 128], gp[:, 0:72],
            start=True, stop=True), reads=[oh_fm, gp], writes=[ps_m1])
        p.op("vector", lambda e, tb=tb: e.tensor_copy(
            out=gath[:, tb, :], in_=ps_m1[:, 0:72]),
            reads=[ps_m1], writes=[gath])

    cw1 = load_weight_bf16(ctx, "l1_cw1", ctx.hbm["L1_ffn_compress_w1"],
                           1024, 256)
    b1 = ctx.alloc("l1_b1", [1, 256], BF16)
    load_convert(ctx, b1[:], ctx.hbm["L1_ffn_compress_b1"][None, :], 1, 256)
    h_fm = ctx.alloc("l1_h_fm", [128, 2, T], BF16)
    ones512 = ctx.consts["ones512"]
    for m in range(2):
        for tb in range(4):
            tsl = slice(tb * 512, (tb + 1) * 512)
            for c in range(8):
                p.op("tensor", lambda e, m=m, tsl=tsl, c=c: e.matmul(
                    ps_m2[:, :], cw1[:, c, m * 128:(m + 1) * 128],
                    xn_fm[:, c, tsl], start=(c == 0), stop=False),
                    reads=[cw1, xn_fm], writes=[ps_m2])
            p.op("tensor", lambda e, m=m, tsl=tsl: e.matmul(
                ps_m2[:, :], b1[:, m * 128:(m + 1) * 128], ones512[:],
                start=False, stop=True), reads=[b1, "c_ones512"],
                writes=[ps_m2])
            p.op("scalar", lambda e, m=m, tsl=tsl: e.activation(
                out=h_fm[:, m, tsl], in_=ps_m2[:, :], func=AF.Gelu),
                reads=[ps_m2], writes=[h_fm])
    cw2 = ctx.alloc("l1_cw2", [128, 2, 2], BF16)
    ws_c2 = ctx.sb["wstage"][:, 0:4].rearrange("p (a b) -> p a b", b=2)
    p.dma("sync", lambda e: e.dma_start(
        out=ws_c2, in_=ctx.hbm["L1_ffn_compress_w2"].rearrange(
            "(a p) b -> p a b", p=128)), writes=["wstage"])
    p.op("gpsimd", lambda e: e.tensor_copy(out=cw2[:], in_=ws_c2),
         reads=["wstage"], writes=[cw2])
    b2 = ctx.alloc("l1_b2", [1, 2], BF16)
    load_convert(ctx, b2[:], ctx.hbm["L1_ffn_compress_b2"][None, :], 1, 2)
    comp = ctx.alloc("l1_comp", [128, NT, 2], F32)
    for tb in range(NT):
        tsl = slice(tb * 128, (tb + 1) * 128)
        for m in range(2):
            p.op("tensor", lambda e, tsl=tsl, m=m: e.matmul(
                ps_m1[:, 0:2], h_fm[:, m, tsl], cw2[:, m, :],
                start=(m == 0), stop=False),
                reads=[h_fm, cw2], writes=[ps_m1])
        p.op("tensor", lambda e, tsl=tsl: e.matmul(
            ps_m1[:, 0:2], ctx.consts["onesrow"][:], b2[:],
            start=False, stop=True), reads=[b2, "c_onesrow"],
            writes=[ps_m1])
        p.op("scalar", lambda e, tb=tb: e.activation(
            out=comp[:, tb, :], in_=ps_m1[:, 0:2], func=AF.Tanh),
            reads=[ps_m1], writes=[comp])

    dirs = ctx.alloc("l1_dirs", [64, 1024], BF16)
    load_convert(ctx, dirs[:], ctx.hbm["L1_ffn_directions"], 64, 1024)
    grow = ctx.alloc("l1_grow", [128, 1024], BF16)
    brow = ctx.alloc("l1_brow", [128, 1024], BF16)
    gb_bf = ctx.alloc("l1_gb_bf", [1, 2048], BF16)
    load_convert(ctx, gb_bf[:, 0:1024],
                 ctx.hbm["L1_ffn_norm_gamma"][None, :], 1, 1024)
    load_convert(ctx, gb_bf[:, 1024:2048],
                 ctx.hbm["L1_ffn_norm_beta"][None, :], 1, 1024)
    for half in range(2):
        sl = slice(half * 512, (half + 1) * 512)
        p.op("tensor", lambda e, sl=sl: e.matmul(
            ps_m1[:, :], ctx.consts["onesrow"][:], gb_bf[:, sl],
            start=True, stop=True), reads=[gb_bf, "c_onesrow"],
            writes=[ps_m1])
        p.op("vector", lambda e, sl=sl: e.tensor_copy(
            out=grow[:, sl], in_=ps_m1[:, :]), reads=[ps_m1], writes=[grow])
        sl2 = slice(1024 + half * 512, 1024 + (half + 1) * 512)
        p.op("tensor", lambda e, sl2=sl2: e.matmul(
            ps_m1[:, :], ctx.consts["onesrow"][:], gb_bf[:, sl2],
            start=True, stop=True), reads=[gb_bf, "c_onesrow"],
            writes=[ps_m1])
        p.op("vector", lambda e, sl=sl: e.tensor_copy(
            out=brow[:, sl], in_=ps_m1[:, :]), reads=[ps_m1], writes=[brow])
    ctx.alloc("l1_spl", [128, 32], F32)
    ctx.alloc("l1_sq", [128, 1024], F32)
    ctx.alloc("l1_xnr2", [128, 1024], F32)
    osrow = ctx.alloc("l1_osrow", [1, 1], F32)
    p.dma("sync", lambda e: e.dma_start(
        out=osrow[:], in_=ctx.hbm["L1_ffn_output_scale"][:, None]),
        writes=[osrow])
    oscol = ctx.alloc("l1_oscol", [128, 1], F32)
    bcast_col(ctx, osrow[:], oscol[:], [osrow])


def emit_l1_ffn_tile(ctx, tb, gout):
    p = ctx.p
    gath = ctx.sb["l1_gath"]
    comp = ctx.sb["l1_comp"]
    oh_fm = ctx.sb["l1_oh_fm"]
    dirs = ctx.sb["l1_dirs"]
    grow = ctx.sb["l1_grow"]
    brow = ctx.sb["l1_brow"]
    oscol = ctx.sb["l1_oscol"]
    col = ctx.sb["scratch_col"]
    spl = ctx.sb["l1_spl"]
    ps_m1 = ctx.sb["ps_m1"]
    xnr = ctx.sb["l1_xnr2"]
    p.op("vector", lambda e: e.tensor_scalar(
        out=spl[:, 0:16], in0=gath[:, tb, 0:16],
        scalar1=comp[:, tb, 0:1], scalar2=None, op0=ALU.mult),
        reads=[gath, comp], writes=[spl])
    p.op("vector", lambda e: e.scalar_tensor_tensor(
        out=spl[:, 0:16], in0=gath[:, tb, 16:32],
        scalar=comp[:, tb, 1:2], in1=spl[:, 0:16],
        op0=ALU.mult, op1=ALU.add), reads=[gath, comp, spl], writes=[spl])
    p.op("vector", lambda e: e.tensor_add(
        out=spl[:, 0:16], in0=spl[:, 0:16], in1=gath[:, tb, 32:48]),
        reads=[gath, spl], writes=[spl])
    p.op("vector", lambda e: e.tensor_scalar_max(
        out=spl[:, 0:16], in0=spl[:, 0:16], scalar1=0.0),
        reads=[spl], writes=[spl])
    p.op("vector", lambda e: e.tensor_mul(
        out=spl[:, 16:32], in0=spl[:, 0:16], in1=gath[:, tb, 48:64]),
        reads=[gath, spl], writes=[spl])
    p.op("vector", lambda e: e.reduce_sum(
        out=col[:, 0:1], in_=spl[:, 16:32], axis=AX.X),
        reads=[spl], writes=[col])
    p.op("vector", lambda e: e.tensor_add(
        out=col[:, 0:1], in0=col[:, 0:1], in1=gath[:, tb, 64:65]),
        reads=[gath, col], writes=[col])
    p.op("vector", lambda e: e.tensor_mul(
        out=col[:, 0:1], in0=col[:, 0:1], in1=oscol[:]),
        reads=[oscol, col], writes=[col])
    xn_rec(ctx, tb, xnr[:])
    for half in range(2):
        hsl = slice(half * 512, (half + 1) * 512)
        p.op("tensor", lambda e, hsl=hsl: e.matmul(
            ps_m1[:, :], oh_fm[:, tb * 128:(tb + 1) * 128], dirs[:, hsl],
            start=True, stop=True), reads=[oh_fm, dirs], writes=[ps_m1])
        p.op("vector", lambda e, hsl=hsl: e.tensor_scalar(
            out=gout[:, hsl], in0=ps_m1[:, :], scalar1=col[:, 0:1],
            scalar2=None, op0=ALU.mult), reads=[ps_m1, col], writes=[gout])
    p.op("vector", lambda e: e.tensor_add(
        out=gout[:], in0=gout[:], in1=xnr[:]), reads=[xnr, gout],
        writes=[gout])
    p.op("vector", lambda e: e.reduce_sum(
        out=col[:, 1:2], in_=gout[:], axis=AX.X), reads=[gout], writes=[col])
    p.op("vector", lambda e: e.tensor_scalar_mul(
        out=col[:, 1:2], in0=col[:, 1:2], scalar1=1.0 / D),
        reads=[col], writes=[col])
    p.op("vector", lambda e: e.tensor_scalar(
        out=gout[:], in0=gout[:], scalar1=col[:, 1:2], scalar2=None,
        op0=ALU.subtract), reads=[col, gout], writes=[gout])
    sq = ctx.sb["l1_sq"]
    p.op("scalar", lambda e: e.activation(
        out=sq[:], in_=gout[:], func=AF.Square), reads=[gout], writes=[sq])
    p.op("vector", lambda e: e.reduce_sum(
        out=col[:, 2:3], in_=sq[:], axis=AX.X), reads=[sq], writes=[col])
    epsc = ctx.fconst(1e-5)
    p.op("scalar", lambda e, epsc=epsc: e.activation(
        out=col[:, 3:4], in_=col[:, 2:3], func=AF.Sqrt, bias=epsc,
        scale=1.0 / D), reads=[col, "fconsts"], writes=[col])
    p.op("vector", lambda e: e.reciprocal(out=col[:, 3:4], in_=col[:, 3:4]),
         reads=[col], writes=[col])
    p.op("vector", lambda e: e.tensor_scalar(
        out=gout[:], in0=gout[:], scalar1=col[:, 3:4], scalar2=None,
        op0=ALU.mult), reads=[col, gout], writes=[gout])
    p.op("vector", lambda e: e.tensor_mul(
        out=gout[:], in0=gout[:], in1=grow[:]), reads=[grow, gout],
        writes=[gout])
    p.op("vector", lambda e: e.tensor_add(
        out=gout[:], in0=gout[:], in1=brow[:]), reads=[brow, gout],
        writes=[gout])
    p.op("scalar", lambda e: e.activation(
        out=gout[:], in_=gout[:], func=AF.Silu), reads=[gout],
        writes=[gout])



def row_bcast(ctx, dst, row_bf_ap, width, reads):
    p = ctx.p
    ps = ctx.sb["ps_m1"]
    for c0 in range(0, width, 512):
        w = min(512, width - c0)
        p.op("tensor", lambda e, c0=c0, w=w: e.matmul(
            ps[:, 0:w], ctx.consts["onesrow"][:], row_bf_ap[:, c0:c0 + w],
            start=True, stop=True), reads=list(reads) + ["c_onesrow"],
            writes=[ps])
        p.op("vector", lambda e, c0=c0, w=w: e.tensor_copy(
            out=dst[:, c0:c0 + w], in_=ps[:, 0:w]), reads=[ps], writes=[dst])


def emit_layer2a(ctx):
    import os
    NPAIR = int(os.environ.get("P2A_PAIRS", "8"))
    PHASE = int(os.environ.get("P2A_PHASE", "9"))
    p = ctx.p
    emit_ln(ctx)
    xn_fm = ctx.sb["xn_fm"]
    ident = ctx.consts["ident"]
    ps_m1 = ctx.sb["ps_m1"]
    ps_m2 = ctx.sb["ps_m2"]
    ws = ctx.sb["wstage"]
    s_state = ctx.state_out("s_state", [T, D])

    convw = {}
    for nm in ("conv_q", "conv_k", "conv_v"):
        t = ctx.alloc(f"l2_{nm}", [128, 8, 4], F32)
        p.dma("sync", lambda e, t=t, nm=nm: e.dma_start(
            out=t[:], in_=ctx.hbm[f"L2_attn_{nm}"].rearrange(
                "(c p) j -> p c j", p=128)), writes=[t])
        convw[nm] = t
    w0 = ctx.alloc("l2_w0", [128, 8, 64], BF16)
    sv = ws[:, 0:512].rearrange("p (c j) -> p c j", j=64)
    p.dma("sync", lambda e: e.dma_start(
        out=sv, in_=ctx.hbm["L2_attn_W_init"].rearrange(
            "o h a b -> (o h a) b").rearrange("(c p) b -> p c b", p=128)),
        writes=[ws])
    p.op("gpsimd", lambda e: e.tensor_copy(out=w0[:], in_=sv),
         reads=[ws], writes=[w0])
    gbrow = ctx.alloc("l2_gbrow", [1, 128], BF16)
    load_convert(ctx, gbrow[:, 0:64], ctx.hbm["L2_attn_ln_gamma"][None, :],
                 1, 64)
    load_convert(ctx, gbrow[:, 64:128], ctx.hbm["L2_attn_ln_beta"][None, :],
                 1, 64)
    grow64 = ctx.alloc("l2_grow64", [128, 128], F32)
    row_bcast(ctx, grow64[:, 0:64], gbrow[:, 0:64], 64, [gbrow])
    row_bcast(ctx, grow64[:, 64:128], gbrow[:, 64:128], 64, [gbrow])

    O_all = ctx.alloc("l2_O_all", [128, NT, 1024], BF16)
    tril = ctx.consts["tril_incl"]

    wTp = ctx.alloc("l2_wTp", [128, 8, 128], BF16)
    ctx.alloc("l2_vstage", [128, T], BF16)
    ctx.alloc("l2_nrowf", [1, 512], F32)
    ctx.alloc("l2_rkcol_bf", [128, 1], BF16)
    wpc = ctx.alloc("l2_wpc", [128, 1024], BF16)
    prj = ctx.alloc("l2_prj", [128, T], F32)
    cnv = ctx.alloc("l2_cnv", [128, T], F32)
    Qfm = ctx.alloc("l2_Qfm", [128, T], BF16)
    Kfm = ctx.alloc("l2_Kfm", [128, T], BF16)
    Vtm = ctx.alloc("l2_Vtm", [128, NT, 128], BF16)
    Ktm = ctx.alloc("l2_Ktm", [128, NT, 128], BF16)
    Rk = ctx.alloc("l2_Rk", [128, 2 * NT], F32)
    nrow = ctx.alloc("l2_nrow", [1, 2 * T], BF16)
    state_sb = ctx.alloc("l2_state_sb", [128, 64], BF16)
    state_f32 = ctx.alloc("l2_state_f32", [128, 64], F32)
    Pbf = [ctx.alloc(f"l2_Pbf{h}", [128, 128], BF16) for h in range(2)]
    ps_sc = [ctx.sb["ps_ava"], ctx.sb["ps_avb"]]
    ps_o = [ctx.sb["ps_sb0"], ctx.sb["ps_sb1"]]
    ps_st = [ctx.sb["ps_sa0"], ctx.sb["ps_sa1"]]
    stg = ctx.sb["stage_bf"]

    def conv_silu(nm, dst_bf, hp, square_norms):
        cw = convw[nm]
        w3 = cw[:, hp, 3:4]
        p.op("vector", lambda e, w3=w3: e.scalar_tensor_tensor(
            out=cnv[:], in0=prj[:], scalar=w3, in1=prj[:],
            op0=ALU.mult, op1=ALU.add), reads=[prj, cw], writes=[cnv])
        for j in (2, 1, 0):
            sft = 3 - j
            wj = cw[:, hp, j:j + 1]
            p.op("vector", lambda e, wj=wj, sft=sft: e.scalar_tensor_tensor(
                out=cnv[:, sft:T], in0=prj[:, 0:T - sft], scalar=wj,
                in1=cnv[:, sft:T], op0=ALU.mult, op1=ALU.add),
                reads=[prj, cw, cnv], writes=[cnv])
        p.op("scalar", lambda e: e.activation(
            out=dst_bf[:], in_=cnv[:], func=AF.Silu),
            reads=[cnv], writes=[dst_bf])
        if square_norms is not None:
            base = square_norms
            for half in range(2):
                hsl = slice(half * 1024, (half + 1) * 1024)
                p.op("scalar", lambda e, hsl=hsl: e.activation(
                    out=stg[:], in_=dst_bf[:, hsl], func=AF.Square),
                    reads=[dst_bf], writes=[stg])
                for hh, ones in enumerate((ctx.consts["onesA"],
                                           ctx.consts["onesB"])):
                    for qq in range(2):
                        qsl = slice(qq * 512, (qq + 1) * 512)
                        p.op("tensor", lambda e, ones=ones, qsl=qsl: e.matmul(
                            ps_m1[0:1, :], ones[:], stg[:, qsl],
                            start=True, stop=True),
                            reads=[stg, ones], writes=[ps_m1])
                        dsl = slice(base + hh * T + half * 1024 + qq * 512,
                                    base + hh * T + half * 1024 + (qq + 1) * 512)
                        p.op("vector", lambda e, dsl=dsl: e.tensor_copy(
                            out=nrow[:, dsl], in_=ps_m1[0:1, :]),
                            reads=[ps_m1], writes=[nrow])

    import os as _os
    BAR = _os.environ.get("P2A_BARRIER", "0") == "1"
    for hp in range(NPAIR):
        if BAR and hp > 0:
            p.full_barrier()
        for nm, wname, dst in (("conv_q", "L2_attn_Wq", Qfm),
                               ("conv_k", "L2_attn_Wk", Kfm),
                               ("conv_v", "L2_attn_Wv", Vtm)):
            hb = ctx.hbm[wname][hp * 128:(hp + 1) * 128, :]
            sv2 = ws[:, 0:1024]
            p.dma("sync", lambda e, hb=hb, sv2=sv2: e.dma_start(
                out=sv2, in_=hb), writes=[ws])
            p.op("gpsimd", lambda e, sv2=sv2: e.tensor_copy(
                out=wpc[:], in_=sv2), reads=[ws], writes=[wpc])
            for c in range(8):
                pe_transpose(ctx, wTp[:, c, :],
                             wpc[:, c * 128:(c + 1) * 128], 128, 128)
            for tb in range(4):
                tsl = slice(tb * 512, (tb + 1) * 512)
                for c in range(8):
                    p.op("tensor", lambda e, tsl=tsl, c=c: e.matmul(
                        ps_m2[:, :], wTp[:, c, :], xn_fm[:, c, tsl],
                        start=(c == 0), stop=(c == 7)),
                        reads=[wTp, xn_fm], writes=[ps_m2])
                p.op("vector", lambda e, tsl=tsl: e.tensor_copy(
                    out=prj[:, tsl], in_=ps_m2[:, :]),
                    reads=[ps_m2], writes=[prj])
            if nm == "conv_q":
                conv_silu(nm, Qfm, hp, None)
            elif nm == "conv_k":
                conv_silu(nm, Kfm, hp, 0)
            else:
                conv_silu(nm, ctx.sb["l2_vstage"], hp, None)
        for hh in range(2):
            nrf = ctx.sb["l2_nrowf"]
            ec = ctx.fconst(1e-12)
            for seg in range(4):
                sl = slice(hh * T + seg * 512, hh * T + (seg + 1) * 512)
                p.op("scalar", lambda e, sl=sl, nrf=nrf, ec=ec: e.activation(
                    out=nrf[:], in_=nrow[:, sl], func=AF.Sqrt, bias=ec[0:1, :]),
                    reads=[nrow, "fconsts"], writes=[nrf])
                p.op("vector", lambda e, nrf=nrf: e.reciprocal(
                    out=nrf[:], in_=nrf[:]), reads=[nrf], writes=[nrf])
                p.op("vector", lambda e, sl=sl, nrf=nrf: e.tensor_copy(
                    out=nrow[:, sl], in_=nrf[:]), reads=[nrf], writes=[nrow])
            rkc = ctx.sb["l2_rkcol_bf"]
            for kt in range(NT):
                pe_transpose(ctx, rkc[:],
                             nrow[:, hh * T + kt * 128:hh * T + (kt + 1) * 128],
                             1, 128)
                p.op("vector", lambda e, kt=kt, hh=hh, rkc=rkc: e.tensor_copy(
                    out=Rk[:, hh * NT + kt:hh * NT + kt + 1], in_=rkc[:]),
                    reads=[rkc], writes=[Rk])
        vst = ctx.sb["l2_vstage"]
        for kt in range(NT):
            pe_transpose(ctx, Vtm[:, kt, :], vst[:, kt * 128:(kt + 1) * 128],
                         128, 128)
            pe_transpose(ctx, Ktm[:, kt, :], Kfm[:, kt * 128:(kt + 1) * 128],
                         128, 128)
            for hh in range(2):
                p.op("vector", lambda e, kt=kt, hh=hh: e.tensor_scalar(
                    out=Vtm[:, kt, hh * 64:(hh + 1) * 64],
                    in0=Vtm[:, kt, hh * 64:(hh + 1) * 64],
                    scalar1=Rk[:, hh * NT + kt:hh * NT + kt + 1],
                    scalar2=None, op0=ALU.mult),
                    reads=[Rk, Vtm], writes=[Vtm])
        SCAN = int(os.environ.get("P2A_SCAN", "9"))
        for c in range(NT if PHASE >= 2 else 0):
            csl = slice(c * 128, (c + 1) * 128)
            if c == 0 or SCAN < 3:
                p.op("vector", lambda e, hp=hp: e.tensor_copy(
                    out=state_sb[:], in_=w0[:, hp, :]),
                    reads=[w0], writes=[state_sb])
            else:
                p.op("vector", lambda e, hp=hp: e.scalar_tensor_tensor(
                    out=state_sb[:], in0=state_f32[:], scalar=1.0,
                    in1=w0[:, hp, :], op0=ALU.mult, op1=ALU.add),
                    reads=[state_f32, w0], writes=[state_sb])
            for hh, tp in (((0, (0, 0)), (1, (64, 0))) if SCAN >= 1 else ()):
                p.op("tensor", lambda e, hh=hh, tp=tp, csl=csl:
                     e.matmul(
                         ps_o[hh][:, 0:64],
                         Qfm[hh * 64:(hh + 1) * 64, csl],
                         state_sb[hh * 64:(hh + 1) * 64, :],
                         start=True, stop=(SCAN < 2), tile_position=tp),
                     reads=[Qfm, state_sb], writes=[ps_o[hh]])
            for hh, tp in (((0, (0, 0)), (1, (64, 0))) if SCAN >= 2 else ()):
                p.op("tensor", lambda e, hh=hh, tp=tp, csl=csl: e.matmul(
                    ps_sc[hh][:, 0:128], Kfm[hh * 64:(hh + 1) * 64, csl],
                    Qfm[hh * 64:(hh + 1) * 64, csl],
                    start=True, stop=True, tile_position=tp),
                    reads=[Kfm, Qfm], writes=[ps_sc[hh]])
                p.op("vector", lambda e, hh=hh: e.tensor_tensor(
                    out=Pbf[hh][:], in0=ps_sc[hh][:, 0:128], in1=tril[:],
                    op=ALU.mult), reads=[ps_sc[hh], tril, "c_tril_incl"],
                    writes=[Pbf[hh]])
                p.op("tensor", lambda e, hh=hh, c=c: e.matmul(
                    ps_o[hh][:, 0:64], Pbf[hh][:],
                    Vtm[:, c, hh * 64:(hh + 1) * 64],
                    start=False, stop=True),
                    reads=[Pbf[hh], Vtm], writes=[ps_o[hh]])
            if SCAN >= 1:
                for hh in range(2):
                    p.op("vector", lambda e, c=c, hh=hh, hp=hp: e.tensor_copy(
                        out=O_all[:, c,
                                  hp * 128 + hh * 64:hp * 128 + hh * 64 + 64],
                        in_=ps_o[hh][:, 0:64]),
                        reads=[ps_o[hh]], writes=[O_all])
            for hh, tp in (((0, (0, 0)), (1, (0, 64))) if SCAN >= 3 else ()):
                p.op("tensor", lambda e, hh=hh, tp=tp, c=c: e.matmul(
                    ps_st[hh][hh * 64:(hh + 1) * 64, 0:64],
                    Ktm[:, c, hh * 64:(hh + 1) * 64],
                    Vtm[:, c, hh * 64:(hh + 1) * 64],
                    start=True, stop=True, tile_position=tp),
                    reads=[Ktm, Vtm], writes=[ps_st[hh]])
            if SCAN >= 3:
                if c == 0:
                    for hh in range(2):
                        p.op("vector", lambda e, hh=hh: e.tensor_copy(
                            out=state_f32[hh * 64:(hh + 1) * 64, :],
                            in_=ps_st[hh][hh * 64:(hh + 1) * 64, 0:64]),
                            reads=[ps_st[hh]], writes=[state_f32])
                else:
                    for hh in range(2):
                        p.op("vector", lambda e, hh=hh: e.scalar_tensor_tensor(
                            out=state_f32[hh * 64:(hh + 1) * 64, :],
                            in0=ps_st[hh][hh * 64:(hh + 1) * 64, 0:64],
                            scalar=1.0,
                            in1=state_f32[hh * 64:(hh + 1) * 64, :],
                            op0=ALU.mult, op1=ALU.add),
                            reads=[ps_st[hh], state_f32], writes=[state_f32])
        if hp == 0:
            ctx.tap_full("l2_O_p0", O_all)

    ctx.tap_full("l2_O_raw", O_all)
    ctx.tap_full("l2_Qfm", Qfm)
    ctx.tap_full("l2_Kfm", Kfm)
    ctx.tap_full("l2_Vtm", Vtm)
    ctx.tap_full("l2_O_mid", O_all)
    ctx.free("l2_wTp", "l2_wpc", "l2_prj", "l2_cnv", "l2_Qfm", "l2_Kfm",
             "l2_Vtm", "l2_Ktm", "l2_Rk", "l2_nrow", "l2_state_sb",
             "l2_Pbf0", "l2_Pbf1", "l2_vstage", "l2_nrowf", "l2_rkcol_bf",
             "l2_state_f32")

    import os as _os3
    if _os3.environ.get("P2A_BAR3", "0") == "1":
        p.full_barrier()
    if PHASE < 3:
        ctx.free(*[n for n in list(ctx.live) if n.startswith("l2_")])
        zz = ctx.sb["scratch_f32_a"]
        p.op("vector", lambda e: e.memset(zz[:], 0.0), writes=[zz])
        for tb in range(NT):
            p.dma("sync", lambda e, tb=tb: e.dma_start(
                out=s_state.rearrange("(i p) d -> p i d", p=128)[:, tb, :],
                in_=zz[:]), reads=[zz], writes=["s_state_d"], is_out=True)
        return
    col2 = ctx.alloc("l2_col2", [128, 64], F32)
    cen = ctx.sb["scratch_f32_a"]
    sq = ctx.sb["scratch_f32_b"]
    epsc = ctx.fconst(1e-5)
    for kt in range(NT):
        ov = O_all[:, kt, :].rearrange("p (h d) -> p h d", d=64)
        cv = cen[:].rearrange("p (h d) -> p h d", d=64)
        p.op("vector", lambda e, ov=ov: e.reduce_sum(
            out=col2[:, 0:16], in_=ov, axis=AX.X),
            reads=[O_all], writes=[col2])
        p.op("vector", lambda e: e.tensor_scalar_mul(
            out=col2[:, 0:16], in0=col2[:, 0:16], scalar1=1.0 / 64),
            reads=[col2], writes=[col2])
        p.op("vector", lambda e, ov=ov, cv=cv: e.tensor_tensor(
            out=cv, in0=ov,
            in1=col2[:, 0:16, None].broadcast_to([128, 16, 64]),
            op=ALU.subtract), reads=[O_all, col2], writes=[cen])
        p.op("scalar", lambda e: e.activation(
            out=sq[:], in_=cen[:], func=AF.Square), reads=[cen], writes=[sq])
        p.op("vector", lambda e: e.reduce_sum(
            out=col2[:, 16:32], in_=sq[:].rearrange(
                "p (h d) -> p h d", d=64), axis=AX.X),
            reads=[sq], writes=[col2])
        p.op("scalar", lambda e, epsc=epsc: e.activation(
            out=col2[:, 32:48], in_=col2[:, 16:32], func=AF.Sqrt,
            bias=epsc, scale=1.0 / 64),
            reads=[col2, "fconsts"], writes=[col2])
        p.op("vector", lambda e: e.reciprocal(
            out=col2[:, 32:48], in_=col2[:, 32:48]),
            reads=[col2], writes=[col2])
        p.op("vector", lambda e, cv=cv: e.tensor_tensor(
            out=cv, in0=cv,
            in1=col2[:, 32:48, None].broadcast_to([128, 16, 64]),
            op=ALU.mult), reads=[cen, col2], writes=[cen])
        p.op("vector", lambda e, cv=cv: e.tensor_tensor(
            out=cv, in0=cv,
            in1=grow64[:, 0:64][:, None, :].broadcast_to([128, 16, 64]),
            op=ALU.mult), reads=[cen, grow64], writes=[cen])
        p.op("vector", lambda e, ov=ov, cv=cv: e.tensor_tensor(
            out=ov, in0=cv,
            in1=grow64[:, 64:128][:, None, :].broadcast_to([128, 16, 64]),
            op=ALU.add), reads=[cen, grow64], writes=[O_all])

    ctx.tap_full("l2_O_ln", O_all)
    import os as _os2
    if _os2.environ.get("P2A_BAR2", "0") == "1":
        p.full_barrier()
    if PHASE < 4:
        ctx.free(*[n for n in list(ctx.live) if n.startswith("l2_")])
        zz = ctx.sb["scratch_f32_a"]
        p.op("vector", lambda e: e.memset(zz[:], 0.0), writes=[zz])
        for tb in range(NT):
            p.dma("sync", lambda e, tb=tb: e.dma_start(
                out=s_state.rearrange("(i p) d -> p i d", p=128)[:, tb, :],
                in_=zz[:]), reads=[zz], writes=["s_state_d"], is_out=True)
        return
    wgT = ctx.alloc("l2_wgT", [128, 8, 1024], BF16)
    woT = ctx.alloc("l2_woT", [128, 8, 1024], BF16)
    wpc2 = ctx.alloc("l2_wpc2", [128, 1024], BF16)
    import os as _os5
    OWNSTAGE = _os5.environ.get("P2A_OWNSTAGE", "0") == "1"
    if OWNSTAGE:
        gstage = ctx.alloc("l2_gstage", [128, 1024], F32)
    for wname, wT in (("L2_attn_Wg", wgT), ("L2_attn_Wo", woT)):
        for r in range(8):
            if OWNSTAGE:
                hbr = ctx.hbm[wname][r * 128:(r + 1) * 128, :]
                p.dma("sync", lambda e, hbr=hbr: e.dma_start(
                    out=gstage[:], in_=hbr), writes=[gstage])
                p.op("gpsimd", lambda e: e.tensor_copy(
                    out=wpc2[:], in_=gstage[:]),
                    reads=[gstage], writes=[wpc2])
            else:
                load_convert(ctx, wpc2[:],
                             ctx.hbm[wname][r * 128:(r + 1) * 128, :],
                             128, 1024)
            for c in range(8):
                pe_transpose(ctx, wT[:, c, r * 128:(r + 1) * 128],
                             wpc2[:, c * 128:(c + 1) * 128], 128, 128)
    og_bf = ctx.alloc("l2_og_bf", [128, 1024], BF16)
    og_fm = ctx.alloc("l2_og_fm", [128, 8, 128], BF16)
    import os as _os4
    if _os4.environ.get("P2A_BAR4", "0") == "1":
        p.full_barrier()
    for tb in range(NT):
        for half in range(2):
            esl = slice(half * 512, (half + 1) * 512)
            for c in range(8):
                p.op("tensor", lambda e, tb=tb, c=c, esl=esl: e.matmul(
                    ps_m2[:, :], xn_fm[:, c, tb * 128:(tb + 1) * 128],
                    wgT[:, c, esl], start=(c == 0), stop=(c == 7)),
                    reads=[xn_fm, wgT], writes=[ps_m2])
            p.op("vector", lambda e, tb=tb, esl=esl: e.tensor_tensor(
                out=og_bf[:, esl], in0=ps_m2[:, :],
                in1=O_all[:, tb, esl], op=ALU.mult),
                reads=[ps_m2, O_all], writes=[og_bf])
        for c in range(8):
            pe_transpose(ctx, og_fm[:, c, :],
                         og_bf[:, c * 128:(c + 1) * 128], 128, 128)
        for half in range(2):
            esl = slice(half * 512, (half + 1) * 512)
            for c in range(8):
                p.op("tensor", lambda e, c=c, esl=esl: e.matmul(
                    ps_m2[:, :], og_fm[:, c, :], woT[:, c, esl],
                    start=(c == 0), stop=(c == 7)),
                    reads=[og_fm, woT], writes=[ps_m2])
            p.op("vector", lambda e, esl=esl: e.tensor_copy(
                out=cen[:, esl], in_=ps_m2[:, :]), reads=[ps_m2],
                writes=[cen])
        p.dma("sync", lambda e, tb=tb: e.dma_start(
            out=s_state.rearrange("(i p) d -> p i d", p=128)[:, tb, :],
            in_=cen[:]), reads=[cen], writes=["s_state_d"], is_out=True)
    ctx.free(*[n for n in list(ctx.live) if n.startswith("l2_")])


def emit_layer2b(ctx):
    p = ctx.p
    emit_ln(ctx)
    xn_fm = ctx.sb["xn_fm"]
    ident = ctx.consts["ident"]
    ps_m1 = ctx.sb["ps_m1"]
    ps_m2 = ctx.sb["ps_m2"]
    ws = ctx.sb["wstage"]
    s_state = ctx.state_in("s_state", [T, D])
    nsc = ctx.alloc("l2b_nsc", [1, 64], F32)
    nsc_bf = ctx.alloc("l2b_nscbf", [1, 2], BF16)
    bias_a = ctx.alloc("l2b_bias_a", [128, 1], F32)
    bias_b = ctx.alloc("l2b_bias_b", [128, 1], F32)
    ctx.alloc("fl_rc_a", [128, 1], F32)
    ctx.alloc("fl_rc_b", [128, 1], F32)

    wvT = ctx.alloc("l2b_wvT", [128, 8, 512], BF16)
    wpc = ctx.alloc("l2b_wpc", [128, 1024], BF16)
    for r in range(4):
        load_convert(ctx, wpc[:],
                     ctx.hbm["L2_ffn_Wv"].rearrange(
                         "v d c -> (v d) c")[r * 128:(r + 1) * 128, :],
                     128, 1024)
        for c in range(8):
            pe_transpose(ctx, wvT[:, c, r * 128:(r + 1) * 128],
                         wpc[:, c * 128:(c + 1) * 128], 128, 128)
    PV = ctx.alloc("l2b_PV", [128, NT, 512], BF16)
    for tb in range(NT):
        for c in range(8):
            p.op("tensor", lambda e, tb=tb, c=c: e.matmul(
                ps_m2[:, :], xn_fm[:, c, tb * 128:(tb + 1) * 128],
                wvT[:, c, :], start=(c == 0), stop=(c == 7)),
                reads=[xn_fm, wvT], writes=[ps_m2])
        p.op("vector", lambda e, tb=tb: e.tensor_copy(
            out=PV[:, tb, :], in_=ps_m2[:, :]), reads=[ps_m2], writes=[PV])
    ctx.free("l2b_wvT")
    wsT = ctx.alloc("l2b_wsT", [128, 8, 128], BF16)
    GV = ctx.alloc("l2b_GV", [128, NT, 128], F32)
    MIX = ctx.alloc("l2b_MIX", [128, NT, 128], BF16)
    MASK = ctx.alloc("l2b_MASK", [128, NT, 128], BF16)
    m16 = ctx.alloc("l2b_m16", [128, NT * 16], F32)
    eqt = ctx.alloc("l2b_eqt", [128, NT * 128], F32)
    for wname, dst, use_sig in (("L2_ffn_Ws", MIX, True),
                                ("L2_ffn_Wd", MASK, False)):
        load_convert(ctx, wpc[:],
                     ctx.hbm[wname].rearrange("h v c -> (h v) c"), 128, 1024)
        for c in range(8):
            pe_transpose(ctx, wsT[:, c, :],
                         wpc[:, c * 128:(c + 1) * 128], 128, 128)
        for tb in range(NT):
            for c in range(8):
                p.op("tensor", lambda e, tb=tb, c=c: e.matmul(
                    ps_m1[:, 0:128], xn_fm[:, c, tb * 128:(tb + 1) * 128],
                    wsT[:, c, :], start=(c == 0), stop=(c == 7)),
                    reads=[xn_fm, wsT], writes=[ps_m1])
            p.op("vector", lambda e, tb=tb: e.tensor_copy(
                out=GV[:, tb, :], in_=ps_m1[:, 0:128]),
                reads=[ps_m1], writes=[GV])
        gv4 = GV[:].rearrange("p t (h v) -> p (t h) v", v=8)
        eq4 = eqt[:].rearrange("p (a v) -> p a v", v=8)
        p.op("vector", lambda e, gv4=gv4: e.tensor_reduce(
            out=m16[:], in_=gv4, axis=AX.X, op=ALU.max),
            reads=[GV], writes=[m16])
        p.op("vector", lambda e, gv4=gv4, eq4=eq4: e.tensor_tensor(
            out=eq4, in0=gv4,
            in1=m16[:, :, None].broadcast_to([128, NT * 16, 8]),
            op=ALU.is_equal), reads=[GV, m16], writes=[eqt])
        p.op("vector", lambda e, gv4=gv4, eq4=eq4: e.scalar_tensor_tensor(
            out=eqt[:], in0=eqt[:], scalar=-1e30,
            in1=GV[:].rearrange("p t v -> p (t v)"),
            op0=ALU.mult, op1=ALU.add), reads=[eqt, GV], writes=[eqt])
        p.op("vector", lambda e, eq4=eq4: e.tensor_reduce(
            out=m16[:], in_=eq4, axis=AX.X, op=ALU.max),
            reads=[eqt], writes=[m16])
        p.op("vector", lambda e, gv4=gv4, eq4=eq4: e.tensor_tensor(
            out=eq4, in0=gv4,
            in1=m16[:, :, None].broadcast_to([128, NT * 16, 8]),
            op=ALU.is_ge), reads=[GV, m16], writes=[eqt])
        if use_sig:
            p.op("scalar", lambda e: e.activation(
                out=GV[:], in_=GV[:], func=AF.Sigmoid),
                reads=[GV], writes=[GV])
            p.op("vector", lambda e, dst=dst: e.tensor_tensor(
                out=dst[:].rearrange("p t v -> p (t v)"),
                in0=GV[:].rearrange("p t v -> p (t v)"), in1=eqt[:],
                op=ALU.mult), reads=[GV, eqt], writes=[dst])
        else:
            p.op("vector", lambda e, dst=dst: e.tensor_copy(
                out=dst[:].rearrange("p t v -> p (t v)"), in_=eqt[:]),
                reads=[eqt], writes=[dst])
    ctx.free("l2b_wsT", "l2b_GV", "l2b_m16", "l2b_eqt", "l2b_wpc")

    ctx.alloc("fl_PT0", [128, 16, 256], BF16)
    ctx.alloc("fl_PT1", [128, 16, 256], BF16)
    wT = ctx.alloc("l2b_wT", [128, 8, 128], BF16)
    zacc = ctx.alloc("l2b_zacc", [128, NT, 512], F32)
    p.op("vector", lambda e: e.memset(zacc[:], 0.0), writes=[zacc])
    VA = ctx.alloc("l2b_VA", [128, NT, 130], BF16)
    Qp = ctx.alloc("l2b_Qp", [128, T], BF16)
    Kp = ctx.alloc("l2b_Kp", [128, T], BF16)
    Opair = ctx.alloc("l2b_Opair", [128, NT, 128], BF16)
    wpc3 = ctx.sb["stage_bf"]
    vmt = ctx.alloc("l2b_ztmp", [128, NT * 64], BF16)
    sqst = ctx.sb["stage_bf"]
    ps_ava = ctx.sb["ps_ava"]
    ps_avb = ctx.sb["ps_avb"]
    onesA, onesB = ctx.consts["onesA"], ctx.consts["onesB"]
    onesrow = ctx.consts["onesrow"]

    for hp in range(8):
        for wname, dstp in (("L2_ffn_Wq", Qp), ("L2_ffn_Wk", Kp)):
            hbq = ctx.hbm[wname].rearrange(
                "h v c -> (h v) c")[hp * 128:(hp + 1) * 128, :]
            p.dma("sync", lambda e, hbq=hbq: e.dma_start(
                out=ctx.sb["wstage"][:, 0:1024], in_=hbq),
                writes=["wstage"])
            p.op("gpsimd", lambda e: e.tensor_copy(
                out=wpc3[:], in_=ctx.sb["wstage"][:, 0:1024]),
                reads=["wstage"], writes=["stage_bf"])
            for c in range(8):
                pe_transpose(ctx, wT[:, c, :],
                             wpc3[:, c * 128:(c + 1) * 128], 128, 128)
            for tb in range(4):
                tsl = slice(tb * 512, (tb + 1) * 512)
                for c in range(8):
                    p.op("tensor", lambda e, tsl=tsl, c=c: e.matmul(
                        ps_m2[:, :], wT[:, c, :], xn_fm[:, c, tsl],
                        start=(c == 0), stop=(c == 7)),
                        reads=[wT, xn_fm], writes=[ps_m2])
                p.op("vector", lambda e, tsl=tsl, dstp=dstp: e.tensor_copy(
                    out=dstp[:, tsl], in_=ps_m2[:, :]),
                    reads=[ps_m2], writes=[dstp])
        p.op("vector", lambda e: e.memset(VA[:, :, 64:65], 1.0), writes=[VA])
        p.op("vector", lambda e: e.memset(VA[:, :, 129:130], 1.0),
             writes=[VA])
        for hh in range(2):
            h = hp * 2 + hh
            voff = hh * 65
            for e_ in range(8):
                srcs = [PV, MIX]
                if e_ == 0:
                    p.op("vector", lambda e, h=h, voff=voff, e_=e_:
                         e.tensor_mul(
                             out=VA[:, :, voff:voff + 64],
                             in0=PV[:, :, e_ * 64:(e_ + 1) * 64],
                             in1=MIX[:, :, h * 8 + e_:h * 8 + e_ + 1]
                             .broadcast_to([128, NT, 64])),
                         reads=srcs, writes=[VA])
                else:
                    p.op("vector", lambda e, h=h, voff=voff, e_=e_:
                         e.tensor_mul(
                             out=vmt[:].rearrange("p (t d) -> p t d", d=64),
                             in0=PV[:, :, e_ * 64:(e_ + 1) * 64],
                             in1=MIX[:, :, h * 8 + e_:h * 8 + e_ + 1]
                             .broadcast_to([128, NT, 64])),
                         reads=srcs, writes=[vmt])
                    p.op("gpsimd", lambda e, voff=voff: e.tensor_add(
                        out=VA[:, :, voff:voff + 64],
                        in0=VA[:, :, voff:voff + 64],
                        in1=vmt[:].rearrange("p (t d) -> p t d", d=64)),
                        reads=[vmt, VA], writes=[VA])
        for si, src_ in enumerate((Qp, Kp)):
            for tb in range(4):
                sl = slice(tb * 512, (tb + 1) * 512)
                p.op("scalar", lambda e, src_=src_, sl=sl: e.activation(
                    out=sqst[:, 0:512], in_=src_[:, sl], func=AF.Square),
                    reads=[src_], writes=[sqst])
                for h2, ones in enumerate((onesA, onesB)):
                    p.op("tensor", lambda e, ones=ones: e.matmul(
                        ps_m1[0:1, :], ones[:], sqst[:, 0:512],
                        start=True, stop=True),
                        reads=[sqst, ones], writes=[ps_m1])
                    o = (si * 2 + h2) * 4 + tb
                    p.op("vector", lambda e, o=o: e.reduce_max(
                        out=nsc[:, o:o + 1], in_=ps_m1[0:1, :], axis=AX.X),
                        reads=[ps_m1], writes=[nsc])
        for h2 in range(2):
            o = 16 + h2 * 8
            qs4 = slice(h2 * 4, h2 * 4 + 4)
            ks4 = slice(8 + h2 * 4, 8 + h2 * 4 + 4)
            p.op("vector", lambda e, o=o, qs4=qs4: e.reduce_max(
                out=nsc[:, o:o + 1], in_=nsc[:, qs4], axis=AX.X),
                reads=[nsc], writes=[nsc])
            p.op("vector", lambda e, o=o, ks4=ks4: e.reduce_max(
                out=nsc[:, o + 1:o + 2], in_=nsc[:, ks4], axis=AX.X),
                reads=[nsc], writes=[nsc])
            p.op("vector", lambda e, o=o: e.tensor_mul(
                out=nsc[:, o + 2:o + 3], in0=nsc[:, o:o + 1],
                in1=nsc[:, o + 1:o + 2]), reads=[nsc], writes=[nsc])
            p.op("scalar", lambda e, o=o: e.activation(
                out=nsc[:, o + 3:o + 4], in_=nsc[:, o + 2:o + 3],
                func=AF.Sqrt), reads=[nsc], writes=[nsc])
            p.op("vector", lambda e, o=o: e.tensor_scalar_mul(
                out=nsc[:, o + 4:o + 5], in0=nsc[:, o + 3:o + 4],
                scalar1=-0.125), reads=[nsc], writes=[nsc])
        p.op("vector", lambda e: e.tensor_copy(out=nsc_bf[:, 0:1],
                                               in_=nsc[:, 20:21]),
             reads=[nsc], writes=[nsc_bf])
        p.op("vector", lambda e: e.tensor_copy(out=nsc_bf[:, 1:2],
                                               in_=nsc[:, 28:29]),
             reads=[nsc], writes=[nsc_bf])
        for biast, cs in ((bias_a, slice(0, 1)), (bias_b, slice(1, 2))):
            p.op("tensor", lambda e, cs=cs: e.matmul(
                ps_m1[:, 0:1], onesrow[:], nsc_bf[:, cs],
                start=True, stop=True),
                reads=[nsc_bf, onesrow], writes=[ps_m1])
            p.op("vector", lambda e, biast=biast: e.tensor_copy(
                out=biast[:], in_=ps_m1[:, 0:1]), reads=[ps_m1],
                writes=[biast])
        ctx.sb["fl_K0"] = Kp
        ctx.sb["fl_K1"] = Kp
        ctx.sb["fl_V"] = VA

        def l2b_sink(qtile, psa, psb, rca, rcb):
            for h2, (psav, rc) in enumerate(((psa, rca), (psb, rcb))):
                p.op("vector", lambda e, h2=h2, psav=psav, rc=rc,
                     qtile=qtile: e.tensor_scalar(
                         out=Opair[:, qtile, h2 * 64:(h2 + 1) * 64],
                         in0=psav[:, 0:64], scalar1=rc[:], scalar2=None,
                         op0=ALU.mult),
                     reads=[psav, rc], writes=[Opair])

        flash256(ctx, Qp,
                 lambda h2, ksl: Kp[h2 * 64:(h2 + 1) * 64, ksl],
                 lambda h2, kt: VA[:, kt, h2 * 65:(h2 + 1) * 65],
                 bias_a, bias_b, l2b_sink,
                 lambda qb: list(range(16)),
                 lambda qb, kt, qs: None)
        ztmp = ctx.sb["l2b_ztmp"]
        for h2 in range(2):
            h = hp * 2 + h2
            for e_ in range(8):
                p.op("vector", lambda e, h2=h2, h=h, e_=e_: e.tensor_mul(
                    out=ztmp[:].rearrange("p (t d) -> p t d", d=64),
                    in0=Opair[:, :, h2 * 64:(h2 + 1) * 64],
                    in1=MASK[:, :, h * 8 + e_:h * 8 + e_ + 1]
                    .broadcast_to([128, NT, 64])),
                    reads=[Opair, MASK], writes=[ztmp])
                p.op("gpsimd", lambda e, e_=e_: e.tensor_add(
                    out=zacc[:].rearrange(
                        "p t (v d) -> p t v d", d=64)[:, :, e_, :],
                    in0=zacc[:].rearrange(
                        "p t (v d) -> p t v d", d=64)[:, :, e_, :],
                    in1=ztmp[:].rearrange("p (t d) -> p t d", d=64)),
                    reads=[ztmp, zacc], writes=[zacc])

    ctx.free("l2b_VA", "l2b_Qp", "l2b_Kp", "fl_PT0", "fl_PT1",
             "fl_rc_a", "fl_rc_b", "l2b_Opair", "l2b_wT",
             "l2b_PV", "l2b_MIX", "l2b_MASK", "l2b_ztmp")

    zbf = ctx.alloc("l2b_zbf", [128, NT, 512], BF16)
    p.op("vector", lambda e: e.tensor_copy(out=zbf[:], in_=zacc[:]),
         reads=[zacc], writes=[zbf])
    ctx.free("l2b_zacc")
    z_fm = ctx.alloc("l2b_zfm", [128, 4, T], BF16)
    for tb in range(NT):
        for pc in range(4):
            pe_transpose(ctx, z_fm[:, pc, tb * 128:(tb + 1) * 128],
                         zbf[:, tb, pc * 128:(pc + 1) * 128], 128, 128)
    ctx.free("l2b_zbf")
    woT2 = ctx.alloc("l2b_woT2", [128, 4, 1024], BF16)
    wpc4 = ctx.alloc("l2b_wpc4", [128, 8, 64], BF16)
    for e_ in range(8):
        sv3 = ws[:, 0:512].rearrange("p (c j) -> p c j", j=64)
        p.dma("sync", lambda e, e_=e_, sv3=sv3: e.dma_start(
            out=sv3, in_=ctx.hbm["L2_ffn_Wo"][e_].rearrange(
                "(c p) j -> p c j", p=128)), writes=[ws])
        p.op("gpsimd", lambda e, sv3=sv3: e.tensor_copy(
            out=wpc4[:], in_=sv3), reads=[ws], writes=[wpc4])
        for r in range(8):
            pe_transpose(ctx, woT2[(e_ % 2) * 64:(e_ % 2) * 64 + 64,
                                   e_ // 2, r * 128:(r + 1) * 128],
                         wpc4[:, r, :], 128, 64)
    x = ctx.sb["x_tm"]
    gt = ctx.sb["scratch_f32_b"]
    ctx.sb["scratch_f32_c"] = ctx.alloc("l2b_scrc", [128, D], F32)
    st = ctx.sb["scratch_f32_c"]
    for tb in range(NT):
        for half in range(2):
            esl = slice(half * 512, (half + 1) * 512)
            for pc in range(4):
                p.op("tensor", lambda e, tb=tb, pc=pc, esl=esl: e.matmul(
                    ps_m2[:, :], z_fm[:, pc, tb * 128:(tb + 1) * 128],
                    woT2[:, pc, esl], start=(pc == 0), stop=(pc == 3)),
                    reads=[z_fm, woT2], writes=[ps_m2])
            p.op("scalar", lambda e, esl=esl: e.activation(
                out=gt[:, esl], in_=ps_m2[:, :], func=AF.Silu),
                reads=[ps_m2], writes=[gt])
        p.dma("sync", lambda e, tb=tb: e.dma_start(
            out=st[:], in_=s_state.rearrange(
                "(i p) d -> p i d", p=128)[:, tb, :]),
            reads=["s_state_d"], writes=[st])
        p.op("vector", lambda e: e.tensor_mul(
            out=gt[:], in0=gt[:], in1=st[:]), reads=[gt, st], writes=[gt])
        p.op("vector", lambda e, tb=tb: e.tensor_add(
            out=x[:, tb, :], in0=x[:, tb, :], in1=gt[:]),
            reads=[gt, x], writes=[x])
    ctx.free(*[n for n in list(ctx.live) if n.startswith("l2b_")])



def emit_quant(ctx, dst_bf, src_f32, width, tag):
    p = ctx.p
    nb = width // 16
    q = ctx.sb["l3_qs"]
    qr = ctx.sb["l3_qr"]
    ab = ctx.sb["l3_qab"]
    g1 = ctx.sb["l3_qg1"]
    g2 = ctx.sb["l3_qg2"]
    sg = ctx.sb["l3_qsg"]
    v3 = lambda t, w=width: t[0:128, 0:w].rearrange("p (b k) -> p b k", k=16)
    p.op("vector", lambda e: e.tensor_reduce(
        out=q[:, 0:nb], in_=src_f32.rearrange("p (b k) -> p b k", k=16),
        axis=AX.X, op=ALU.max, apply_absolute_value=True),
        reads=[src_f32], writes=[q])
    p.op("vector", lambda e: e.tensor_scalar(
        out=q[:, 0:nb], in0=q[:, 0:nb], scalar1=1.0 / 6.0, scalar2=1e-12,
        op0=ALU.mult, op1=ALU.max), reads=[q], writes=[q])
    p.op("vector", lambda e: e.reciprocal(out=qr[:, 0:nb], in_=q[:, 0:nb]),
         reads=[q], writes=[qr])
    p.op("scalar", lambda e: e.activation(
        out=ab[:, 0:width], in_=src_f32, func=AF.Abs),
        reads=[src_f32], writes=[ab])
    p.op("scalar", lambda e: e.activation(
        out=sg[:, 0:width], in_=src_f32, func=AF.Sign),
        reads=[src_f32], writes=[sg])
    p.op("vector", lambda e: e.tensor_tensor(
        out=v3(ab), in0=v3(ab),
        in1=qr[:, 0:nb, None].broadcast_to([128, nb, 16]), op=ALU.mult),
        reads=[ab, qr], writes=[ab])
    p.op("vector", lambda e: e.tensor_scalar(
        out=g1[:, 0:width], in0=ab[:, 0:width], scalar1=0.25, scalar2=None,
        op0=ALU.is_gt), reads=[ab], writes=[g1])
    for m in (0.75, 1.25, 1.75):
        p.op("vector", lambda e, m=m: e.scalar_tensor_tensor(
            out=g1[:, 0:width], in0=ab[:, 0:width], scalar=m,
            in1=g1[:, 0:width], op0=ALU.is_gt, op1=ALU.add),
            reads=[ab, g1], writes=[g1])
    p.op("vector", lambda e: e.tensor_scalar(
        out=g2[:, 0:width], in0=ab[:, 0:width], scalar1=2.5, scalar2=None,
        op0=ALU.is_gt), reads=[ab], writes=[g2])
    p.op("vector", lambda e: e.scalar_tensor_tensor(
        out=g2[:, 0:width], in0=ab[:, 0:width], scalar=3.5,
        in1=g2[:, 0:width], op0=ALU.is_gt, op1=ALU.add),
        reads=[ab, g2], writes=[g2])
    p.op("vector", lambda e: e.tensor_scalar(
        out=ab[:, 0:width], in0=ab[:, 0:width], scalar1=5.0, scalar2=2.0,
        op0=ALU.is_gt, op1=ALU.mult), reads=[ab], writes=[ab])
    p.op("gpsimd", lambda e: e.tensor_add(
        out=g2[:, 0:width], in0=g2[:, 0:width], in1=ab[:, 0:width]),
        reads=[ab, g2], writes=[g2])
    p.op("vector", lambda e: e.scalar_tensor_tensor(
        out=g1[:, 0:width], in0=g1[:, 0:width], scalar=0.5,
        in1=g2[:, 0:width], op0=ALU.mult, op1=ALU.add),
        reads=[g1, g2], writes=[g1])
    p.op("gpsimd", lambda e: e.tensor_mul(
        out=g1[:, 0:width], in0=g1[:, 0:width], in1=sg[:, 0:width]),
        reads=[g1, sg], writes=[g1])
    p.op("vector", lambda e: e.tensor_tensor(
        out=dst_bf.rearrange("p (b k) -> p b k", k=16), in0=v3(g1),
        in1=q[:, 0:nb, None].broadcast_to([128, nb, 16]), op=ALU.mult),
        reads=[g1, q], writes=[dst_bf])


def emit_layer3(ctx):
    p = ctx.p
    nc = ctx.nc
    emit_ln(ctx)
    xn_fm = ctx.sb["xn_fm"]
    ident = ctx.consts["ident"]
    onesrow = ctx.consts["onesrow"]
    ones512 = ctx.consts["ones512"]
    ps_m1 = ctx.sb["ps_m1"]
    ps_m2 = ctx.sb["ps_m2"]
    ws = ctx.sb["wstage"]

    ctx.alloc("l3_qs", [128, 176], F32)
    ctx.alloc("l3_qr", [128, 176], F32)
    ctx.alloc("l3_qab", [128, 2816], F32)
    ctx.alloc("l3_qg1", [128, 2816], F32)
    ctx.alloc("l3_qg2", [128, 2816], F32)
    ctx.alloc("l3_qsg", [128, 2816], BF16)

    h_dram = nc.dram_tensor("l3_hq_fm", [22 * 128, T], BF16,
                            kind="Internal").ap()
    wdqT_dram = nc.dram_tensor("l3_wdqT", [22 * 128, 1024], BF16,
                               kind="Internal").ap()
    ypart_dram = nc.dram_tensor("l3_ypart", [T, D], F32,
                                kind="Internal").ap()
    g_dram = nc.dram_tensor("l3_g", [T, D], F32, kind="Internal").ap()

    xq_fm = ctx.alloc("l3_xq_fm", [128, ND, T], BF16)
    xnr = ctx.sb["scratch_f32_a"]
    xq_t = ctx.sb["stage_bf"]
    for tb in range(NT):
        xn_rec(ctx, tb, xnr[:])
        emit_quant(ctx, xq_t[:], xnr[:], D, f"xq{tb}")
        for c in range(ND):
            pe_transpose(ctx, xq_fm[:, c, tb * 128:(tb + 1) * 128],
                         xq_t[:, c * 128:(c + 1) * 128], 128, 128)

    wq_f = ctx.alloc("l3_wq_f", [128, 1024], BF16)
    wu_f = ctx.alloc("l3_wu_f", [128, 1024], BF16)
    wqT = ctx.alloc("l3_wqT", [128, 8, 128], BF16)
    wuT = ctx.alloc("l3_wuT", [128, 8, 128], BF16)
    hslab = ctx.alloc("l3_hslab", [128, NT, 128], BF16)
    hf32 = ctx.alloc("l3_hf32", [128, 128], F32)
    ps_g = ctx.sb["ps_ava"]
    ps_u = ctx.sb["ps_avb"]
    for r in range(22):
        for wname, wqt, wqc in (("L3_ffn_w_gate", wqT, wq_f),
                                ("L3_ffn_w_up", wuT, wu_f)):
            sv = ws[:, 0:1024]
            hb = ctx.hbm[wname][r * 128:(r + 1) * 128, :]
            p.dma("sync", lambda e, hb=hb, sv=sv: e.dma_start(
                out=sv, in_=hb), writes=[ws])
            emit_quant(ctx, wqc[:], sv, 1024, f"w{r}")
            for c in range(8):
                pe_transpose(ctx, wqt[:, c, :],
                             wqc[:, c * 128:(c + 1) * 128], 128, 128)
        for tb in range(NT):
            for c in range(8):
                p.op("tensor", lambda e, tb=tb, c=c: e.matmul(
                    ps_g[:, 0:128], xq_fm[:, c, tb * 128:(tb + 1) * 128],
                    wqT[:, c, :], start=(c == 0), stop=(c == 7)),
                    reads=[xq_fm, wqT], writes=[ps_g])
            for c in range(8):
                p.op("tensor", lambda e, tb=tb, c=c: e.matmul(
                    ps_u[:, 0:128], xq_fm[:, c, tb * 128:(tb + 1) * 128],
                    wuT[:, c, :], start=(c == 0), stop=(c == 7)),
                    reads=[xq_fm, wuT], writes=[ps_u])
            p.op("scalar", lambda e: e.activation(
                out=hf32[:], in_=ps_g[:, 0:128], func=AF.Silu),
                reads=[ps_g], writes=[hf32])
            p.op("vector", lambda e: e.tensor_tensor(
                out=hf32[:], in0=hf32[:], in1=ps_u[:, 0:128], op=ALU.mult),
                reads=[hf32, ps_u], writes=[hf32])
            emit_quant(ctx, xq_t[:, 0:128], hf32[:], 128, f"h{r}_{tb}")
            pe_transpose(ctx, hslab[:, tb, :], xq_t[:, 0:128], 128, 128)
        p.dma("sync", lambda e, r=r: e.dma_start(
            out=h_dram[r * 128:(r + 1) * 128, :],
            in_=hslab[:].rearrange("p t k -> p (t k)")), reads=[hslab],
            writes=["l3_hq_d"], )
    ctx.free("l3_wq_f", "l3_wu_f", "l3_wqT", "l3_wuT", "l3_hslab",
             "l3_hf32")

    wd_f = ctx.alloc("l3_wd_f", [128, 2816], BF16)
    for r in range(8):
        for c0, cw in ((0, 1024), (1024, 1024), (2048, 768)):
            sv = ws[:, 0:cw]
            hb1 = ctx.hbm["L3_ffn_w_down"][r * 128:(r + 1) * 128,
                                           c0:c0 + cw]
            p.dma("sync", lambda e, hb1=hb1, sv=sv: e.dma_start(
                out=sv, in_=hb1), writes=[ws])
            emit_quant(ctx, wd_f[:, c0:c0 + cw], sv, cw, f"wd{r}_{c0}")
        for kc in range(22):
            pe_transpose(ctx, xq_t[:, 0:128],
                         wd_f[:, kc * 128:(kc + 1) * 128], 128, 128)
            p.dma("sync", lambda e, kc=kc, r=r: e.dma_start(
                out=wdqT_dram[kc * 128:(kc + 1) * 128,
                              r * 128:(r + 1) * 128],
                in_=xq_t[:, 0:128]), reads=[xq_t], writes=["l3_wdq_d"])
    ctx.free("l3_wd_f", "l3_xq_fm")
    ctx.free("l3_qab", "l3_qg1", "l3_qg2", "l3_qsg", "l3_qs", "l3_qr")

    hq = ctx.alloc("l3_hq", [128, 11, T], BF16)
    wdT = ctx.alloc("l3_wdT", [128, 11, 1024], BF16)
    yt = ctx.sb["scratch_f32_b"]
    for half in range(2):
        k0 = half * 11
        for kc in range(11):
            p.dma("sync", lambda e, kc=kc, k0=k0: e.dma_start(
                out=hq[:, kc, :],
                in_=h_dram[(k0 + kc) * 128:(k0 + kc + 1) * 128, :]),
                reads=["l3_hq_d"], writes=[hq])
            p.dma("sync", lambda e, kc=kc, k0=k0: e.dma_start(
                out=wdT[:, kc, :],
                in_=wdqT_dram[(k0 + kc) * 128:(k0 + kc + 1) * 128, :]),
                reads=["l3_wdq_d"], writes=[wdT])
        for tb in range(NT):
            if half == 1:
                p.dma("sync", lambda e, tb=tb: e.dma_start(
                    out=yt[:], in_=ypart_dram.rearrange(
                        "(i p) d -> p i d", p=128)[:, tb, :]),
                    reads=["l3_yp_d"], writes=[yt])
            for eh in range(2):
                esl = slice(eh * 512, (eh + 1) * 512)
                for kc in range(11):
                    p.op("tensor", lambda e, tb=tb, kc=kc, esl=esl: e.matmul(
                        ps_m2[:, :], hq[:, kc, tb * 128:(tb + 1) * 128],
                        wdT[:, kc, esl], start=(kc == 0), stop=(kc == 10)),
                        reads=[hq, wdT], writes=[ps_m2])
                if half == 0:
                    p.op("vector", lambda e, esl=esl: e.tensor_copy(
                        out=yt[:, esl], in_=ps_m2[:, :]),
                        reads=[ps_m2], writes=[yt])
                else:
                    p.op("vector", lambda e, esl=esl: e.tensor_add(
                        out=yt[:, esl], in0=yt[:, esl], in1=ps_m2[:, :]),
                        reads=[ps_m2, yt], writes=[yt])
            if half == 0:
                p.dma("sync", lambda e, tb=tb: e.dma_start(
                    out=ypart_dram.rearrange(
                        "(i p) d -> p i d", p=128)[:, tb, :], in_=yt[:]),
                    reads=[yt], writes=["l3_yp_d"])
            else:
                p.op("scalar", lambda e: e.activation(
                    out=yt[:], in_=yt[:], func=AF.Silu),
                    reads=[yt], writes=[yt])
                p.dma("sync", lambda e, tb=tb: e.dma_start(
                    out=g_dram.rearrange(
                        "(i p) d -> p i d", p=128)[:, tb, :], in_=yt[:]),
                    reads=[yt], writes=["l3_g_d"])
    ctx.free("l3_hq", "l3_wdT")

    triu = ctx.consts["tril_incl"]
    trilx = ctx.consts["trilx"]
    wk_bf = ctx.alloc("l3_wk_bf", [128, 8, 256], BF16)
    wv_bf = ctx.alloc("l3_wv_bf", [128, 8, 256], BF16)
    for wname, wbf in (("L3_attn_Wk", wk_bf), ("L3_attn_Wv", wv_bf)):
        for cp in range(2):
            sv = ws[:, 0:1024].rearrange("p (c m) -> p c m", c=4)
            hb = ctx.hbm[wname].rearrange(
                "(c p) m -> p c m", p=128)[:, cp * 4:(cp + 1) * 4, :]
            p.dma("sync", lambda e, hb=hb, sv=sv: e.dma_start(
                out=sv, in_=hb), writes=[ws])
            p.op("gpsimd", lambda e, wbf=wbf, sv=sv, cp=cp: e.tensor_copy(
                out=wbf[:, cp * 4:(cp + 1) * 4, :], in_=sv),
                reads=[ws], writes=[wbf])
    bias_rows = ctx.alloc("l3_bias_rows", [1, 2048 + 1024], BF16)
    for nm, off, w in (("L3_attn_bq", 0, 1024), ("L3_attn_bk", 1024, 256),
                       ("L3_attn_bv", 1280, 256), ("L3_attn_bo", 2048, 1024)):
        load_convert(ctx, bias_rows[:, off:off + w],
                     ctx.hbm[nm][None, :], 1, w)
    wkd = ctx.alloc("l3_wkd", [128, 8, 128], BF16)
    Kd = ctx.alloc("l3_Kd", [128, 4, T], BF16)
    Vx = ctx.alloc("l3_Vx", [128, NT, 260], BF16)
    bkd = ctx.alloc("l3_bkd", [1, 128], BF16)
    for g in range(4):
        for c in range(8):
            for dup in range(2):
                p.op("vector", lambda e, g=g, c=c, dup=dup: e.tensor_copy(
                    out=wkd[:, c, dup * 64:(dup + 1) * 64],
                    in_=wk_bf[:, c, g * 64:(g + 1) * 64]),
                    reads=[wk_bf], writes=[wkd])
        for dup in range(2):
            p.op("vector", lambda e, g=g, dup=dup: e.tensor_copy(
                out=bkd[:, dup * 64:(dup + 1) * 64],
                in_=bias_rows[:, 1024 + g * 64:1024 + (g + 1) * 64]),
                reads=[bias_rows], writes=[bkd])
        for tb in range(4):
            tsl = slice(tb * 512, (tb + 1) * 512)
            for c in range(8):
                p.op("tensor", lambda e, tsl=tsl, c=c: e.matmul(
                    ps_m2[:, :], wkd[:, c, :], xn_fm[:, c, tsl],
                    start=(c == 0), stop=False),
                    reads=[wkd, xn_fm], writes=[ps_m2])
            p.op("tensor", lambda e, tsl=tsl: e.matmul(
                ps_m2[:, :], bkd[:], ones512[:], start=False, stop=True),
                reads=[bkd, "c_ones512"], writes=[ps_m2])
            p.op("vector", lambda e, g=g, tsl=tsl: e.tensor_copy(
                out=Kd[:, g, tsl], in_=ps_m2[:, :]),
                reads=[ps_m2], writes=[Kd])
    for tb in range(NT):
        for c in range(8):
            p.op("tensor", lambda e, tb=tb, c=c: e.matmul(
                ps_m2[:, 0:256], xn_fm[:, c, tb * 128:(tb + 1) * 128],
                wv_bf[:, c, :], start=(c == 0), stop=False),
                reads=[wv_bf, xn_fm], writes=[ps_m2])
        p.op("tensor", lambda e, tb=tb: e.matmul(
            ps_m2[:, 0:256], onesrow[:], bias_rows[:, 1280:1536],
            start=False, stop=True),
            reads=[bias_rows, "c_onesrow"], writes=[ps_m2])
        p.op("vector", lambda e, tb=tb: e.tensor_copy(
            out=Vx[:, tb, :].rearrange("p (g u) -> p g u", u=65)[:, :, 0:64],
            in_=ps_m2[:, 0:256].rearrange("p (g u) -> p g u", u=64)),
            reads=[ps_m2], writes=[Vx])
    p.op("vector", lambda e: e.memset(
        Vx[:].rearrange("p t (g u) -> p t g u", u=65)[:, :, :, 64:65], 1.0),
        writes=[Vx])

    ctx.free("l3_wk_bf", "l3_wv_bf", "l3_wkd", "l3_bkd")
    O_fm = ctx.alloc("l3_O_fm", [128, ND, T], BF16)
    Qp = ctx.alloc("l3_Qp", [128, T], BF16)
    wq_p = ctx.alloc("l3_wq_p", [128, 8, 128], BF16)
    ctx.alloc("fl_PT0", [128, 16, 256], BF16)
    ctx.alloc("fl_PT1", [128, 16, 256], BF16)
    ctx.alloc("fl_rc_a", [128, 1], F32)
    ctx.alloc("fl_rc_b", [128, 1], F32)
    nsc = ctx.alloc("l3_nsc", [1, 64], F32)
    nsc_bf = ctx.alloc("l3_nscbf", [1, 2], BF16)
    bias_a = ctx.alloc("l3_bias_a", [128, 1], F32)
    bias_b = ctx.alloc("l3_bias_b", [128, 1], F32)
    stageO = ctx.alloc("l3_stageO", [128, 128], BF16)
    sqst = ctx.sb["stage_bf"]
    ps_ava = ctx.sb["ps_ava"]
    ps_avb = ctx.sb["ps_avb"]
    onesA, onesB = ctx.consts["onesA"], ctx.consts["onesB"]

    for hp in range(8):
        g = (2 * hp) // 4
        sv = ws[:, 0:1024].rearrange("p (c m) -> p c m", c=8)
        hb = ctx.hbm["L3_attn_Wq"][:, hp * 128:(hp + 1) * 128].rearrange(
            "(c p) m -> p c m", p=128)
        p.dma("sync", lambda e, hb=hb, sv=sv: e.dma_start(out=sv, in_=hb),
              writes=[ws])
        p.op("gpsimd", lambda e, sv=sv: e.tensor_copy(
            out=wq_p[:], in_=sv), reads=[ws], writes=[wq_p])
        for tb in range(4):
            tsl = slice(tb * 512, (tb + 1) * 512)
            for c in range(8):
                p.op("tensor", lambda e, tsl=tsl, c=c: e.matmul(
                    ps_m2[:, :], wq_p[:, c, :], xn_fm[:, c, tsl],
                    start=(c == 0), stop=False),
                    reads=[wq_p, xn_fm], writes=[ps_m2])
            p.op("tensor", lambda e, tsl=tsl, hp=hp: e.matmul(
                ps_m2[:, :], bias_rows[:, hp * 128:(hp + 1) * 128],
                ones512[:], start=False, stop=True),
                reads=[bias_rows, "c_ones512"], writes=[ps_m2])
            p.op("vector", lambda e, tsl=tsl: e.tensor_copy(
                out=Qp[:, tsl], in_=ps_m2[:, :]), reads=[ps_m2], writes=[Qp])
        for si in range(2):
            for tb in range(4):
                sl = slice(tb * 512, (tb + 1) * 512)
                if si == 0:
                    p.op("scalar", lambda e, sl=sl: e.activation(
                        out=sqst[:, 0:512], in_=Qp[:, sl], func=AF.Square),
                        reads=[Qp], writes=[sqst])
                else:
                    p.op("scalar", lambda e, sl=sl, g=g: e.activation(
                        out=sqst[:, 0:512], in_=Kd[:, g, sl],
                        func=AF.Square), reads=[Kd], writes=[sqst])
                for h2, ones in enumerate((onesA, onesB)):
                    p.op("tensor", lambda e, ones=ones: e.matmul(
                        ps_m1[0:1, :], ones[:], sqst[:, 0:512],
                        start=True, stop=True),
                        reads=[sqst, ones], writes=[ps_m1])
                    o = (si * 2 + h2) * 4 + tb
                    p.op("vector", lambda e, o=o: e.reduce_max(
                        out=nsc[:, o:o + 1], in_=ps_m1[0:1, :], axis=AX.X),
                        reads=[ps_m1], writes=[nsc])
        for h2 in range(2):
            o = 16 + h2 * 8
            qs4 = slice(h2 * 4, h2 * 4 + 4)
            ks4 = slice(8 + h2 * 4, 8 + h2 * 4 + 4)
            p.op("vector", lambda e, o=o, qs4=qs4: e.reduce_max(
                out=nsc[:, o:o + 1], in_=nsc[:, qs4], axis=AX.X),
                reads=[nsc], writes=[nsc])
            p.op("vector", lambda e, o=o, ks4=ks4: e.reduce_max(
                out=nsc[:, o + 1:o + 2], in_=nsc[:, ks4], axis=AX.X),
                reads=[nsc], writes=[nsc])
            p.op("vector", lambda e, o=o: e.tensor_mul(
                out=nsc[:, o + 2:o + 3], in0=nsc[:, o:o + 1],
                in1=nsc[:, o + 1:o + 2]), reads=[nsc], writes=[nsc])
            p.op("scalar", lambda e, o=o: e.activation(
                out=nsc[:, o + 3:o + 4], in_=nsc[:, o + 2:o + 3],
                func=AF.Sqrt), reads=[nsc], writes=[nsc])
            p.op("vector", lambda e, o=o: e.tensor_scalar_mul(
                out=nsc[:, o + 4:o + 5], in0=nsc[:, o + 3:o + 4],
                scalar1=-0.125), reads=[nsc], writes=[nsc])
        p.op("vector", lambda e: e.tensor_copy(out=nsc_bf[:, 0:1],
                                               in_=nsc[:, 20:21]),
             reads=[nsc], writes=[nsc_bf])
        p.op("vector", lambda e: e.tensor_copy(out=nsc_bf[:, 1:2],
                                               in_=nsc[:, 28:29]),
             reads=[nsc], writes=[nsc_bf])
        for biast, cs in ((bias_a, slice(0, 1)), (bias_b, slice(1, 2))):
            p.op("tensor", lambda e, cs=cs: e.matmul(
                ps_m1[:, 0:1], onesrow[:], nsc_bf[:, cs],
                start=True, stop=True),
                reads=[nsc_bf, onesrow], writes=[ps_m1])
            p.op("vector", lambda e, biast=biast: e.tensor_copy(
                out=biast[:], in_=ps_m1[:, 0:1]),
                reads=[ps_m1], writes=[biast])
        ctx.sb["fl_K0"] = Kd
        ctx.sb["fl_K1"] = Kd
        ctx.sb["fl_V"] = Vx

        def l3_sink(qtile, psa, psb, rca, rcb, hp=hp):
            p.op("vector", lambda e: e.tensor_scalar(
                out=stageO[:, 0:64], in0=psa[:, 0:64], scalar1=rca[:],
                scalar2=None, op0=ALU.mult),
                reads=[psa, rca], writes=[stageO])
            p.op("vector", lambda e: e.tensor_scalar(
                out=stageO[:, 64:128], in0=psb[:, 0:64], scalar1=rcb[:],
                scalar2=None, op0=ALU.mult),
                reads=[psb, rcb], writes=[stageO])
            pe_transpose(ctx, O_fm[:, hp, qtile * 128:(qtile + 1) * 128],
                         stageO[:], 128, 128)

        def l3_kts(qb):
            return [2 * qb + d for d in range(-4, 2) if 2 * qb + d >= 0]

        def l3_mask(qb, kt, qs):
            d = kt - 2 * qb
            if d == qs:
                return triu
            if d > qs:
                return "skip"
            if d == qs - 4:
                return trilx
            if d < qs - 4:
                return "skip"
            return None

        flash256(ctx, Qp,
                 lambda h2, ksl, g=g: Kd[h2 * 64:(h2 + 1) * 64, g, ksl],
                 lambda h2, kt, g=g: Vx[:, kt, g * 65:(g + 1) * 65],
                 bias_a, bias_b, l3_sink, l3_kts, l3_mask)

    ctx.tap_full("l3_O", O_fm)
    ctx.tap_full("l3_Kd", Kd)
    ctx.tap_full("l3_Qp", Qp)
    ctx.free("l3_Qp", "l3_wq_p", "fl_PT0", "fl_PT1", "fl_rc_a",
             "fl_rc_b", "l3_Kd", "l3_Vx", "l3_stageO")

    wo_bf = load_weight_bf16(ctx, "l3_wo", ctx.hbm["L3_attn_Wo"], 1024, 1024)
    x = ctx.sb["x_tm"]
    ctx.sb["scratch_f32_c"] = ctx.alloc("l3_scrc", [128, D], F32)
    st = ctx.sb["scratch_f32_c"]
    bo_brow = ctx.alloc("l3_bo_brow", [128, D], F32)
    row_bcast(ctx, bo_brow[:], ctx.sb["l3_bias_rows"][:, 2048:3072], D,
              ["l3_bias_rows"])
    gt = ctx.sb["scratch_f32_b"]
    s_tap = None
    if "l3_s" in ctx.dev_taps:
        s_tap = ctx.nc.dram_tensor("tap_l3_s", [128, NT, D], F32,
                                   kind="ExternalOutput").ap()
        ctx.taps["l3_s"] = [128, NT, D]
    if "l3_g" in ctx.dev_taps:
        gt_tap = ctx.nc.dram_tensor("tap_l3_g", [T, D], F32,
                                    kind="ExternalOutput").ap()
        ctx.taps["l3_g"] = [T, D]
        p.dma("sync", lambda e: e.dma_start(out=gt_tap, in_=g_dram),
              reads=["l3_g_d"], is_out=True)
    for tb in range(NT):
        p.dma("sync", lambda e, tb=tb: e.dma_start(
            out=gt[:], in_=g_dram.rearrange(
                "(i p) d -> p i d", p=128)[:, tb, :]),
            reads=["l3_g_d"], writes=[gt])
        for half in range(2):
            esl = slice(half * 512, (half + 1) * 512)
            for c in range(8):
                p.op("tensor", lambda e, tb=tb, c=c, esl=esl: e.matmul(
                    ps_m2[:, :], O_fm[:, c, tb * 128:(tb + 1) * 128],
                    wo_bf[:, c, esl], start=(c == 0), stop=(c == 7)),
                    reads=[O_fm, wo_bf], writes=[ps_m2])
            p.op("vector", lambda e, esl=esl: e.scalar_tensor_tensor(
                out=st[:, esl], in0=ps_m2[:, :], scalar=1.0,
                in1=bo_brow[:, esl], op0=ALU.mult, op1=ALU.add),
                reads=[ps_m2, bo_brow], writes=[st])
        if s_tap is not None:
            p.dma("sync", lambda e, tb=tb: e.dma_start(
                out=s_tap[:, tb, :], in_=st[:]), reads=[st], is_out=True)
        p.op("vector", lambda e: e.tensor_mul(
            out=gt[:], in0=gt[:], in1=st[:]), reads=[gt, st], writes=[gt])
        p.op("vector", lambda e, tb=tb: e.tensor_add(
            out=x[:, tb, :], in0=x[:, tb, :], in1=gt[:]),
            reads=[gt, x], writes=[x])
    ctx.free(*[n for n in list(ctx.live) if n.startswith("l3_")])


def make_consts():
    ident = np.eye(128).astype(BF)
    onesrow = np.ones((1, 128), dtype=BF)
    onesA = np.zeros((128, 1), dtype=BF)
    onesA[0:64] = 1
    onesB = np.zeros((128, 1), dtype=BF)
    onesB[64:128] = 1
    e16sel = np.zeros((8, 128), dtype=BF)
    for e in range(8):
        e16sel[e, e * 16:(e + 1) * 16] = 1
    ones512 = np.ones((1, 512), dtype=BF)
    tril_incl = np.triu(np.ones((128, 128))).astype(BF)
    trilx = (np.tril(np.ones((128, 128)), -1)).astype(BF)
    grp8 = np.full((64, 8), 0.125, dtype=BF)
    g8 = np.zeros((64, 8), dtype=BF)
    for c in range(8):
        g8[c * 8:(c + 1) * 8, c] = 0.125
    return dict(ident=ident, onesrow=onesrow, onesA=onesA, onesB=onesB,
                e16sel=e16sel, ones512=ones512, grp8=g8,
                tril_incl=tril_incl, trilx=trilx)


def make_consts32():
    iota8m9 = np.zeros((128, 128), dtype=np.float32)
    for j in range(128):
        iota8m9[:, j] = (j % 8) - 9.0
    iotacol = np.arange(128, dtype=np.float32).reshape(128, 1)
    return dict(iota8m9=iota8m9, iotacol=iotacol)


WEIGHT_SPECS = {
    0: ["L0_attn_Wq", "L0_attn_Wk", "L0_attn_Wv", "L0_attn_Wout",
        "L0_ffn_router_w", "L0_ffn_lora_down", "L0_ffn_lora_up"],
    1: ["L1_attn_log_A_real", "L1_attn_log_A_imag", "L1_attn_B_w",
        "L1_attn_C_w", "L1_attn_D_w", "L1_ffn_signatures_raw",
        "L1_ffn_knot_values", "L1_ffn_temperature", "L1_ffn_compress_w1",
        "L1_ffn_compress_b1", "L1_ffn_compress_w2", "L1_ffn_compress_b2",
        "L1_ffn_spline_w1", "L1_ffn_spline_b1", "L1_ffn_spline_w2",
        "L1_ffn_spline_b2", "L1_ffn_directions", "L1_ffn_norm_gamma",
        "L1_ffn_norm_beta", "L1_ffn_output_scale"],
    "2a": ["L2_attn_Wq", "L2_attn_Wk", "L2_attn_Wv", "L2_attn_conv_q",
           "L2_attn_conv_k", "L2_attn_conv_v", "L2_attn_W_init",
           "L2_attn_ln_gamma", "L2_attn_ln_beta", "L2_attn_Wg",
           "L2_attn_Wo"],
    "2b": ["L2_ffn_Wq", "L2_ffn_Wk", "L2_ffn_Wv", "L2_ffn_Ws", "L2_ffn_Wd",
           "L2_ffn_Wo"],
    3: ["L3_attn_Wq", "L3_attn_bq", "L3_attn_Wk", "L3_attn_bk",
        "L3_attn_Wv", "L3_attn_bv", "L3_attn_Wo", "L3_attn_bo",
        "L3_ffn_w_gate", "L3_ffn_w_up", "L3_ffn_w_down"],
}

WEIGHT_SHAPES = {
    "L0_attn_Wq": (1024, 1024), "L0_attn_Wk": (1024, 1024),
    "L0_attn_Wv": (1024, 1024), "L0_attn_Wout": (1024, 1024),
    "L0_ffn_router_w": (8, 1024), "L0_ffn_lora_down": (8, 16, 1024),
    "L0_ffn_lora_up": (8, 1024, 16),
    "L1_attn_log_A_real": (64,), "L1_attn_log_A_imag": (64,),
    "L1_attn_B_w": (128, 1024), "L1_attn_C_w": (1024, 128),
    "L1_attn_D_w": (1024, 1024), "L1_ffn_signatures_raw": (64, 1024),
    "L1_ffn_knot_values": (8,), "L1_ffn_temperature": (1,),
    "L1_ffn_compress_w1": (1024, 256), "L1_ffn_compress_b1": (256,),
    "L1_ffn_compress_w2": (256, 2), "L1_ffn_compress_b2": (2,),
    "L1_ffn_spline_w1": (64, 2, 16), "L1_ffn_spline_b1": (64, 16),
    "L1_ffn_spline_w2": (64, 16, 1), "L1_ffn_spline_b2": (64, 1),
    "L1_ffn_directions": (64, 1024), "L1_ffn_norm_gamma": (1024,),
    "L1_ffn_norm_beta": (1024,), "L1_ffn_output_scale": (1,),
    "L2_attn_Wq": (1024, 1024), "L2_attn_Wk": (1024, 1024),
    "L2_attn_Wv": (1024, 1024), "L2_attn_conv_q": (1024, 4),
    "L2_attn_conv_k": (1024, 4), "L2_attn_conv_v": (1024, 4),
    "L2_attn_W_init": (1, 16, 64, 64), "L2_attn_ln_gamma": (64,),
    "L2_attn_ln_beta": (64,), "L2_attn_Wg": (1024, 1024),
    "L2_attn_Wo": (1024, 1024),
    "L2_ffn_Wq": (16, 64, 1024), "L2_ffn_Wk": (16, 64, 1024),
    "L2_ffn_Wv": (8, 64, 1024), "L2_ffn_Ws": (16, 8, 1024),
    "L2_ffn_Wd": (16, 8, 1024), "L2_ffn_Wo": (8, 1024, 64),
    "L3_attn_Wq": (1024, 1024), "L3_attn_bq": (1024,),
    "L3_attn_Wk": (1024, 256), "L3_attn_bk": (256,),
    "L3_attn_Wv": (1024, 256), "L3_attn_bv": (256,),
    "L3_attn_Wo": (1024, 1024), "L3_attn_bo": (1024,),
    "L3_ffn_w_gate": (2816, 1024), "L3_ffn_w_up": (2816, 1024),
    "L3_ffn_w_down": (1024, 2816),
}

LAYER_EMITTERS = {0: emit_layer0, 1: emit_layer1,
                  '2a': emit_layer2a, '2b': emit_layer2b,
                  3: emit_layer3}


def build_program(layers, dev_taps=()):
    nc = bass.Bass("TRN2", target_bir_lowering=False, debug=False)
    p = Prog(nc)
    ctx = Ctx(nc, p, dev_taps=dev_taps)
    ctx.fused = len(layers) > 1

    x_in = nc.dram_tensor("x_in", [T, D], F32, kind="ExternalInput").ap()
    x_out = nc.dram_tensor("x_out", [T, D], F32, kind="ExternalOutput").ap()
    for li in layers:
        for w in WEIGHT_SPECS[li]:
            ctx.hbm[w] = nc.dram_tensor(
                w, list(WEIGHT_SHAPES[w]), F32, kind="ExternalInput").ap()
    cvals = make_consts()
    for name, arr in cvals.items():
        ap = nc.dram_tensor(f"const_{name}", list(arr.shape), BF16,
                            kind="ExternalInput").ap()
        sb = ctx.alloc(f"c_{name}", list(arr.shape), BF16)
        p.dma("sync", lambda e, ap=ap, sb=sb: e.dma_start(out=sb[:], in_=ap),
              writes=[sb])
        ctx.consts[name] = sb
    c32 = make_consts32()
    for name, arr in c32.items():
        ap = nc.dram_tensor(f"const32_{name}", list(arr.shape), F32,
                            kind="ExternalInput").ap()
        sb = ctx.alloc(f"c32_{name}", list(arr.shape), F32)
        p.dma("sync", lambda e, ap=ap, sb=sb: e.dma_start(out=sb[:], in_=ap),
              writes=[sb])
        ctx.consts32[name] = sb

    x_tm = ctx.alloc("x_tm", [128, NT, D], F32)
    ctx.alloc("xn_fm", [128, ND, T], BF16)
    ctx.alloc("ln_mu", [128, NT], F32)
    ctx.alloc("ln_rstd", [128, NT], F32)
    ctx.alloc("scratch_f32_a", [128, D], F32)
    ctx.alloc("scratch_f32_b", [128, D], F32)
    ctx.alloc("stage_bf", [128, D], BF16)
    ctx.alloc("scratch_col", [128, 4], F32)
    ctx.alloc("wstage", [128, 1024], F32)
    fc = ctx.alloc("fconsts", [128, 16], F32)
    fc_ap = nc.dram_tensor("const_fconsts", [128, 16], F32,
                           kind="ExternalInput").ap()
    p.dma("sync", lambda e: e.dma_start(out=fc[:], in_=fc_ap),
          writes=[fc])
    ctx.psum("ps_sa0", [128, 512])
    ctx.psum("ps_sa1", [128, 512])
    ctx.psum("ps_sb0", [128, 512])
    ctx.psum("ps_sb1", [128, 512])
    ctx.psum("ps_ava", [128, 512])
    ctx.psum("ps_avb", [128, 512])
    ctx.psum("ps_m1", [128, 512])
    ctx.psum("ps_m2", [128, 512])
    ctx.sb["ps_tr"] = ctx.sb["ps_m1"]

    p.dma("sync", lambda e: e.dma_start(
        out=x_tm[:], in_=x_in.rearrange("(i p) d -> p i d", p=128)),
        writes=[x_tm])

    ctx.start_dynamic()
    for li in layers:
        LAYER_EMITTERS[li](ctx)

    p.dma("sync", lambda e: e.dma_start(
        out=x_out.rearrange("(i p) d -> p i d", p=128), in_=x_tm[:]),
        reads=[x_tm], is_out=True)

    p.finalize()
    return nc, ctx, cvals


def run_layers(inputs, layers, x, dev_taps=(), carry=None):
    carry = carry if carry is not None else [{} for _ in range(8)]
    nc, ctx, cvals = build_program(layers, dev_taps=dev_taps)
    in_maps = []
    for core in range(8):
        m = {"x_in": np.ascontiguousarray(x[core], dtype=np.float32)}
        for name in ctx.extra_in:
            m[name] = carry[core][name]
        for li in layers:
            for w in WEIGHT_SPECS[li]:
                m[w] = np.ascontiguousarray(inputs[w], dtype=np.float32)
        for name, arr in cvals.items():
            m[f"const_{name}"] = arr
        for name, arr in make_consts32().items():
            m[f"const32_{name}"] = arr
        fcarr = np.zeros((128, 16), dtype=np.float32)
        for v, j in ctx.fconst_vals.items():
            fcarr[:, j] = v
        m["const_fconsts"] = fcarr
        in_maps.append(m)
    res = run_bass_kernel_spmd(nc, in_maps, core_ids=list(range(8)))
    xo = np.stack([np.asarray(r["x_out"]) for r in res.results])
    for core in range(8):
        for name in ctx.extra_out:
            carry[core][name] = np.asarray(res.results[core][name])
    taps = {}
    for name in ctx.taps:
        taps[name] = np.stack([np.asarray(r[f"tap_{name}"])
                               for r in res.results])
    return xo, taps


FUSED = True


def kernel(**inputs):
    assert inputs["L0_attn_mask"].shape == (T, T)
    x = np.ascontiguousarray(inputs["L0_attn_x"]).astype(np.float32)
    if FUSED:
        x, _ = run_layers(inputs, [0, 1, "2a", "2b", 3], x)
        return x
    carry = [{} for _ in range(8)]
    for group in ([0], [1], ["2a"], ["2b"], [3]):
        x, _ = run_layers(inputs, group, x, carry=carry)
    return x
